# version 07
# baseline (speedup 1.0000x reference)
import numpy as np
import ml_dtypes
import concourse.bass as bass
import concourse.mybir as mybir
from concourse.bass_utils import run_bass_kernel_spmd

F32 = mybir.dt.float32
BF16 = mybir.dt.bfloat16
I32 = mybir.dt.int32
AF = mybir.ActivationFunctionType
ALU = mybir.AluOpType
AX = mybir.AxisListType
BF16NP = ml_dtypes.bfloat16

B, T, D = 4, 2048, 2048
HEADS, HD = 16, 128
NKV = 4
LAT = 512
DFF = 5504
DFFP = 6144
NTOK = 1024
NTT = 8
RC = 12582912.0
NEGBIG = -3.0e38
NEGM = -3.3e28
EPS_LN = 1e-5
EPS_RMS = 1e-6
EPS_Q = 1e-5
SOFT = 30.0
ISQD = 1.0 / np.sqrt(128.0)

SLOPES = None
def _alibi_slopes(n):
    import math
    def pot(p):
        start = 2.0 ** (-(2.0 ** (-(math.log2(p) - 3))))
        return [start * start ** i for i in range(p)]
    if math.log2(n).is_integer():
        return pot(n)
    cp = 2 ** int(math.floor(math.log2(n)))
    s = pot(cp)
    s += _alibi_slopes(2 * cp)[0::2][: n - cp]
    return s
SLOPES = _alibi_slopes(HEADS)

QCH_KT = [8, 16]
SEL_NCH = [2, 2, 2, 2, 4, 4, 4, 4]


class Stream:
    def __init__(self, nc, name, step=1):
        self._cm = nc.semaphore(name)
        self.sem = self._cm.__enter__()
        self.n = 0
        self.step = step
        self.name = name

    def inc(self, inst, k=None):
        k = self.step if k is None else k
        inst.then_inc(self.sem, k)
        self.n += k
        return self.n


class Ctx:

    def __init__(self, nc, blk):
        self.nc = nc
        self.blk = blk
        self.st = {}
        self.wm = {}

    def stream(self, name, step=1):
        s = Stream(self.nc, name, step)
        self.st[name] = s
        return s

    def wait(self, eng, engname, stname, tok):
        if tok <= 0:
            return
        wm = self.wm.setdefault(engname, {})
        if wm.get(stname, 0) >= tok:
            return
        eng.wait_ge(self.st[stname].sem, tok)
        wm[stname] = tok

    def inc(self, inst, stname, k=None):
        return self.st[stname].inc(inst, k)

    def join(self, e, engname):
        for stn in ("pT", "pV", "pS", "pG"):
            if stn in self.st:
                self.wait(e, engname, stn, self.st[stn].n)

    def dma(self, e, engname, lane, first=True, **kw):
        if lane not in self.st:
            self.st[lane] = Stream(self.nc, lane, 16)
        if first:
            self.wait(e, engname, lane, self.st[lane].n)
        return self.st[lane].inc(e.dma_start(**kw))


def _pos_quarters(core):
    return [0, 3] if core % 2 == 0 else [1, 2]


def _own_positions(core):
    qs = _pos_quarters(core)
    return np.concatenate([np.arange(q * 512, (q + 1) * 512) for q in qs])


def _rope_tables():
    inv = (1.0 / (10000.0 ** (np.arange(0, HD, 2, dtype=np.float32) / HD))).astype(np.float32)
    f = (np.arange(T, dtype=np.float32)[:, None] * inv[None, :]).astype(np.float32)
    idx = (2 * np.arange(64)) % 64
    emb2 = f[:, idx]
    return np.cos(emb2).astype(np.float32), np.sin(emb2).astype(np.float32)


def _host_tables(core):
    qs = _pos_quarters(core)
    ctab, stab = _rope_tables()
    own = _own_positions(core)
    t = {}
    t["rope_c"] = ctab
    t["rope_s"] = stab
    t["rope_own_c"] = ctab[own]
    t["rope_own_s"] = stab[own]
    offsel = np.full((24,), 0.0, np.float32)
    k = 0
    offsel_list = []
    for tt in range(NTT):
        q0 = qs[tt // 4] * 512 + (tt % 4) * 128
        for ch in range(SEL_NCH[tt]):
            offsel_list.append(float(q0 - ch * 512))
    offsel = np.array(offsel_list, np.float32)
    t["offtab_sel"] = np.tile(offsel[None, :], (128, 1)).astype(np.float32)
    offb = []
    for ci in range(2):
        q0 = qs[ci] * 512
        for kt in range(QCH_KT[ci]):
            k0 = kt * 128
            if k0 + 127 < q0:
                offb.append(-1e9)
            elif k0 > q0 + 511:
                offb.append(1e9)
            else:
                offb.append(float(k0 - q0))
    t["offtab_b"] = np.tile(np.array(offb, np.float32)[None, :], (128, 1))
    ab = np.zeros((HEADS, 24), np.float32)
    col = 0
    cols = []
    for ci in range(2):
        q0 = qs[ci] * 512
        for kt in range(QCH_KT[ci]):
            k0 = kt * 128
            cols.append((ci, kt, k0 - q0))
    for h in range(HEADS):
        for j, (ci, kt, dk) in enumerate(cols):
            ab[h, j] = SLOPES[h] * dk
    t["aa_bias"] = np.tile(ab.reshape(1, -1), (128, 1)).astype(np.float32)
    return t


def _shared_consts():
    c = {}
    c["ident32"] = np.eye(128, dtype=np.float32)
    c["ident16"] = np.eye(128, dtype=BF16NP)
    c["ones_col16"] = np.ones((128, 1), BF16NP)
    c["ones_row"] = np.ones((1, 128), np.float32)
    c["half_row"] = np.full((1, 128), 0.5, np.float32)
    return c


PREP_SKIP = set()

def build_program(upto=99):
    import contextlib
    nc = bass.Bass("TRN2", target_bir_lowering=False, num_devices=8)

    dt_in = {}
    def din(name, shape, dtype=F32):
        dt_in[name] = nc.dram_tensor(name, list(shape), dtype, kind="ExternalInput")
        return dt_in[name]

    x_own = din("x_own", [NTOK, D])
    w_q = din("w_q", [D, D])
    b_q = din("b_q", [1, D])
    w_down_a = din("w_down_a", [D, LAT])
    b_down_r = din("b_down_r", [128, 4])
    w_up_a = din("w_up_a", [LAT, 2 * NKV * HD])
    b_up_r = din("b_up_r", [128, 8])
    w_o = din("w_o", [D, D])
    b_o = din("b_o", [1, D])
    qn_w = din("qn_w", [1, HD])
    kn_w = din("kn_w", [1, HD])
    gate_w = din("gate_w", [HEADS, HD, HD])
    gate_b_r = din("gate_b_r", [128, HEADS])
    w_qkv = din("w_qkv", [3 * D, D])
    w_proj = din("w_proj", [D, D])
    w_gate_f = din("w_gate_f", [DFF, D])
    w_up_f = din("w_up_f", [DFF, D])
    w_down_f = din("w_down_f", [D, DFFP])
    rope_c = din("rope_c", [T, 64])
    rope_s = din("rope_s", [T, 64])
    rope_own_c = din("rope_own_c", [NTOK, 64])
    rope_own_s = din("rope_own_s", [NTOK, 64])
    offtab_sel = din("offtab_sel", [128, 24])
    offtab_b = din("offtab_b", [128, 24])
    aa_bias = din("aa_bias", [128, 384])
    ident32_d = din("ident32", [128, 128])
    ident16_d = din("ident16", [128, 128], BF16)
    ones_col16_d = din("ones_col16", [128, 1], BF16)
    ones_row_d = din("ones_row", [1, 128])
    half_row_d = din("half_row", [1, 128])
    mdiv_d = din("mdiv", [1, 2])

    out_t = nc.dram_tensor("out", [NTOK, D], F32, kind="ExternalOutput")

    def dint(name, shape, dtype=F32):
        return nc.dram_tensor(name, list(shape), dtype, kind="Internal")

    xres = [dint("xres0", [NTOK, D]), dint("xres1", [NTOK, D])]
    cc_in_x = dint("cc_in_x", [D, NTOK])
    cc_out_x = dint("cc_out_x", [2 * D, NTOK])
    cc_in_x2 = dint("cc_in_x2", [D, NTOK], BF16)
    cc_out_x2 = dint("cc_out_x2", [2 * D, NTOK], BF16)
    cc_ps_in = dint("cc_ps_in", [1, 512])
    cc_ps_out = dint("cc_ps_out", [1, 512])
    cc_wg_in = dint("cc_wg_in", [256, DFF], BF16)
    cc_wg_out = dint("cc_wg_out", [D, DFF], BF16)
    cc_wu_in = dint("cc_wu_in", [256, DFF], BF16)
    cc_wu_out = dint("cc_wu_out", [D, DFF], BF16)
    cc_wd_in = dint("cc_wd_in", [768, D], BF16)
    cc_wd_out = dint("cc_wd_out", [DFFP, D], BF16)
    cc_wqkv_in = dint("cc_wqkv_in", [256, 3 * D], BF16)
    cc_wqkv_out = dint("cc_wqkv_out", [D, 3 * D], BF16)
    cc_wp_in = dint("cc_wp_in", [256, D], BF16)
    cc_wp_out = dint("cc_wp_out", [D, D], BF16)

    ES = contextlib.ExitStack()
    with ES:
        blk = ES.enter_context(nc.Block())
        k = Ctx(nc, blk)
        for snm in ("dS", "dSc", "dG"):
            s = Stream(nc, snm, 16)
            ES.enter_context(contextlib.nullcontext())
            k.st[snm] = s
        for snm in ("cc",):
            k.st[snm] = Stream(nc, snm, 1)
        for snm in ("pV", "pS", "pT", "pG"):
            k.st[snm] = Stream(nc, snm, 1)

        def sb(name, shape, dtype=F32):
            return ES.enter_context(nc.sbuf_tensor(name, list(shape), dtype))

        ident32 = sb("ident32_s", [128, 128])
        ident16 = sb("ident16_s", [128, 128], BF16)
        ones_col16 = sb("ones_col16_s", [128, 1], BF16)
        ones_col32 = sb("ones_col32_s", [128, 1])
        ones_row = sb("ones_row_s", [1, 128])
        half_row = sb("half_row_s", [1, 128])
        iotaJI = sb("iotaJI", [128, 512], I32)
        bq_s = sb("bq_s", [1, D])
        bo_s = sb("bo_s", [1, D])
        bdr_s = sb("bdr_s", [128, 4])
        bur_s = sb("bur_s", [128, 8])
        gbh_s = sb("gbh_s", [128, HEADS])
        offsel_s = sb("offsel_s", [128, 24])
        offb_s = sb("offb_s", [128, 24])
        aab_s = sb("aab_s", [128, 384])
        mdiv_s = sb("mdiv_s", [1, 2])
        qnw_t = sb("qnw_t", [128, 128])
        knw_t = sb("knw_t", [128, 128])

        pA = ES.enter_context(nc.psum_tensor("pA", [128, 512], F32))
        pB = ES.enter_context(nc.psum_tensor("pB", [128, 512], F32))
        pC = ES.enter_context(nc.psum_tensor("pC", [128, 512], F32))
        pD = ES.enter_context(nc.psum_tensor("pD", [128, 512], F32))
        pE = ES.enter_context(nc.psum_tensor("pE", [128, 512], F32))
        pF = ES.enter_context(nc.psum_tensor("pF", [128, 512], F32))
        pT16 = ES.enter_context(nc.psum_tensor("pT16", [128, 1024], BF16))
        pT32 = ES.enter_context(nc.psum_tensor("pT32", [128, 512], F32))

        @blk.sync
        def _(e):
            for i_, (src, dst) in enumerate((
                (ident32_d, ident32), (ident16_d, ident16),
                (ones_col16_d, ones_col16), (ones_row_d, ones_row),
                (half_row_d, half_row),
                (b_q, bq_s), (b_o, bo_s), (b_down_r, bdr_s), (b_up_r, bur_s),
                (gate_b_r, gbh_s), (offtab_sel, offsel_s), (offtab_b, offb_s),
                (aa_bias, aab_s), (mdiv_d, mdiv_s),
            )):
                k.dma(e, "sync", "Lconst", first=(i_ == 0), out=dst[:], in_=src.ap())
        t_constload = k.st["Lconst"].n

        @blk.gpsimd
        def _(e):
            k.inc(e.iota(iotaJI[:], pattern=[[1, 512]], base=0, channel_multiplier=-1), "pG")
            k.inc(e.memset(ones_col32[:], 1.0), "pG")
        t_iota = k.st["pG"].n

        @blk.vector
        def _(e):
            k.wait(e, "v", "Lconst", t_constload)
            k.inc(e.tensor_scalar_mul(gbh_s[:], gbh_s[:], 0.5), "pV")
        t_gbh = k.st["pV"].n

        @blk.sync
        def _(e):
            k.dma(e, "sync", "Lqnw", first=True, out=qnw_t[0:1, :], in_=qn_w.ap())
            k.dma(e, "sync", "Lqnw", first=False, out=knw_t[0:1, :], in_=kn_w.ap())
        t_qnwload = k.st["Lqnw"].n

        @blk.tensor
        def _(e):
            k.wait(e, "t", "Lqnw", t_qnwload)
            k.wait(e, "t", "Lconst", t_constload)
            k.inc(e.matmul(pT32[:, 0:128], ones_row[:], qnw_t[0:1, :], start=True, stop=True), "pT")
            k.inc(e.matmul(pT32[:, 128:256], ones_row[:], knw_t[0:1, :], start=True, stop=True), "pT")
        t_qnwmm = k.st["pT"].n

        @blk.vector
        def _(e):
            k.wait(e, "v", "pT", t_qnwmm)
            k.inc(e.tensor_copy(qnw_t[:], pT32[:, 0:128]), "pV")
            k.inc(e.tensor_copy(knw_t[:], pT32[:, 128:256]), "pV")
        t_qnwt = k.st["pV"].n

        w_gate_sl = din("w_gate_sl", [DFF, 256])
        w_up_sl = din("w_up_sl", [DFF, 256])
        w_down_sl = din("w_down_sl", [D, 768])
        w_qkv_sl = din("w_qkv_sl", [3 * D, 256])
        w_proj_sl = din("w_proj_sl", [D, 256])

        prepES = contextlib.ExitStack()
        with prepES:
            wslab = prepES.enter_context(nc.sbuf_tensor("wslab", [128, 768], F32))
            wslabB = prepES.enter_context(nc.sbuf_tensor("wslabB", [128, 768], F32))
            wslab2 = [wslab, wslabB]
            wabs = prepES.enter_context(nc.sbuf_tensor("wabs", [128, 768], F32))
            sclb = prepES.enter_context(nc.sbuf_tensor("sclb", [128, 6], F32))
            rsclb = prepES.enter_context(nc.sbuf_tensor("rsclb", [128, 6], F32))
            tq1 = prepES.enter_context(nc.sbuf_tensor("tq1", [128, 768], F32))
            tqi = prepES.enter_context(nc.sbuf_tensor("tqi", [128, 768], I32))
            wqv = prepES.enter_context(nc.sbuf_tensor("wqv", [128, 768], F32))
            stage = prepES.enter_context(nc.sbuf_tensor("stage", [128, 768], BF16))
            acc_ps = prepES.enter_context(nc.sbuf_tensor("acc_ps", [128, 2], F32))
            red1 = prepES.enter_context(nc.sbuf_tensor("red1", [128, 1], F32))
            pstot = prepES.enter_context(nc.sbuf_tensor("pstot", [1, 2], F32))
            cmb = prepES.enter_context(nc.sbuf_tensor("cmb", [128, 2], F32))
            sbq = prepES.enter_context(nc.sbuf_tensor("sbq", [128, 2], F32))
            zrow = prepES.enter_context(nc.sbuf_tensor("zrow", [1, 512], F32))

            @blk.vector
            def _(e):
                k.inc(e.memset(acc_ps[:], 0.0), "pV")
                k.inc(e.memset(zrow[:], 0.0), "pV")
            tv0 = k.st["pV"].n

            for mi, (mat, ntiles) in enumerate(((w_qkv_sl, 48), (w_proj_sl, 16))):
                ot = 0
                while ot < ntiles:
                    nb = min(3, ntiles - ot)
                    w = nb * 256
                    @blk.scalar
                    def _(e, mat=mat, ot=ot, nb=nb, w=w):
                        k.wait(e, "s", "pV", k.st["pV"].n)
                        td = k.dma(e, "s", "Lslab",
                                   out=wslab[:, 0:w].rearrange("p (a c) -> p a c", c=256),
                                   in_=mat.ap()[ot * 128:(ot + nb) * 128, :].rearrange("(a p) c -> p a c", p=128))
                        k.wait(e, "s", "Lslab", td)
                        k.inc(e.activation(wabs[:, 0:w], wslab[:, 0:w], AF.Abs), "pS")
                    ts_ = k.st["pS"].n

                    @blk.vector
                    def _(e, mi=mi, ts_=ts_, w=w):
                        k.wait(e, "v", "pS", ts_)
                        k.wait(e, "v", "pV", k.st["pV"].n)
                        k.inc(e.reduce_sum(red1[:], wabs[:, 0:w], axis=AX.X), "pV")
                        k.wait(e, "v", "pV", k.st["pV"].n)
                        k.inc(e.tensor_add(acc_ps[:, mi:mi + 1], acc_ps[:, mi:mi + 1], red1[:]), "pV")
                    ot += nb
            tv_acc = k.st["pV"].n

            @blk.tensor
            def _(e):
                k.wait(e, "t", "pV", tv_acc)
                k.wait(e, "t", "pV", t_qnwt)
                k.wait(e, "t", "pG", t_iota)
                k.inc(e.matmul(pT32[0:1, 0:1], acc_ps[:, 0:1], ones_col32[:], start=True, stop=True), "pT")
                k.inc(e.matmul(pT32[0:1, 1:2], acc_ps[:, 1:2], ones_col32[:], start=True, stop=True), "pT")
            tps = k.st["pT"].n

            @blk.vector
            def _(e):
                k.wait(e, "v", "pT", tps)
                k.inc(e.tensor_copy(pstot[0:1, 0:2], pT32[0:1, 0:2]), "pV")
            tv_pstot = k.st["pV"].n

            @blk.scalar
            def _(e):
                k.wait(e, "s", "pV", tv_pstot)
                k.dma(e, "s", "Lccps", out=cc_ps_in.ap(), in_=zrow[:])
                k.dma(e, "s", "Lccps", out=cc_ps_in.ap()[0:1, 0:2], in_=pstot[:])
            t_psin = k.st["Lccps"].n

            @blk.gpsimd
            def _(e):
                k.wait(e, "g", "Lccps", t_psin)
                if "cr" in PREP_SKIP:
                    k.inc(e.engine_nop(), "cc")
                else:
                    k.inc(e.collective_compute(
                        "AllReduce", ALU.add,
                        replica_groups=[[0, 1, 2, 3, 4, 5, 6, 7]],
                        ins=[cc_ps_in.ap().opt()], outs=[cc_ps_out.ap().opt()]), "cc")
            t_ccps = k.st["cc"].n

            def quant_group(mat_sl, ntiles, icols, ngrp, cc_dst, tscale_col=None):
                nbmax = max(1, 768 // icols)
                last_mul = [0, 0]
                ts_entry = k.st["pS"].n
                pv_entry = k.st["pV"].n
                it = 0
                ot = 0
                while ot < ntiles:
                    nb = min(nbmax, ntiles - ot)
                    W = nb * icols
                    G = nb * ngrp
                    buf = it % 2
                    wsl = wslab2[buf]
                    @blk.scalar
                    def _(e, ot=ot, nb=nb, W=W, buf=buf, wsl=wsl, it=it):
                        if it < 2:
                            k.wait(e, "s", "pS", ts_entry)
                            k.wait(e, "s", "pV", pv_entry)
                        else:
                            k.wait(e, "s", "pV", last_mul[buf])
                        td = k.dma(e, "s", f"Lslab{buf}",
                                   out=wsl[:, 0:W].rearrange("p (a c) -> p a c", c=icols),
                                   in_=mat_sl.ap()[ot * 128:(ot + nb) * 128, :].rearrange("(a p) c -> p a c", p=128))
                        k.wait(e, "s", f"Lslab{buf}", td)
                        if tscale_col is None:
                            k.wait(e, "s", "pV", k.st["pV"].n)
                            k.inc(e.activation(wabs[:, 0:W], wsl[:, 0:W], AF.Abs), "pS")
                    ts_ = k.st["pS"].n
                    td_slab = k.st[f"Lslab{buf}"].n

                    @blk.vector
                    def _(e, ts_=ts_, td_slab=td_slab, W=W, G=G, buf=buf, wsl=wsl):
                        k.wait(e, "v", "pS", ts_)
                        k.wait(e, "v", f"Lslab{buf}", td_slab)
                        k.wait(e, "v", "pV", k.st["pV"].n)
                        if tscale_col is None:
                            k.inc(e.reduce_sum(
                                sclb[:, 0:G],
                                wabs[:, 0:W].rearrange("p (g c) -> p g c", c=128),
                                axis=AX.X), "pV")
                            k.wait(e, "v", "pV", k.st["pV"].n)
                            k.inc(e.tensor_scalar(sclb[:, 0:G], sclb[:, 0:G],
                                                  1.0 / 128.0, EPS_LN, ALU.mult, ALU.add), "pV")
                            k.wait(e, "v", "pV", k.st["pV"].n)
                            k.inc(e.reciprocal(rsclb[:, 0:G], sclb[:, 0:G]), "pV")
                            k.wait(e, "v", "pV", k.st["pV"].n)
                            rscl_b = rsclb[:, 0:G].unsqueeze(2).to_broadcast([128, G, 128])
                            k.inc(e.tensor_mul(tq1[:, 0:W].rearrange("p (g c) -> p g c", c=128),
                                               wsl[:, 0:W].rearrange("p (g c) -> p g c", c=128),
                                               rscl_b), "pV")
                        else:
                            k.inc(e.tensor_scalar_mul(tq1[:, 0:W], wsl[:, 0:W],
                                                      sbq[:, tscale_col:tscale_col + 1]), "pV")
                        last_mul[buf] = k.st["pV"].n
                        k.wait(e, "v", "pV", k.st["pV"].n)
                        k.inc(e.tensor_scalar(tq1[:, 0:W], tq1[:, 0:W],
                                              1.0, -1.0, ALU.min, ALU.max), "pV")
                        k.wait(e, "v", "pV", k.st["pV"].n)
                        k.inc(e.tensor_scalar_add(tq1[:, 0:W], tq1[:, 0:W], RC), "pV")
                        k.wait(e, "v", "pV", k.st["pV"].n)
                        k.inc(e.tensor_copy(tqi[:, 0:W], tq1[:, 0:W]), "pV")
                        k.wait(e, "v", "pV", k.st["pV"].n)
                        if tscale_col is None:
                            scl_b = sclb[:, 0:G].unsqueeze(2).to_broadcast([128, G, 128])
                            k.inc(e.scalar_tensor_tensor(
                                wqv[:, 0:W].rearrange("p (g c) -> p g c", c=128),
                                tqi[:, 0:W].rearrange("p (g c) -> p g c", c=128),
                                -RC, scl_b, ALU.add, ALU.mult), "pV")
                        else:
                            cm_b = cmb[:, tscale_col:tscale_col + 1].to_broadcast([128, W])
                            k.inc(e.scalar_tensor_tensor(
                                wqv[:, 0:W], tqi[:, 0:W],
                                -RC, cm_b, ALU.add, ALU.mult), "pV")
                    tv_ = k.st["pV"].n

                    glist = [] if "nt" in PREP_SKIP else [
                        (j, i) for j in range(ngrp) for i in range(nb)]
                    for gidx, (j, i) in enumerate(glist):
                        @blk.tensor
                        def _(e, gidx=gidx, j=j, i=i, tv_=tv_):
                            k.wait(e, "t", "pV", tv_)
                            k.wait(e, "t", "pV", k.st["pV"].n)
                            k.inc(e.transpose(pT32[:, (gidx % 4) * 128:(gidx % 4) * 128 + 128],
                                              wqv[:, (i * icols + j * 128):(i * icols + j * 128) + 128],
                                              ident32[:]), "pT")
                        tt_ = k.st["pT"].n

                        @blk.vector
                        def _(e, gidx=gidx, j=j, i=i, tt_=tt_, nb=nb):
                            k.wait(e, "v", "pT", tt_)
                            k.inc(e.tensor_copy(stage[:, (j * nb + i) * 128:(j * nb + i) * 128 + 128],
                                                pT32[:, (gidx % 4) * 128:(gidx % 4) * 128 + 128]), "pV")
                    tv2_ = k.st["pV"].n

                    @blk.sync
                    def _(e, ot=ot, nb=nb, W=W, tv2_=tv2_):
                        if "ns" in PREP_SKIP or "nt" in PREP_SKIP:
                            return
                        k.wait(e, "sync", "pV", tv2_)
                        k.dma(e, "sync", "Lstage",
                              out=cc_dst.ap().rearrange("(a p) o -> p a o", p=128)[:, :, ot * 128:(ot + nb) * 128],
                              in_=stage[:, 0:W].rearrange("p (a o) -> p a o", o=nb * 128))
                    @blk.vector
                    def _(e):
                        k.wait(e, "v", "Lstage", k.st["Lstage"].n if "Lstage" in k.st else 0)
                    ot += nb
                    it += 1

            if "g" not in PREP_SKIP:
                quant_group(w_gate_sl, 43, 256, 2, cc_wg_in)
            @blk.gpsimd
            def _(e):
                k.wait(e, "g", "Lstage", k.st["Lstage"].n if "Lstage" in k.st else 0)
                if "cg" in PREP_SKIP:
                    k.inc(e.engine_nop(), "cc")
                else:
                    k.inc(e.collective_compute(
                        "AllGather", ALU.bypass,
                        replica_groups=[[0, 1, 2, 3, 4, 5, 6, 7]],
                        ins=[cc_wg_in.ap().opt()], outs=[cc_wg_out.ap().opt()]), "cc")

            if "u" not in PREP_SKIP:
                quant_group(w_up_sl, 43, 256, 2, cc_wu_in)
            @blk.gpsimd
            def _(e):
                k.wait(e, "g", "Lstage", k.st["Lstage"].n if "Lstage" in k.st else 0)
                if "cu" in PREP_SKIP:
                    k.inc(e.engine_nop(), "cc")
                else:
                    k.inc(e.collective_compute(
                        "AllGather", ALU.bypass,
                        replica_groups=[[0, 1, 2, 3, 4, 5, 6, 7]],
                        ins=[cc_wu_in.ap().opt()], outs=[cc_wu_out.ap().opt()]), "cc")

            if "d" not in PREP_SKIP:
                quant_group(w_down_sl, 16, 768, 6, cc_wd_in)
            @blk.gpsimd
            def _(e):
                k.wait(e, "g", "Lstage", k.st["Lstage"].n if "Lstage" in k.st else 0)
                if "cd" in PREP_SKIP:
                    k.inc(e.engine_nop(), "cc")
                else:
                    k.inc(e.collective_compute(
                        "AllGather", ALU.bypass,
                        replica_groups=[[0, 1, 2, 3, 4, 5, 6, 7]],
                        ins=[cc_wd_in.ap().opt()], outs=[cc_wd_out.ap().opt()]), "cc")

            @blk.scalar
            def _(e):
                k.wait(e, "s", "cc", t_ccps)
                td = k.dma(e, "s", "Lccps", out=pstot[:], in_=cc_ps_out.ap()[0:1, 0:2])
                k.wait(e, "s", "Lccps", td)
            t_pstot2 = k.st["Lccps"].n

            @blk.vector
            def _(e):
                k.wait(e, "v", "Lccps", t_pstot2)
                k.wait(e, "v", "Lconst", t_constload)
                k.inc(e.tensor_mul(pstot[:], pstot[:], mdiv_s[:]), "pV")
                k.wait(e, "v", "pV", k.st["pV"].n)
                k.inc(e.tensor_scalar_max(pstot[:], pstot[:], EPS_Q), "pV")
            tv_cm = k.st["pV"].n

            @blk.tensor
            def _(e):
                k.wait(e, "t", "pV", tv_cm)
                k.wait(e, "t", "pV", k.st["pV"].n)
                k.inc(e.matmul(pT32[:, 0:2], ones_row[:], pstot[:], start=True, stop=True), "pT")
            t_cmmm = k.st["pT"].n

            @blk.vector
            def _(e):
                k.wait(e, "v", "pT", t_cmmm)
                k.inc(e.tensor_copy(cmb[:], pT32[:, 0:2]), "pV")
                k.wait(e, "v", "pV", k.st["pV"].n)
                k.inc(e.reciprocal(sbq[:], cmb[:]), "pV")
            tv_sbq = k.st["pV"].n

            if "q" not in PREP_SKIP:
                quant_group(w_qkv_sl, 48, 256, 2, cc_wqkv_in, tscale_col=0)
            @blk.gpsimd
            def _(e):
                k.wait(e, "g", "Lstage", k.st["Lstage"].n if "Lstage" in k.st else 0)
                if "cq" in PREP_SKIP:
                    k.inc(e.engine_nop(), "cc")
                else:
                    k.inc(e.collective_compute(
                        "AllGather", ALU.bypass,
                        replica_groups=[[0, 1, 2, 3, 4, 5, 6, 7]],
                        ins=[cc_wqkv_in.ap().opt()], outs=[cc_wqkv_out.ap().opt()]), "cc")

            if "p" not in PREP_SKIP:
                quant_group(w_proj_sl, 16, 256, 2, cc_wp_in, tscale_col=1)
            @blk.gpsimd
            def _(e):
                k.wait(e, "g", "Lstage", k.st["Lstage"].n if "Lstage" in k.st else 0)
                if "cp" in PREP_SKIP:
                    k.inc(e.engine_nop(), "cc")
                else:
                    k.inc(e.collective_compute(
                        "AllGather", ALU.bypass,
                        replica_groups=[[0, 1, 2, 3, 4, 5, 6, 7]],
                        ins=[cc_wp_in.ap().opt()], outs=[cc_wp_out.ap().opt()]), "cc")
            t_cc_weights = k.st["cc"].n

        def ln_stats(src_d, t, xtile, xsq, st1, st2, var_all, mean_all, col):
            @blk.scalar
            def _(e, t=t):
                k.join(e, "s")
                k.wait(e, "s", "Lresw", k.st["Lresw"].n if "Lresw" in k.st else 0)
                td = k.dma(e, "s", "Lx", out=xtile[:], in_=src_d.ap()[t * 128:(t + 1) * 128, :])
                k.wait(e, "s", "Lx", td)
                k.inc(e.activation(xsq[:], xtile[:], AF.Square, accum_out=st2[:]), "pS")
            tS = k.st["pS"].n
            td_x = k.st["Lx"].n

            @blk.vector
            def _(e, tS=tS, td_x=td_x, col=col):
                k.wait(e, "v", "Lx", td_x)
                k.wait(e, "v", "pS", tS)
                k.wait(e, "v", "pV", k.st["pV"].n)
                k.inc(e.reduce_sum(st1[:], xtile[:], axis=AX.X), "pV")
                k.wait(e, "v", "pV", k.st["pV"].n)
                k.inc(e.tensor_scalar_mul(mean_all[:, col:col + 1], st1[:], 1.0 / D), "pV")
                k.wait(e, "v", "pV", k.st["pV"].n)
                k.inc(e.tensor_mul(st1[:], mean_all[:, col:col + 1], mean_all[:, col:col + 1]), "pV")
                k.wait(e, "v", "pV", k.st["pV"].n)
                k.inc(e.scalar_tensor_tensor(var_all[:, col:col + 1], st2[:], 1.0 / D, st1[:],
                                             ALU.mult, ALU.subtract), "pV")

        def batch_rsqrt(var_all, rstd_all, n, eps):
            @blk.vector
            def _(e):
                k.wait(e, "v", "pV", k.st["pV"].n)
                k.inc(e.tensor_scalar_add(var_all[:, 0:n], var_all[:, 0:n], eps), "pV")
                k.wait(e, "v", "pV", k.st["pV"].n)
                k.inc(e.reciprocal(rstd_all[:, 0:n], var_all[:, 0:n]), "pV")
            tv = k.st["pV"].n
            @blk.scalar
            def _(e, tv=tv):
                k.wait(e, "s", "pV", tv)
                k.wait(e, "s", "pS", k.st["pS"].n)
                k.inc(e.activation(rstd_all[:, 0:n], rstd_all[:, 0:n], AF.Sqrt), "pS")
            return k.st["pS"].n

        def ln_apply(src_d, t, xtile, rstd_all, mean_all, nmr, col, xn_out):
            @blk.scalar
            def _(e, t=t):
                k.join(e, "s")
                td = k.dma(e, "s", "Lx", out=xtile[:], in_=src_d.ap()[t * 128:(t + 1) * 128, :])
                k.wait(e, "s", "Lx", td)
            td_x = k.st["Lx"].n
            @blk.vector
            def _(e, td_x=td_x, col=col):
                k.wait(e, "v", "pV", k.st["pV"].n)
                k.wait(e, "v", "pS", k.st["pS"].n)
                k.inc(e.scalar_tensor_tensor(nmr[:], mean_all[:, col:col + 1], -1.0,
                                             rstd_all[:, col:col + 1], ALU.mult, ALU.mult), "pV")
            tV2 = k.st["pV"].n
            @blk.scalar
            def _(e, tV2=tV2, td_x=td_x, col=col):
                k.wait(e, "s", "pV", tV2)
                k.wait(e, "s", "Lx", td_x)
                k.wait(e, "s", "pT", k.st["pT"].n)
                k.inc(e.activation(xn_out[:], xtile[:], AF.Identity,
                                   bias=nmr[:], scale=rstd_all[:, col:col + 1]), "pS")
            return k.st["pS"].n

        def ln_tile(src_d, t, xtile, xsq, st1, st2, st3, rstd, nmr, xn_out):
            @blk.scalar
            def _(e, t=t):
                k.join(e, "s")
                k.wait(e, "s", "Lresw", k.st["Lresw"].n if "Lresw" in k.st else 0)
                td = k.dma(e, "s", "Lx", out=xtile[:], in_=src_d.ap()[t * 128:(t + 1) * 128, :])
                k.wait(e, "s", "Lx", td)
                k.inc(e.activation(xsq[:], xtile[:], AF.Square, accum_out=st2[:]), "pS")
            tS = k.st["pS"].n
            td_x = k.st["Lx"].n

            @blk.vector
            def _(e, tS=tS, td_x=td_x):
                k.wait(e, "v", "Lx", td_x)
                k.wait(e, "v", "pS", tS)
                k.wait(e, "v", "pV", k.st["pV"].n)
                k.inc(e.reduce_sum(st1[:], xtile[:], axis=AX.X), "pV")
                k.wait(e, "v", "pV", k.st["pV"].n)
                k.inc(e.tensor_scalar_mul(st1[:], st1[:], 1.0 / D), "pV")
                k.wait(e, "v", "pV", k.st["pV"].n)
                k.inc(e.tensor_mul(st3[:], st1[:], st1[:]), "pV")
                k.wait(e, "v", "pV", k.st["pV"].n)
                k.inc(e.scalar_tensor_tensor(st3[:], st2[:], 1.0 / D, st3[:],
                                             ALU.mult, ALU.subtract), "pV")
                k.wait(e, "v", "pV", k.st["pV"].n)
                k.inc(e.tensor_scalar_add(st3[:], st3[:], EPS_LN), "pV")
                k.wait(e, "v", "pV", k.st["pV"].n)
                k.inc(e.reciprocal(st2[:], st3[:]), "pV")
            tV = k.st["pV"].n

            @blk.scalar
            def _(e, tV=tV):
                k.wait(e, "s", "pV", tV)
                k.inc(e.activation(rstd[:], st2[:], AF.Sqrt), "pS")
            tS2 = k.st["pS"].n

            @blk.vector
            def _(e, tS2=tS2):
                k.wait(e, "v", "pS", tS2)
                k.wait(e, "v", "pV", k.st["pV"].n)
                k.inc(e.scalar_tensor_tensor(nmr[:], st1[:], -1.0, rstd[:],
                                             ALU.mult, ALU.mult), "pV")
            tV2 = k.st["pV"].n

            @blk.scalar
            def _(e, tV2=tV2):
                k.wait(e, "s", "pV", tV2)
                k.wait(e, "s", "pT", k.st["pT"].n)
                k.inc(e.activation(xn_out[:], xtile[:], AF.Identity,
                                   bias=nmr[:], scale=rstd[:]), "pS")
            return k.st["pS"].n

        def transposeN(src, src_cols, dst, dst_cols, dtype, n, extra=None):
            ps = pT32 if dtype == F32 else pT16
            idn = ident32 if dtype == F32 else ident16
            for i in range(n):
                @blk.tensor
                def _(e, i=i):
                    k.wait(e, "t", "pV", k.st["pV"].n)
                    k.wait(e, "t", "pS", k.st["pS"].n)
                    c0 = src_cols[i]
                    k.inc(e.transpose(ps[:, (i % 4) * 128:(i % 4) * 128 + 128],
                                      src[:, c0:c0 + 128], idn[:]), "pT")
                tt = k.st["pT"].n

                @blk.vector
                def _(e, i=i, tt=tt):
                    k.wait(e, "v", "pT", tt)
                    k.wait(e, "v", "pS", k.st["pS"].n)
                    if dst is not None:
                        d0 = dst_cols[i]
                        k.inc(e.tensor_copy(dst[:, d0:d0 + 128],
                                            ps[:, (i % 4) * 128:(i % 4) * 128 + 128]), "pV")
                    if extra is not None:
                        extra(i, e)

        _rp_count = [0]
        def residual_pass(src_d, proj_d, dst_d):
            _rp_count[0] += 1
            rix = _rp_count[0]
            rES = contextlib.ExitStack()
            with rES:
                xo = rES.enter_context(nc.sbuf_tensor(f"xo_r{rix}", [128, D], F32))
                yo = rES.enter_context(nc.sbuf_tensor(f"yo_r{rix}", [128, D], F32))
                xnw = rES.enter_context(nc.sbuf_tensor(f"xnw_r{rix}", [128, D], F32))
                for t in range(NTT):
                    @blk.scalar
                    def _(e, t=t):
                        k.join(e, "s")
                        k.wait(e, "s", "Lyp", k.st["Lyp"].n if "Lyp" in k.st else 0)
                        k.wait(e, "s", "Lresw", k.st["Lresw"].n if "Lresw" in k.st else 0)
                        k.dma(e, "s", "Lres", first=True, out=xo[:], in_=src_d.ap()[t * 128:(t + 1) * 128, :])
                        k.dma(e, "s", "Lres", first=False, out=yo[:], in_=proj_d.ap()[t * 128:(t + 1) * 128, :])
                    td = k.st["Lres"].n

                    @blk.vector
                    def _(e, t=t, td=td):
                        k.wait(e, "v", "Lres", td)
                        k.wait(e, "v", "Lresw", k.st["Lresw"].n if "Lresw" in k.st else 0)
                        k.wait(e, "v", "pV", k.st["pV"].n)
                        k.inc(e.tensor_add(xnw[:], xo[:], yo[:]), "pV")
                    tv = k.st["pV"].n

                    @blk.sync
                    def _(e, t=t, tv=tv):
                        k.wait(e, "sync", "pV", tv)
                        k.dma(e, "sync", "Lresw", out=dst_d.ap()[t * 128:(t + 1) * 128, :], in_=xnw[:])
                @blk.vector
                def _(e):
                    k.wait(e, "v", "Lresw", k.st["Lresw"].n)

        yproj = dint("yproj", [NTOK, D])
        q_d = dint("q_d", [NTOK, D])
        ygall_d = dint("ygall_d", [D, NTOK], BF16)
        kaT_d = dint("kaT_d", [NKV * HD, T], BF16)
        kraw_d = dint("kraw_d", [T, NKV * HD])
        vaT_d = dint("vaT_d", [NKV * HD, T], BF16)

        if upto >= 1:
            aES = contextlib.ExitStack()
            with aES:
                qhT = aES.enter_context(nc.sbuf_tensor("qhT", [128, 16 * NTOK], BF16))
                qselT = aES.enter_context(nc.sbuf_tensor("qselT", [64, NTOK], F32))
                kselT = aES.enter_context(nc.sbuf_tensor("kselT", [64, T], F32))
                m01T_buf = aES.enter_context(nc.sbuf_tensor("m01T_buf", [128, 24 * 512], BF16))
                ropec_s = aES.enter_context(nc.sbuf_tensor("ropec_s", [128, 16 * 64], F32))
                ropes_s = aES.enter_context(nc.sbuf_tensor("ropes_s", [128, 16 * 64], F32))
                ropeoc_s = aES.enter_context(nc.sbuf_tensor("ropeoc_s", [128, 8 * 64], F32))
                ropeos_s = aES.enter_context(nc.sbuf_tensor("ropeos_s", [128, 8 * 64], F32))
                xtile = aES.enter_context(nc.sbuf_tensor("xtile", [128, D], F32))
                xsq = aES.enter_context(nc.sbuf_tensor("xsq", [128, D], F32))
                xn = aES.enter_context(nc.sbuf_tensor("xn", [128, D], F32))
                qr = aES.enter_context(nc.sbuf_tensor("qr", [128, D], F32))
                st1 = aES.enter_context(nc.sbuf_tensor("st1", [128, 1], F32))
                st2 = aES.enter_context(nc.sbuf_tensor("st2", [128, 1], F32))
                st3 = aES.enter_context(nc.sbuf_tensor("st3", [128, 1], F32))
                rstd = aES.enter_context(nc.sbuf_tensor("rstd", [128, 1], F32))
                nmr = aES.enter_context(nc.sbuf_tensor("nmr", [128, 1], F32))
                ssh = aES.enter_context(nc.sbuf_tensor("ssh", [128, 16], F32))
                qssh = aES.enter_context(nc.sbuf_tensor("qssh", [128, 128], F32))
                kssh = aES.enter_context(nc.sbuf_tensor("kssh", [128, 64], F32))
                ttmp = xtile[:, 0:1024]
                ttmp2 = xtile[:, 1024:2048]

                @blk.sync
                def _(e):
                    k.dma(e, "sync", "Lrope", first=True, out=ropec_s[:].rearrange("p (a j) -> p a j", j=64),
                          in_=rope_c.ap().rearrange("(a p) j -> p a j", p=128))
                    k.dma(e, "sync", "Lrope", first=False, out=ropes_s[:].rearrange("p (a j) -> p a j", j=64),
                          in_=rope_s.ap().rearrange("(a p) j -> p a j", p=128))
                    k.dma(e, "sync", "Lrope", first=False, out=ropeoc_s[:].rearrange("p (a j) -> p a j", j=64),
                          in_=rope_own_c.ap().rearrange("(a p) j -> p a j", p=128))
                    k.dma(e, "sync", "Lrope", first=False, out=ropeos_s[:].rearrange("p (a j) -> p a j", j=64),
                          in_=rope_own_s.ap().rearrange("(a p) j -> p a j", p=128))
                t_ropeload = k.st["Lrope"].n

                for t in range(NTT):
                    tS = ln_tile(x_own, t, xtile, xsq, st1, st2, st3, rstd, nmr, xn)
                    @blk.vector
                    def _(e, tS=tS):
                        k.wait(e, "v", "pS", tS)
                        k.wait(e, "v", "Lccx", k.st["Lccx"].n if "Lccx" in k.st else 0)
                    transposeN(xn, [c * 128 for c in range(16)], qr,
                               [c * 128 for c in range(16)], F32, 16)
                    tv_t = k.st["pV"].n
                    @blk.sync
                    def _(e, t=t, tv_t=tv_t):
                        k.wait(e, "sync", "pV", tv_t)
                        k.dma(e, "sync", "Lccx",
                              out=cc_in_x.ap().rearrange("(c p) tok -> p c tok", p=128)[:, :, t * 128:(t + 1) * 128],
                              in_=qr[:].rearrange("p (c j) -> p c j", j=128))
                t_ccin = k.st["Lccx"].n

                @blk.gpsimd
                def _(e):
                    k.wait(e, "g", "Lccx", t_ccin)
                    k.inc(e.collective_compute(
                        "AllGather", ALU.bypass,
                        replica_groups=[[0, 1], [2, 3], [4, 5], [6, 7]],
                        ins=[cc_in_x.ap().opt()], outs=[cc_out_x.ap().opt()]), "cc")
                t_cc_x = k.st["cc"].n

                a2ES = contextlib.ExitStack()
                with a2ES:
                    xnTb_own = a2ES.enter_context(nc.sbuf_tensor("xnTb_own", [128, 16 * NTOK], BF16))
                    wq16 = [a2ES.enter_context(nc.sbuf_tensor(f"wq16_{i}", [128, 16 * 512], BF16)) for i in range(2)]
                    qslab = a2ES.enter_context(nc.sbuf_tensor("qslab", [128, 512], F32))

                    @blk.gpsimd
                    def _(e):
                        k.join(e, "g")
                        k.wait(e, "g", "Lccx", t_ccin)
                        k.dma(e, "g", "Lxno",
                              out=xnTb_own[:].rearrange("p (c tok) -> p c tok", tok=NTOK),
                              in_=cc_in_x.ap().rearrange("(c p) tok -> p c tok", p=128))
                    t_xnTo = k.st["Lxno"].n

                    t_wq_ready = [0, 0]
                    t_wq_used = [0, 0]
                    for qd in range(4):
                        @blk.gpsimd
                        def _(e, qd=qd):
                            k.join(e, "g") if qd < 2 else k.wait(e, "g", "pT", t_wq_used[qd % 2])
                            k.dma(e, "g", f"Lwq{qd % 2}",
                                  out=wq16[qd % 2][:].rearrange("p (c n) -> p c n", n=512),
                                  in_=w_q.ap().rearrange("(c p) o -> p c o", p=128)[:, :, qd * 512:(qd + 1) * 512])
                        t_wq_ready[qd % 2] = k.st[f"Lwq{qd % 2}"].n

                        for t in range(NTT):
                            @blk.tensor
                            def _(e, qd=qd, t=t):
                                k.wait(e, "t", f"Lwq{qd % 2}", t_wq_ready[qd % 2])
                                k.wait(e, "t", "Lxno", t_xnTo)
                                k.wait(e, "t", "Lconst", t_constload)
                                ps = [pA, pB][t % 2]
                                k.wait(e, "t", "pS", k.st["pS"].n)
                                k.inc(e.matmul(ps[:], ones_row[:], bq_s[0:1, qd * 512:(qd + 1) * 512],
                                               start=True, stop=False), "pT")
                                for dd in range(16):
                                    k.inc(e.matmul(ps[:], xnTb_own[:, dd * NTOK + t * 128:dd * NTOK + t * 128 + 128],
                                                   wq16[qd % 2][:, dd * 512:(dd + 1) * 512],
                                                   start=False, stop=(dd == 15)), "pT")
                            tt_ = k.st["pT"].n
                            t_wq_used[qd % 2] = tt_

                            @blk.scalar
                            def _(e, qd=qd, t=t, tt_=tt_):
                                k.wait(e, "s", "pS", k.st["pS"].n)
                                k.wait(e, "s", "Lqs", k.st["Lqs"].n if "Lqs" in k.st else 0)
                                k.wait(e, "s", "pT", tt_)
                                ps = [pA, pB][t % 2]
                                k.inc(e.activation(qslab[:], ps[:], AF.Identity), "pS")
                                k.wait(e, "s", "pS", k.st["pS"].n)
                                k.dma(e, "s", "Lqs", out=q_d.ap()[t * 128:(t + 1) * 128, qd * 512:(qd + 1) * 512],
                                      in_=qslab[:])

                    def rms_stats(src, W, nh, ssh_all, col0):
                        @blk.scalar
                        def _(e):
                            k.wait(e, "s", "pV", k.st["pV"].n)
                            k.wait(e, "s", "pS", k.st["pS"].n)
                            k.inc(e.activation(xsq[:, 0:W], src, AF.Square), "pS")
                        tS = k.st["pS"].n
                        @blk.vector
                        def _(e, tS=tS, nh=nh, col0=col0):
                            k.wait(e, "v", "pS", tS)
                            k.wait(e, "v", "pV", k.st["pV"].n)
                            k.inc(e.reduce_sum(ssh[:, 0:nh],
                                               xsq[:, 0:W].rearrange("p (h d) -> p h d", d=128),
                                               axis=AX.X), "pV")
                            k.wait(e, "v", "pV", k.st["pV"].n)
                            k.inc(e.tensor_scalar_mul(ssh_all[:, col0:col0 + nh], ssh[:, 0:nh],
                                                      1.0 / 128.0), "pV")

                    def rope_apply(src, W, nh, ctab, stab, gain_tile, rstd_all, col0,
                                   dst16, dst_cols, sel_dst=None, sel_col=None):
                        @blk.vector
                        def _(e, nh=nh, W=W, col0=col0):
                            k.wait(e, "v", "pV", k.st["pV"].n)
                            k.wait(e, "v", "pS", k.st["pS"].n)
                            sv = src.rearrange("p (h d) -> p h d", d=128)
                            k.inc(e.tensor_mul(sv, sv,
                                               rstd_all[:, col0:col0 + nh].unsqueeze(2).to_broadcast([128, nh, 128])), "pV")
                            k.wait(e, "v", "pV", k.st["pV"].n)
                            k.inc(e.tensor_mul(sv, sv,
                                               gain_tile[:].unsqueeze(1).to_broadcast([128, nh, 128])), "pV")
                            k.wait(e, "v", "pV", k.st["pV"].n)
                            s_e = src.rearrange("p (h j t) -> p h j t", j=64, t=2)[:, :, :, 0]
                            s_o = src.rearrange("p (h j t) -> p h j t", j=64, t=2)[:, :, :, 1]
                            q_r = qr[:, 0:W].rearrange("p (h j t) -> p h j t", j=64, t=2)
                            c_b = ctab.unsqueeze(1).to_broadcast([128, nh, 64])
                            sb_ = stab.unsqueeze(1).to_broadcast([128, nh, 64])
                            tt1 = ttmp[:, 0:nh * 64].rearrange("p (h j) -> p h j", j=64)
                            tt2 = ttmp2[:, 0:nh * 64].rearrange("p (h j) -> p h j", j=64)
                            k.inc(e.tensor_mul(tt1, s_e, c_b), "pV")
                            k.inc(e.tensor_mul(tt2, s_o, sb_), "pV")
                            k.wait(e, "v", "pV", k.st["pV"].n)
                            k.inc(e.tensor_sub(q_r[:, :, :, 0], tt1, tt2), "pV")
                            k.wait(e, "v", "pV", k.st["pV"].n)
                            k.inc(e.tensor_mul(tt1, s_e, sb_), "pV")
                            k.inc(e.tensor_mul(tt2, s_o, c_b), "pV")
                            k.wait(e, "v", "pV", k.st["pV"].n)
                            k.inc(e.tensor_add(q_r[:, :, :, 1], tt1, tt2), "pV")

                        def sel_extra(i, e):
                            if sel_dst is not None and i == 0:
                                k.inc(e.tensor_copy(sel_dst[0:64, sel_col:sel_col + 128],
                                                    pT32[0:64, 0:128]), "pV")
                        transposeN(qr, [c * 128 for c in range(nh)], dst16, dst_cols,
                                   F32, nh, extra=sel_extra)

                    ctv = ropeoc_s[:].rearrange("p (a j) -> p a j", j=64)
                    stv = ropeos_s[:].rearrange("p (a j) -> p a j", j=64)
                    def q_reload(t):
                        @blk.scalar
                        def _(e, t=t):
                            k.wait(e, "s", "pV", k.st["pV"].n)
                            k.wait(e, "s", "pS", k.st["pS"].n)
                            k.wait(e, "s", "Lqs", k.st["Lqs"].n)
                            td = k.dma(e, "s", "Lx", out=xn[:], in_=q_d.ap()[t * 128:(t + 1) * 128, :])
                            k.wait(e, "s", "Lx", td)
                        td_q = k.st["Lx"].n
                        @blk.vector
                        def _(e, td_q=td_q):
                            k.wait(e, "v", "Lx", td_q)
                            k.wait(e, "v", "Lrope", t_ropeload)
                    for t in range(NTT):
                        q_reload(t)
                        rms_stats(xn[:], D, 16, qssh, t * 16)
                    batch_rsqrt(qssh, qssh, 128, EPS_RMS)
                    for t in range(NTT):
                        q_reload(t)
                        rope_apply(xn[:], D, 16, ctv[:, t, :], stv[:, t, :], qnw_t,
                                   qssh, t * 16,
                                   qhT, [c * NTOK + t * 128 for c in range(16)],
                                   sel_dst=qselT, sel_col=t * 128)

                a3ES = contextlib.ExitStack()
                with a3ES:
                    xnb_tc = [a3ES.enter_context(nc.sbuf_tensor(f"xnb_tc{i}", [128, 16 * 512], BF16)) for i in range(2)]
                    kvd_tc = a3ES.enter_context(nc.sbuf_tensor("kvd_tc", [128, 4 * 512], BF16))
                    wkv16 = a3ES.enter_context(nc.sbuf_tensor("wkv16", [128, 16 * 128], BF16))
                    ktok_tc = a3ES.enter_context(nc.sbuf_tensor("ktok_tc", [128, 4 * 512], F32))
                    kvtmp = a3ES.enter_context(nc.sbuf_tensor("kvtmp", [128, 512], F32))
                    kvtmp16 = a3ES.enter_context(nc.sbuf_tensor("kvtmp16", [128, 512], BF16))
                    kstage = a3ES.enter_context(nc.sbuf_tensor("kstage", [128, 512], BF16))
                    POS_BLOCKS = ((0, 0), (1, 0), (1, 512), (0, 512))
                    ctk = ropec_s[:].rearrange("p (a j) -> p a j", j=64)
                    stk = ropes_s[:].rearrange("p (a j) -> p a j", j=64)
                    t_xnb = [0, 0]
                    for tc in range(4):
                        r_, co_ = POS_BLOCKS[tc]
                        @blk.gpsimd
                        def _(e, tc=tc, r_=r_, co_=co_):
                            k.wait(e, "g", "cc", t_cc_x)
                            k.join(e, "g")
                            ccv = cc_out_x.ap().rearrange("(r c p) tok -> r p c tok", p=128, c=16)
                            k.dma(e, "g", f"Lxnb{tc % 2}",
                                  out=xnb_tc[tc % 2][:].rearrange("p (c j) -> p c j", j=512),
                                  in_=ccv[r_, :, :, co_:co_ + 512])
                        t_xnb[tc % 2] = k.st[f"Lxnb{tc % 2}"].n

                        for lc in range(4):
                            @blk.gpsimd
                            def _(e, lc=lc):
                                k.join(e, "g")
                                k.dma(e, "g", "Lwkv",
                                      out=wkv16[:].rearrange("p (c n) -> p c n", n=128),
                                      in_=w_down_a.ap().rearrange("(c p) o -> p c o", p=128)[:, :, lc * 128:(lc + 1) * 128])
                            t_wkv = k.st["Lwkv"].n

                            @blk.tensor
                            def _(e, tc=tc, lc=lc, t_wkv=t_wkv):
                                k.wait(e, "t", "Lwkv", t_wkv)
                                k.wait(e, "t", f"Lxnb{tc % 2}", t_xnb[tc % 2])
                                ps = [pA, pB][lc % 2]
                                k.wait(e, "t", "pS", k.st["pS"].n)
                                for dd in range(16):
                                    k.inc(e.matmul(ps[:], wkv16[:, dd * 128:(dd + 1) * 128],
                                                   xnb_tc[tc % 2][:, dd * 512:(dd + 1) * 512],
                                                   start=(dd == 0), stop=(dd == 15)), "pT")
                            tt_ = k.st["pT"].n

                            @blk.scalar
                            def _(e, lc=lc, tt_=tt_):
                                k.wait(e, "s", "pT", tt_)
                                k.wait(e, "s", "pS", k.st["pS"].n)
                                ps = [pA, pB][lc % 2]
                                k.inc(e.activation(kvd_tc[:, lc * 512:(lc + 1) * 512],
                                                   ps[:], AF.Identity, bias=bdr_s[:, lc:lc + 1]), "pS")
                        tS_kvd = k.st["pS"].n

                        for kc in range(8):
                            @blk.gpsimd
                            def _(e, kc=kc):
                                k.join(e, "g")
                                k.dma(e, "g", "Lwkv",
                                      out=wkv16[:, 0:4 * 128].rearrange("p (c n) -> p c n", n=128),
                                      in_=w_up_a.ap().rearrange("(c p) o -> p c o", p=128)[:, :, kc * 128:(kc + 1) * 128])
                            t_wkv = k.st["Lwkv"].n

                            @blk.tensor
                            def _(e, kc=kc, t_wkv=t_wkv, tS_kvd=tS_kvd):
                                k.wait(e, "t", "Lwkv", t_wkv)
                                k.wait(e, "t", "pS", tS_kvd)
                                ps = [pA, pB][kc % 2]
                                k.wait(e, "t", "pS", k.st["pS"].n)
                                k.wait(e, "t", "pV", k.st["pV"].n)
                                for lc in range(4):
                                    k.inc(e.matmul(ps[:], wkv16[:, lc * 128:(lc + 1) * 128],
                                                   kvd_tc[:, lc * 512:(lc + 1) * 512],
                                                   start=(lc == 0), stop=(lc == 3)), "pT")
                            tt_ = k.st["pT"].n

                            @blk.scalar
                            def _(e, kc=kc, tt_=tt_):
                                k.wait(e, "s", "pT", tt_)
                                k.wait(e, "s", "pS", k.st["pS"].n)
                                ps = [pA, pB][kc % 2]
                                dst = kvtmp[:] if kc < 4 else kvtmp16[:]
                                k.inc(e.activation(dst, ps[:], AF.Identity,
                                                   bias=bur_s[:, kc:kc + 1]), "pS")
                            tS_ = k.st["pS"].n

                            if kc < 4:
                                for j in range(4):
                                    @blk.tensor
                                    def _(e, j=j, tS_=tS_):
                                        k.wait(e, "t", "pS", tS_)
                                        k.wait(e, "t", "pV", k.st["pV"].n)
                                        k.inc(e.transpose(pT32[:, (j % 4) * 128:(j % 4) * 128 + 128],
                                                          kvtmp[:, j * 128:(j + 1) * 128], ident32[:]), "pT")
                                    tt2 = k.st["pT"].n
                                    @blk.vector
                                    def _(e, kc=kc, j=j, tt2=tt2):
                                        k.wait(e, "v", "pT", tt2)
                                        k.wait(e, "v", "Lkraw", k.st["Lkraw"].n if "Lkraw" in k.st else 0)
                                        k.inc(e.tensor_copy(
                                            ktok_tc[:, j * 512 + kc * 128:j * 512 + kc * 128 + 128],
                                            pT32[:, (j % 4) * 128:(j % 4) * 128 + 128]), "pV")
                            else:
                                for j in range(4):
                                    @blk.tensor
                                    def _(e, j=j, tS_=tS_):
                                        k.wait(e, "t", "pS", tS_)
                                        k.wait(e, "t", "pV", k.st["pV"].n)
                                        k.inc(e.transpose(pT16[:, (j % 4) * 128:(j % 4) * 128 + 128],
                                                          kvtmp16[:, j * 128:(j + 1) * 128], ident16[:]), "pT")
                                    tt2 = k.st["pT"].n
                                    @blk.vector
                                    def _(e, j=j, tt2=tt2):
                                        k.wait(e, "v", "pT", tt2)
                                        k.wait(e, "v", "Lkst", k.st["Lkst"].n if "Lkst" in k.st else 0)
                                        k.wait(e, "v", "Lvst", k.st["Lvst"].n if "Lvst" in k.st else 0)
                                        k.inc(e.tensor_copy(kstage[:, j * 128:(j + 1) * 128],
                                                            pT16[:, (j % 4) * 128:(j % 4) * 128 + 128]), "pV")
                                tv_v = k.st["pV"].n
                                @blk.scalar
                                def _(e, kc=kc, tc=tc, tv_v=tv_v):
                                    k.wait(e, "s", "pV", tv_v)
                                    k.dma(e, "s", "Lvst",
                                          out=vaT_d.ap()[(kc - 4) * 128:(kc - 3) * 128, tc * 512:(tc + 1) * 512],
                                          in_=kstage[:])

                        for j in range(4):
                            tt_idx = tc * 4 + j
                            rms_stats(ktok_tc[:, j * 512:(j + 1) * 512], 512, 4,
                                      kssh, tt_idx * 4)
                            tv_k = k.st["pV"].n
                            @blk.scalar
                            def _(e, j=j, tt_idx=tt_idx, tv_k=tv_k):
                                k.wait(e, "s", "pV", tv_k)
                                k.wait(e, "s", "pS", k.st["pS"].n)
                                k.dma(e, "s", "Lkraw",
                                      out=kraw_d.ap()[tt_idx * 128:(tt_idx + 1) * 128, :],
                                      in_=ktok_tc[:, j * 512:(j + 1) * 512])

                batch_rsqrt(kssh, kssh, 64, EPS_RMS)
                for tt_idx in range(16):
                    @blk.scalar
                    def _(e, tt_idx=tt_idx):
                        k.join(e, "s")
                        k.wait(e, "s", "Lkraw", k.st["Lkraw"].n)
                        td = k.dma(e, "s", "Lx", out=ktok_tc[:, 0:512],
                                   in_=kraw_d.ap()[tt_idx * 128:(tt_idx + 1) * 128, :])
                        k.wait(e, "s", "Lx", td)
                    td_k = k.st["Lx"].n
                    @blk.vector
                    def _(e, td_k=td_k):
                        k.wait(e, "v", "Lx", td_k)
                        k.wait(e, "v", "Lkst", k.st["Lkst"].n if "Lkst" in k.st else 0)
                        k.wait(e, "v", "Lvst", k.st["Lvst"].n if "Lvst" in k.st else 0)
                        k.wait(e, "v", "Lrope", t_ropeload)
                    rope_apply(ktok_tc[:, 0:512], 512, 4,
                               ctk[:, tt_idx, :], stk[:, tt_idx, :], knw_t,
                               kssh, tt_idx * 4,
                               kstage, [h * 128 for h in range(4)],
                               sel_dst=kselT, sel_col=tt_idx * 128)
                    tv_k2 = k.st["pV"].n
                    @blk.scalar
                    def _(e, tt_idx=tt_idx, tv_k2=tv_k2):
                        k.wait(e, "s", "pV", tv_k2)
                        k.dma(e, "s", "Lkst",
                              out=kaT_d.ap().rearrange("(a p) c -> p a c", p=128)[:, :, tt_idx * 128:(tt_idx + 1) * 128],
                              in_=kstage[:].rearrange("p (a c) -> p a c", c=128))

                a4ES = contextlib.ExitStack()
                with a4ES:
                    selbuf = a4ES.enter_context(nc.sbuf_tensor("selbuf", [128, 2048], F32))
                    mrb = [a4ES.enter_context(nc.sbuf_tensor(f"mrb{i}", [128, 2048], F32)) for i in range(2)]
                    m8 = a4ES.enter_context(nc.sbuf_tensor("m8", [128, 8], F32))
                    thr = a4ES.enter_context(nc.sbuf_tensor("thr", [128, 1], F32))
                    g01 = a4ES.enter_context(nc.sbuf_tensor("g01", [128, 512], F32))
                    m01 = a4ES.enter_context(nc.sbuf_tensor("m01", [128, 512], F32))
                    m01a = a4ES.enter_context(nc.sbuf_tensor("m01a", [128, 512], BF16))

                    colidx = 0
                    for t in range(NTT):
                        nch = SEL_NCH[t]
                        ci = t // 4
                        cols_t = list(range(colidx, colidx + nch))
                        colidx += nch
                        for ch in range(nch):
                            @blk.tensor
                            def _(e, t=t, ch=ch):
                                k.wait(e, "t", "pV", k.st["pV"].n)
                                k.inc(e.matmul(pE[:], qselT[:, t * 128:(t + 1) * 128],
                                               kselT[:, ch * 512:(ch + 1) * 512],
                                               start=True, stop=True), "pT")
                            tt_ = k.st["pT"].n

                            @blk.vector
                            def _(e, t=t, ch=ch, tt_=tt_, col=cols_t[ch]):
                                k.wait(e, "v", "pT", tt_)
                                k.wait(e, "v", "pV", k.st["pV"].n)
                                k.wait(e, "v", "pG", t_iota)
                                k.inc(e.tensor_scalar(g01[:], iotaJI[:], offsel_s[:, col:col + 1],
                                                      None, ALU.is_gt), "pV")
                                k.wait(e, "v", "pV", k.st["pV"].n)
                                k.inc(e.scalar_tensor_tensor(selbuf[:, ch * 512:(ch + 1) * 512],
                                                             g01[:], NEGBIG, pE[:],
                                                             ALU.mult, ALU.add), "pV")

                        W = nch * 512
                        for r in range(8):
                            @blk.vector
                            def _(e, r=r, W=W):
                                k.wait(e, "v", "pV", k.st["pV"].n)
                                src = selbuf if r == 0 else mrb[(r - 1) % 2]
                                k.inc(e.max(out=m8[:], in_=src[:, 0:W]), "pV")
                                if r < 7:
                                    k.wait(e, "v", "pV", k.st["pV"].n)
                                    k.inc(e.match_replace(out=mrb[r % 2][:, 0:W], in_to_replace=m8[:],
                                                          in_values=src[:, 0:W], imm_value=NEGBIG), "pV")
                        @blk.vector
                        def _(e):
                            k.wait(e, "v", "pV", k.st["pV"].n)
                            k.inc(e.tensor_scalar_max(thr[:], m8[:, 7:8], -1e29), "pV")

                        for ch in range(nch):
                            @blk.vector
                            def _(e, t=t, ch=ch, col=cols_t[ch]):
                                k.wait(e, "v", "pV", k.st["pV"].n)
                                k.inc(e.tensor_scalar(m01[:], selbuf[:, ch * 512:(ch + 1) * 512],
                                                      thr[:], None, ALU.is_ge), "pV")
                                k.inc(e.tensor_scalar(g01[:], iotaJI[:], offsel_s[:, col:col + 1],
                                                      None, ALU.is_equal), "pV")
                                k.wait(e, "v", "pV", k.st["pV"].n)
                                k.inc(e.tensor_max(m01[:], m01[:], g01[:]), "pV")
                                k.wait(e, "v", "pV", k.st["pV"].n)
                                k.inc(e.tensor_scalar(m01a[:], m01[:], -1.0, 3.3e28,
                                                      ALU.add, ALU.mult), "pV")
                            slot0 = (0 if ci == 0 else 8)
                            for j in range(4):
                                kt = 4 * ch + j
                                s_ = slot0 + kt
                                @blk.tensor
                                def _(e, j=j):
                                    k.wait(e, "t", "pV", k.st["pV"].n)
                                    k.inc(e.transpose(pT16[:, (j % 4) * 128:(j % 4) * 128 + 128],
                                                      m01a[:, j * 128:(j + 1) * 128], ident16[:]), "pT")
                                tt2 = k.st["pT"].n
                                @blk.vector
                                def _(e, j=j, s_=s_, t=t, tt2=tt2):
                                    k.wait(e, "v", "pT", tt2)
                                    k.inc(e.tensor_copy(
                                        m01T_buf[:, s_ * 512 + (t % 4) * 128:s_ * 512 + (t % 4) * 128 + 128],
                                        pT16[:, (j % 4) * 128:(j % 4) * 128 + 128]), "pV")

                a5ES = contextlib.ExitStack()
                with a5ES:
                    gw16 = a5ES.enter_context(nc.sbuf_tensor("gw16", [128, 16 * 128], BF16))
                    khb = [a5ES.enter_context(nc.sbuf_tensor(f"khb{i}", [128, T], BF16)) for i in range(2)]
                    vhb = [a5ES.enter_context(nc.sbuf_tensor(f"vhb{i}", [128, T], BF16)) for i in range(2)]
                    d0w_c = [a5ES.enter_context(nc.sbuf_tensor(f"d0wc{i}", [128, 512], F32)) for i in range(2)]
                    zb = [a5ES.enter_context(nc.sbuf_tensor(f"zb{i}", [128, 512], F32)) for i in range(3)]
                    z2b = [a5ES.enter_context(nc.sbuf_tensor(f"z2b{i}", [128, 512], F32)) for i in range(3)]
                    pb16 = [a5ES.enter_context(nc.sbuf_tensor(f"pb16_{i}", [128, 512], BF16)) for i in range(3)]
                    rrow = a5ES.enter_context(nc.sbuf_tensor("rrow", [1, 512], F32))
                    rb = a5ES.enter_context(nc.sbuf_tensor("rb", [128, 512], F32))
                    ybuf = a5ES.enter_context(nc.sbuf_tensor("ybuf", [128, 512], F32))
                    yb16 = a5ES.enter_context(nc.sbuf_tensor("yb16", [128, 512], BF16))
                    tg = a5ES.enter_context(nc.sbuf_tensor("tg", [128, 512], F32))
                    ygslab = a5ES.enter_context(nc.sbuf_tensor("ygslab", [128, 512], BF16))

                    @blk.vector
                    def _(e):
                        k.join(e, "v")
                    tv_d0 = k.st["pV"].n

                    @blk.gpsimd
                    def _(e):
                        k.join(e, "g")
                        k.dma(e, "g", "Lgw", out=gw16[:].rearrange("p (h e) -> p h e", e=128),
                              in_=gate_w.ap().rearrange("h d e -> d h e"))
                    t_gw = k.st["Lgw"].n

                    NBK = 3
                    banksS = [pA, pB, pT32]
                    last_tanh = {0: 0, 1: 0, 2: 0}
                    t_kvload = 0
                    for ci in range(2):
                        KT = QCH_KT[ci]
                        for h in range(HEADS):
                            hkv = h // 4
                            pT0 = k.st["pT"].n
                            pV0 = k.st["pV"].n
                            pS0 = k.st["pS"].n
                            tt_mm = [pT0 + 3 * kt + 1 for kt in range(KT)]
                            tt_pv = pT0 + 3 * KT
                            tt_bc = tt_pv + 1
                            tt_g = tt_bc + 1
                            tv_z2 = [pV0 + 1 + 2 * (kt + 1) for kt in range(KT)]
                            tv_r = pV0 + 1 + 2 * KT + 1
                            tv_y = pV0 + 1 + 2 * KT + 4
                            tv_yg = pV0 + 1 + 2 * KT + 5
                            t_tanh_l = [pS0 + 2 * kt + 1 for kt in range(KT)]
                            t_exp_l = [pS0 + 2 * kt + 2 for kt in range(KT)]
                            t_tg = pS0 + 2 * KT + 1
                            lyg_prev = k.st["Lyg"].n if "Lyg" in k.st else 0

                            @blk.scalar
                            def _(e, ci=ci, h=h, KT=KT, hkv=hkv, tt_mm=tt_mm,
                                  tv_z2=tv_z2, tt_g=tt_g, tv_yg=tv_yg):
                                if h % 4 == 0:
                                    k.join(e, "s")
                                    k.wait(e, "s", "Lkst", k.st["Lkst"].n)
                                    k.wait(e, "s", "Lvst", k.st["Lvst"].n)
                                    k.dma(e, "s", f"Lkv{hkv % 2}", first=True, out=khb[hkv % 2][:],
                                          in_=kaT_d.ap()[hkv * 128:(hkv + 1) * 128, :])
                                    k.dma(e, "s", f"Lkv{hkv % 2}", first=False, out=vhb[hkv % 2][:],
                                          in_=vaT_d.ap()[hkv * 128:(hkv + 1) * 128, :])
                                for kt in range(KT):
                                    bank = kt % NBK
                                    psS = banksS[bank]
                                    jcol = kt if ci == 0 else 8 + kt
                                    k.wait(e, "s", "pS", k.st["pS"].n)
                                    k.wait(e, "s", "pT", tt_mm[kt])
                                    k.inc(e.activation(zb[bank][:], psS[:], AF.Tanh,
                                                       scale=float(ISQD / SOFT)), "pS")
                                    k.wait(e, "s", "pV", tv_z2[kt])
                                    k.inc(e.activation(pb16[bank][:], z2b[bank][:], AF.Exp,
                                                       scale=SOFT, bias=aab_s[:, h * 24 + jcol:h * 24 + jcol + 1]), "pS")
                                k.wait(e, "s", "pT", tt_g)
                                k.inc(e.activation(tg[:], pF[:], AF.Tanh,
                                                   scale=1.0, bias=gbh_s[:, h:h + 1]), "pS")
                                k.wait(e, "s", "pV", tv_yg)
                                k.dma(e, "s", "Lyg",
                                      out=ygall_d.ap()[h * 128:(h + 1) * 128, ci * 512:(ci + 1) * 512],
                                      in_=ygslab[:])
                            assert k.st["pS"].n == t_tg, (k.st["pS"].n, t_tg)
                            if h % 4 == 0:
                                t_kvload = k.st[f"Lkv{hkv % 2}"].n

                            @blk.tensor
                            def _(e, ci=ci, h=h, KT=KT, hkv=hkv, pV0=pV0,
                                  t_kvload=t_kvload, t_tanh_l=t_tanh_l, t_exp_l=t_exp_l,
                                  tv_r=tv_r, tv_y=tv_y,
                                  lts=dict(last_tanh)):
                                k.wait(e, "t", f"Lkv{hkv % 2}", t_kvload)
                                for kt in range(KT):
                                    bank = kt % NBK
                                    psS = banksS[bank]
                                    k.wait(e, "t", "pS",
                                           t_tanh_l[kt - NBK] if kt >= NBK else lts[bank])
                                    if kt == 1:
                                        k.wait(e, "t", "pV", pV0)
                                    k.inc(e.matmul(psS[:], khb[hkv % 2][:, kt * 128:(kt + 1) * 128],
                                                   qhT[:, h * NTOK + ci * 512:h * NTOK + ci * 512 + 512],
                                                   start=True, stop=True), "pT")
                                    if kt >= 1:
                                        k.wait(e, "t", "pS", t_exp_l[kt - 1])
                                        pbprev = pb16[(kt - 1) % NBK]
                                        k.inc(e.matmul(pD[0:1, 0:512], ones_col16[:], pbprev[:],
                                                       start=(kt - 1 == 0), stop=False), "pT")
                                        k.inc(e.matmul(pC[:], vhb[hkv % 2][:, (kt - 1) * 128:kt * 128],
                                                       pbprev[:], start=(kt - 1 == 0), stop=False), "pT")
                                k.wait(e, "t", "pS", t_exp_l[KT - 1])
                                pbprev = pb16[(KT - 1) % NBK]
                                k.inc(e.matmul(pD[0:1, 0:512], ones_col16[:], pbprev[:],
                                               start=(KT - 1 == 0), stop=True), "pT")
                                k.inc(e.matmul(pC[:], vhb[hkv % 2][:, (KT - 1) * 128:KT * 128],
                                               pbprev[:], start=(KT - 1 == 0), stop=True), "pT")
                                k.wait(e, "t", "pV", tv_r)
                                k.inc(e.matmul(pE[:], half_row[:], rrow[:], start=True, stop=True), "pT")
                                k.wait(e, "t", "pV", tv_y)
                                k.wait(e, "t", "Lgw", t_gw)
                                k.inc(e.matmul(pF[:], gw16[:, h * 128:(h + 1) * 128], yb16[:],
                                               start=True, stop=True), "pT")
                            assert k.st["pT"].n == tt_g, (k.st["pT"].n, tt_g)

                            @blk.vector
                            def _(e, ci=ci, h=h, KT=KT, tt_pv=tt_pv, tt_bc=tt_bc,
                                  t_tanh_l=t_tanh_l, t_tg=t_tg, lyg_prev=lyg_prev):
                                if ci == 0 and h == 0:
                                    k.wait(e, "v", "pV", tv_d0)
                                    k.wait(e, "v", "pG", t_iota)
                                k.wait(e, "v", "pV", k.st["pV"].n)
                                k.inc(e.tensor_scalar_mul(d0w_c[h % 2][:], iotaJI[:],
                                                          -SLOPES[h] / SOFT), "pV")
                                k.wait(e, "v", "pV", k.st["pV"].n)
                                for kt in range(KT):
                                    bank = kt % NBK
                                    k.wait(e, "v", "pS", t_tanh_l[kt])
                                    k.inc(e.tensor_add(z2b[bank][:], zb[bank][:],
                                                       d0w_c[h % 2][:]), "pV")
                                    k.wait(e, "v", "pV", k.st["pV"].n)
                                    s_ = (0 if ci == 0 else 8) + kt
                                    k.inc(e.tensor_add(z2b[bank][:], z2b[bank][:],
                                                       m01T_buf[:, s_ * 512:(s_ + 1) * 512]), "pV")
                                k.wait(e, "v", "pT", tt_pv)
                                k.inc(e.reciprocal(rrow[:], pD[0:1, 0:512]), "pV")
                                k.wait(e, "v", "pT", tt_bc)
                                k.inc(e.tensor_copy(rb[:], pE[:]), "pV")
                                k.wait(e, "v", "pV", k.st["pV"].n)
                                k.inc(e.tensor_mul(ybuf[:], pC[:], rb[:]), "pV")
                                k.wait(e, "v", "pV", k.st["pV"].n)
                                k.inc(e.tensor_copy(yb16[:], ybuf[:]), "pV")
                                k.wait(e, "v", "pS", t_tg)
                                k.wait(e, "v", "Lyg", lyg_prev)
                                k.inc(e.scalar_tensor_tensor(ygslab[:], tg[:], 1.0, ybuf[:],
                                                             ALU.add, ALU.mult), "pV")
                            assert k.st["pV"].n == tv_yg, (k.st["pV"].n, tv_yg)

                            for kt in range(max(0, KT - NBK), KT):
                                last_tanh[kt % NBK] = t_tanh_l[kt]

                a6ES = contextlib.ExitStack()
                with a6ES:
                    wo16 = [a6ES.enter_context(nc.sbuf_tensor(f"wo16_{i}", [128, 16 * 512], BF16)) for i in range(2)]
                    ygl = a6ES.enter_context(nc.sbuf_tensor("ygl", [128, 16 * 128], BF16))
                    yslab = a6ES.enter_context(nc.sbuf_tensor("yslab", [128, 512], F32))
                    t_wo_used = [0, 0]
                    for dm in range(4):
                        @blk.gpsimd
                        def _(e, dm=dm):
                            k.join(e, "g") if dm < 2 else k.wait(e, "g", "pT", t_wo_used[dm % 2])
                            k.dma(e, "g", f"Lwo{dm % 2}",
                                  out=wo16[dm % 2][:].rearrange("p (c n) -> p c n", n=512),
                                  in_=w_o.ap().rearrange("(c p) o -> p c o", p=128)[:, :, dm * 512:(dm + 1) * 512])
                        t_wo = k.st[f"Lwo{dm % 2}"].n

                        for t in range(NTT):
                            @blk.scalar
                            def _(e, t=t):
                                k.join(e, "s")
                                k.wait(e, "s", "Lyg", k.st["Lyg"].n)
                                k.dma(e, "s", "Lygl",
                                      out=ygl[:].rearrange("p (h j) -> p h j", j=128),
                                      in_=ygall_d.ap().rearrange("(h p) tok -> p h tok", p=128)[:, :, t * 128:(t + 1) * 128])
                            t_ygl = k.st["Lygl"].n

                            @blk.tensor
                            def _(e, dm=dm, t=t, t_wo=t_wo, t_ygl=t_ygl):
                                k.wait(e, "t", f"Lwo{dm % 2}", t_wo)
                                k.wait(e, "t", "Lygl", t_ygl)
                                k.wait(e, "t", "pS", k.st["pS"].n)
                                ps = [pA, pB][t % 2]
                                k.inc(e.matmul(ps[:], ones_row[:], bo_s[0:1, dm * 512:(dm + 1) * 512],
                                               start=True, stop=False), "pT")
                                for hh in range(16):
                                    k.inc(e.matmul(ps[:], ygl[:, hh * 128:(hh + 1) * 128],
                                                   wo16[dm % 2][:, hh * 512:(hh + 1) * 512],
                                                   start=False, stop=(hh == 15)), "pT")
                            tt_ = k.st["pT"].n
                            t_wo_used[dm % 2] = tt_

                            @blk.scalar
                            def _(e, dm=dm, t=t, tt_=tt_):
                                k.wait(e, "s", "pS", k.st["pS"].n)
                                k.wait(e, "s", "Lyp", k.st["Lyp"].n if "Lyp" in k.st else 0)
                                k.wait(e, "s", "pT", tt_)
                                ps = [pA, pB][t % 2]
                                k.inc(e.activation(yslab[:], ps[:], AF.Identity), "pS")
                            tS_y = k.st["pS"].n

                            @blk.sync
                            def _(e, dm=dm, t=t, tS_y=tS_y):
                                k.wait(e, "sync", "pS", tS_y)
                                k.dma(e, "sync", "Lyp", out=yproj.ap()[t * 128:(t + 1) * 128, dm * 512:(dm + 1) * 512],
                                      in_=yslab[:])

            @blk.scalar
            def _(e):
                k.wait(e, "s", "Lyp", k.st["Lyp"].n)
            residual_pass(x_own, yproj, xres[0])

        hq_d = dint("hq_d", [DFF, NTOK], BF16)

        def ffn_phase(src_d, dst_d, tag):
            fES = contextlib.ExitStack()
            with fES:
                xqT = fES.enter_context(nc.sbuf_tensor(f"xqT_{tag}", [128, 16 * NTOK], BF16))
                am = fES.enter_context(nc.sbuf_tensor(f"am_{tag}", [128, 1], F32))
                s127 = fES.enter_context(nc.sbuf_tensor(f"s127_{tag}", [128, 1], F32))
                rs_t = fES.enter_context(nc.sbuf_tensor(f"rs_{tag}", [128, 8], F32))
                rs2_t = fES.enter_context(nc.sbuf_tensor(f"rs2_{tag}", [128, 8], F32))
                rs3_t = fES.enter_context(nc.sbuf_tensor(f"rs3_{tag}", [128, 8], F32))
                rsh_t = fES.enter_context(nc.sbuf_tensor(f"rsh_{tag}", [128, 8], F32))
                hq16 = fES.enter_context(nc.sbuf_tensor(f"hq16_{tag}", [128, 512], BF16))

                fqES = contextlib.ExitStack()
                with fqES:
                    xtile_f = fqES.enter_context(nc.sbuf_tensor(f"xtf_{tag}", [128, D], F32))
                    xsq_f = fqES.enter_context(nc.sbuf_tensor(f"xsqf_{tag}", [128, D], F32))
                    xn_f = fqES.enter_context(nc.sbuf_tensor(f"xnf_{tag}", [128, D], F32))
                    yq = fqES.enter_context(nc.sbuf_tensor(f"yq_{tag}", [128, D], F32))
                    yqi = fqES.enter_context(nc.sbuf_tensor(f"yqi_{tag}", [128, D], I32))
                    xq16 = fqES.enter_context(nc.sbuf_tensor(f"xq16_{tag}", [128, D], BF16))
                    s1 = fqES.enter_context(nc.sbuf_tensor(f"s1_{tag}", [128, 1], F32))
                    s2 = fqES.enter_context(nc.sbuf_tensor(f"s2_{tag}", [128, 1], F32))
                    nmr_f = fqES.enter_context(nc.sbuf_tensor(f"nmrf_{tag}", [128, 1], F32))
                    var_f = fqES.enter_context(nc.sbuf_tensor(f"varf_{tag}", [128, 8], F32))
                    mean_f = fqES.enter_context(nc.sbuf_tensor(f"meanf_{tag}", [128, 8], F32))
                    rstd_fa = fqES.enter_context(nc.sbuf_tensor(f"rstdfa_{tag}", [128, 8], F32))

                    for t in range(NTT):
                        ln_stats(src_d, t, xtile_f, xsq_f, s1, s2, var_f, mean_f, t)
                    batch_rsqrt(var_f, rstd_fa, NTT, EPS_LN)
                    for t in range(NTT):
                        tS = ln_apply(src_d, t, xtile_f, rstd_fa, mean_f, nmr_f, t, xn_f)
                        @blk.vector
                        def _(e, t=t, tS=tS):
                            k.wait(e, "v", "pS", tS)
                            k.wait(e, "v", "pV", k.st["pV"].n)
                            k.inc(e.reduce_max(am[:], xn_f[:], axis=AX.X, apply_absolute_value=True), "pV")
                            k.wait(e, "v", "pV", k.st["pV"].n)
                            k.inc(e.tensor_scalar_max(am[:], am[:], EPS_Q), "pV")
                            k.wait(e, "v", "pV", k.st["pV"].n)
                            k.inc(e.reciprocal(s127[:], am[:]), "pV")
                            k.wait(e, "v", "pV", k.st["pV"].n)
                            k.inc(e.tensor_scalar_mul(s127[:], s127[:], 127.0), "pV")
                            k.inc(e.tensor_scalar_mul(rs_t[:, t:t + 1], am[:], 1.0 / 127.0), "pV")
                            k.wait(e, "v", "pV", k.st["pV"].n)
                            k.inc(e.tensor_scalar_mul(rs2_t[:, t:t + 1], rs_t[:, t:t + 1], 0.5), "pV")
                            k.inc(e.tensor_mul(rs3_t[:, t:t + 1], rs_t[:, t:t + 1], rs_t[:, t:t + 1]), "pV")
                            k.wait(e, "v", "pV", k.st["pV"].n)
                            k.inc(e.tensor_scalar_mul(rs3_t[:, t:t + 1], rs3_t[:, t:t + 1], 0.5), "pV")
                            k.inc(e.tensor_scalar_mul(yq[:], xn_f[:], s127[:]), "pV")
                            k.wait(e, "v", "pV", k.st["pV"].n)
                            k.inc(e.tensor_scalar_add(yq[:], yq[:], RC), "pV")
                            k.wait(e, "v", "pV", k.st["pV"].n)
                            k.inc(e.tensor_copy(yqi[:], yq[:]), "pV")
                            k.wait(e, "v", "pV", k.st["pV"].n)
                            k.inc(e.tensor_scalar_add(xq16[:], yqi[:], -RC), "pV")
                        transposeN(xq16, [c * 128 for c in range(16)], xqT,
                                   [c * NTOK + t * 128 for c in range(16)], BF16, 16)

                f2ES = contextlib.ExitStack()
                with f2ES:
                    h16 = f2ES.enter_context(nc.sbuf_tensor(f"h16_{tag}", [128, 8 * DFF], BF16))
                    f3ES = contextlib.ExitStack()
                    with f3ES:
                        FW = 256
                        NFC = 22
                        wgb = [f3ES.enter_context(nc.sbuf_tensor(f"wgb{i}_{tag}", [128, 16 * FW], BF16)) for i in range(2)]
                        wub = [f3ES.enter_context(nc.sbuf_tensor(f"wub{i}_{tag}", [128, 16 * FW], BF16)) for i in range(2)]
                        tga = f3ES.enter_context(nc.sbuf_tensor(f"tga_{tag}", [128, FW], F32))
                        tgb = f3ES.enter_context(nc.sbuf_tensor(f"tgb_{tag}", [128, FW], F32))
                        t_w_used = [0, 0]
                        prev_tanh = 0
                        prev_silu = 0
                        for fc in range(NFC):
                            fw = FW if fc < 21 else DFF - 21 * FW
                            @blk.sync
                            def _(e, fc=fc, fw=fw):
                                k.join(e, "sync") if fc < 2 else k.wait(e, "sync", "pT", t_w_used[fc % 2])
                                k.wait(e, "sync", "cc", 3)
                                k.dma(e, "sync", f"Lwg{fc % 2}", first=True,
                                      out=wgb[fc % 2][:, 0:16 * fw].rearrange("p (c n) -> p c n", n=fw),
                                      in_=cc_wg_out.ap().rearrange("(c p) o -> p c o", p=128)[:, :, fc * FW:fc * FW + fw])
                                k.dma(e, "sync", f"Lwg{fc % 2}", first=False,
                                      out=wub[fc % 2][:, 0:16 * fw].rearrange("p (c n) -> p c n", n=fw),
                                      in_=cc_wu_out.ap().rearrange("(c p) o -> p c o", p=128)[:, :, fc * FW:fc * FW + fw])
                            t_w = k.st[f"Lwg{fc % 2}"].n

                            pT0 = k.st["pT"].n
                            pV0 = k.st["pV"].n
                            pS0 = k.st["pS"].n
                            tt_l = [pT0 + 32 * (t + 1) for t in range(NTT)]
                            ts_l = [pS0 + t + 1 for t in range(NTT)]
                            sv1_l = [pV0 + 2 * t + 1 for t in range(NTT)]
                            sv2_l = [pV0 + 2 * t + 2 for t in range(NTT)]

                            @blk.scalar
                            def _(e, fw=fw, tt_l=tt_l, sv1_l=sv1_l, prev_silu=prev_silu):
                                for t in range(NTT):
                                    k.wait(e, "s", "pS", k.st["pS"].n)
                                    k.wait(e, "s", "pT", tt_l[t])
                                    k.wait(e, "s", "pV", sv1_l[t - 1] if t >= 1 else prev_silu)
                                    k.inc(e.activation(tga[:, 0:fw], pA[:, 0:fw], AF.Tanh,
                                                       scale=rs2_t[:, t:t + 1]), "pS")
                            assert k.st["pS"].n == ts_l[-1]

                            for t in range(NTT):
                                @blk.tensor
                                def _(e, fc=fc, fw=fw, t=t, t_w=t_w, ts_l=ts_l, sv2_l=sv2_l,
                                      prev_tanh=prev_tanh, prev_silu=prev_silu):
                                    k.wait(e, "t", f"Lwg{fc % 2}", t_w)
                                    k.wait(e, "t", "pV", sv2_l[t - 1] if t >= 1 else prev_silu)
                                    k.wait(e, "t", "pS", ts_l[t - 1] if t >= 1 else prev_tanh)
                                    for dd in range(16):
                                        k.inc(e.matmul(pA[:, 0:fw], xqT[:, dd * NTOK + t * 128:dd * NTOK + t * 128 + 128],
                                                       wgb[fc % 2][:, dd * fw:(dd + 1) * fw],
                                                       start=(dd == 0), stop=(dd == 15)), "pT")
                                    for dd in range(16):
                                        k.inc(e.matmul(pB[:, 0:fw], xqT[:, dd * NTOK + t * 128:dd * NTOK + t * 128 + 128],
                                                       wub[fc % 2][:, dd * fw:(dd + 1) * fw],
                                                       start=(dd == 0), stop=(dd == 15)), "pT")
                            assert k.st["pT"].n == tt_l[-1]
                            t_w_used[fc % 2] = tt_l[-1]

                            @blk.vector
                            def _(e, fc=fc, fw=fw, ts_l=ts_l):
                                for t in range(NTT):
                                    k.wait(e, "v", "pS", ts_l[t])
                                    k.wait(e, "v", "pV", k.st["pV"].n)
                                    k.inc(e.scalar_tensor_tensor(tgb[:, 0:fw], tga[:, 0:fw], 1.0,
                                                                 pA[:, 0:fw], ALU.add, ALU.mult), "pV")
                                    k.wait(e, "v", "pV", k.st["pV"].n)
                                    k.inc(e.scalar_tensor_tensor(
                                        h16[:, t * DFF + fc * FW:t * DFF + fc * FW + fw],
                                        tgb[:, 0:fw], rs3_t[:, t:t + 1], pB[:, 0:fw],
                                        ALU.mult, ALU.mult), "pV")
                            assert k.st["pV"].n == sv2_l[-1]
                            prev_tanh = ts_l[-1]
                            prev_silu = sv2_l[-1]

                    f4ES = contextlib.ExitStack()
                    with f4ES:
                        yq2 = f4ES.enter_context(nc.sbuf_tensor(f"yq2_{tag}", [128, 512], F32))
                        yqi2 = f4ES.enter_context(nc.sbuf_tensor(f"yqi2_{tag}", [128, 512], I32))
                        hstage = f4ES.enter_context(nc.sbuf_tensor(f"hstage_{tag}", [128, 512], BF16))
                        for t in range(NTT):
                            @blk.vector
                            def _(e, t=t):
                                k.wait(e, "v", "pV", k.st["pV"].n)
                                k.inc(e.reduce_max(am[:], h16[:, t * DFF:(t + 1) * DFF],
                                                   axis=AX.X, apply_absolute_value=True), "pV")
                                k.wait(e, "v", "pV", k.st["pV"].n)
                                k.inc(e.tensor_scalar_max(am[:], am[:], EPS_Q), "pV")
                                k.wait(e, "v", "pV", k.st["pV"].n)
                                k.inc(e.reciprocal(s127[:], am[:]), "pV")
                                k.wait(e, "v", "pV", k.st["pV"].n)
                                k.inc(e.tensor_scalar_mul(s127[:], s127[:], 127.0), "pV")
                                k.inc(e.tensor_scalar_mul(rsh_t[:, t:t + 1], am[:], 1.0 / 127.0), "pV")
                            for fc in range(11):
                                fw = 512 if fc < 10 else DFF - 10 * 512
                                nblk = fw // 128
                                @blk.vector
                                def _(e, t=t, fc=fc, fw=fw):
                                    k.wait(e, "v", "pV", k.st["pV"].n)
                                    k.inc(e.tensor_scalar_mul(yq2[:, 0:fw],
                                                              h16[:, t * DFF + fc * 512:t * DFF + fc * 512 + fw],
                                                              s127[:]), "pV")
                                    k.wait(e, "v", "pV", k.st["pV"].n)
                                    k.inc(e.tensor_scalar_add(yq2[:, 0:fw], yq2[:, 0:fw], RC), "pV")
                                    k.wait(e, "v", "pV", k.st["pV"].n)
                                    k.inc(e.tensor_copy(yqi2[:, 0:fw], yq2[:, 0:fw]), "pV")
                                    k.wait(e, "v", "pV", k.st["pV"].n)
                                    k.inc(e.tensor_scalar_add(hq16[:, 0:fw], yqi2[:, 0:fw], -RC), "pV")
                                @blk.vector
                                def _(e):
                                    k.wait(e, "v", "Lhst", k.st["Lhst"].n if "Lhst" in k.st else 0)
                                transposeN(hq16, [j * 128 for j in range(nblk)], hstage,
                                           [j * 128 for j in range(nblk)], BF16, nblk)
                                tv_h = k.st["pV"].n
                                @blk.scalar
                                def _(e, t=t, fc=fc, nblk=nblk, tv_h=tv_h):
                                    k.wait(e, "s", "pV", tv_h)
                                    k.dma(e, "s", "Lhst",
                                          out=hq_d.ap().rearrange("(a p) tok -> p a tok", p=128)[:, fc * 4:fc * 4 + nblk, t * 128:(t + 1) * 128],
                                          in_=hstage[:, 0:nblk * 128].rearrange("p (a j) -> p a j", j=128))

                f5ES = contextlib.ExitStack()
                with f5ES:
                    wdb = f5ES.enter_context(nc.sbuf_tensor(f"wdb_{tag}", [128, 43 * 512], BF16))
                    hql = [f5ES.enter_context(nc.sbuf_tensor(f"hql{i}_{tag}", [128, 43 * 128], BF16)) for i in range(2)]
                    dslab = f5ES.enter_context(nc.sbuf_tensor(f"dslab_{tag}", [128, 512], F32))
                    t_hql_used = [0, 0]
                    prev_evac = [0, 0]
                    for dm in range(4):
                        @blk.sync
                        def _(e, dm=dm):
                            k.join(e, "sync")
                            k.wait(e, "sync", "cc", 4)
                            for ft in range(43):
                                r, sl = ft // 6, ft % 6
                                row = r * 768 + sl * 128
                                k.dma(e, "sync", "Lwd", first=(ft == 0),
                                      out=wdb[:, ft * 512:(ft + 1) * 512],
                                      in_=cc_wd_out.ap()[row:row + 128, dm * 512:(dm + 1) * 512])
                        t_wd = k.st["Lwd"].n

                        pT0 = k.st["pT"].n
                        pS0 = k.st["pS"].n
                        lyp0 = k.st["Lyp"].n if "Lyp" in k.st else 0
                        tt_l = [pT0 + 43 * (t + 1) for t in range(NTT)]
                        ts_l = [pS0 + t + 1 for t in range(NTT)]
                        lhq_tok = []

                        @blk.scalar
                        def _(e, dm=dm, tt_l=tt_l, lyp0=lyp0):
                            for t in range(NTT):
                                if dm == 0 and t < 2:
                                    k.join(e, "s")
                                else:
                                    k.wait(e, "s", "pT", t_hql_used[t % 2])
                                k.wait(e, "s", "Lhst", k.st["Lhst"].n)
                                lhq_tok.append(k.dma(e, "s", f"Lhq{t % 2}",
                                      out=hql[t % 2][:].rearrange("p (a j) -> p a j", j=128),
                                      in_=hq_d.ap().rearrange("(a p) tok -> p a tok", p=128)[:, :, t * 128:(t + 1) * 128]))
                                k.wait(e, "s", "pS", k.st["pS"].n)
                                k.wait(e, "s", "Lyp", lyp0 + 16 * t)
                                k.wait(e, "s", "pT", tt_l[t])
                                ps = [pA, pB][t % 2]
                                k.inc(e.activation(dslab[:], ps[:], AF.Identity,
                                                   scale=rsh_t[:, t:t + 1]), "pS")
                                t_hql_used[t % 2] = tt_l[t]
                        assert k.st["pS"].n == ts_l[-1]

                        for t in range(NTT):
                            @blk.tensor
                            def _(e, dm=dm, t=t, t_wd=t_wd, t_hq=lhq_tok[t], ts_l=ts_l,
                                  pe0=prev_evac[0], pe1=prev_evac[1]):
                                k.wait(e, "t", "Lwd", t_wd)
                                k.wait(e, "t", f"Lhq{t % 2}", t_hq)
                                k.wait(e, "t", "pS",
                                       ts_l[t - 2] if t >= 2 else (pe0 if t % 2 == 0 else pe1))
                                ps = [pA, pB][t % 2]
                                for ft in range(43):
                                    k.inc(e.matmul(ps[:], hql[t % 2][:, ft * 128:(ft + 1) * 128],
                                                   wdb[:, ft * 512:(ft + 1) * 512],
                                                   start=(ft == 0), stop=(ft == 42)), "pT")

                            @blk.sync
                            def _(e, dm=dm, t=t, ts_l=ts_l):
                                k.wait(e, "sync", "pS", ts_l[t])
                                k.dma(e, "sync", "Lyp", out=yproj.ap()[t * 128:(t + 1) * 128, dm * 512:(dm + 1) * 512],
                                      in_=dslab[:])
                        assert k.st["pT"].n == tt_l[-1]
                        prev_evac[0] = ts_l[NTT - 2]
                        prev_evac[1] = ts_l[NTT - 1]

            residual_pass(src_d, yproj, dst_d)

        if upto >= 2:
            ffn_phase(xres[0], xres[1], "f1")

        def attnb_phase(src_d, dst_d):
            kT_d = dint("kT_d", [D, T], BF16)
            vT_d = dint("vT_d", [D, T], BF16)
            yqT_d = dint("yqT_d", [D, NTOK], BF16)
            bES = contextlib.ExitStack()
            with bES:
                qbT = bES.enter_context(nc.sbuf_tensor("qbT", [128, 16 * NTOK], BF16))
                yF = bES.enter_context(nc.sbuf_tensor("yF", [128, 16 * NTOK], BF16))
                am_b = bES.enter_context(nc.sbuf_tensor("amb", [128, 1], F32))
                s127b = bES.enter_context(nc.sbuf_tensor("s127b", [128, 1], F32))
                rsx_t = bES.enter_context(nc.sbuf_tensor("rsx_t", [128, 8], F32))
                rsy_t = bES.enter_context(nc.sbuf_tensor("rsy_t", [128, 8], F32))
                yqb = bES.enter_context(nc.sbuf_tensor("yqb", [128, D], F32))
                yqib = bES.enter_context(nc.sbuf_tensor("yqib", [128, D], I32))
                xq16b = bES.enter_context(nc.sbuf_tensor("xq16b", [128, D], BF16))
                wkvb = [bES.enter_context(nc.sbuf_tensor(f"wkvb{i}", [128, 16 * 128], BF16)) for i in range(2)]

                b1ES = contextlib.ExitStack()
                with b1ES:
                    xtile_b = b1ES.enter_context(nc.sbuf_tensor("xtb", [128, D], F32))
                    xsq_b = b1ES.enter_context(nc.sbuf_tensor("xsqb", [128, D], F32))
                    xn_b = b1ES.enter_context(nc.sbuf_tensor("xnb", [128, D], F32))
                    s1b = b1ES.enter_context(nc.sbuf_tensor("s1b", [128, 1], F32))
                    s2b = b1ES.enter_context(nc.sbuf_tensor("s2b", [128, 1], F32))
                    nmr_b = b1ES.enter_context(nc.sbuf_tensor("nmrb", [128, 1], F32))
                    var_b = b1ES.enter_context(nc.sbuf_tensor("varb", [128, 8], F32))
                    mean_b = b1ES.enter_context(nc.sbuf_tensor("meanb", [128, 8], F32))
                    rstd_ba = b1ES.enter_context(nc.sbuf_tensor("rstdba", [128, 8], F32))
                    xqT2 = b1ES.enter_context(nc.sbuf_tensor("xqT2", [128, 16 * NTOK], BF16))

                    for t in range(NTT):
                        ln_stats(src_d, t, xtile_b, xsq_b, s1b, s2b, var_b, mean_b, t)
                    batch_rsqrt(var_b, rstd_ba, NTT, EPS_LN)
                    for t in range(NTT):
                        tS = ln_apply(src_d, t, xtile_b, rstd_ba, mean_b, nmr_b, t, xn_b)
                        @blk.vector
                        def _(e, t=t, tS=tS):
                            k.wait(e, "v", "pS", tS)
                            k.wait(e, "v", "pV", k.st["pV"].n)
                            k.inc(e.reduce_max(am_b[:], xn_b[:], axis=AX.X, apply_absolute_value=True), "pV")
                            k.wait(e, "v", "pV", k.st["pV"].n)
                            k.inc(e.tensor_scalar_max(am_b[:], am_b[:], EPS_Q), "pV")
                            k.wait(e, "v", "pV", k.st["pV"].n)
                            k.inc(e.reciprocal(s127b[:], am_b[:]), "pV")
                            k.wait(e, "v", "pV", k.st["pV"].n)
                            k.inc(e.tensor_scalar_mul(s127b[:], s127b[:], 127.0), "pV")
                            k.inc(e.tensor_scalar_mul(rsx_t[:, t:t + 1], am_b[:], 1.0 / 127.0), "pV")
                            k.wait(e, "v", "pV", k.st["pV"].n)
                            k.inc(e.tensor_scalar_mul(yqb[:], xn_b[:], s127b[:]), "pV")
                            k.wait(e, "v", "pV", k.st["pV"].n)
                            k.inc(e.tensor_scalar_add(yqb[:], yqb[:], RC), "pV")
                            k.wait(e, "v", "pV", k.st["pV"].n)
                            k.inc(e.tensor_copy(yqib[:], yqb[:]), "pV")
                            k.wait(e, "v", "pV", k.st["pV"].n)
                            k.inc(e.tensor_scalar(xq16b[:], yqib[:], -RC, rsx_t[:, t:t + 1],
                                                  ALU.add, ALU.mult), "pV")
                        transposeN(xq16b, [c * 128 for c in range(16)], xqT2,
                                   [c * NTOK + t * 128 for c in range(16)], BF16, 16)
                    tv_xq = k.st["pV"].n

                    @blk.sync
                    def _(e):
                        k.wait(e, "sync", "pV", tv_xq)
                        k.dma(e, "sync", "Lccx2",
                              out=cc_in_x2.ap().rearrange("(c p) tok -> p c tok", p=128),
                              in_=xqT2[:].rearrange("p (c tok) -> p c tok", tok=NTOK))
                    t_ccin2 = k.st["Lccx2"].n

                    @blk.gpsimd
                    def _(e):
                        k.wait(e, "g", "Lccx2", t_ccin2)
                        k.inc(e.collective_compute(
                            "AllGather", ALU.bypass,
                            replica_groups=[[0, 1], [2, 3], [4, 5], [6, 7]],
                            ins=[cc_in_x2.ap().opt()], outs=[cc_out_x2.ap().opt()]), "cc")
                    t_cc_x2 = k.st["cc"].n

                    t_wb_used = [0, 0]
                    for oc in range(16):
                        @blk.sync
                        def _(e, oc=oc):
                            k.join(e, "sync") if oc < 2 else k.wait(e, "sync", "pT", t_wb_used[oc % 2])
                            k.wait(e, "sync", "cc", 5)
                            k.dma(e, "sync", f"Lwb{oc % 2}",
                                  out=wkvb[oc % 2][:].rearrange("p (c n) -> p c n", n=128),
                                  in_=cc_wqkv_out.ap().rearrange("(c p) o -> p c o", p=128)[:, :, oc * 128:(oc + 1) * 128])
                        t_wb = k.st[f"Lwb{oc % 2}"].n

                        for tc in range(2):
                            @blk.tensor
                            def _(e, oc=oc, tc=tc, t_wb=t_wb):
                                k.wait(e, "t", f"Lwb{oc % 2}", t_wb)
                                k.wait(e, "t", "pV", k.st["pV"].n)
                                ps = [pA, pB][tc % 2]
                                for cd in range(16):
                                    k.inc(e.matmul(ps[:], wkvb[oc % 2][:, cd * 128:(cd + 1) * 128],
                                                   xqT2[:, cd * NTOK + tc * 512:cd * NTOK + tc * 512 + 512],
                                                   start=(cd == 0), stop=(cd == 15)), "pT")
                            tt_ = k.st["pT"].n
                            t_wb_used[oc % 2] = tt_

                            @blk.vector
                            def _(e, oc=oc, tc=tc, tt_=tt_):
                                k.wait(e, "v", "pT", tt_)
                                ps = [pA, pB][tc % 2]
                                k.inc(e.tensor_copy(qbT[:, oc * NTOK + tc * 512:oc * NTOK + tc * 512 + 512], ps[:]), "pV")

                b2ES = contextlib.ExitStack()
                with b2ES:
                    xqTall = b2ES.enter_context(nc.sbuf_tensor("xqTall", [128, 16 * T], BF16))
                    ktmp16 = b2ES.enter_context(nc.sbuf_tensor("ktmp16", [128, 512], BF16))
                    vstage = b2ES.enter_context(nc.sbuf_tensor("vstage", [128, 512], BF16))

                    @blk.gpsimd
                    def _(e):
                        k.wait(e, "g", "cc", t_cc_x2)
                        k.join(e, "g")
                        ccv = cc_out_x2.ap().rearrange("(r c p) tok -> r p c tok", p=128, c=16)
                        dst = xqTall[:].rearrange("p (c tok) -> p c tok", tok=T)
                        for bi, (r, co) in enumerate(((0, 0), (1, 0), (1, 512), (0, 512))):
                            k.dma(e, "g", "Lxqa", first=(bi == 0), out=dst[:, :, bi * 512:(bi + 1) * 512],
                                  in_=ccv[r, :, :, co:co + 512])
                    t_xqall = k.st["Lxqa"].n

                    t_wb_used = [0, 0]
                    for oc in range(16, 48):
                        @blk.sync
                        def _(e, oc=oc):
                            k.join(e, "sync") if oc < 18 else k.wait(e, "sync", "pT", t_wb_used[oc % 2])
                            k.dma(e, "sync", f"Lwb{oc % 2}",
                                  out=wkvb[oc % 2][:].rearrange("p (c n) -> p c n", n=128),
                                  in_=cc_wqkv_out.ap().rearrange("(c p) o -> p c o", p=128)[:, :, oc * 128:(oc + 1) * 128])
                        t_wb = k.st[f"Lwb{oc % 2}"].n

                        for tc in range(4):
                            @blk.tensor
                            def _(e, oc=oc, tc=tc, t_wb=t_wb):
                                k.wait(e, "t", f"Lwb{oc % 2}", t_wb)
                                k.wait(e, "t", "Lxqa", t_xqall)
                                k.wait(e, "t", "pV", k.st["pV"].n)
                                k.wait(e, "t", "pS", k.st["pS"].n)
                                ps = [pA, pB][tc % 2]
                                for cd in range(16):
                                    k.inc(e.matmul(ps[:], wkvb[oc % 2][:, cd * 128:(cd + 1) * 128],
                                                   xqTall[:, cd * T + tc * 512:cd * T + tc * 512 + 512],
                                                   start=(cd == 0), stop=(cd == 15)), "pT")
                            tt_ = k.st["pT"].n
                            t_wb_used[oc % 2] = tt_

                            if oc < 32:
                                @blk.vector
                                def _(e, tc=tc, tt_=tt_):
                                    k.wait(e, "v", "pT", tt_)
                                    k.wait(e, "v", "Lkst2", k.st["Lkst2"].n if "Lkst2" in k.st else 0)
                                    k.wait(e, "v", "Lvst2", k.st["Lvst2"].n if "Lvst2" in k.st else 0)
                                    ps = [pA, pB][tc % 2]
                                    k.inc(e.tensor_copy(ktmp16[:], ps[:]), "pV")
                                tv_ = k.st["pV"].n
                                @blk.scalar
                                def _(e, oc=oc, tc=tc, tv_=tv_):
                                    k.wait(e, "s", "pV", tv_)
                                    k.dma(e, "s", "Lkst2",
                                          out=kT_d.ap()[(oc - 16) * 128:(oc - 15) * 128, tc * 512:(tc + 1) * 512],
                                          in_=ktmp16[:])
                            else:
                                @blk.vector
                                def _(e, tc=tc, tt_=tt_):
                                    k.wait(e, "v", "pT", tt_)
                                    k.wait(e, "v", "pV", k.st["pV"].n)
                                    k.wait(e, "v", "Lkst2", k.st["Lkst2"].n if "Lkst2" in k.st else 0)
                                    ps = [pA, pB][tc % 2]
                                    k.inc(e.tensor_copy(ktmp16[:], ps[:]), "pV")
                                tv_ = k.st["pV"].n
                                for j in range(4):
                                    @blk.tensor
                                    def _(e, j=j, tv_=tv_):
                                        k.wait(e, "t", "pV", k.st["pV"].n)
                                        k.inc(e.transpose(pT16[:, (j % 4) * 128:(j % 4) * 128 + 128],
                                                          ktmp16[:, j * 128:(j + 1) * 128], ident16[:]), "pT")
                                    tt2 = k.st["pT"].n
                                    @blk.vector
                                    def _(e, j=j, tt2=tt2):
                                        k.wait(e, "v", "pT", tt2)
                                        k.wait(e, "v", "Lvst2", k.st["Lvst2"].n if "Lvst2" in k.st else 0)
                                        k.inc(e.tensor_copy(vstage[:, j * 128:(j + 1) * 128],
                                                            pT16[:, (j % 4) * 128:(j % 4) * 128 + 128]), "pV")
                                tv2_ = k.st["pV"].n
                                @blk.scalar
                                def _(e, oc=oc, tc=tc, tv2_=tv2_):
                                    k.wait(e, "s", "pV", tv2_)
                                    k.dma(e, "s", "Lvst2",
                                          out=vT_d.ap()[(oc - 32) * 128:(oc - 31) * 128, tc * 512:(tc + 1) * 512],
                                          in_=vstage[:])

                b3ES = contextlib.ExitStack()
                with b3ES:
                    nrmsc = b3ES.enter_context(nc.sbuf_tensor("nrmsc", [128, T], F32))
                    nqk = b3ES.enter_context(nc.sbuf_tensor("nqk", [128, 256], F32))
                    nred = b3ES.enter_context(nc.sbuf_tensor("nred", [128, 16], F32))
                    ntr = b3ES.enter_context(nc.sbuf_tensor("ntr", [16, 128], F32))
                    nqm = b3ES.enter_context(nc.sbuf_tensor("nqm", [16, 1], F32))
                    nkm = b3ES.enter_context(nc.sbuf_tensor("nkm", [16, 1], F32))
                    brow = b3ES.enter_context(nc.sbuf_tensor("brow", [1, 16], F32))
                    Bneg = b3ES.enter_context(nc.sbuf_tensor("Bneg", [128, 16], F32))
                    khb2 = [b3ES.enter_context(nc.sbuf_tensor(f"khb2_{i}", [128, T], BF16)) for i in range(2)]
                    vhb2 = [b3ES.enter_context(nc.sbuf_tensor(f"vhb2_{i}", [128, T], BF16)) for i in range(2)]
                    b01 = b3ES.enter_context(nc.sbuf_tensor("b01", [128, 16 * 512], BF16))
                    pb16b = [b3ES.enter_context(nc.sbuf_tensor(f"pb16b{i}", [128, 512], BF16)) for i in range(4)]
                    rrow_b = b3ES.enter_context(nc.sbuf_tensor("rrow_b", [1, 512], F32))
                    rb_b = b3ES.enter_context(nc.sbuf_tensor("rb_b", [128, 512], F32))

                    def norm_pass(is_k, out_nm):
                        nt = 16 if is_k else 8
                        for h in range(HEADS):
                            if is_k:
                                @blk.scalar
                                def _(e, h=h):
                                    k.join(e, "s")
                                    k.wait(e, "s", "Lkst2", k.st["Lkst2"].n)
                                    k.dma(e, "s", f"Lkv2{h % 2}", out=khb2[h % 2][:],
                                          in_=kT_d.ap()[h * 128:(h + 1) * 128, :])
                                tS0 = k.st[f"Lkv2{h % 2}"].n
                            else:
                                tS0 = 0
                            @blk.scalar
                            def _(e, h=h, tS0=tS0, is_k=is_k):
                                k.wait(e, "s", "pS", k.st["pS"].n)
                                k.wait(e, "s", "pT", k.st["pT"].n)
                                if is_k:
                                    k.wait(e, "s", f"Lkv2{h % 2}", tS0)
                                    k.inc(e.activation(nrmsc[:, 0:T], khb2[h % 2][:], AF.Square), "pS")
                                else:
                                    k.inc(e.activation(nrmsc[:, 0:NTOK],
                                                       qbT[:, h * NTOK:(h + 1) * NTOK], AF.Square), "pS")
                            tS2 = k.st["pS"].n
                            @blk.tensor
                            def _(e, h=h, nt=nt, tS2=tS2):
                                k.wait(e, "t", "pS", tS2)
                                k.wait(e, "t", "pG", t_iota)
                                if h == 0:
                                    k.wait(e, "t", "pV", k.st["pV"].n)
                                for tt in range(nt):
                                    k.inc(e.matmul(pD[0:128, h * nt + tt:h * nt + tt + 1],
                                                   nrmsc[:, tt * 128:(tt + 1) * 128], ones_col32[:],
                                                   start=True, stop=True), "pT")
                        tt_ = k.st["pT"].n
                        @blk.vector
                        def _(e, nt=nt, tt_=tt_):
                            k.wait(e, "v", "pT", tt_)
                            k.wait(e, "v", "pV", k.st["pV"].n)
                            k.inc(e.tensor_copy(nqk[:, 0:16 * nt], pD[:, 0:16 * nt]), "pV")
                            k.wait(e, "v", "pV", k.st["pV"].n)
                            k.inc(e.reduce_max(nred[:], nqk[:, 0:16 * nt].rearrange(
                                "p (h t) -> p h t", t=nt), axis=AX.X), "pV")
                        tv_ = k.st["pV"].n
                        @blk.tensor
                        def _(e, tv_=tv_):
                            k.wait(e, "t", "pV", tv_)
                            k.inc(e.matmul(pT32[0:16, 0:128], nred[:], ident32[:], start=True, stop=True), "pT")
                        tt2 = k.st["pT"].n
                        @blk.vector
                        def _(e, out_nm=out_nm, tt2=tt2):
                            k.wait(e, "v", "pT", tt2)
                            k.wait(e, "v", "pV", k.st["pV"].n)
                            k.inc(e.tensor_copy(ntr[:], pT32[0:16, 0:128]), "pV")
                            k.wait(e, "v", "pV", k.st["pV"].n)
                            k.inc(e.reduce_max(out_nm[:], ntr[:], axis=AX.X), "pV")

                    norm_pass(False, nqm)
                    norm_pass(True, nkm)

                    @blk.vector
                    def _(e):
                        k.wait(e, "v", "pV", k.st["pV"].n)
                        k.inc(e.tensor_mul(nqm[:], nqm[:], nkm[:]), "pV")
                    tv_p = k.st["pV"].n

                    @blk.scalar
                    def _(e, tv_p=tv_p):
                        k.wait(e, "s", "pV", tv_p)
                        k.wait(e, "s", "pS", k.st["pS"].n)
                        k.inc(e.activation(nkm[:], nqm[:], AF.Sqrt, scale=1.0 / 128.0), "pS")
                    tS_B = k.st["pS"].n

                    @blk.tensor
                    def _(e, tS_B=tS_B):
                        k.wait(e, "t", "pS", tS_B)
                        k.inc(e.matmul(pT32[0:1, 0:16], nkm[:], ident32[0:16, 0:16], start=True, stop=True), "pT")
                    tt_B = k.st["pT"].n

                    @blk.vector
                    def _(e, tt_B=tt_B):
                        k.wait(e, "v", "pT", tt_B)
                        k.wait(e, "v", "pV", k.st["pV"].n)
                        k.inc(e.tensor_scalar_mul(brow[:], pT32[0:1, 0:16], -1.0), "pV")
                    tv_br = k.st["pV"].n

                    @blk.tensor
                    def _(e, tv_br=tv_br):
                        k.wait(e, "t", "pV", tv_br)
                        k.inc(e.matmul(pE[:, 0:16], ones_row[:], brow[:], start=True, stop=True), "pT")
                    tt_Bn = k.st["pT"].n

                    @blk.vector
                    def _(e, tt_Bn=tt_Bn):
                        k.wait(e, "v", "pT", tt_Bn)
                        k.wait(e, "v", "pV", k.st["pV"].n)
                        k.inc(e.tensor_copy(Bneg[:], pE[:, 0:16]), "pV")
                        k.wait(e, "v", "pG", t_iota)
                        k.wait(e, "v", "pV", k.st["pV"].n)
                        for ci in range(2):
                            for mi in range(8):
                                col = (0 + mi) if ci == 0 else (16 + mi)
                                s_ = ci * 8 + mi
                                k.inc(e.tensor_scalar(b01[:, s_ * 512:(s_ + 1) * 512], iotaJI[:],
                                                      offb_s[:, col:col + 1], None, ALU.is_ge), "pV")
                    tv_b01 = k.st["pV"].n

                    NB4 = 4
                    banks4 = [pA, pB, pF, pT32]
                    last_expb = {0: 0, 1: 0, 2: 0, 3: 0}
                    last_maskb = {0: 0, 1: 0, 2: 0, 3: 0}
                    for ci in range(2):
                        KT = QCH_KT[ci]
                        for h in range(HEADS):
                            masked_l = [(ci == 0) or (kt >= 8) for kt in range(KT)]
                            s_l = [(kt if ci == 0 else 8 + (kt - 8)) if masked_l[kt] else 0
                                   for kt in range(KT)]
                            pT0 = k.st["pT"].n
                            pV0 = k.st["pV"].n
                            pS0 = k.st["pS"].n
                            tt_mm = [pT0 + 3 * kt + 1 for kt in range(KT)]
                            tt_pv = pT0 + 3 * KT
                            tt_bc = tt_pv + 1
                            t_exp_l = [pS0 + kt + 1 for kt in range(KT)]
                            tv_mask = []
                            mv = 0
                            for kt in range(KT):
                                if masked_l[kt]:
                                    mv += 1
                                tv_mask.append(pV0 + mv)
                            tv_r = pV0 + mv + 1
                            tv_y = pV0 + mv + 3

                            @blk.scalar
                            def _(e, ci=ci, h=h, KT=KT, tt_mm=tt_mm, tv_mask=tv_mask, pV0=pV0):
                                k.join(e, "s")
                                k.wait(e, "s", "Lvst2", k.st["Lvst2"].n)
                                k.dma(e, "s", f"Lkv2{h % 2}", first=True, out=khb2[h % 2][:],
                                      in_=kT_d.ap()[h * 128:(h + 1) * 128, :])
                                k.dma(e, "s", f"Lkv2{h % 2}", first=False, out=vhb2[h % 2][:],
                                      in_=vT_d.ap()[h * 128:(h + 1) * 128, :])
                                for kt in range(KT):
                                    bank = kt % NB4
                                    psS = banks4[bank]
                                    k.wait(e, "s", "pS", k.st["pS"].n)
                                    k.wait(e, "s", "pT", tt_mm[kt])
                                    k.wait(e, "s", "pV", tv_mask[kt - NB4] if kt >= NB4 else pV0)
                                    k.inc(e.activation(pb16b[bank][:], psS[:], AF.Exp,
                                                       scale=float(ISQD), bias=Bneg[:, h:h + 1]), "pS")
                            assert k.st["pS"].n == t_exp_l[KT - 1], (k.st["pS"].n,)
                            t_kv = k.st[f"Lkv2{h % 2}"].n

                            @blk.tensor
                            def _(e, ci=ci, h=h, KT=KT, t_kv=t_kv, pV0=pV0,
                                  t_exp_l=t_exp_l, tv_mask=tv_mask, tv_r=tv_r,
                                  les=dict(last_expb), lms=dict(last_maskb)):
                                k.wait(e, "t", f"Lkv2{h % 2}", t_kv)
                                for kt in range(KT):
                                    bank = kt % NB4
                                    psS = banks4[bank]
                                    k.wait(e, "t", "pS",
                                           t_exp_l[kt - NB4] if kt >= NB4 else les[bank])
                                    if kt < NB4:
                                        k.wait(e, "t", "pV", lms[bank])
                                    if kt == 1:
                                        k.wait(e, "t", "pV", pV0)
                                    k.inc(e.matmul(psS[:], khb2[h % 2][:, kt * 128:(kt + 1) * 128],
                                                   qbT[:, h * NTOK + ci * 512:h * NTOK + ci * 512 + 512],
                                                   start=True, stop=True), "pT")
                                    if kt >= 1:
                                        k.wait(e, "t", "pS", t_exp_l[kt - 1])
                                        k.wait(e, "t", "pV", tv_mask[kt - 1])
                                        pbprev = pb16b[(kt - 1) % NB4]
                                        k.inc(e.matmul(pD[0:1, 0:512], ones_col16[:], pbprev[:],
                                                       start=(kt - 1 == 0), stop=False), "pT")
                                        k.inc(e.matmul(pC[:], vhb2[h % 2][:, (kt - 1) * 128:kt * 128],
                                                       pbprev[:], start=(kt - 1 == 0), stop=False), "pT")
                                k.wait(e, "t", "pS", t_exp_l[KT - 1])
                                k.wait(e, "t", "pV", tv_mask[KT - 1])
                                pbprev = pb16b[(KT - 1) % NB4]
                                k.inc(e.matmul(pD[0:1, 0:512], ones_col16[:], pbprev[:],
                                               start=(KT - 1 == 0), stop=True), "pT")
                                k.inc(e.matmul(pC[:], vhb2[h % 2][:, (KT - 1) * 128:KT * 128],
                                               pbprev[:], start=(KT - 1 == 0), stop=True), "pT")
                                k.wait(e, "t", "pV", tv_r)
                                k.inc(e.matmul(pE[:], ones_row[:], rrow_b[:], start=True, stop=True), "pT")
                            assert k.st["pT"].n == tt_bc, (k.st["pT"].n, tt_bc)

                            @blk.vector
                            def _(e, ci=ci, h=h, KT=KT, masked_l=masked_l, s_l=s_l,
                                  t_exp_l=t_exp_l, tt_pv=tt_pv, tt_bc=tt_bc):
                                for kt in range(KT):
                                    if not masked_l[kt]:
                                        continue
                                    bank = kt % NB4
                                    k.wait(e, "v", "pS", t_exp_l[kt])
                                    k.wait(e, "v", "pV", tv_b01)
                                    k.inc(e.tensor_mul(pb16b[bank][:], pb16b[bank][:],
                                                       b01[:, s_l[kt] * 512:(s_l[kt] + 1) * 512]), "pV")
                                k.wait(e, "v", "pT", tt_pv)
                                k.inc(e.reciprocal(rrow_b[:], pD[0:1, 0:512]), "pV")
                                k.wait(e, "v", "pT", tt_bc)
                                k.inc(e.tensor_copy(rb_b[:], pE[:]), "pV")
                                k.wait(e, "v", "pV", k.st["pV"].n)
                                k.inc(e.tensor_mul(yF[:, h * NTOK + ci * 512:h * NTOK + ci * 512 + 512],
                                                   pC[:], rb_b[:]), "pV")
                            assert k.st["pV"].n == tv_y, (k.st["pV"].n, tv_y)

                            for kt in range(max(0, KT - NB4), KT):
                                last_expb[kt % NB4] = t_exp_l[kt]
                            for kt in range(KT):
                                if masked_l[kt]:
                                    last_maskb[kt % NB4] = tv_mask[kt]

                    ytok = b3ES.enter_context(nc.sbuf_tensor("ytok", [128, D], BF16))
                    ystage = b3ES.enter_context(nc.sbuf_tensor("ystage", [128, D], BF16))
                    for t in range(NTT):
                        @blk.vector
                        def _(e):
                            k.wait(e, "v", "pV", k.st["pV"].n)
                        transposeN(yF, [h * NTOK + t * 128 for h in range(16)], ytok,
                                   [h * 128 for h in range(16)], BF16, 16)
                        @blk.vector
                        def _(e, t=t):
                            k.wait(e, "v", "pV", k.st["pV"].n)
                            k.inc(e.reduce_max(am_b[:], ytok[:], axis=AX.X, apply_absolute_value=True), "pV")
                            k.wait(e, "v", "pV", k.st["pV"].n)
                            k.inc(e.tensor_scalar_max(am_b[:], am_b[:], EPS_Q), "pV")
                            k.wait(e, "v", "pV", k.st["pV"].n)
                            k.inc(e.reciprocal(s127b[:], am_b[:]), "pV")
                            k.wait(e, "v", "pV", k.st["pV"].n)
                            k.inc(e.tensor_scalar_mul(s127b[:], s127b[:], 127.0), "pV")
                            k.inc(e.tensor_scalar_mul(rsy_t[:, t:t + 1], am_b[:], 1.0 / 127.0), "pV")
                            k.wait(e, "v", "pV", k.st["pV"].n)
                            k.inc(e.tensor_scalar_mul(yqb[:], ytok[:], s127b[:]), "pV")
                            k.wait(e, "v", "pV", k.st["pV"].n)
                            k.inc(e.tensor_scalar_add(yqb[:], yqb[:], RC), "pV")
                            k.wait(e, "v", "pV", k.st["pV"].n)
                            k.inc(e.tensor_copy(yqib[:], yqb[:]), "pV")
                            k.wait(e, "v", "pV", k.st["pV"].n)
                            k.wait(e, "v", "Lyst", k.st["Lyst"].n if "Lyst" in k.st else 0)
                            k.inc(e.tensor_scalar_add(xq16b[:], yqib[:], -RC), "pV")
                        @blk.vector
                        def _(e):
                            k.wait(e, "v", "Lyst", k.st["Lyst"].n if "Lyst" in k.st else 0)
                        transposeN(xq16b, [c * 128 for c in range(16)], ystage,
                                   [c * 128 for c in range(16)], BF16, 16)
                        tv_y = k.st["pV"].n
                        @blk.scalar
                        def _(e, t=t, tv_y=tv_y):
                            k.wait(e, "s", "pV", tv_y)
                            k.dma(e, "s", "Lyst",
                                  out=yqT_d.ap().rearrange("(c p) tok -> p c tok", p=128)[:, :, t * 128:(t + 1) * 128],
                                  in_=ystage[:].rearrange("p (c j) -> p c j", j=128))

                b4ES = contextlib.ExitStack()
                with b4ES:
                    wpb = [b4ES.enter_context(nc.sbuf_tensor(f"wpb{i}", [128, 16 * 512], BF16)) for i in range(2)]
                    yql = [b4ES.enter_context(nc.sbuf_tensor(f"yql{i}", [128, 16 * 128], BF16)) for i in range(2)]
                    pslab = b4ES.enter_context(nc.sbuf_tensor("pslab", [128, 512], F32))
                    t_wp_used = [0, 0]
                    t_yql_used = [0, 0]
                    for dm in range(4):
                        @blk.sync
                        def _(e, dm=dm):
                            k.join(e, "sync") if dm < 2 else k.wait(e, "sync", "pT", t_wp_used[dm % 2])
                            k.wait(e, "sync", "cc", 6)
                            k.dma(e, "sync", f"Lwp{dm % 2}",
                                  out=wpb[dm % 2][:].rearrange("p (c n) -> p c n", n=512),
                                  in_=cc_wp_out.ap().rearrange("(c p) o -> p c o", p=128)[:, :, dm * 512:(dm + 1) * 512])
                        t_wp = k.st[f"Lwp{dm % 2}"].n

                        for t in range(NTT):
                            @blk.scalar
                            def _(e, t=t):
                                k.join(e, "s") if t < 2 else k.wait(e, "s", "pT", t_yql_used[t % 2])
                                k.wait(e, "s", "Lyst", k.st["Lyst"].n)
                                k.dma(e, "s", f"Lyql{t % 2}",
                                      out=yql[t % 2][:].rearrange("p (c j) -> p c j", j=128),
                                      in_=yqT_d.ap().rearrange("(c p) tok -> p c tok", p=128)[:, :, t * 128:(t + 1) * 128])
                            t_yq = k.st[f"Lyql{t % 2}"].n

                            @blk.tensor
                            def _(e, dm=dm, t=t, t_wp=t_wp, t_yq=t_yq):
                                k.wait(e, "t", f"Lwp{dm % 2}", t_wp)
                                k.wait(e, "t", f"Lyql{t % 2}", t_yq)
                                k.wait(e, "t", "pS", k.st["pS"].n)
                                ps = [pA, pB][t % 2]
                                for cd in range(16):
                                    k.inc(e.matmul(ps[:], yql[t % 2][:, cd * 128:(cd + 1) * 128],
                                                   wpb[dm % 2][:, cd * 512:(cd + 1) * 512],
                                                   start=(cd == 0), stop=(cd == 15)), "pT")
                            tt_ = k.st["pT"].n
                            t_wp_used[dm % 2] = tt_
                            t_yql_used[t % 2] = tt_

                            @blk.scalar
                            def _(e, dm=dm, t=t, tt_=tt_):
                                k.wait(e, "s", "pS", k.st["pS"].n)
                                k.wait(e, "s", "Lyp", k.st["Lyp"].n)
                                k.wait(e, "s", "pT", tt_)
                                ps = [pA, pB][t % 2]
                                k.inc(e.activation(pslab[:], ps[:], AF.Identity,
                                                   scale=rsy_t[:, t:t + 1]), "pS")
                            tS_y = k.st["pS"].n

                            @blk.sync
                            def _(e, dm=dm, t=t, tS_y=tS_y):
                                k.wait(e, "sync", "pS", tS_y)
                                k.dma(e, "sync", "Lyp", out=yproj.ap()[t * 128:(t + 1) * 128, dm * 512:(dm + 1) * 512],
                                      in_=pslab[:])

            residual_pass(src_d, yproj, dst_d)

        if upto >= 3:
            attnb_phase(xres[1], xres[0])
        if upto >= 4:
            ffn_phase(xres[0], xres[1], "f2")

        cur = {0: x_own, 1: xres[0], 2: xres[1], 3: xres[0]}.get(min(upto, 4), xres[1])
        @blk.sync
        def _(e):
            k.wait(e, "sync", "pV", k.st["pV"].n)
            k.wait(e, "sync", "pS", k.st["pS"].n)
            k.wait(e, "sync", "Lresw", k.st["Lresw"].n if "Lresw" in k.st else 0)
            k.dma(e, "sync", "Lout", out=out_t.ap(), in_=cur.ap())
            k.wait(e, "sync", "Lout", k.st["Lout"].n)

    return nc


_PROG_CACHE = {}


def _get_program(upto=99):
    if upto not in _PROG_CACHE:
        _PROG_CACHE[upto] = build_program(upto)
    return _PROG_CACHE[upto]


def _in_map_for_core(core, inp, shared, upto=99):
    x = inp["L0_attn_a_x"]
    qs = _pos_quarters(core)
    b = core // 2
    xsh = np.concatenate([x[b, q * 512:(q + 1) * 512, :] for q in qs], axis=0)
    t = _host_tables(core)
    wdp = shared["w_down_padded"]
    m = {
        "x_own": np.ascontiguousarray(xsh, np.float32),
        "w_q": inp["L0_attn_a_w_q"],
        "b_q": inp["L0_attn_a_b_q"].reshape(1, -1),
        "w_down_a": inp["L0_attn_a_w_down"],
        "b_down_r": np.ascontiguousarray(inp["L0_attn_a_b_down"].reshape(4, 128).T),
        "w_up_a": inp["L0_attn_a_w_up"],
        "b_up_r": np.ascontiguousarray(inp["L0_attn_a_b_up"].reshape(8, 128).T),
        "w_o": inp["L0_attn_a_w_o"],
        "b_o": inp["L0_attn_a_b_o"].reshape(1, -1),
        "qn_w": inp["L0_attn_a_qn_w"].reshape(1, -1),
        "kn_w": inp["L0_attn_a_kn_w"].reshape(1, -1),
        "gate_w": inp["L0_attn_a_gate_w"],
        "gate_b_r": np.ascontiguousarray(inp["L0_attn_a_gate_b"].T),
        "w_qkv": inp["L0_attn_b_w_qkv"],
        "w_proj": inp["L0_attn_b_w_proj"],
        "w_gate_f": inp["L0_ffn_w_gate"],
        "w_up_f": inp["L0_ffn_w_up"],
        "w_down_f": wdp,
        "rope_c": t["rope_c"], "rope_s": t["rope_s"],
        "rope_own_c": t["rope_own_c"], "rope_own_s": t["rope_own_s"],
        "offtab_sel": t["offtab_sel"], "offtab_b": t["offtab_b"],
        "aa_bias": t["aa_bias"],
        "ident32": shared["ident32"], "ident16": shared["ident16"],
        "ones_col16": shared["ones_col16"], "ones_row": shared["ones_row"],
        "half_row": shared["half_row"],
        "mdiv": np.array([[1.0 / (6144 * 2048), 1.0 / (2048 * 2048)]], np.float32),
        "w_gate_sl": np.ascontiguousarray(inp["L0_ffn_w_gate"][:, core * 256:(core + 1) * 256]),
        "w_up_sl": np.ascontiguousarray(inp["L0_ffn_w_up"][:, core * 256:(core + 1) * 256]),
        "w_down_sl": np.ascontiguousarray(wdp[:, core * 768:(core + 1) * 768]),
        "w_qkv_sl": np.ascontiguousarray(inp["L0_attn_b_w_qkv"][:, core * 256:(core + 1) * 256]),
        "w_proj_sl": np.ascontiguousarray(inp["L0_attn_b_w_proj"][:, core * 256:(core + 1) * 256]),
    }
    for kk in list(m):
        a = np.asarray(m[kk])
        if a.dtype == np.float64:
            m[kk] = a.astype(np.float32)
    return m


def kernel(**inputs):
    upto = int(inputs.pop("_upto", 99))
    shared = _shared_consts()
    shared["w_down_padded"] = np.ascontiguousarray(
        np.pad(np.asarray(inputs["L0_ffn_w_down"], np.float32), ((0, 0), (0, DFFP - DFF))))
    nc = _get_program(upto)
    in_maps = [_in_map_for_core(c, inputs, shared, upto) for c in range(8)]
    res = run_bass_kernel_spmd(nc, in_maps, core_ids=list(range(8)))
    out = np.zeros((B, T, D), np.float32)
    for c in range(8):
        o = res.results[c]["out"]
        b = c // 2
        for qi, q in enumerate(_pos_quarters(c)):
            out[b, q * 512:(q + 1) * 512, :] = o[qi * 512:(qi + 1) * 512, :]
    return out


# version 08
# speedup vs baseline: 1.0181x; 1.0181x over previous
import numpy as np
import ml_dtypes
import concourse.bass as bass
import concourse.mybir as mybir
from concourse.bass_utils import run_bass_kernel_spmd

F32 = mybir.dt.float32
BF16 = mybir.dt.bfloat16
I32 = mybir.dt.int32
AF = mybir.ActivationFunctionType
ALU = mybir.AluOpType
AX = mybir.AxisListType
BF16NP = ml_dtypes.bfloat16

B, T, D = 4, 2048, 2048
HEADS, HD = 16, 128
NKV = 4
LAT = 512
DFF = 5504
DFFP = 6144
NTOK = 1024
NTT = 8
RC = 12582912.0
NEGBIG = -3.0e38
NEGM = -3.3e28
EPS_LN = 1e-5
EPS_RMS = 1e-6
EPS_Q = 1e-5
SOFT = 30.0
ISQD = 1.0 / np.sqrt(128.0)

SLOPES = None
def _alibi_slopes(n):
    import math
    def pot(p):
        start = 2.0 ** (-(2.0 ** (-(math.log2(p) - 3))))
        return [start * start ** i for i in range(p)]
    if math.log2(n).is_integer():
        return pot(n)
    cp = 2 ** int(math.floor(math.log2(n)))
    s = pot(cp)
    s += _alibi_slopes(2 * cp)[0::2][: n - cp]
    return s
SLOPES = _alibi_slopes(HEADS)

QCH_KT = [8, 16]
SEL_NCH = [2, 2, 2, 2, 4, 4, 4, 4]


class Stream:
    def __init__(self, nc, name, step=1):
        self._cm = nc.semaphore(name)
        self.sem = self._cm.__enter__()
        self.n = 0
        self.step = step
        self.name = name

    def inc(self, inst, k=None):
        k = self.step if k is None else k
        inst.then_inc(self.sem, k)
        self.n += k
        return self.n


class Ctx:

    def __init__(self, nc, blk):
        self.nc = nc
        self.blk = blk
        self.st = {}
        self.wm = {}

    def stream(self, name, step=1):
        s = Stream(self.nc, name, step)
        self.st[name] = s
        return s

    def wait(self, eng, engname, stname, tok):
        if tok <= 0:
            return
        wm = self.wm.setdefault(engname, {})
        if wm.get(stname, 0) >= tok:
            return
        eng.wait_ge(self.st[stname].sem, tok)
        wm[stname] = tok

    def inc(self, inst, stname, k=None):
        return self.st[stname].inc(inst, k)

    def join(self, e, engname):
        for stn in ("pT", "pV", "pS", "pG"):
            if stn in self.st:
                self.wait(e, engname, stn, self.st[stn].n)

    def dma(self, e, engname, lane, first=True, **kw):
        if lane not in self.st:
            self.st[lane] = Stream(self.nc, lane, 16)
        if first:
            self.wait(e, engname, lane, self.st[lane].n)
        return self.st[lane].inc(e.dma_start(**kw))


def _pos_quarters(core):
    return [0, 3] if core % 2 == 0 else [1, 2]


def _own_positions(core):
    qs = _pos_quarters(core)
    return np.concatenate([np.arange(q * 512, (q + 1) * 512) for q in qs])


def _rope_tables():
    inv = (1.0 / (10000.0 ** (np.arange(0, HD, 2, dtype=np.float32) / HD))).astype(np.float32)
    f = (np.arange(T, dtype=np.float32)[:, None] * inv[None, :]).astype(np.float32)
    idx = (2 * np.arange(64)) % 64
    emb2 = f[:, idx]
    return np.cos(emb2).astype(np.float32), np.sin(emb2).astype(np.float32)


def _host_tables(core):
    qs = _pos_quarters(core)
    ctab, stab = _rope_tables()
    own = _own_positions(core)
    t = {}
    t["rope_c"] = ctab
    t["rope_s"] = stab
    t["rope_own_c"] = ctab[own]
    t["rope_own_s"] = stab[own]
    offsel = np.full((24,), 0.0, np.float32)
    k = 0
    offsel_list = []
    for tt in range(NTT):
        q0 = qs[tt // 4] * 512 + (tt % 4) * 128
        for ch in range(SEL_NCH[tt]):
            offsel_list.append(float(q0 - ch * 512))
    offsel = np.array(offsel_list, np.float32)
    t["offtab_sel"] = np.tile(offsel[None, :], (128, 1)).astype(np.float32)
    offb = []
    for ci in range(2):
        q0 = qs[ci] * 512
        for kt in range(QCH_KT[ci]):
            k0 = kt * 128
            if k0 + 127 < q0:
                offb.append(-1e9)
            elif k0 > q0 + 511:
                offb.append(1e9)
            else:
                offb.append(float(k0 - q0))
    t["offtab_b"] = np.tile(np.array(offb, np.float32)[None, :], (128, 1))
    ab = np.zeros((HEADS, 24), np.float32)
    col = 0
    cols = []
    for ci in range(2):
        q0 = qs[ci] * 512
        for kt in range(QCH_KT[ci]):
            k0 = kt * 128
            cols.append((ci, kt, k0 - q0))
    for h in range(HEADS):
        for j, (ci, kt, dk) in enumerate(cols):
            ab[h, j] = SLOPES[h] * dk
    t["aa_bias"] = np.tile(ab.reshape(1, -1), (128, 1)).astype(np.float32)
    return t


def _shared_consts():
    c = {}
    c["ident32"] = np.eye(128, dtype=np.float32)
    c["ident16"] = np.eye(128, dtype=BF16NP)
    c["ones_col16"] = np.ones((128, 1), BF16NP)
    c["ones_row"] = np.ones((1, 128), np.float32)
    c["half_row"] = np.full((1, 128), 0.5, np.float32)
    return c


PREP_SKIP = set()

def build_program(upto=99):
    import contextlib
    nc = bass.Bass("TRN2", target_bir_lowering=False, num_devices=8)

    dt_in = {}
    def din(name, shape, dtype=F32):
        dt_in[name] = nc.dram_tensor(name, list(shape), dtype, kind="ExternalInput")
        return dt_in[name]

    x_own = din("x_own", [NTOK, D])
    w_q = din("w_q", [D, D])
    b_q = din("b_q", [1, D])
    w_down_a = din("w_down_a", [D, LAT])
    b_down_r = din("b_down_r", [128, 4])
    w_up_a = din("w_up_a", [LAT, 2 * NKV * HD])
    b_up_r = din("b_up_r", [128, 8])
    w_o = din("w_o", [D, D])
    b_o = din("b_o", [1, D])
    qn_w = din("qn_w", [1, HD])
    kn_w = din("kn_w", [1, HD])
    gate_w = din("gate_w", [HEADS, HD, HD])
    gate_b_r = din("gate_b_r", [128, HEADS])
    w_qkv = din("w_qkv", [3 * D, D])
    w_proj = din("w_proj", [D, D])
    w_gate_f = din("w_gate_f", [DFF, D])
    w_up_f = din("w_up_f", [DFF, D])
    w_down_f = din("w_down_f", [D, DFFP])
    rope_c = din("rope_c", [T, 64])
    rope_s = din("rope_s", [T, 64])
    rope_own_c = din("rope_own_c", [NTOK, 64])
    rope_own_s = din("rope_own_s", [NTOK, 64])
    offtab_sel = din("offtab_sel", [128, 24])
    offtab_b = din("offtab_b", [128, 24])
    aa_bias = din("aa_bias", [128, 384])
    ident32_d = din("ident32", [128, 128])
    ident16_d = din("ident16", [128, 128], BF16)
    ones_col16_d = din("ones_col16", [128, 1], BF16)
    ones_row_d = din("ones_row", [1, 128])
    half_row_d = din("half_row", [1, 128])
    mdiv_d = din("mdiv", [1, 2])

    out_t = nc.dram_tensor("out", [NTOK, D], F32, kind="ExternalOutput")

    def dint(name, shape, dtype=F32):
        return nc.dram_tensor(name, list(shape), dtype, kind="Internal")

    xres = [dint("xres0", [NTOK, D]), dint("xres1", [NTOK, D])]
    cc_in_x = dint("cc_in_x", [D, NTOK])
    cc_out_x = dint("cc_out_x", [2 * D, NTOK])
    cc_in_x2 = dint("cc_in_x2", [D, NTOK], BF16)
    cc_out_x2 = dint("cc_out_x2", [2 * D, NTOK], BF16)
    cc_ps_in = dint("cc_ps_in", [1, 512])
    cc_ps_out = dint("cc_ps_out", [1, 512])
    cc_wg_in = dint("cc_wg_in", [256, DFF], BF16)
    cc_wg_out = dint("cc_wg_out", [D, DFF], BF16)
    cc_wu_in = dint("cc_wu_in", [256, DFF], BF16)
    cc_wu_out = dint("cc_wu_out", [D, DFF], BF16)
    cc_wd_in = dint("cc_wd_in", [768, D], BF16)
    cc_wd_out = dint("cc_wd_out", [DFFP, D], BF16)
    cc_wqkv_in = dint("cc_wqkv_in", [256, 3 * D], BF16)
    cc_wqkv_out = dint("cc_wqkv_out", [D, 3 * D], BF16)
    cc_wp_in = dint("cc_wp_in", [256, D], BF16)
    cc_wp_out = dint("cc_wp_out", [D, D], BF16)

    ES = contextlib.ExitStack()
    with ES:
        blk = ES.enter_context(nc.Block())
        k = Ctx(nc, blk)
        for snm in ("dS", "dSc", "dG"):
            s = Stream(nc, snm, 16)
            ES.enter_context(contextlib.nullcontext())
            k.st[snm] = s
        for snm in ("cc",):
            k.st[snm] = Stream(nc, snm, 1)
        for snm in ("pV", "pS", "pT", "pG"):
            k.st[snm] = Stream(nc, snm, 1)

        def sb(name, shape, dtype=F32):
            return ES.enter_context(nc.sbuf_tensor(name, list(shape), dtype))

        ident32 = sb("ident32_s", [128, 128])
        ident16 = sb("ident16_s", [128, 128], BF16)
        ones_col16 = sb("ones_col16_s", [128, 1], BF16)
        ones_col32 = sb("ones_col32_s", [128, 1])
        ones_row = sb("ones_row_s", [1, 128])
        half_row = sb("half_row_s", [1, 128])
        iotaJI = sb("iotaJI", [128, 512], I32)
        bq_s = sb("bq_s", [1, D])
        bo_s = sb("bo_s", [1, D])
        bdr_s = sb("bdr_s", [128, 4])
        bur_s = sb("bur_s", [128, 8])
        gbh_s = sb("gbh_s", [128, HEADS])
        offsel_s = sb("offsel_s", [128, 24])
        offb_s = sb("offb_s", [128, 24])
        aab_s = sb("aab_s", [128, 384])
        mdiv_s = sb("mdiv_s", [1, 2])
        qnw_t = sb("qnw_t", [128, 128])
        knw_t = sb("knw_t", [128, 128])

        pA = ES.enter_context(nc.psum_tensor("pA", [128, 512], F32))
        pB = ES.enter_context(nc.psum_tensor("pB", [128, 512], F32))
        pC = ES.enter_context(nc.psum_tensor("pC", [128, 512], F32))
        pD = ES.enter_context(nc.psum_tensor("pD", [128, 512], F32))
        pE = ES.enter_context(nc.psum_tensor("pE", [128, 512], F32))
        pF = ES.enter_context(nc.psum_tensor("pF", [128, 512], F32))
        pT16 = ES.enter_context(nc.psum_tensor("pT16", [128, 1024], BF16))
        pT32 = ES.enter_context(nc.psum_tensor("pT32", [128, 512], F32))

        @blk.sync
        def _(e):
            for i_, (src, dst) in enumerate((
                (ident32_d, ident32), (ident16_d, ident16),
                (ones_col16_d, ones_col16), (ones_row_d, ones_row),
                (half_row_d, half_row),
                (b_q, bq_s), (b_o, bo_s), (b_down_r, bdr_s), (b_up_r, bur_s),
                (gate_b_r, gbh_s), (offtab_sel, offsel_s), (offtab_b, offb_s),
                (aa_bias, aab_s), (mdiv_d, mdiv_s),
            )):
                k.dma(e, "sync", "Lconst", first=(i_ == 0), out=dst[:], in_=src.ap())
        t_constload = k.st["Lconst"].n

        @blk.gpsimd
        def _(e):
            k.inc(e.iota(iotaJI[:], pattern=[[1, 512]], base=0, channel_multiplier=-1), "pG")
            k.inc(e.memset(ones_col32[:], 1.0), "pG")
        t_iota = k.st["pG"].n

        @blk.vector
        def _(e):
            k.wait(e, "v", "Lconst", t_constload)
            k.inc(e.tensor_scalar_mul(gbh_s[:], gbh_s[:], 0.5), "pV")
        t_gbh = k.st["pV"].n

        @blk.sync
        def _(e):
            k.dma(e, "sync", "Lqnw", first=True, out=qnw_t[0:1, :], in_=qn_w.ap())
            k.dma(e, "sync", "Lqnw", first=False, out=knw_t[0:1, :], in_=kn_w.ap())
        t_qnwload = k.st["Lqnw"].n

        @blk.tensor
        def _(e):
            k.wait(e, "t", "Lqnw", t_qnwload)
            k.wait(e, "t", "Lconst", t_constload)
            k.inc(e.matmul(pT32[:, 0:128], ones_row[:], qnw_t[0:1, :], start=True, stop=True), "pT")
            k.inc(e.matmul(pT32[:, 128:256], ones_row[:], knw_t[0:1, :], start=True, stop=True), "pT")
        t_qnwmm = k.st["pT"].n

        @blk.vector
        def _(e):
            k.wait(e, "v", "pT", t_qnwmm)
            k.inc(e.tensor_copy(qnw_t[:], pT32[:, 0:128]), "pV")
            k.inc(e.tensor_copy(knw_t[:], pT32[:, 128:256]), "pV")
        t_qnwt = k.st["pV"].n

        w_gate_sl = din("w_gate_sl", [DFF, 256])
        w_up_sl = din("w_up_sl", [DFF, 256])
        w_down_sl = din("w_down_sl", [D, 768])
        w_qkv_sl = din("w_qkv_sl", [3 * D, 256])
        w_proj_sl = din("w_proj_sl", [D, 256])

        prepES = contextlib.ExitStack()
        with prepES:
            wslab = prepES.enter_context(nc.sbuf_tensor("wslab", [128, 768], F32))
            wslabB = prepES.enter_context(nc.sbuf_tensor("wslabB", [128, 768], F32))
            wslab2 = [wslab, wslabB]
            wabs = prepES.enter_context(nc.sbuf_tensor("wabs", [128, 768], F32))
            sclb = prepES.enter_context(nc.sbuf_tensor("sclb", [128, 6], F32))
            rsclb = prepES.enter_context(nc.sbuf_tensor("rsclb", [128, 6], F32))
            tq1 = prepES.enter_context(nc.sbuf_tensor("tq1", [128, 768], F32))
            tqi = prepES.enter_context(nc.sbuf_tensor("tqi", [128, 768], I32))
            wqv = prepES.enter_context(nc.sbuf_tensor("wqv", [128, 768], F32))
            stage = prepES.enter_context(nc.sbuf_tensor("stage", [128, 768], BF16))
            acc_ps = prepES.enter_context(nc.sbuf_tensor("acc_ps", [128, 2], F32))
            red1 = prepES.enter_context(nc.sbuf_tensor("red1", [128, 1], F32))
            pstot = prepES.enter_context(nc.sbuf_tensor("pstot", [1, 2], F32))
            cmb = prepES.enter_context(nc.sbuf_tensor("cmb", [128, 2], F32))
            sbq = prepES.enter_context(nc.sbuf_tensor("sbq", [128, 2], F32))
            zrow = prepES.enter_context(nc.sbuf_tensor("zrow", [1, 512], F32))

            @blk.vector
            def _(e):
                k.inc(e.memset(acc_ps[:], 0.0), "pV")
                k.inc(e.memset(zrow[:], 0.0), "pV")
            tv0 = k.st["pV"].n

            for mi, (mat, ntiles) in enumerate(((w_qkv_sl, 48), (w_proj_sl, 16))):
                ot = 0
                while ot < ntiles:
                    nb = min(3, ntiles - ot)
                    w = nb * 256
                    @blk.scalar
                    def _(e, mat=mat, ot=ot, nb=nb, w=w):
                        k.wait(e, "s", "pV", k.st["pV"].n)
                        td = k.dma(e, "s", "Lslab",
                                   out=wslab[:, 0:w].rearrange("p (a c) -> p a c", c=256),
                                   in_=mat.ap()[ot * 128:(ot + nb) * 128, :].rearrange("(a p) c -> p a c", p=128))
                        k.wait(e, "s", "Lslab", td)
                        k.inc(e.activation(wabs[:, 0:w], wslab[:, 0:w], AF.Abs), "pS")
                    ts_ = k.st["pS"].n

                    @blk.vector
                    def _(e, mi=mi, ts_=ts_, w=w):
                        k.wait(e, "v", "pS", ts_)
                        k.wait(e, "v", "pV", k.st["pV"].n)
                        k.inc(e.reduce_sum(red1[:], wabs[:, 0:w], axis=AX.X), "pV")
                        k.wait(e, "v", "pV", k.st["pV"].n)
                        k.inc(e.tensor_add(acc_ps[:, mi:mi + 1], acc_ps[:, mi:mi + 1], red1[:]), "pV")
                    ot += nb
            tv_acc = k.st["pV"].n

            @blk.tensor
            def _(e):
                k.wait(e, "t", "pV", tv_acc)
                k.wait(e, "t", "pV", t_qnwt)
                k.wait(e, "t", "pG", t_iota)
                k.inc(e.matmul(pT32[0:1, 0:1], acc_ps[:, 0:1], ones_col32[:], start=True, stop=True), "pT")
                k.inc(e.matmul(pT32[0:1, 1:2], acc_ps[:, 1:2], ones_col32[:], start=True, stop=True), "pT")
            tps = k.st["pT"].n

            @blk.vector
            def _(e):
                k.wait(e, "v", "pT", tps)
                k.inc(e.tensor_copy(pstot[0:1, 0:2], pT32[0:1, 0:2]), "pV")
            tv_pstot = k.st["pV"].n

            @blk.scalar
            def _(e):
                k.wait(e, "s", "pV", tv_pstot)
                k.dma(e, "s", "Lccps", out=cc_ps_in.ap(), in_=zrow[:])
                k.dma(e, "s", "Lccps", out=cc_ps_in.ap()[0:1, 0:2], in_=pstot[:])
            t_psin = k.st["Lccps"].n

            @blk.gpsimd
            def _(e):
                k.wait(e, "g", "Lccps", t_psin)
                if "cr" in PREP_SKIP:
                    k.inc(e.engine_nop(), "cc")
                else:
                    k.inc(e.collective_compute(
                        "AllReduce", ALU.add,
                        replica_groups=[[0, 1, 2, 3, 4, 5, 6, 7]],
                        ins=[cc_ps_in.ap().opt()], outs=[cc_ps_out.ap().opt()]), "cc")
            t_ccps = k.st["cc"].n

            def quant_group(mat_sl, ntiles, icols, ngrp, cc_dst, tscale_col=None):
                nbmax = max(1, 768 // icols)
                last_mul = [0, 0]
                ts_entry = k.st["pS"].n
                pv_entry = k.st["pV"].n
                it = 0
                ot = 0
                while ot < ntiles:
                    nb = min(nbmax, ntiles - ot)
                    W = nb * icols
                    G = nb * ngrp
                    buf = it % 2
                    wsl = wslab2[buf]
                    @blk.scalar
                    def _(e, ot=ot, nb=nb, W=W, buf=buf, wsl=wsl, it=it):
                        if it < 2:
                            k.wait(e, "s", "pS", ts_entry)
                            k.wait(e, "s", "pV", pv_entry)
                        else:
                            k.wait(e, "s", "pV", last_mul[buf])
                        td = k.dma(e, "s", f"Lslab{buf}",
                                   out=wsl[:, 0:W].rearrange("p (a c) -> p a c", c=icols),
                                   in_=mat_sl.ap()[ot * 128:(ot + nb) * 128, :].rearrange("(a p) c -> p a c", p=128))
                        k.wait(e, "s", f"Lslab{buf}", td)
                        if tscale_col is None:
                            k.wait(e, "s", "pV", k.st["pV"].n)
                            k.inc(e.activation(wabs[:, 0:W], wsl[:, 0:W], AF.Abs), "pS")
                    ts_ = k.st["pS"].n
                    td_slab = k.st[f"Lslab{buf}"].n

                    @blk.vector
                    def _(e, ts_=ts_, td_slab=td_slab, W=W, G=G, buf=buf, wsl=wsl):
                        k.wait(e, "v", "pS", ts_)
                        k.wait(e, "v", f"Lslab{buf}", td_slab)
                        k.wait(e, "v", "pV", k.st["pV"].n)
                        if tscale_col is None:
                            k.inc(e.reduce_sum(
                                sclb[:, 0:G],
                                wabs[:, 0:W].rearrange("p (g c) -> p g c", c=128),
                                axis=AX.X), "pV")
                            k.wait(e, "v", "pV", k.st["pV"].n)
                            k.inc(e.tensor_scalar(sclb[:, 0:G], sclb[:, 0:G],
                                                  1.0 / 128.0, EPS_LN, ALU.mult, ALU.add), "pV")
                            k.wait(e, "v", "pV", k.st["pV"].n)
                            k.inc(e.reciprocal(rsclb[:, 0:G], sclb[:, 0:G]), "pV")
                            k.wait(e, "v", "pV", k.st["pV"].n)
                            rscl_b = rsclb[:, 0:G].unsqueeze(2).to_broadcast([128, G, 128])
                            k.inc(e.tensor_mul(tq1[:, 0:W].rearrange("p (g c) -> p g c", c=128),
                                               wsl[:, 0:W].rearrange("p (g c) -> p g c", c=128),
                                               rscl_b), "pV")
                        else:
                            k.inc(e.tensor_scalar_mul(tq1[:, 0:W], wsl[:, 0:W],
                                                      sbq[:, tscale_col:tscale_col + 1]), "pV")
                        last_mul[buf] = k.st["pV"].n
                        k.wait(e, "v", "pV", k.st["pV"].n)
                        k.inc(e.tensor_scalar(tq1[:, 0:W], tq1[:, 0:W],
                                              1.0, -1.0, ALU.min, ALU.max), "pV")
                        k.wait(e, "v", "pV", k.st["pV"].n)
                        k.inc(e.tensor_scalar_add(tq1[:, 0:W], tq1[:, 0:W], RC), "pV")
                        k.wait(e, "v", "pV", k.st["pV"].n)
                        k.inc(e.tensor_copy(tqi[:, 0:W], tq1[:, 0:W]), "pV")
                        k.wait(e, "v", "pV", k.st["pV"].n)
                        if tscale_col is None:
                            scl_b = sclb[:, 0:G].unsqueeze(2).to_broadcast([128, G, 128])
                            k.inc(e.scalar_tensor_tensor(
                                wqv[:, 0:W].rearrange("p (g c) -> p g c", c=128),
                                tqi[:, 0:W].rearrange("p (g c) -> p g c", c=128),
                                -RC, scl_b, ALU.add, ALU.mult), "pV")
                        else:
                            cm_b = cmb[:, tscale_col:tscale_col + 1].to_broadcast([128, W])
                            k.inc(e.scalar_tensor_tensor(
                                wqv[:, 0:W], tqi[:, 0:W],
                                -RC, cm_b, ALU.add, ALU.mult), "pV")
                    tv_ = k.st["pV"].n

                    glist = [] if "nt" in PREP_SKIP else [
                        (j, i) for j in range(ngrp) for i in range(nb)]
                    for gidx, (j, i) in enumerate(glist):
                        @blk.tensor
                        def _(e, gidx=gidx, j=j, i=i, tv_=tv_):
                            k.wait(e, "t", "pV", tv_)
                            k.wait(e, "t", "pV", k.st["pV"].n)
                            k.inc(e.transpose(pT32[:, (gidx % 4) * 128:(gidx % 4) * 128 + 128],
                                              wqv[:, (i * icols + j * 128):(i * icols + j * 128) + 128],
                                              ident32[:]), "pT")
                        tt_ = k.st["pT"].n

                        @blk.vector
                        def _(e, gidx=gidx, j=j, i=i, tt_=tt_, nb=nb):
                            k.wait(e, "v", "pT", tt_)
                            k.inc(e.tensor_copy(stage[:, (j * nb + i) * 128:(j * nb + i) * 128 + 128],
                                                pT32[:, (gidx % 4) * 128:(gidx % 4) * 128 + 128]), "pV")
                    tv2_ = k.st["pV"].n

                    @blk.sync
                    def _(e, ot=ot, nb=nb, W=W, tv2_=tv2_):
                        if "ns" in PREP_SKIP or "nt" in PREP_SKIP:
                            return
                        k.wait(e, "sync", "pV", tv2_)
                        k.dma(e, "sync", "Lstage",
                              out=cc_dst.ap().rearrange("(a p) o -> p a o", p=128)[:, :, ot * 128:(ot + nb) * 128],
                              in_=stage[:, 0:W].rearrange("p (a o) -> p a o", o=nb * 128))
                    @blk.vector
                    def _(e):
                        k.wait(e, "v", "Lstage", k.st["Lstage"].n if "Lstage" in k.st else 0)
                    ot += nb
                    it += 1

            if "g" not in PREP_SKIP:
                quant_group(w_gate_sl, 43, 256, 2, cc_wg_in)
            @blk.gpsimd
            def _(e):
                k.wait(e, "g", "Lstage", k.st["Lstage"].n if "Lstage" in k.st else 0)
                if "cg" in PREP_SKIP:
                    k.inc(e.engine_nop(), "cc")
                else:
                    k.inc(e.collective_compute(
                        "AllGather", ALU.bypass,
                        replica_groups=[[0, 1, 2, 3, 4, 5, 6, 7]],
                        ins=[cc_wg_in.ap().opt()], outs=[cc_wg_out.ap().opt()]), "cc")

            if "u" not in PREP_SKIP:
                quant_group(w_up_sl, 43, 256, 2, cc_wu_in)
            @blk.gpsimd
            def _(e):
                k.wait(e, "g", "Lstage", k.st["Lstage"].n if "Lstage" in k.st else 0)
                if "cu" in PREP_SKIP:
                    k.inc(e.engine_nop(), "cc")
                else:
                    k.inc(e.collective_compute(
                        "AllGather", ALU.bypass,
                        replica_groups=[[0, 1, 2, 3, 4, 5, 6, 7]],
                        ins=[cc_wu_in.ap().opt()], outs=[cc_wu_out.ap().opt()]), "cc")

            if "d" not in PREP_SKIP:
                quant_group(w_down_sl, 16, 768, 6, cc_wd_in)
            @blk.gpsimd
            def _(e):
                k.wait(e, "g", "Lstage", k.st["Lstage"].n if "Lstage" in k.st else 0)
                if "cd" in PREP_SKIP:
                    k.inc(e.engine_nop(), "cc")
                else:
                    k.inc(e.collective_compute(
                        "AllGather", ALU.bypass,
                        replica_groups=[[0, 1, 2, 3, 4, 5, 6, 7]],
                        ins=[cc_wd_in.ap().opt()], outs=[cc_wd_out.ap().opt()]), "cc")

            @blk.scalar
            def _(e):
                k.wait(e, "s", "cc", t_ccps)
                td = k.dma(e, "s", "Lccps", out=pstot[:], in_=cc_ps_out.ap()[0:1, 0:2])
                k.wait(e, "s", "Lccps", td)
            t_pstot2 = k.st["Lccps"].n

            @blk.vector
            def _(e):
                k.wait(e, "v", "Lccps", t_pstot2)
                k.wait(e, "v", "Lconst", t_constload)
                k.inc(e.tensor_mul(pstot[:], pstot[:], mdiv_s[:]), "pV")
                k.wait(e, "v", "pV", k.st["pV"].n)
                k.inc(e.tensor_scalar_max(pstot[:], pstot[:], EPS_Q), "pV")
            tv_cm = k.st["pV"].n

            @blk.tensor
            def _(e):
                k.wait(e, "t", "pV", tv_cm)
                k.wait(e, "t", "pV", k.st["pV"].n)
                k.inc(e.matmul(pT32[:, 0:2], ones_row[:], pstot[:], start=True, stop=True), "pT")
            t_cmmm = k.st["pT"].n

            @blk.vector
            def _(e):
                k.wait(e, "v", "pT", t_cmmm)
                k.inc(e.tensor_copy(cmb[:], pT32[:, 0:2]), "pV")
                k.wait(e, "v", "pV", k.st["pV"].n)
                k.inc(e.reciprocal(sbq[:], cmb[:]), "pV")
            tv_sbq = k.st["pV"].n

            if "q" not in PREP_SKIP:
                quant_group(w_qkv_sl, 48, 256, 2, cc_wqkv_in, tscale_col=0)
            @blk.gpsimd
            def _(e):
                k.wait(e, "g", "Lstage", k.st["Lstage"].n if "Lstage" in k.st else 0)
                if "cq" in PREP_SKIP:
                    k.inc(e.engine_nop(), "cc")
                else:
                    k.inc(e.collective_compute(
                        "AllGather", ALU.bypass,
                        replica_groups=[[0, 1, 2, 3, 4, 5, 6, 7]],
                        ins=[cc_wqkv_in.ap().opt()], outs=[cc_wqkv_out.ap().opt()]), "cc")

            if "p" not in PREP_SKIP:
                quant_group(w_proj_sl, 16, 256, 2, cc_wp_in, tscale_col=1)
            @blk.gpsimd
            def _(e):
                k.wait(e, "g", "Lstage", k.st["Lstage"].n if "Lstage" in k.st else 0)
                if "cp" in PREP_SKIP:
                    k.inc(e.engine_nop(), "cc")
                else:
                    k.inc(e.collective_compute(
                        "AllGather", ALU.bypass,
                        replica_groups=[[0, 1, 2, 3, 4, 5, 6, 7]],
                        ins=[cc_wp_in.ap().opt()], outs=[cc_wp_out.ap().opt()]), "cc")
            t_cc_weights = k.st["cc"].n

        def ln_stats(src_d, t, xtile, xsq, st1, st2, var_all, mean_all, col):
            @blk.scalar
            def _(e, t=t):
                k.join(e, "s")
                k.wait(e, "s", "Lresw", k.st["Lresw"].n if "Lresw" in k.st else 0)
                td = k.dma(e, "s", "Lx", out=xtile[:], in_=src_d.ap()[t * 128:(t + 1) * 128, :])
                k.wait(e, "s", "Lx", td)
                k.inc(e.activation(xsq[:], xtile[:], AF.Square, accum_out=st2[:]), "pS")
            tS = k.st["pS"].n
            td_x = k.st["Lx"].n

            @blk.vector
            def _(e, tS=tS, td_x=td_x, col=col):
                k.wait(e, "v", "Lx", td_x)
                k.wait(e, "v", "pS", tS)
                k.wait(e, "v", "pV", k.st["pV"].n)
                k.inc(e.reduce_sum(st1[:], xtile[:], axis=AX.X), "pV")
                k.wait(e, "v", "pV", k.st["pV"].n)
                k.inc(e.tensor_scalar_mul(mean_all[:, col:col + 1], st1[:], 1.0 / D), "pV")
                k.wait(e, "v", "pV", k.st["pV"].n)
                k.inc(e.tensor_mul(st1[:], mean_all[:, col:col + 1], mean_all[:, col:col + 1]), "pV")
                k.wait(e, "v", "pV", k.st["pV"].n)
                k.inc(e.scalar_tensor_tensor(var_all[:, col:col + 1], st2[:], 1.0 / D, st1[:],
                                             ALU.mult, ALU.subtract), "pV")

        def batch_rsqrt(var_all, rstd_all, n, eps):
            @blk.vector
            def _(e):
                k.wait(e, "v", "pV", k.st["pV"].n)
                k.inc(e.tensor_scalar_add(var_all[:, 0:n], var_all[:, 0:n], eps), "pV")
                k.wait(e, "v", "pV", k.st["pV"].n)
                k.inc(e.reciprocal(rstd_all[:, 0:n], var_all[:, 0:n]), "pV")
            tv = k.st["pV"].n
            @blk.scalar
            def _(e, tv=tv):
                k.wait(e, "s", "pV", tv)
                k.wait(e, "s", "pS", k.st["pS"].n)
                k.inc(e.activation(rstd_all[:, 0:n], rstd_all[:, 0:n], AF.Sqrt), "pS")
            return k.st["pS"].n

        def ln_apply(src_d, t, xtile, rstd_all, mean_all, nmr, col, xn_out):
            @blk.scalar
            def _(e, t=t):
                k.join(e, "s")
                td = k.dma(e, "s", "Lx", out=xtile[:], in_=src_d.ap()[t * 128:(t + 1) * 128, :])
                k.wait(e, "s", "Lx", td)
            td_x = k.st["Lx"].n
            @blk.vector
            def _(e, td_x=td_x, col=col):
                k.wait(e, "v", "pV", k.st["pV"].n)
                k.wait(e, "v", "pS", k.st["pS"].n)
                k.inc(e.scalar_tensor_tensor(nmr[:], mean_all[:, col:col + 1], -1.0,
                                             rstd_all[:, col:col + 1], ALU.mult, ALU.mult), "pV")
            tV2 = k.st["pV"].n
            @blk.scalar
            def _(e, tV2=tV2, td_x=td_x, col=col):
                k.wait(e, "s", "pV", tV2)
                k.wait(e, "s", "Lx", td_x)
                k.wait(e, "s", "pT", k.st["pT"].n)
                k.inc(e.activation(xn_out[:], xtile[:], AF.Identity,
                                   bias=nmr[:], scale=rstd_all[:, col:col + 1]), "pS")
            return k.st["pS"].n

        def ln_tile(src_d, t, xtile, xsq, st1, st2, st3, rstd, nmr, xn_out):
            @blk.scalar
            def _(e, t=t):
                k.join(e, "s")
                k.wait(e, "s", "Lresw", k.st["Lresw"].n if "Lresw" in k.st else 0)
                td = k.dma(e, "s", "Lx", out=xtile[:], in_=src_d.ap()[t * 128:(t + 1) * 128, :])
                k.wait(e, "s", "Lx", td)
                k.inc(e.activation(xsq[:], xtile[:], AF.Square, accum_out=st2[:]), "pS")
            tS = k.st["pS"].n
            td_x = k.st["Lx"].n

            @blk.vector
            def _(e, tS=tS, td_x=td_x):
                k.wait(e, "v", "Lx", td_x)
                k.wait(e, "v", "pS", tS)
                k.wait(e, "v", "pV", k.st["pV"].n)
                k.inc(e.reduce_sum(st1[:], xtile[:], axis=AX.X), "pV")
                k.wait(e, "v", "pV", k.st["pV"].n)
                k.inc(e.tensor_scalar_mul(st1[:], st1[:], 1.0 / D), "pV")
                k.wait(e, "v", "pV", k.st["pV"].n)
                k.inc(e.tensor_mul(st3[:], st1[:], st1[:]), "pV")
                k.wait(e, "v", "pV", k.st["pV"].n)
                k.inc(e.scalar_tensor_tensor(st3[:], st2[:], 1.0 / D, st3[:],
                                             ALU.mult, ALU.subtract), "pV")
                k.wait(e, "v", "pV", k.st["pV"].n)
                k.inc(e.tensor_scalar_add(st3[:], st3[:], EPS_LN), "pV")
                k.wait(e, "v", "pV", k.st["pV"].n)
                k.inc(e.reciprocal(st2[:], st3[:]), "pV")
            tV = k.st["pV"].n

            @blk.scalar
            def _(e, tV=tV):
                k.wait(e, "s", "pV", tV)
                k.inc(e.activation(rstd[:], st2[:], AF.Sqrt), "pS")
            tS2 = k.st["pS"].n

            @blk.vector
            def _(e, tS2=tS2):
                k.wait(e, "v", "pS", tS2)
                k.wait(e, "v", "pV", k.st["pV"].n)
                k.inc(e.scalar_tensor_tensor(nmr[:], st1[:], -1.0, rstd[:],
                                             ALU.mult, ALU.mult), "pV")
            tV2 = k.st["pV"].n

            @blk.scalar
            def _(e, tV2=tV2):
                k.wait(e, "s", "pV", tV2)
                k.wait(e, "s", "pT", k.st["pT"].n)
                k.inc(e.activation(xn_out[:], xtile[:], AF.Identity,
                                   bias=nmr[:], scale=rstd[:]), "pS")
            return k.st["pS"].n

        def transposeN(src, src_cols, dst, dst_cols, dtype, n, extra=None):
            ps = pT32 if dtype == F32 else pT16
            idn = ident32 if dtype == F32 else ident16
            for i in range(n):
                @blk.tensor
                def _(e, i=i):
                    k.wait(e, "t", "pV", k.st["pV"].n)
                    k.wait(e, "t", "pS", k.st["pS"].n)
                    c0 = src_cols[i]
                    k.inc(e.transpose(ps[:, (i % 4) * 128:(i % 4) * 128 + 128],
                                      src[:, c0:c0 + 128], idn[:]), "pT")
                tt = k.st["pT"].n

                @blk.vector
                def _(e, i=i, tt=tt):
                    k.wait(e, "v", "pT", tt)
                    k.wait(e, "v", "pS", k.st["pS"].n)
                    if dst is not None:
                        d0 = dst_cols[i]
                        k.inc(e.tensor_copy(dst[:, d0:d0 + 128],
                                            ps[:, (i % 4) * 128:(i % 4) * 128 + 128]), "pV")
                    if extra is not None:
                        extra(i, e)

        _rp_count = [0]
        def residual_pass(src_d, proj_d, dst_d):
            _rp_count[0] += 1
            rix = _rp_count[0]
            rES = contextlib.ExitStack()
            with rES:
                xo = rES.enter_context(nc.sbuf_tensor(f"xo_r{rix}", [128, D], F32))
                yo = rES.enter_context(nc.sbuf_tensor(f"yo_r{rix}", [128, D], F32))
                xnw = rES.enter_context(nc.sbuf_tensor(f"xnw_r{rix}", [128, D], F32))
                for t in range(NTT):
                    @blk.scalar
                    def _(e, t=t):
                        k.join(e, "s")
                        k.wait(e, "s", "Lyp", k.st["Lyp"].n if "Lyp" in k.st else 0)
                        k.wait(e, "s", "Lresw", k.st["Lresw"].n if "Lresw" in k.st else 0)
                        k.dma(e, "s", "Lres", first=True, out=xo[:], in_=src_d.ap()[t * 128:(t + 1) * 128, :])
                        k.dma(e, "s", "Lres", first=False, out=yo[:], in_=proj_d.ap()[t * 128:(t + 1) * 128, :])
                    td = k.st["Lres"].n

                    @blk.vector
                    def _(e, t=t, td=td):
                        k.wait(e, "v", "Lres", td)
                        k.wait(e, "v", "Lresw", k.st["Lresw"].n if "Lresw" in k.st else 0)
                        k.wait(e, "v", "pV", k.st["pV"].n)
                        k.inc(e.tensor_add(xnw[:], xo[:], yo[:]), "pV")
                    tv = k.st["pV"].n

                    @blk.sync
                    def _(e, t=t, tv=tv):
                        k.wait(e, "sync", "pV", tv)
                        k.dma(e, "sync", "Lresw", out=dst_d.ap()[t * 128:(t + 1) * 128, :], in_=xnw[:])
                @blk.vector
                def _(e):
                    k.wait(e, "v", "Lresw", k.st["Lresw"].n)

        yproj = dint("yproj", [NTOK, D])
        q_d = dint("q_d", [NTOK, D])
        ygall_d = dint("ygall_d", [D, NTOK], BF16)
        kaT_d = dint("kaT_d", [NKV * HD, T], BF16)
        kraw_d = dint("kraw_d", [T, NKV * HD])
        vaT_d = dint("vaT_d", [NKV * HD, T], BF16)

        if upto >= 1:
            aES = contextlib.ExitStack()
            with aES:
                qhT = aES.enter_context(nc.sbuf_tensor("qhT", [128, 16 * NTOK], BF16))
                qselT = aES.enter_context(nc.sbuf_tensor("qselT", [64, NTOK], F32))
                kselT = aES.enter_context(nc.sbuf_tensor("kselT", [64, T], F32))
                m01T_buf = aES.enter_context(nc.sbuf_tensor("m01T_buf", [128, 24 * 512], BF16))
                ropec_s = aES.enter_context(nc.sbuf_tensor("ropec_s", [128, 16 * 64], F32))
                ropes_s = aES.enter_context(nc.sbuf_tensor("ropes_s", [128, 16 * 64], F32))
                ropeoc_s = aES.enter_context(nc.sbuf_tensor("ropeoc_s", [128, 8 * 64], F32))
                ropeos_s = aES.enter_context(nc.sbuf_tensor("ropeos_s", [128, 8 * 64], F32))
                xtile = aES.enter_context(nc.sbuf_tensor("xtile", [128, D], F32))
                xsq = aES.enter_context(nc.sbuf_tensor("xsq", [128, D], F32))
                xn = aES.enter_context(nc.sbuf_tensor("xn", [128, D], F32))
                qr = aES.enter_context(nc.sbuf_tensor("qr", [128, D], F32))
                st1 = aES.enter_context(nc.sbuf_tensor("st1", [128, 1], F32))
                st2 = aES.enter_context(nc.sbuf_tensor("st2", [128, 1], F32))
                st3 = aES.enter_context(nc.sbuf_tensor("st3", [128, 1], F32))
                rstd = aES.enter_context(nc.sbuf_tensor("rstd", [128, 1], F32))
                nmr = aES.enter_context(nc.sbuf_tensor("nmr", [128, 1], F32))
                ssh = aES.enter_context(nc.sbuf_tensor("ssh", [128, 16], F32))
                qssh = aES.enter_context(nc.sbuf_tensor("qssh", [128, 128], F32))
                kssh = aES.enter_context(nc.sbuf_tensor("kssh", [128, 64], F32))
                ttmp = xtile[:, 0:1024]
                ttmp2 = xtile[:, 1024:2048]

                @blk.sync
                def _(e):
                    k.dma(e, "sync", "Lrope", first=True, out=ropec_s[:].rearrange("p (a j) -> p a j", j=64),
                          in_=rope_c.ap().rearrange("(a p) j -> p a j", p=128))
                    k.dma(e, "sync", "Lrope", first=False, out=ropes_s[:].rearrange("p (a j) -> p a j", j=64),
                          in_=rope_s.ap().rearrange("(a p) j -> p a j", p=128))
                    k.dma(e, "sync", "Lrope", first=False, out=ropeoc_s[:].rearrange("p (a j) -> p a j", j=64),
                          in_=rope_own_c.ap().rearrange("(a p) j -> p a j", p=128))
                    k.dma(e, "sync", "Lrope", first=False, out=ropeos_s[:].rearrange("p (a j) -> p a j", j=64),
                          in_=rope_own_s.ap().rearrange("(a p) j -> p a j", p=128))
                t_ropeload = k.st["Lrope"].n

                for t in range(NTT):
                    tS = ln_tile(x_own, t, xtile, xsq, st1, st2, st3, rstd, nmr, xn)
                    @blk.vector
                    def _(e, tS=tS):
                        k.wait(e, "v", "pS", tS)
                        k.wait(e, "v", "Lccx", k.st["Lccx"].n if "Lccx" in k.st else 0)
                    transposeN(xn, [c * 128 for c in range(16)], qr,
                               [c * 128 for c in range(16)], F32, 16)
                    tv_t = k.st["pV"].n
                    @blk.sync
                    def _(e, t=t, tv_t=tv_t):
                        k.wait(e, "sync", "pV", tv_t)
                        k.dma(e, "sync", "Lccx",
                              out=cc_in_x.ap().rearrange("(c p) tok -> p c tok", p=128)[:, :, t * 128:(t + 1) * 128],
                              in_=qr[:].rearrange("p (c j) -> p c j", j=128))
                t_ccin = k.st["Lccx"].n

                @blk.gpsimd
                def _(e):
                    k.wait(e, "g", "Lccx", t_ccin)
                    k.inc(e.collective_compute(
                        "AllGather", ALU.bypass,
                        replica_groups=[[0, 1], [2, 3], [4, 5], [6, 7]],
                        ins=[cc_in_x.ap().opt()], outs=[cc_out_x.ap().opt()]), "cc")
                t_cc_x = k.st["cc"].n

                a2ES = contextlib.ExitStack()
                with a2ES:
                    xnTb_own = a2ES.enter_context(nc.sbuf_tensor("xnTb_own", [128, 16 * NTOK], BF16))
                    wq16 = [a2ES.enter_context(nc.sbuf_tensor(f"wq16_{i}", [128, 16 * 512], BF16)) for i in range(2)]
                    qslab = a2ES.enter_context(nc.sbuf_tensor("qslab", [128, 512], F32))

                    @blk.gpsimd
                    def _(e):
                        k.join(e, "g")
                        k.wait(e, "g", "Lccx", t_ccin)
                        k.dma(e, "g", "Lxno",
                              out=xnTb_own[:].rearrange("p (c tok) -> p c tok", tok=NTOK),
                              in_=cc_in_x.ap().rearrange("(c p) tok -> p c tok", p=128))
                    t_xnTo = k.st["Lxno"].n

                    t_wq_ready = [0, 0]
                    t_wq_used = [0, 0]
                    for qd in range(4):
                        @blk.gpsimd
                        def _(e, qd=qd):
                            k.join(e, "g") if qd < 2 else k.wait(e, "g", "pT", t_wq_used[qd % 2])
                            k.dma(e, "g", f"Lwq{qd % 2}",
                                  out=wq16[qd % 2][:].rearrange("p (c n) -> p c n", n=512),
                                  in_=w_q.ap().rearrange("(c p) o -> p c o", p=128)[:, :, qd * 512:(qd + 1) * 512])
                        t_wq_ready[qd % 2] = k.st[f"Lwq{qd % 2}"].n

                        for t in range(NTT):
                            @blk.tensor
                            def _(e, qd=qd, t=t):
                                k.wait(e, "t", f"Lwq{qd % 2}", t_wq_ready[qd % 2])
                                k.wait(e, "t", "Lxno", t_xnTo)
                                k.wait(e, "t", "Lconst", t_constload)
                                ps = [pA, pB][t % 2]
                                k.wait(e, "t", "pS", k.st["pS"].n)
                                k.inc(e.matmul(ps[:], ones_row[:], bq_s[0:1, qd * 512:(qd + 1) * 512],
                                               start=True, stop=False), "pT")
                                for dd in range(16):
                                    k.inc(e.matmul(ps[:], xnTb_own[:, dd * NTOK + t * 128:dd * NTOK + t * 128 + 128],
                                                   wq16[qd % 2][:, dd * 512:(dd + 1) * 512],
                                                   start=False, stop=(dd == 15)), "pT")
                            tt_ = k.st["pT"].n
                            t_wq_used[qd % 2] = tt_

                            @blk.scalar
                            def _(e, qd=qd, t=t, tt_=tt_):
                                k.wait(e, "s", "pS", k.st["pS"].n)
                                k.wait(e, "s", "Lqs", k.st["Lqs"].n if "Lqs" in k.st else 0)
                                k.wait(e, "s", "pT", tt_)
                                ps = [pA, pB][t % 2]
                                k.inc(e.activation(qslab[:], ps[:], AF.Identity), "pS")
                                k.wait(e, "s", "pS", k.st["pS"].n)
                                k.dma(e, "s", "Lqs", out=q_d.ap()[t * 128:(t + 1) * 128, qd * 512:(qd + 1) * 512],
                                      in_=qslab[:])

                    def rms_stats(src, W, nh, ssh_all, col0):
                        @blk.scalar
                        def _(e):
                            k.wait(e, "s", "pV", k.st["pV"].n)
                            k.wait(e, "s", "pS", k.st["pS"].n)
                            k.inc(e.activation(xsq[:, 0:W], src, AF.Square), "pS")
                        tS = k.st["pS"].n
                        @blk.vector
                        def _(e, tS=tS, nh=nh, col0=col0):
                            k.wait(e, "v", "pS", tS)
                            k.wait(e, "v", "pV", k.st["pV"].n)
                            k.inc(e.reduce_sum(ssh[:, 0:nh],
                                               xsq[:, 0:W].rearrange("p (h d) -> p h d", d=128),
                                               axis=AX.X), "pV")
                            k.wait(e, "v", "pV", k.st["pV"].n)
                            k.inc(e.tensor_scalar_mul(ssh_all[:, col0:col0 + nh], ssh[:, 0:nh],
                                                      1.0 / 128.0), "pV")

                    def rope_apply(src, W, nh, ctab, stab, gain_tile, rstd_all, col0,
                                   dst16, dst_cols, sel_dst=None, sel_col=None):
                        @blk.vector
                        def _(e, nh=nh, W=W, col0=col0):
                            k.wait(e, "v", "pV", k.st["pV"].n)
                            k.wait(e, "v", "pS", k.st["pS"].n)
                            sv = src.rearrange("p (h d) -> p h d", d=128)
                            k.inc(e.tensor_mul(sv, sv,
                                               rstd_all[:, col0:col0 + nh].unsqueeze(2).to_broadcast([128, nh, 128])), "pV")
                            k.wait(e, "v", "pV", k.st["pV"].n)
                            k.inc(e.tensor_mul(sv, sv,
                                               gain_tile[:].unsqueeze(1).to_broadcast([128, nh, 128])), "pV")
                            k.wait(e, "v", "pV", k.st["pV"].n)
                            s_e = src.rearrange("p (h j t) -> p h j t", j=64, t=2)[:, :, :, 0]
                            s_o = src.rearrange("p (h j t) -> p h j t", j=64, t=2)[:, :, :, 1]
                            q_r = qr[:, 0:W].rearrange("p (h j t) -> p h j t", j=64, t=2)
                            c_b = ctab.unsqueeze(1).to_broadcast([128, nh, 64])
                            sb_ = stab.unsqueeze(1).to_broadcast([128, nh, 64])
                            tt1 = ttmp[:, 0:nh * 64].rearrange("p (h j) -> p h j", j=64)
                            tt2 = ttmp2[:, 0:nh * 64].rearrange("p (h j) -> p h j", j=64)
                            k.inc(e.tensor_mul(tt1, s_e, c_b), "pV")
                            k.inc(e.tensor_mul(tt2, s_o, sb_), "pV")
                            k.wait(e, "v", "pV", k.st["pV"].n)
                            k.inc(e.tensor_sub(q_r[:, :, :, 0], tt1, tt2), "pV")
                            k.wait(e, "v", "pV", k.st["pV"].n)
                            k.inc(e.tensor_mul(tt1, s_e, sb_), "pV")
                            k.inc(e.tensor_mul(tt2, s_o, c_b), "pV")
                            k.wait(e, "v", "pV", k.st["pV"].n)
                            k.inc(e.tensor_add(q_r[:, :, :, 1], tt1, tt2), "pV")

                        def sel_extra(i, e):
                            if sel_dst is not None and i == 0:
                                k.inc(e.tensor_copy(sel_dst[0:64, sel_col:sel_col + 128],
                                                    pT32[0:64, 0:128]), "pV")
                        transposeN(qr, [c * 128 for c in range(nh)], dst16, dst_cols,
                                   F32, nh, extra=sel_extra)

                    ctv = ropeoc_s[:].rearrange("p (a j) -> p a j", j=64)
                    stv = ropeos_s[:].rearrange("p (a j) -> p a j", j=64)
                    def q_reload(t):
                        @blk.scalar
                        def _(e, t=t):
                            k.wait(e, "s", "pV", k.st["pV"].n)
                            k.wait(e, "s", "pS", k.st["pS"].n)
                            k.wait(e, "s", "Lqs", k.st["Lqs"].n)
                            td = k.dma(e, "s", "Lx", out=xn[:], in_=q_d.ap()[t * 128:(t + 1) * 128, :])
                            k.wait(e, "s", "Lx", td)
                        td_q = k.st["Lx"].n
                        @blk.vector
                        def _(e, td_q=td_q):
                            k.wait(e, "v", "Lx", td_q)
                            k.wait(e, "v", "Lrope", t_ropeload)
                    for t in range(NTT):
                        q_reload(t)
                        rms_stats(xn[:], D, 16, qssh, t * 16)
                    batch_rsqrt(qssh, qssh, 128, EPS_RMS)
                    for t in range(NTT):
                        q_reload(t)
                        rope_apply(xn[:], D, 16, ctv[:, t, :], stv[:, t, :], qnw_t,
                                   qssh, t * 16,
                                   qhT, [c * NTOK + t * 128 for c in range(16)],
                                   sel_dst=qselT, sel_col=t * 128)

                a3ES = contextlib.ExitStack()
                with a3ES:
                    xnb_tc = [a3ES.enter_context(nc.sbuf_tensor(f"xnb_tc{i}", [128, 16 * 512], BF16)) for i in range(2)]
                    kvd_tc = a3ES.enter_context(nc.sbuf_tensor("kvd_tc", [128, 4 * 512], BF16))
                    wkv16 = a3ES.enter_context(nc.sbuf_tensor("wkv16", [128, 16 * 128], BF16))
                    ktok_tc = a3ES.enter_context(nc.sbuf_tensor("ktok_tc", [128, 4 * 512], F32))
                    kvtmp = a3ES.enter_context(nc.sbuf_tensor("kvtmp", [128, 512], F32))
                    kvtmp16 = a3ES.enter_context(nc.sbuf_tensor("kvtmp16", [128, 512], BF16))
                    kstage = a3ES.enter_context(nc.sbuf_tensor("kstage", [128, 512], BF16))
                    POS_BLOCKS = ((0, 0), (1, 0), (1, 512), (0, 512))
                    ctk = ropec_s[:].rearrange("p (a j) -> p a j", j=64)
                    stk = ropes_s[:].rearrange("p (a j) -> p a j", j=64)
                    t_xnb = [0, 0]
                    for tc in range(4):
                        r_, co_ = POS_BLOCKS[tc]
                        @blk.gpsimd
                        def _(e, tc=tc, r_=r_, co_=co_):
                            k.wait(e, "g", "cc", t_cc_x)
                            k.join(e, "g")
                            ccv = cc_out_x.ap().rearrange("(r c p) tok -> r p c tok", p=128, c=16)
                            k.dma(e, "g", f"Lxnb{tc % 2}",
                                  out=xnb_tc[tc % 2][:].rearrange("p (c j) -> p c j", j=512),
                                  in_=ccv[r_, :, :, co_:co_ + 512])
                        t_xnb[tc % 2] = k.st[f"Lxnb{tc % 2}"].n

                        for lc in range(4):
                            @blk.gpsimd
                            def _(e, lc=lc):
                                k.join(e, "g")
                                k.dma(e, "g", "Lwkv",
                                      out=wkv16[:].rearrange("p (c n) -> p c n", n=128),
                                      in_=w_down_a.ap().rearrange("(c p) o -> p c o", p=128)[:, :, lc * 128:(lc + 1) * 128])
                            t_wkv = k.st["Lwkv"].n

                            @blk.tensor
                            def _(e, tc=tc, lc=lc, t_wkv=t_wkv):
                                k.wait(e, "t", "Lwkv", t_wkv)
                                k.wait(e, "t", f"Lxnb{tc % 2}", t_xnb[tc % 2])
                                ps = [pA, pB][lc % 2]
                                k.wait(e, "t", "pS", k.st["pS"].n)
                                for dd in range(16):
                                    k.inc(e.matmul(ps[:], wkv16[:, dd * 128:(dd + 1) * 128],
                                                   xnb_tc[tc % 2][:, dd * 512:(dd + 1) * 512],
                                                   start=(dd == 0), stop=(dd == 15)), "pT")
                            tt_ = k.st["pT"].n

                            @blk.scalar
                            def _(e, lc=lc, tt_=tt_):
                                k.wait(e, "s", "pT", tt_)
                                k.wait(e, "s", "pS", k.st["pS"].n)
                                ps = [pA, pB][lc % 2]
                                k.inc(e.activation(kvd_tc[:, lc * 512:(lc + 1) * 512],
                                                   ps[:], AF.Identity, bias=bdr_s[:, lc:lc + 1]), "pS")
                        tS_kvd = k.st["pS"].n

                        for kc in range(8):
                            @blk.gpsimd
                            def _(e, kc=kc):
                                k.join(e, "g")
                                k.dma(e, "g", "Lwkv",
                                      out=wkv16[:, 0:4 * 128].rearrange("p (c n) -> p c n", n=128),
                                      in_=w_up_a.ap().rearrange("(c p) o -> p c o", p=128)[:, :, kc * 128:(kc + 1) * 128])
                            t_wkv = k.st["Lwkv"].n

                            @blk.tensor
                            def _(e, kc=kc, t_wkv=t_wkv, tS_kvd=tS_kvd):
                                k.wait(e, "t", "Lwkv", t_wkv)
                                k.wait(e, "t", "pS", tS_kvd)
                                ps = [pA, pB][kc % 2]
                                k.wait(e, "t", "pS", k.st["pS"].n)
                                k.wait(e, "t", "pV", k.st["pV"].n)
                                for lc in range(4):
                                    k.inc(e.matmul(ps[:], wkv16[:, lc * 128:(lc + 1) * 128],
                                                   kvd_tc[:, lc * 512:(lc + 1) * 512],
                                                   start=(lc == 0), stop=(lc == 3)), "pT")
                            tt_ = k.st["pT"].n

                            @blk.scalar
                            def _(e, kc=kc, tt_=tt_):
                                k.wait(e, "s", "pT", tt_)
                                k.wait(e, "s", "pS", k.st["pS"].n)
                                ps = [pA, pB][kc % 2]
                                dst = kvtmp[:] if kc < 4 else kvtmp16[:]
                                k.inc(e.activation(dst, ps[:], AF.Identity,
                                                   bias=bur_s[:, kc:kc + 1]), "pS")
                            tS_ = k.st["pS"].n

                            if kc < 4:
                                for j in range(4):
                                    @blk.tensor
                                    def _(e, j=j, tS_=tS_):
                                        k.wait(e, "t", "pS", tS_)
                                        k.wait(e, "t", "pV", k.st["pV"].n)
                                        k.inc(e.transpose(pT32[:, (j % 4) * 128:(j % 4) * 128 + 128],
                                                          kvtmp[:, j * 128:(j + 1) * 128], ident32[:]), "pT")
                                    tt2 = k.st["pT"].n
                                    @blk.vector
                                    def _(e, kc=kc, j=j, tt2=tt2):
                                        k.wait(e, "v", "pT", tt2)
                                        k.wait(e, "v", "Lkraw", k.st["Lkraw"].n if "Lkraw" in k.st else 0)
                                        k.inc(e.tensor_copy(
                                            ktok_tc[:, j * 512 + kc * 128:j * 512 + kc * 128 + 128],
                                            pT32[:, (j % 4) * 128:(j % 4) * 128 + 128]), "pV")
                            else:
                                for j in range(4):
                                    @blk.tensor
                                    def _(e, j=j, tS_=tS_):
                                        k.wait(e, "t", "pS", tS_)
                                        k.wait(e, "t", "pV", k.st["pV"].n)
                                        k.inc(e.transpose(pT16[:, (j % 4) * 128:(j % 4) * 128 + 128],
                                                          kvtmp16[:, j * 128:(j + 1) * 128], ident16[:]), "pT")
                                    tt2 = k.st["pT"].n
                                    @blk.vector
                                    def _(e, j=j, tt2=tt2):
                                        k.wait(e, "v", "pT", tt2)
                                        k.wait(e, "v", "Lkst", k.st["Lkst"].n if "Lkst" in k.st else 0)
                                        k.wait(e, "v", "Lvst", k.st["Lvst"].n if "Lvst" in k.st else 0)
                                        k.inc(e.tensor_copy(kstage[:, j * 128:(j + 1) * 128],
                                                            pT16[:, (j % 4) * 128:(j % 4) * 128 + 128]), "pV")
                                tv_v = k.st["pV"].n
                                @blk.scalar
                                def _(e, kc=kc, tc=tc, tv_v=tv_v):
                                    k.wait(e, "s", "pV", tv_v)
                                    k.dma(e, "s", "Lvst",
                                          out=vaT_d.ap()[(kc - 4) * 128:(kc - 3) * 128, tc * 512:(tc + 1) * 512],
                                          in_=kstage[:])

                        for j in range(4):
                            tt_idx = tc * 4 + j
                            rms_stats(ktok_tc[:, j * 512:(j + 1) * 512], 512, 4,
                                      kssh, tt_idx * 4)
                            tv_k = k.st["pV"].n
                            @blk.scalar
                            def _(e, j=j, tt_idx=tt_idx, tv_k=tv_k):
                                k.wait(e, "s", "pV", tv_k)
                                k.wait(e, "s", "pS", k.st["pS"].n)
                                k.dma(e, "s", "Lkraw",
                                      out=kraw_d.ap()[tt_idx * 128:(tt_idx + 1) * 128, :],
                                      in_=ktok_tc[:, j * 512:(j + 1) * 512])

                batch_rsqrt(kssh, kssh, 64, EPS_RMS)
                for tt_idx in range(16):
                    @blk.scalar
                    def _(e, tt_idx=tt_idx):
                        k.join(e, "s")
                        k.wait(e, "s", "Lkraw", k.st["Lkraw"].n)
                        td = k.dma(e, "s", "Lx", out=ktok_tc[:, 0:512],
                                   in_=kraw_d.ap()[tt_idx * 128:(tt_idx + 1) * 128, :])
                        k.wait(e, "s", "Lx", td)
                    td_k = k.st["Lx"].n
                    @blk.vector
                    def _(e, td_k=td_k):
                        k.wait(e, "v", "Lx", td_k)
                        k.wait(e, "v", "Lkst", k.st["Lkst"].n if "Lkst" in k.st else 0)
                        k.wait(e, "v", "Lvst", k.st["Lvst"].n if "Lvst" in k.st else 0)
                        k.wait(e, "v", "Lrope", t_ropeload)
                    rope_apply(ktok_tc[:, 0:512], 512, 4,
                               ctk[:, tt_idx, :], stk[:, tt_idx, :], knw_t,
                               kssh, tt_idx * 4,
                               kstage, [h * 128 for h in range(4)],
                               sel_dst=kselT, sel_col=tt_idx * 128)
                    tv_k2 = k.st["pV"].n
                    @blk.scalar
                    def _(e, tt_idx=tt_idx, tv_k2=tv_k2):
                        k.wait(e, "s", "pV", tv_k2)
                        k.dma(e, "s", "Lkst",
                              out=kaT_d.ap().rearrange("(a p) c -> p a c", p=128)[:, :, tt_idx * 128:(tt_idx + 1) * 128],
                              in_=kstage[:].rearrange("p (a c) -> p a c", c=128))

                a4ES = contextlib.ExitStack()
                with a4ES:
                    selbuf = a4ES.enter_context(nc.sbuf_tensor("selbuf", [128, 2048], F32))
                    mrb = [a4ES.enter_context(nc.sbuf_tensor(f"mrb{i}", [128, 2048], F32)) for i in range(2)]
                    m8 = a4ES.enter_context(nc.sbuf_tensor("m8", [128, 8], F32))
                    thr = a4ES.enter_context(nc.sbuf_tensor("thr", [128, 1], F32))
                    g01 = a4ES.enter_context(nc.sbuf_tensor("g01", [128, 512], F32))
                    m01 = a4ES.enter_context(nc.sbuf_tensor("m01", [128, 512], F32))
                    m01a = a4ES.enter_context(nc.sbuf_tensor("m01a", [128, 512], BF16))

                    colidx = 0
                    for t in range(NTT):
                        nch = SEL_NCH[t]
                        ci = t // 4
                        cols_t = list(range(colidx, colidx + nch))
                        colidx += nch
                        for ch in range(nch):
                            @blk.tensor
                            def _(e, t=t, ch=ch):
                                k.wait(e, "t", "pV", k.st["pV"].n)
                                k.inc(e.matmul(pE[:], qselT[:, t * 128:(t + 1) * 128],
                                               kselT[:, ch * 512:(ch + 1) * 512],
                                               start=True, stop=True), "pT")
                            tt_ = k.st["pT"].n

                            @blk.vector
                            def _(e, t=t, ch=ch, tt_=tt_, col=cols_t[ch]):
                                k.wait(e, "v", "pT", tt_)
                                k.wait(e, "v", "pV", k.st["pV"].n)
                                k.wait(e, "v", "pG", t_iota)
                                k.inc(e.tensor_scalar(g01[:], iotaJI[:], offsel_s[:, col:col + 1],
                                                      None, ALU.is_gt), "pV")
                                k.wait(e, "v", "pV", k.st["pV"].n)
                                k.inc(e.scalar_tensor_tensor(selbuf[:, ch * 512:(ch + 1) * 512],
                                                             g01[:], NEGBIG, pE[:],
                                                             ALU.mult, ALU.add), "pV")

                        W = nch * 512
                        for r in range(8):
                            @blk.vector
                            def _(e, r=r, W=W):
                                k.wait(e, "v", "pV", k.st["pV"].n)
                                src = selbuf if r == 0 else mrb[(r - 1) % 2]
                                k.inc(e.max(out=m8[:], in_=src[:, 0:W]), "pV")
                                if r < 7:
                                    k.wait(e, "v", "pV", k.st["pV"].n)
                                    k.inc(e.match_replace(out=mrb[r % 2][:, 0:W], in_to_replace=m8[:],
                                                          in_values=src[:, 0:W], imm_value=NEGBIG), "pV")
                        @blk.vector
                        def _(e):
                            k.wait(e, "v", "pV", k.st["pV"].n)
                            k.inc(e.tensor_scalar_max(thr[:], m8[:, 7:8], -1e29), "pV")

                        for ch in range(nch):
                            @blk.vector
                            def _(e, t=t, ch=ch, col=cols_t[ch]):
                                k.wait(e, "v", "pV", k.st["pV"].n)
                                k.inc(e.tensor_scalar(m01[:], selbuf[:, ch * 512:(ch + 1) * 512],
                                                      thr[:], None, ALU.is_ge), "pV")
                                k.inc(e.tensor_scalar(g01[:], iotaJI[:], offsel_s[:, col:col + 1],
                                                      None, ALU.is_equal), "pV")
                                k.wait(e, "v", "pV", k.st["pV"].n)
                                k.inc(e.tensor_max(m01[:], m01[:], g01[:]), "pV")
                                k.wait(e, "v", "pV", k.st["pV"].n)
                                k.inc(e.tensor_scalar(m01a[:], m01[:], -1.0, 3.3e28,
                                                      ALU.add, ALU.mult), "pV")
                            slot0 = (0 if ci == 0 else 8)
                            for j in range(4):
                                kt = 4 * ch + j
                                s_ = slot0 + kt
                                @blk.tensor
                                def _(e, j=j):
                                    k.wait(e, "t", "pV", k.st["pV"].n)
                                    k.inc(e.transpose(pT16[:, (j % 4) * 128:(j % 4) * 128 + 128],
                                                      m01a[:, j * 128:(j + 1) * 128], ident16[:]), "pT")
                                tt2 = k.st["pT"].n
                                @blk.vector
                                def _(e, j=j, s_=s_, t=t, tt2=tt2):
                                    k.wait(e, "v", "pT", tt2)
                                    k.inc(e.tensor_copy(
                                        m01T_buf[:, s_ * 512 + (t % 4) * 128:s_ * 512 + (t % 4) * 128 + 128],
                                        pT16[:, (j % 4) * 128:(j % 4) * 128 + 128]), "pV")

                a5ES = contextlib.ExitStack()
                with a5ES:
                    gw16 = a5ES.enter_context(nc.sbuf_tensor("gw16", [128, 16 * 128], BF16))
                    khb = [a5ES.enter_context(nc.sbuf_tensor(f"khb{i}", [128, T], BF16)) for i in range(2)]
                    vhb = [a5ES.enter_context(nc.sbuf_tensor(f"vhb{i}", [128, T], BF16)) for i in range(2)]
                    d0w_c = [a5ES.enter_context(nc.sbuf_tensor(f"d0wc{i}", [128, 512], F32)) for i in range(2)]
                    zb = [a5ES.enter_context(nc.sbuf_tensor(f"zb{i}", [128, 512], F32)) for i in range(3)]
                    z2b = [a5ES.enter_context(nc.sbuf_tensor(f"z2b{i}", [128, 512], F32)) for i in range(3)]
                    pb16 = [a5ES.enter_context(nc.sbuf_tensor(f"pb16_{i}", [128, 512], BF16)) for i in range(3)]
                    rrow = a5ES.enter_context(nc.sbuf_tensor("rrow", [1, 512], F32))
                    rb = a5ES.enter_context(nc.sbuf_tensor("rb", [128, 512], F32))
                    ybuf = a5ES.enter_context(nc.sbuf_tensor("ybuf", [128, 512], F32))
                    yb16 = a5ES.enter_context(nc.sbuf_tensor("yb16", [128, 512], BF16))
                    tg = a5ES.enter_context(nc.sbuf_tensor("tg", [128, 512], F32))
                    ygslab = a5ES.enter_context(nc.sbuf_tensor("ygslab", [128, 512], BF16))

                    @blk.vector
                    def _(e):
                        k.join(e, "v")
                    tv_d0 = k.st["pV"].n

                    @blk.gpsimd
                    def _(e):
                        k.join(e, "g")
                        k.dma(e, "g", "Lgw", out=gw16[:].rearrange("p (h e) -> p h e", e=128),
                              in_=gate_w.ap().rearrange("h d e -> d h e"))
                    t_gw = k.st["Lgw"].n

                    NBK = 3
                    banksS = [pA, pB, pT32]
                    last_tanh = {0: 0, 1: 0, 2: 0}
                    t_kvload = 0
                    for ci in range(2):
                        KT = QCH_KT[ci]
                        for h in range(HEADS):
                            hkv = h // 4
                            pT0 = k.st["pT"].n
                            pV0 = k.st["pV"].n
                            pS0 = k.st["pS"].n
                            tt_mm = [pT0 + 3 * kt + 1 for kt in range(KT)]
                            tt_pv = pT0 + 3 * KT
                            tt_bc = tt_pv + 1
                            tt_g = tt_bc + 1
                            tv_z2 = [pV0 + 1 + 2 * (kt + 1) for kt in range(KT)]
                            tv_r = pV0 + 1 + 2 * KT + 1
                            tv_y = pV0 + 1 + 2 * KT + 4
                            tv_yg = pV0 + 1 + 2 * KT + 5
                            t_tanh_l = [pS0 + 2 * kt + 1 for kt in range(KT)]
                            t_exp_l = [pS0 + 2 * kt + 2 for kt in range(KT)]
                            t_tg = pS0 + 2 * KT + 1
                            lyg_prev = k.st["Lyg"].n if "Lyg" in k.st else 0

                            @blk.scalar
                            def _(e, ci=ci, h=h, KT=KT, hkv=hkv, tt_mm=tt_mm,
                                  tv_z2=tv_z2, tt_g=tt_g, tv_yg=tv_yg):
                                if h % 4 == 0:
                                    k.join(e, "s")
                                    k.wait(e, "s", "Lkst", k.st["Lkst"].n)
                                    k.wait(e, "s", "Lvst", k.st["Lvst"].n)
                                    k.dma(e, "s", f"Lkv{hkv % 2}", first=True, out=khb[hkv % 2][:],
                                          in_=kaT_d.ap()[hkv * 128:(hkv + 1) * 128, :])
                                    k.dma(e, "s", f"Lkv{hkv % 2}", first=False, out=vhb[hkv % 2][:],
                                          in_=vaT_d.ap()[hkv * 128:(hkv + 1) * 128, :])
                                for kt in range(KT):
                                    bank = kt % NBK
                                    psS = banksS[bank]
                                    jcol = kt if ci == 0 else 8 + kt
                                    k.wait(e, "s", "pS", k.st["pS"].n)
                                    k.wait(e, "s", "pT", tt_mm[kt])
                                    k.inc(e.activation(zb[bank][:], psS[:], AF.Tanh,
                                                       scale=float(ISQD / SOFT)), "pS")
                                    k.wait(e, "s", "pV", tv_z2[kt])
                                    k.inc(e.activation(pb16[bank][:], z2b[bank][:], AF.Exp,
                                                       scale=SOFT, bias=aab_s[:, h * 24 + jcol:h * 24 + jcol + 1]), "pS")
                                k.wait(e, "s", "pT", tt_g)
                                k.inc(e.activation(tg[:], pF[:], AF.Tanh,
                                                   scale=1.0, bias=gbh_s[:, h:h + 1]), "pS")
                                k.wait(e, "s", "pV", tv_yg)
                                k.dma(e, "s", "Lyg",
                                      out=ygall_d.ap()[h * 128:(h + 1) * 128, ci * 512:(ci + 1) * 512],
                                      in_=ygslab[:])
                            assert k.st["pS"].n == t_tg, (k.st["pS"].n, t_tg)
                            if h % 4 == 0:
                                t_kvload = k.st[f"Lkv{hkv % 2}"].n

                            @blk.tensor
                            def _(e, ci=ci, h=h, KT=KT, hkv=hkv, pV0=pV0,
                                  t_kvload=t_kvload, t_tanh_l=t_tanh_l, t_exp_l=t_exp_l,
                                  tv_r=tv_r, tv_y=tv_y,
                                  lts=dict(last_tanh)):
                                k.wait(e, "t", f"Lkv{hkv % 2}", t_kvload)
                                for kt in range(KT):
                                    bank = kt % NBK
                                    psS = banksS[bank]
                                    k.wait(e, "t", "pS",
                                           t_tanh_l[kt - NBK] if kt >= NBK else lts[bank])
                                    if kt == 1:
                                        k.wait(e, "t", "pV", pV0)
                                    k.inc(e.matmul(psS[:], khb[hkv % 2][:, kt * 128:(kt + 1) * 128],
                                                   qhT[:, h * NTOK + ci * 512:h * NTOK + ci * 512 + 512],
                                                   start=True, stop=True), "pT")
                                    if kt >= 1:
                                        k.wait(e, "t", "pS", t_exp_l[kt - 1])
                                        pbprev = pb16[(kt - 1) % NBK]
                                        k.inc(e.matmul(pD[0:1, 0:512], ones_col16[:], pbprev[:],
                                                       start=(kt - 1 == 0), stop=False), "pT")
                                        k.inc(e.matmul(pC[:], vhb[hkv % 2][:, (kt - 1) * 128:kt * 128],
                                                       pbprev[:], start=(kt - 1 == 0), stop=False), "pT")
                                k.wait(e, "t", "pS", t_exp_l[KT - 1])
                                pbprev = pb16[(KT - 1) % NBK]
                                k.inc(e.matmul(pD[0:1, 0:512], ones_col16[:], pbprev[:],
                                               start=(KT - 1 == 0), stop=True), "pT")
                                k.inc(e.matmul(pC[:], vhb[hkv % 2][:, (KT - 1) * 128:KT * 128],
                                               pbprev[:], start=(KT - 1 == 0), stop=True), "pT")
                                k.wait(e, "t", "pV", tv_r)
                                k.inc(e.matmul(pE[:], half_row[:], rrow[:], start=True, stop=True), "pT")
                                k.wait(e, "t", "pV", tv_y)
                                k.wait(e, "t", "Lgw", t_gw)
                                k.inc(e.matmul(pF[:], gw16[:, h * 128:(h + 1) * 128], yb16[:],
                                               start=True, stop=True), "pT")
                            assert k.st["pT"].n == tt_g, (k.st["pT"].n, tt_g)

                            @blk.vector
                            def _(e, ci=ci, h=h, KT=KT, tt_pv=tt_pv, tt_bc=tt_bc,
                                  t_tanh_l=t_tanh_l, t_tg=t_tg, lyg_prev=lyg_prev):
                                if ci == 0 and h == 0:
                                    k.wait(e, "v", "pV", tv_d0)
                                    k.wait(e, "v", "pG", t_iota)
                                k.wait(e, "v", "pV", k.st["pV"].n)
                                k.inc(e.tensor_scalar_mul(d0w_c[h % 2][:], iotaJI[:],
                                                          -SLOPES[h] / SOFT), "pV")
                                k.wait(e, "v", "pV", k.st["pV"].n)
                                for kt in range(KT):
                                    bank = kt % NBK
                                    k.wait(e, "v", "pS", t_tanh_l[kt])
                                    k.inc(e.tensor_add(z2b[bank][:], zb[bank][:],
                                                       d0w_c[h % 2][:]), "pV")
                                    k.wait(e, "v", "pV", k.st["pV"].n)
                                    s_ = (0 if ci == 0 else 8) + kt
                                    k.inc(e.tensor_add(z2b[bank][:], z2b[bank][:],
                                                       m01T_buf[:, s_ * 512:(s_ + 1) * 512]), "pV")
                                k.wait(e, "v", "pT", tt_pv)
                                k.inc(e.reciprocal(rrow[:], pD[0:1, 0:512]), "pV")
                                k.wait(e, "v", "pT", tt_bc)
                                k.inc(e.tensor_copy(rb[:], pE[:]), "pV")
                                k.wait(e, "v", "pV", k.st["pV"].n)
                                k.inc(e.tensor_mul(ybuf[:], pC[:], rb[:]), "pV")
                                k.wait(e, "v", "pV", k.st["pV"].n)
                                k.inc(e.tensor_copy(yb16[:], ybuf[:]), "pV")
                                k.wait(e, "v", "pS", t_tg)
                                k.wait(e, "v", "Lyg", lyg_prev)
                                k.inc(e.scalar_tensor_tensor(ygslab[:], tg[:], 1.0, ybuf[:],
                                                             ALU.add, ALU.mult), "pV")
                            assert k.st["pV"].n == tv_yg, (k.st["pV"].n, tv_yg)

                            for kt in range(max(0, KT - NBK), KT):
                                last_tanh[kt % NBK] = t_tanh_l[kt]

                a6ES = contextlib.ExitStack()
                with a6ES:
                    wo16 = [a6ES.enter_context(nc.sbuf_tensor(f"wo16_{i}", [128, 16 * 512], BF16)) for i in range(2)]
                    ygl = a6ES.enter_context(nc.sbuf_tensor("ygl", [128, 16 * 128], BF16))
                    yslab = a6ES.enter_context(nc.sbuf_tensor("yslab", [128, 512], F32))
                    t_wo_used = [0, 0]
                    for dm in range(4):
                        @blk.gpsimd
                        def _(e, dm=dm):
                            k.join(e, "g") if dm < 2 else k.wait(e, "g", "pT", t_wo_used[dm % 2])
                            k.dma(e, "g", f"Lwo{dm % 2}",
                                  out=wo16[dm % 2][:].rearrange("p (c n) -> p c n", n=512),
                                  in_=w_o.ap().rearrange("(c p) o -> p c o", p=128)[:, :, dm * 512:(dm + 1) * 512])
                        t_wo = k.st[f"Lwo{dm % 2}"].n

                        for t in range(NTT):
                            @blk.scalar
                            def _(e, t=t):
                                k.join(e, "s")
                                k.wait(e, "s", "Lyg", k.st["Lyg"].n)
                                k.dma(e, "s", "Lygl",
                                      out=ygl[:].rearrange("p (h j) -> p h j", j=128),
                                      in_=ygall_d.ap().rearrange("(h p) tok -> p h tok", p=128)[:, :, t * 128:(t + 1) * 128])
                            t_ygl = k.st["Lygl"].n

                            @blk.tensor
                            def _(e, dm=dm, t=t, t_wo=t_wo, t_ygl=t_ygl):
                                k.wait(e, "t", f"Lwo{dm % 2}", t_wo)
                                k.wait(e, "t", "Lygl", t_ygl)
                                k.wait(e, "t", "pS", k.st["pS"].n)
                                ps = [pA, pB][t % 2]
                                k.inc(e.matmul(ps[:], ones_row[:], bo_s[0:1, dm * 512:(dm + 1) * 512],
                                               start=True, stop=False), "pT")
                                for hh in range(16):
                                    k.inc(e.matmul(ps[:], ygl[:, hh * 128:(hh + 1) * 128],
                                                   wo16[dm % 2][:, hh * 512:(hh + 1) * 512],
                                                   start=False, stop=(hh == 15)), "pT")
                            tt_ = k.st["pT"].n
                            t_wo_used[dm % 2] = tt_

                            @blk.scalar
                            def _(e, dm=dm, t=t, tt_=tt_):
                                k.wait(e, "s", "pS", k.st["pS"].n)
                                k.wait(e, "s", "Lyp", k.st["Lyp"].n if "Lyp" in k.st else 0)
                                k.wait(e, "s", "pT", tt_)
                                ps = [pA, pB][t % 2]
                                k.inc(e.activation(yslab[:], ps[:], AF.Identity), "pS")
                            tS_y = k.st["pS"].n

                            @blk.sync
                            def _(e, dm=dm, t=t, tS_y=tS_y):
                                k.wait(e, "sync", "pS", tS_y)
                                k.dma(e, "sync", "Lyp", out=yproj.ap()[t * 128:(t + 1) * 128, dm * 512:(dm + 1) * 512],
                                      in_=yslab[:])

            @blk.scalar
            def _(e):
                k.wait(e, "s", "Lyp", k.st["Lyp"].n)
            residual_pass(x_own, yproj, xres[0])

        hq_d = dint("hq_d", [DFF, NTOK], BF16)

        def ffn_phase(src_d, dst_d, tag):
            fES = contextlib.ExitStack()
            with fES:
                xqT = fES.enter_context(nc.sbuf_tensor(f"xqT_{tag}", [128, 16 * NTOK], BF16))
                am = fES.enter_context(nc.sbuf_tensor(f"am_{tag}", [128, 1], F32))
                s127 = fES.enter_context(nc.sbuf_tensor(f"s127_{tag}", [128, 1], F32))
                rs_t = fES.enter_context(nc.sbuf_tensor(f"rs_{tag}", [128, 8], F32))
                rs2_t = fES.enter_context(nc.sbuf_tensor(f"rs2_{tag}", [128, 8], F32))
                rs3_t = fES.enter_context(nc.sbuf_tensor(f"rs3_{tag}", [128, 8], F32))
                rsh_t = fES.enter_context(nc.sbuf_tensor(f"rsh_{tag}", [128, 8], F32))
                hq16 = fES.enter_context(nc.sbuf_tensor(f"hq16_{tag}", [128, 512], BF16))

                fqES = contextlib.ExitStack()
                with fqES:
                    xtile_f = fqES.enter_context(nc.sbuf_tensor(f"xtf_{tag}", [128, D], F32))
                    xsq_f = fqES.enter_context(nc.sbuf_tensor(f"xsqf_{tag}", [128, D], F32))
                    xn_f = fqES.enter_context(nc.sbuf_tensor(f"xnf_{tag}", [128, D], F32))
                    yq = fqES.enter_context(nc.sbuf_tensor(f"yq_{tag}", [128, D], F32))
                    yqi = fqES.enter_context(nc.sbuf_tensor(f"yqi_{tag}", [128, D], I32))
                    xq16 = fqES.enter_context(nc.sbuf_tensor(f"xq16_{tag}", [128, D], BF16))
                    s1 = fqES.enter_context(nc.sbuf_tensor(f"s1_{tag}", [128, 1], F32))
                    s2 = fqES.enter_context(nc.sbuf_tensor(f"s2_{tag}", [128, 1], F32))
                    nmr_f = fqES.enter_context(nc.sbuf_tensor(f"nmrf_{tag}", [128, 1], F32))
                    var_f = fqES.enter_context(nc.sbuf_tensor(f"varf_{tag}", [128, 8], F32))
                    mean_f = fqES.enter_context(nc.sbuf_tensor(f"meanf_{tag}", [128, 8], F32))
                    rstd_fa = fqES.enter_context(nc.sbuf_tensor(f"rstdfa_{tag}", [128, 8], F32))

                    for t in range(NTT):
                        ln_stats(src_d, t, xtile_f, xsq_f, s1, s2, var_f, mean_f, t)
                    batch_rsqrt(var_f, rstd_fa, NTT, EPS_LN)
                    for t in range(NTT):
                        tS = ln_apply(src_d, t, xtile_f, rstd_fa, mean_f, nmr_f, t, xn_f)
                        @blk.vector
                        def _(e, t=t, tS=tS):
                            k.wait(e, "v", "pS", tS)
                            k.wait(e, "v", "pV", k.st["pV"].n)
                            k.inc(e.reduce_max(am[:], xn_f[:], axis=AX.X, apply_absolute_value=True), "pV")
                            k.wait(e, "v", "pV", k.st["pV"].n)
                            k.inc(e.tensor_scalar_max(am[:], am[:], EPS_Q), "pV")
                            k.wait(e, "v", "pV", k.st["pV"].n)
                            k.inc(e.reciprocal(s127[:], am[:]), "pV")
                            k.wait(e, "v", "pV", k.st["pV"].n)
                            k.inc(e.tensor_scalar_mul(s127[:], s127[:], 127.0), "pV")
                            k.inc(e.tensor_scalar_mul(rs_t[:, t:t + 1], am[:], 1.0 / 127.0), "pV")
                            k.wait(e, "v", "pV", k.st["pV"].n)
                            k.inc(e.tensor_scalar_mul(rs2_t[:, t:t + 1], rs_t[:, t:t + 1], 0.5), "pV")
                            k.inc(e.tensor_mul(rs3_t[:, t:t + 1], rs_t[:, t:t + 1], rs_t[:, t:t + 1]), "pV")
                            k.wait(e, "v", "pV", k.st["pV"].n)
                            k.inc(e.tensor_scalar_mul(rs3_t[:, t:t + 1], rs3_t[:, t:t + 1], 0.5), "pV")
                            k.inc(e.tensor_scalar_mul(yq[:], xn_f[:], s127[:]), "pV")
                            k.wait(e, "v", "pV", k.st["pV"].n)
                            k.inc(e.tensor_scalar_add(yq[:], yq[:], RC), "pV")
                            k.wait(e, "v", "pV", k.st["pV"].n)
                            k.inc(e.tensor_copy(yqi[:], yq[:]), "pV")
                            k.wait(e, "v", "pV", k.st["pV"].n)
                            k.inc(e.tensor_scalar_add(xq16[:], yqi[:], -RC), "pV")
                        transposeN(xq16, [c * 128 for c in range(16)], xqT,
                                   [c * NTOK + t * 128 for c in range(16)], BF16, 16)

                f2ES = contextlib.ExitStack()
                with f2ES:
                    h16 = f2ES.enter_context(nc.sbuf_tensor(f"h16_{tag}", [128, 8 * DFF], BF16))
                    f3ES = contextlib.ExitStack()
                    with f3ES:
                        FW = 448
                        NFC = 13
                        wgb = [f3ES.enter_context(nc.sbuf_tensor(f"wgb{i}_{tag}", [128, 16 * FW], BF16)) for i in range(2)]
                        wub = [f3ES.enter_context(nc.sbuf_tensor(f"wub{i}_{tag}", [128, 16 * FW], BF16)) for i in range(2)]
                        tgb = f3ES.enter_context(nc.sbuf_tensor(f"tgb_{tag}", [128, FW], F32))
                        t_w_used = [0, 0]
                        prev_tanh = 0
                        prev_silu = 0
                        for fc in range(NFC):
                            fw = FW if fc < NFC - 1 else DFF - (NFC - 1) * FW
                            @blk.sync
                            def _(e, fc=fc, fw=fw):
                                k.join(e, "sync") if fc < 2 else k.wait(e, "sync", "pT", t_w_used[fc % 2])
                                k.wait(e, "sync", "cc", 3)
                                k.dma(e, "sync", f"Lwg{fc % 2}", first=True,
                                      out=wgb[fc % 2][:, 0:16 * fw].rearrange("p (c n) -> p c n", n=fw),
                                      in_=cc_wg_out.ap().rearrange("(c p) o -> p c o", p=128)[:, :, fc * FW:fc * FW + fw])
                                k.dma(e, "sync", f"Lwg{fc % 2}", first=False,
                                      out=wub[fc % 2][:, 0:16 * fw].rearrange("p (c n) -> p c n", n=fw),
                                      in_=cc_wu_out.ap().rearrange("(c p) o -> p c o", p=128)[:, :, fc * FW:fc * FW + fw])
                            t_w = k.st[f"Lwg{fc % 2}"].n

                            pT0 = k.st["pT"].n
                            pV0 = k.st["pV"].n
                            pS0 = k.st["pS"].n
                            tt_l = [pT0 + 32 * (t + 1) for t in range(NTT)]
                            ts_l = [pS0 + t + 1 for t in range(NTT)]
                            sv1_l = [pV0 + 2 * t + 1 for t in range(NTT)]
                            sv2_l = [pV0 + 2 * t + 2 for t in range(NTT)]

                            @blk.scalar
                            def _(e, fw=fw, tt_l=tt_l, sv2_l=sv2_l, prev_silu=prev_silu):
                                for t in range(NTT):
                                    k.wait(e, "s", "pS", k.st["pS"].n)
                                    k.wait(e, "s", "pT", tt_l[t])
                                    k.wait(e, "s", "pV", sv2_l[t - 1] if t >= 1 else prev_silu)
                                    k.inc(e.activation(tgb[:, 0:fw], pA[:, 0:fw], AF.Tanh,
                                                       scale=rs2_t[:, t:t + 1]), "pS")
                            assert k.st["pS"].n == ts_l[-1]

                            for t in range(NTT):
                                @blk.tensor
                                def _(e, fc=fc, fw=fw, t=t, t_w=t_w, ts_l=ts_l, sv2_l=sv2_l,
                                      prev_tanh=prev_tanh, prev_silu=prev_silu):
                                    k.wait(e, "t", f"Lwg{fc % 2}", t_w)
                                    k.wait(e, "t", "pV", sv2_l[t - 1] if t >= 1 else prev_silu)
                                    k.wait(e, "t", "pS", ts_l[t - 1] if t >= 1 else prev_tanh)
                                    for dd in range(16):
                                        k.inc(e.matmul(pA[:, 0:fw], xqT[:, dd * NTOK + t * 128:dd * NTOK + t * 128 + 128],
                                                       wgb[fc % 2][:, dd * fw:(dd + 1) * fw],
                                                       start=(dd == 0), stop=(dd == 15)), "pT")
                                    for dd in range(16):
                                        k.inc(e.matmul(pB[:, 0:fw], xqT[:, dd * NTOK + t * 128:dd * NTOK + t * 128 + 128],
                                                       wub[fc % 2][:, dd * fw:(dd + 1) * fw],
                                                       start=(dd == 0), stop=(dd == 15)), "pT")
                            assert k.st["pT"].n == tt_l[-1]
                            t_w_used[fc % 2] = tt_l[-1]

                            @blk.vector
                            def _(e, fc=fc, fw=fw, ts_l=ts_l):
                                for t in range(NTT):
                                    k.wait(e, "v", "pS", ts_l[t])
                                    k.wait(e, "v", "pV", k.st["pV"].n)
                                    k.inc(e.scalar_tensor_tensor(tgb[:, 0:fw], tgb[:, 0:fw], 1.0,
                                                                 pA[:, 0:fw], ALU.add, ALU.mult), "pV")
                                    k.wait(e, "v", "pV", k.st["pV"].n)
                                    k.inc(e.scalar_tensor_tensor(
                                        h16[:, t * DFF + fc * FW:t * DFF + fc * FW + fw],
                                        tgb[:, 0:fw], rs3_t[:, t:t + 1], pB[:, 0:fw],
                                        ALU.mult, ALU.mult), "pV")
                            assert k.st["pV"].n == sv2_l[-1]
                            prev_tanh = ts_l[-1]
                            prev_silu = sv2_l[-1]

                    f4ES = contextlib.ExitStack()
                    with f4ES:
                        yq2 = f4ES.enter_context(nc.sbuf_tensor(f"yq2_{tag}", [128, 512], F32))
                        yqi2 = f4ES.enter_context(nc.sbuf_tensor(f"yqi2_{tag}", [128, 512], I32))
                        hstage = f4ES.enter_context(nc.sbuf_tensor(f"hstage_{tag}", [128, 512], BF16))
                        for t in range(NTT):
                            @blk.vector
                            def _(e, t=t):
                                k.wait(e, "v", "pV", k.st["pV"].n)
                                k.inc(e.reduce_max(am[:], h16[:, t * DFF:(t + 1) * DFF],
                                                   axis=AX.X, apply_absolute_value=True), "pV")
                                k.wait(e, "v", "pV", k.st["pV"].n)
                                k.inc(e.tensor_scalar_max(am[:], am[:], EPS_Q), "pV")
                                k.wait(e, "v", "pV", k.st["pV"].n)
                                k.inc(e.reciprocal(s127[:], am[:]), "pV")
                                k.wait(e, "v", "pV", k.st["pV"].n)
                                k.inc(e.tensor_scalar_mul(s127[:], s127[:], 127.0), "pV")
                                k.inc(e.tensor_scalar_mul(rsh_t[:, t:t + 1], am[:], 1.0 / 127.0), "pV")
                            for fc in range(11):
                                fw = 512 if fc < 10 else DFF - 10 * 512
                                nblk = fw // 128
                                @blk.vector
                                def _(e, t=t, fc=fc, fw=fw):
                                    k.wait(e, "v", "pV", k.st["pV"].n)
                                    k.inc(e.tensor_scalar_mul(yq2[:, 0:fw],
                                                              h16[:, t * DFF + fc * 512:t * DFF + fc * 512 + fw],
                                                              s127[:]), "pV")
                                    k.wait(e, "v", "pV", k.st["pV"].n)
                                    k.inc(e.tensor_scalar_add(yq2[:, 0:fw], yq2[:, 0:fw], RC), "pV")
                                    k.wait(e, "v", "pV", k.st["pV"].n)
                                    k.inc(e.tensor_copy(yqi2[:, 0:fw], yq2[:, 0:fw]), "pV")
                                    k.wait(e, "v", "pV", k.st["pV"].n)
                                    k.inc(e.tensor_scalar_add(hq16[:, 0:fw], yqi2[:, 0:fw], -RC), "pV")
                                @blk.vector
                                def _(e):
                                    k.wait(e, "v", "Lhst", k.st["Lhst"].n if "Lhst" in k.st else 0)
                                transposeN(hq16, [j * 128 for j in range(nblk)], hstage,
                                           [j * 128 for j in range(nblk)], BF16, nblk)
                                tv_h = k.st["pV"].n
                                @blk.scalar
                                def _(e, t=t, fc=fc, nblk=nblk, tv_h=tv_h):
                                    k.wait(e, "s", "pV", tv_h)
                                    k.dma(e, "s", "Lhst",
                                          out=hq_d.ap().rearrange("(a p) tok -> p a tok", p=128)[:, fc * 4:fc * 4 + nblk, t * 128:(t + 1) * 128],
                                          in_=hstage[:, 0:nblk * 128].rearrange("p (a j) -> p a j", j=128))

                f5ES = contextlib.ExitStack()
                with f5ES:
                    wdb = f5ES.enter_context(nc.sbuf_tensor(f"wdb_{tag}", [128, 43 * 512], BF16))
                    hql = [f5ES.enter_context(nc.sbuf_tensor(f"hql{i}_{tag}", [128, 43 * 128], BF16)) for i in range(2)]
                    dslab = f5ES.enter_context(nc.sbuf_tensor(f"dslab_{tag}", [128, 512], F32))
                    t_hql_used = [0, 0]
                    prev_evac = [0, 0]
                    for dm in range(4):
                        @blk.sync
                        def _(e, dm=dm):
                            k.join(e, "sync")
                            k.wait(e, "sync", "cc", 4)
                            for ft in range(43):
                                r, sl = ft // 6, ft % 6
                                row = r * 768 + sl * 128
                                k.dma(e, "sync", "Lwd", first=(ft == 0),
                                      out=wdb[:, ft * 512:(ft + 1) * 512],
                                      in_=cc_wd_out.ap()[row:row + 128, dm * 512:(dm + 1) * 512])
                        t_wd = k.st["Lwd"].n

                        pT0 = k.st["pT"].n
                        pS0 = k.st["pS"].n
                        lyp0 = k.st["Lyp"].n if "Lyp" in k.st else 0
                        tt_l = [pT0 + 43 * (t + 1) for t in range(NTT)]
                        ts_l = [pS0 + t + 1 for t in range(NTT)]
                        lhq_tok = []

                        @blk.scalar
                        def _(e, dm=dm, tt_l=tt_l, lyp0=lyp0):
                            for t in range(NTT):
                                if dm == 0 and t < 2:
                                    k.join(e, "s")
                                else:
                                    k.wait(e, "s", "pT", t_hql_used[t % 2])
                                k.wait(e, "s", "Lhst", k.st["Lhst"].n)
                                lhq_tok.append(k.dma(e, "s", f"Lhq{t % 2}",
                                      out=hql[t % 2][:].rearrange("p (a j) -> p a j", j=128),
                                      in_=hq_d.ap().rearrange("(a p) tok -> p a tok", p=128)[:, :, t * 128:(t + 1) * 128]))
                                k.wait(e, "s", "pS", k.st["pS"].n)
                                k.wait(e, "s", "Lyp", lyp0 + 16 * t)
                                k.wait(e, "s", "pT", tt_l[t])
                                ps = [pA, pB][t % 2]
                                k.inc(e.activation(dslab[:], ps[:], AF.Identity,
                                                   scale=rsh_t[:, t:t + 1]), "pS")
                                t_hql_used[t % 2] = tt_l[t]
                        assert k.st["pS"].n == ts_l[-1]

                        for t in range(NTT):
                            @blk.tensor
                            def _(e, dm=dm, t=t, t_wd=t_wd, t_hq=lhq_tok[t], ts_l=ts_l,
                                  pe0=prev_evac[0], pe1=prev_evac[1]):
                                k.wait(e, "t", "Lwd", t_wd)
                                k.wait(e, "t", f"Lhq{t % 2}", t_hq)
                                k.wait(e, "t", "pS",
                                       ts_l[t - 2] if t >= 2 else (pe0 if t % 2 == 0 else pe1))
                                ps = [pA, pB][t % 2]
                                for ft in range(43):
                                    k.inc(e.matmul(ps[:], hql[t % 2][:, ft * 128:(ft + 1) * 128],
                                                   wdb[:, ft * 512:(ft + 1) * 512],
                                                   start=(ft == 0), stop=(ft == 42)), "pT")

                            @blk.sync
                            def _(e, dm=dm, t=t, ts_l=ts_l):
                                k.wait(e, "sync", "pS", ts_l[t])
                                k.dma(e, "sync", "Lyp", out=yproj.ap()[t * 128:(t + 1) * 128, dm * 512:(dm + 1) * 512],
                                      in_=dslab[:])
                        assert k.st["pT"].n == tt_l[-1]
                        prev_evac[0] = ts_l[NTT - 2]
                        prev_evac[1] = ts_l[NTT - 1]

            residual_pass(src_d, yproj, dst_d)

        if upto >= 2:
            ffn_phase(xres[0], xres[1], "f1")

        def attnb_phase(src_d, dst_d):
            kT_d = dint("kT_d", [D, T], BF16)
            vT_d = dint("vT_d", [D, T], BF16)
            yqT_d = dint("yqT_d", [D, NTOK], BF16)
            bES = contextlib.ExitStack()
            with bES:
                qbT = bES.enter_context(nc.sbuf_tensor("qbT", [128, 16 * NTOK], BF16))
                yF = bES.enter_context(nc.sbuf_tensor("yF", [128, 16 * NTOK], BF16))
                am_b = bES.enter_context(nc.sbuf_tensor("amb", [128, 1], F32))
                s127b = bES.enter_context(nc.sbuf_tensor("s127b", [128, 1], F32))
                rsx_t = bES.enter_context(nc.sbuf_tensor("rsx_t", [128, 8], F32))
                rsy_t = bES.enter_context(nc.sbuf_tensor("rsy_t", [128, 8], F32))
                yqb = bES.enter_context(nc.sbuf_tensor("yqb", [128, D], F32))
                yqib = bES.enter_context(nc.sbuf_tensor("yqib", [128, D], I32))
                xq16b = bES.enter_context(nc.sbuf_tensor("xq16b", [128, D], BF16))
                wkvb = [bES.enter_context(nc.sbuf_tensor(f"wkvb{i}", [128, 16 * 128], BF16)) for i in range(2)]

                b1ES = contextlib.ExitStack()
                with b1ES:
                    xtile_b = b1ES.enter_context(nc.sbuf_tensor("xtb", [128, D], F32))
                    xsq_b = b1ES.enter_context(nc.sbuf_tensor("xsqb", [128, D], F32))
                    xn_b = b1ES.enter_context(nc.sbuf_tensor("xnb", [128, D], F32))
                    s1b = b1ES.enter_context(nc.sbuf_tensor("s1b", [128, 1], F32))
                    s2b = b1ES.enter_context(nc.sbuf_tensor("s2b", [128, 1], F32))
                    nmr_b = b1ES.enter_context(nc.sbuf_tensor("nmrb", [128, 1], F32))
                    var_b = b1ES.enter_context(nc.sbuf_tensor("varb", [128, 8], F32))
                    mean_b = b1ES.enter_context(nc.sbuf_tensor("meanb", [128, 8], F32))
                    rstd_ba = b1ES.enter_context(nc.sbuf_tensor("rstdba", [128, 8], F32))
                    xqT2 = b1ES.enter_context(nc.sbuf_tensor("xqT2", [128, 16 * NTOK], BF16))

                    for t in range(NTT):
                        ln_stats(src_d, t, xtile_b, xsq_b, s1b, s2b, var_b, mean_b, t)
                    batch_rsqrt(var_b, rstd_ba, NTT, EPS_LN)
                    for t in range(NTT):
                        tS = ln_apply(src_d, t, xtile_b, rstd_ba, mean_b, nmr_b, t, xn_b)
                        @blk.vector
                        def _(e, t=t, tS=tS):
                            k.wait(e, "v", "pS", tS)
                            k.wait(e, "v", "pV", k.st["pV"].n)
                            k.inc(e.reduce_max(am_b[:], xn_b[:], axis=AX.X, apply_absolute_value=True), "pV")
                            k.wait(e, "v", "pV", k.st["pV"].n)
                            k.inc(e.tensor_scalar_max(am_b[:], am_b[:], EPS_Q), "pV")
                            k.wait(e, "v", "pV", k.st["pV"].n)
                            k.inc(e.reciprocal(s127b[:], am_b[:]), "pV")
                            k.wait(e, "v", "pV", k.st["pV"].n)
                            k.inc(e.tensor_scalar_mul(s127b[:], s127b[:], 127.0), "pV")
                            k.inc(e.tensor_scalar_mul(rsx_t[:, t:t + 1], am_b[:], 1.0 / 127.0), "pV")
                            k.wait(e, "v", "pV", k.st["pV"].n)
                            k.inc(e.tensor_scalar_mul(yqb[:], xn_b[:], s127b[:]), "pV")
                            k.wait(e, "v", "pV", k.st["pV"].n)
                            k.inc(e.tensor_scalar_add(yqb[:], yqb[:], RC), "pV")
                            k.wait(e, "v", "pV", k.st["pV"].n)
                            k.inc(e.tensor_copy(yqib[:], yqb[:]), "pV")
                            k.wait(e, "v", "pV", k.st["pV"].n)
                            k.inc(e.tensor_scalar(xq16b[:], yqib[:], -RC, rsx_t[:, t:t + 1],
                                                  ALU.add, ALU.mult), "pV")
                        transposeN(xq16b, [c * 128 for c in range(16)], xqT2,
                                   [c * NTOK + t * 128 for c in range(16)], BF16, 16)
                    tv_xq = k.st["pV"].n

                    @blk.sync
                    def _(e):
                        k.wait(e, "sync", "pV", tv_xq)
                        k.dma(e, "sync", "Lccx2",
                              out=cc_in_x2.ap().rearrange("(c p) tok -> p c tok", p=128),
                              in_=xqT2[:].rearrange("p (c tok) -> p c tok", tok=NTOK))
                    t_ccin2 = k.st["Lccx2"].n

                    @blk.gpsimd
                    def _(e):
                        k.wait(e, "g", "Lccx2", t_ccin2)
                        k.inc(e.collective_compute(
                            "AllGather", ALU.bypass,
                            replica_groups=[[0, 1], [2, 3], [4, 5], [6, 7]],
                            ins=[cc_in_x2.ap().opt()], outs=[cc_out_x2.ap().opt()]), "cc")
                    t_cc_x2 = k.st["cc"].n

                    t_wb_used = [0, 0]
                    for oc in range(16):
                        @blk.sync
                        def _(e, oc=oc):
                            k.join(e, "sync") if oc < 2 else k.wait(e, "sync", "pT", t_wb_used[oc % 2])
                            k.wait(e, "sync", "cc", 5)
                            k.dma(e, "sync", f"Lwb{oc % 2}",
                                  out=wkvb[oc % 2][:].rearrange("p (c n) -> p c n", n=128),
                                  in_=cc_wqkv_out.ap().rearrange("(c p) o -> p c o", p=128)[:, :, oc * 128:(oc + 1) * 128])
                        t_wb = k.st[f"Lwb{oc % 2}"].n

                        for tc in range(2):
                            @blk.tensor
                            def _(e, oc=oc, tc=tc, t_wb=t_wb):
                                k.wait(e, "t", f"Lwb{oc % 2}", t_wb)
                                k.wait(e, "t", "pV", k.st["pV"].n)
                                ps = [pA, pB][tc % 2]
                                for cd in range(16):
                                    k.inc(e.matmul(ps[:], wkvb[oc % 2][:, cd * 128:(cd + 1) * 128],
                                                   xqT2[:, cd * NTOK + tc * 512:cd * NTOK + tc * 512 + 512],
                                                   start=(cd == 0), stop=(cd == 15)), "pT")
                            tt_ = k.st["pT"].n
                            t_wb_used[oc % 2] = tt_

                            @blk.vector
                            def _(e, oc=oc, tc=tc, tt_=tt_):
                                k.wait(e, "v", "pT", tt_)
                                ps = [pA, pB][tc % 2]
                                k.inc(e.tensor_copy(qbT[:, oc * NTOK + tc * 512:oc * NTOK + tc * 512 + 512], ps[:]), "pV")

                b2ES = contextlib.ExitStack()
                with b2ES:
                    xqTall = b2ES.enter_context(nc.sbuf_tensor("xqTall", [128, 16 * T], BF16))
                    ktmp16 = b2ES.enter_context(nc.sbuf_tensor("ktmp16", [128, 512], BF16))
                    vstage = b2ES.enter_context(nc.sbuf_tensor("vstage", [128, 512], BF16))

                    @blk.gpsimd
                    def _(e):
                        k.wait(e, "g", "cc", t_cc_x2)
                        k.join(e, "g")
                        ccv = cc_out_x2.ap().rearrange("(r c p) tok -> r p c tok", p=128, c=16)
                        dst = xqTall[:].rearrange("p (c tok) -> p c tok", tok=T)
                        for bi, (r, co) in enumerate(((0, 0), (1, 0), (1, 512), (0, 512))):
                            k.dma(e, "g", "Lxqa", first=(bi == 0), out=dst[:, :, bi * 512:(bi + 1) * 512],
                                  in_=ccv[r, :, :, co:co + 512])
                    t_xqall = k.st["Lxqa"].n

                    t_wb_used = [0, 0]
                    for oc in range(16, 48):
                        @blk.sync
                        def _(e, oc=oc):
                            k.join(e, "sync") if oc < 18 else k.wait(e, "sync", "pT", t_wb_used[oc % 2])
                            k.dma(e, "sync", f"Lwb{oc % 2}",
                                  out=wkvb[oc % 2][:].rearrange("p (c n) -> p c n", n=128),
                                  in_=cc_wqkv_out.ap().rearrange("(c p) o -> p c o", p=128)[:, :, oc * 128:(oc + 1) * 128])
                        t_wb = k.st[f"Lwb{oc % 2}"].n

                        for tc in range(4):
                            @blk.tensor
                            def _(e, oc=oc, tc=tc, t_wb=t_wb):
                                k.wait(e, "t", f"Lwb{oc % 2}", t_wb)
                                k.wait(e, "t", "Lxqa", t_xqall)
                                k.wait(e, "t", "pV", k.st["pV"].n)
                                k.wait(e, "t", "pS", k.st["pS"].n)
                                ps = [pA, pB][tc % 2]
                                for cd in range(16):
                                    k.inc(e.matmul(ps[:], wkvb[oc % 2][:, cd * 128:(cd + 1) * 128],
                                                   xqTall[:, cd * T + tc * 512:cd * T + tc * 512 + 512],
                                                   start=(cd == 0), stop=(cd == 15)), "pT")
                            tt_ = k.st["pT"].n
                            t_wb_used[oc % 2] = tt_

                            if oc < 32:
                                @blk.vector
                                def _(e, tc=tc, tt_=tt_):
                                    k.wait(e, "v", "pT", tt_)
                                    k.wait(e, "v", "Lkst2", k.st["Lkst2"].n if "Lkst2" in k.st else 0)
                                    k.wait(e, "v", "Lvst2", k.st["Lvst2"].n if "Lvst2" in k.st else 0)
                                    ps = [pA, pB][tc % 2]
                                    k.inc(e.tensor_copy(ktmp16[:], ps[:]), "pV")
                                tv_ = k.st["pV"].n
                                @blk.scalar
                                def _(e, oc=oc, tc=tc, tv_=tv_):
                                    k.wait(e, "s", "pV", tv_)
                                    k.dma(e, "s", "Lkst2",
                                          out=kT_d.ap()[(oc - 16) * 128:(oc - 15) * 128, tc * 512:(tc + 1) * 512],
                                          in_=ktmp16[:])
                            else:
                                @blk.vector
                                def _(e, tc=tc, tt_=tt_):
                                    k.wait(e, "v", "pT", tt_)
                                    k.wait(e, "v", "pV", k.st["pV"].n)
                                    k.wait(e, "v", "Lkst2", k.st["Lkst2"].n if "Lkst2" in k.st else 0)
                                    ps = [pA, pB][tc % 2]
                                    k.inc(e.tensor_copy(ktmp16[:], ps[:]), "pV")
                                tv_ = k.st["pV"].n
                                for j in range(4):
                                    @blk.tensor
                                    def _(e, j=j, tv_=tv_):
                                        k.wait(e, "t", "pV", k.st["pV"].n)
                                        k.inc(e.transpose(pT16[:, (j % 4) * 128:(j % 4) * 128 + 128],
                                                          ktmp16[:, j * 128:(j + 1) * 128], ident16[:]), "pT")
                                    tt2 = k.st["pT"].n
                                    @blk.vector
                                    def _(e, j=j, tt2=tt2):
                                        k.wait(e, "v", "pT", tt2)
                                        k.wait(e, "v", "Lvst2", k.st["Lvst2"].n if "Lvst2" in k.st else 0)
                                        k.inc(e.tensor_copy(vstage[:, j * 128:(j + 1) * 128],
                                                            pT16[:, (j % 4) * 128:(j % 4) * 128 + 128]), "pV")
                                tv2_ = k.st["pV"].n
                                @blk.scalar
                                def _(e, oc=oc, tc=tc, tv2_=tv2_):
                                    k.wait(e, "s", "pV", tv2_)
                                    k.dma(e, "s", "Lvst2",
                                          out=vT_d.ap()[(oc - 32) * 128:(oc - 31) * 128, tc * 512:(tc + 1) * 512],
                                          in_=vstage[:])

                b3ES = contextlib.ExitStack()
                with b3ES:
                    nrmsc = b3ES.enter_context(nc.sbuf_tensor("nrmsc", [128, T], F32))
                    nqk = b3ES.enter_context(nc.sbuf_tensor("nqk", [128, 256], F32))
                    nred = b3ES.enter_context(nc.sbuf_tensor("nred", [128, 16], F32))
                    ntr = b3ES.enter_context(nc.sbuf_tensor("ntr", [16, 128], F32))
                    nqm = b3ES.enter_context(nc.sbuf_tensor("nqm", [16, 1], F32))
                    nkm = b3ES.enter_context(nc.sbuf_tensor("nkm", [16, 1], F32))
                    brow = b3ES.enter_context(nc.sbuf_tensor("brow", [1, 16], F32))
                    Bneg = b3ES.enter_context(nc.sbuf_tensor("Bneg", [128, 16], F32))
                    khb2 = [b3ES.enter_context(nc.sbuf_tensor(f"khb2_{i}", [128, T], BF16)) for i in range(2)]
                    vhb2 = [b3ES.enter_context(nc.sbuf_tensor(f"vhb2_{i}", [128, T], BF16)) for i in range(2)]
                    b01 = b3ES.enter_context(nc.sbuf_tensor("b01", [128, 16 * 512], BF16))
                    pb16b = [b3ES.enter_context(nc.sbuf_tensor(f"pb16b{i}", [128, 512], BF16)) for i in range(4)]
                    rrow_b = b3ES.enter_context(nc.sbuf_tensor("rrow_b", [1, 512], F32))
                    rb_b = b3ES.enter_context(nc.sbuf_tensor("rb_b", [128, 512], F32))

                    def norm_pass(is_k, out_nm):
                        nt = 16 if is_k else 8
                        for h in range(HEADS):
                            if is_k:
                                @blk.scalar
                                def _(e, h=h):
                                    k.join(e, "s")
                                    k.wait(e, "s", "Lkst2", k.st["Lkst2"].n)
                                    k.dma(e, "s", f"Lkv2{h % 2}", out=khb2[h % 2][:],
                                          in_=kT_d.ap()[h * 128:(h + 1) * 128, :])
                                tS0 = k.st[f"Lkv2{h % 2}"].n
                            else:
                                tS0 = 0
                            @blk.scalar
                            def _(e, h=h, tS0=tS0, is_k=is_k):
                                k.wait(e, "s", "pS", k.st["pS"].n)
                                k.wait(e, "s", "pT", k.st["pT"].n)
                                if is_k:
                                    k.wait(e, "s", f"Lkv2{h % 2}", tS0)
                                    k.inc(e.activation(nrmsc[:, 0:T], khb2[h % 2][:], AF.Square), "pS")
                                else:
                                    k.inc(e.activation(nrmsc[:, 0:NTOK],
                                                       qbT[:, h * NTOK:(h + 1) * NTOK], AF.Square), "pS")
                            tS2 = k.st["pS"].n
                            @blk.tensor
                            def _(e, h=h, nt=nt, tS2=tS2):
                                k.wait(e, "t", "pS", tS2)
                                k.wait(e, "t", "pG", t_iota)
                                if h == 0:
                                    k.wait(e, "t", "pV", k.st["pV"].n)
                                for tt in range(nt):
                                    k.inc(e.matmul(pD[0:128, h * nt + tt:h * nt + tt + 1],
                                                   nrmsc[:, tt * 128:(tt + 1) * 128], ones_col32[:],
                                                   start=True, stop=True), "pT")
                        tt_ = k.st["pT"].n
                        @blk.vector
                        def _(e, nt=nt, tt_=tt_):
                            k.wait(e, "v", "pT", tt_)
                            k.wait(e, "v", "pV", k.st["pV"].n)
                            k.inc(e.tensor_copy(nqk[:, 0:16 * nt], pD[:, 0:16 * nt]), "pV")
                            k.wait(e, "v", "pV", k.st["pV"].n)
                            k.inc(e.reduce_max(nred[:], nqk[:, 0:16 * nt].rearrange(
                                "p (h t) -> p h t", t=nt), axis=AX.X), "pV")
                        tv_ = k.st["pV"].n
                        @blk.tensor
                        def _(e, tv_=tv_):
                            k.wait(e, "t", "pV", tv_)
                            k.inc(e.matmul(pT32[0:16, 0:128], nred[:], ident32[:], start=True, stop=True), "pT")
                        tt2 = k.st["pT"].n
                        @blk.vector
                        def _(e, out_nm=out_nm, tt2=tt2):
                            k.wait(e, "v", "pT", tt2)
                            k.wait(e, "v", "pV", k.st["pV"].n)
                            k.inc(e.tensor_copy(ntr[:], pT32[0:16, 0:128]), "pV")
                            k.wait(e, "v", "pV", k.st["pV"].n)
                            k.inc(e.reduce_max(out_nm[:], ntr[:], axis=AX.X), "pV")

                    norm_pass(False, nqm)
                    norm_pass(True, nkm)

                    @blk.vector
                    def _(e):
                        k.wait(e, "v", "pV", k.st["pV"].n)
                        k.inc(e.tensor_mul(nqm[:], nqm[:], nkm[:]), "pV")
                    tv_p = k.st["pV"].n

                    @blk.scalar
                    def _(e, tv_p=tv_p):
                        k.wait(e, "s", "pV", tv_p)
                        k.wait(e, "s", "pS", k.st["pS"].n)
                        k.inc(e.activation(nkm[:], nqm[:], AF.Sqrt, scale=1.0 / 128.0), "pS")
                    tS_B = k.st["pS"].n

                    @blk.tensor
                    def _(e, tS_B=tS_B):
                        k.wait(e, "t", "pS", tS_B)
                        k.inc(e.matmul(pT32[0:1, 0:16], nkm[:], ident32[0:16, 0:16], start=True, stop=True), "pT")
                    tt_B = k.st["pT"].n

                    @blk.vector
                    def _(e, tt_B=tt_B):
                        k.wait(e, "v", "pT", tt_B)
                        k.wait(e, "v", "pV", k.st["pV"].n)
                        k.inc(e.tensor_scalar_mul(brow[:], pT32[0:1, 0:16], -1.0), "pV")
                    tv_br = k.st["pV"].n

                    @blk.tensor
                    def _(e, tv_br=tv_br):
                        k.wait(e, "t", "pV", tv_br)
                        k.inc(e.matmul(pE[:, 0:16], ones_row[:], brow[:], start=True, stop=True), "pT")
                    tt_Bn = k.st["pT"].n

                    @blk.vector
                    def _(e, tt_Bn=tt_Bn):
                        k.wait(e, "v", "pT", tt_Bn)
                        k.wait(e, "v", "pV", k.st["pV"].n)
                        k.inc(e.tensor_copy(Bneg[:], pE[:, 0:16]), "pV")
                        k.wait(e, "v", "pG", t_iota)
                        k.wait(e, "v", "pV", k.st["pV"].n)
                        for ci in range(2):
                            for mi in range(8):
                                col = (0 + mi) if ci == 0 else (16 + mi)
                                s_ = ci * 8 + mi
                                k.inc(e.tensor_scalar(b01[:, s_ * 512:(s_ + 1) * 512], iotaJI[:],
                                                      offb_s[:, col:col + 1], None, ALU.is_ge), "pV")
                    tv_b01 = k.st["pV"].n

                    NB4 = 4
                    banks4 = [pA, pB, pF, pT32]
                    last_expb = {0: 0, 1: 0, 2: 0, 3: 0}
                    last_maskb = {0: 0, 1: 0, 2: 0, 3: 0}
                    for ci in range(2):
                        KT = QCH_KT[ci]
                        for h in range(HEADS):
                            masked_l = [(ci == 0) or (kt >= 8) for kt in range(KT)]
                            s_l = [(kt if ci == 0 else 8 + (kt - 8)) if masked_l[kt] else 0
                                   for kt in range(KT)]
                            pT0 = k.st["pT"].n
                            pV0 = k.st["pV"].n
                            pS0 = k.st["pS"].n
                            tt_mm = [pT0 + 3 * kt + 1 for kt in range(KT)]
                            tt_pv = pT0 + 3 * KT
                            tt_bc = tt_pv + 1
                            t_exp_l = [pS0 + kt + 1 for kt in range(KT)]
                            tv_mask = []
                            mv = 0
                            for kt in range(KT):
                                if masked_l[kt]:
                                    mv += 1
                                tv_mask.append(pV0 + mv)
                            tv_r = pV0 + mv + 1
                            tv_y = pV0 + mv + 3

                            @blk.scalar
                            def _(e, ci=ci, h=h, KT=KT, tt_mm=tt_mm, tv_mask=tv_mask, pV0=pV0):
                                k.join(e, "s")
                                k.wait(e, "s", "Lvst2", k.st["Lvst2"].n)
                                k.dma(e, "s", f"Lkv2{h % 2}", first=True, out=khb2[h % 2][:],
                                      in_=kT_d.ap()[h * 128:(h + 1) * 128, :])
                                k.dma(e, "s", f"Lkv2{h % 2}", first=False, out=vhb2[h % 2][:],
                                      in_=vT_d.ap()[h * 128:(h + 1) * 128, :])
                                for kt in range(KT):
                                    bank = kt % NB4
                                    psS = banks4[bank]
                                    k.wait(e, "s", "pS", k.st["pS"].n)
                                    k.wait(e, "s", "pT", tt_mm[kt])
                                    k.wait(e, "s", "pV", tv_mask[kt - NB4] if kt >= NB4 else pV0)
                                    k.inc(e.activation(pb16b[bank][:], psS[:], AF.Exp,
                                                       scale=float(ISQD), bias=Bneg[:, h:h + 1]), "pS")
                            assert k.st["pS"].n == t_exp_l[KT - 1], (k.st["pS"].n,)
                            t_kv = k.st[f"Lkv2{h % 2}"].n

                            @blk.tensor
                            def _(e, ci=ci, h=h, KT=KT, t_kv=t_kv, pV0=pV0,
                                  t_exp_l=t_exp_l, tv_mask=tv_mask, tv_r=tv_r,
                                  les=dict(last_expb), lms=dict(last_maskb)):
                                k.wait(e, "t", f"Lkv2{h % 2}", t_kv)
                                for kt in range(KT):
                                    bank = kt % NB4
                                    psS = banks4[bank]
                                    k.wait(e, "t", "pS",
                                           t_exp_l[kt - NB4] if kt >= NB4 else les[bank])
                                    if kt < NB4:
                                        k.wait(e, "t", "pV", lms[bank])
                                    if kt == 1:
                                        k.wait(e, "t", "pV", pV0)
                                    k.inc(e.matmul(psS[:], khb2[h % 2][:, kt * 128:(kt + 1) * 128],
                                                   qbT[:, h * NTOK + ci * 512:h * NTOK + ci * 512 + 512],
                                                   start=True, stop=True), "pT")
                                    if kt >= 1:
                                        k.wait(e, "t", "pS", t_exp_l[kt - 1])
                                        k.wait(e, "t", "pV", tv_mask[kt - 1])
                                        pbprev = pb16b[(kt - 1) % NB4]
                                        k.inc(e.matmul(pD[0:1, 0:512], ones_col16[:], pbprev[:],
                                                       start=(kt - 1 == 0), stop=False), "pT")
                                        k.inc(e.matmul(pC[:], vhb2[h % 2][:, (kt - 1) * 128:kt * 128],
                                                       pbprev[:], start=(kt - 1 == 0), stop=False), "pT")
                                k.wait(e, "t", "pS", t_exp_l[KT - 1])
                                k.wait(e, "t", "pV", tv_mask[KT - 1])
                                pbprev = pb16b[(KT - 1) % NB4]
                                k.inc(e.matmul(pD[0:1, 0:512], ones_col16[:], pbprev[:],
                                               start=(KT - 1 == 0), stop=True), "pT")
                                k.inc(e.matmul(pC[:], vhb2[h % 2][:, (KT - 1) * 128:KT * 128],
                                               pbprev[:], start=(KT - 1 == 0), stop=True), "pT")
                                k.wait(e, "t", "pV", tv_r)
                                k.inc(e.matmul(pE[:], ones_row[:], rrow_b[:], start=True, stop=True), "pT")
                            assert k.st["pT"].n == tt_bc, (k.st["pT"].n, tt_bc)

                            @blk.vector
                            def _(e, ci=ci, h=h, KT=KT, masked_l=masked_l, s_l=s_l,
                                  t_exp_l=t_exp_l, tt_pv=tt_pv, tt_bc=tt_bc):
                                for kt in range(KT):
                                    if not masked_l[kt]:
                                        continue
                                    bank = kt % NB4
                                    k.wait(e, "v", "pS", t_exp_l[kt])
                                    k.wait(e, "v", "pV", tv_b01)
                                    k.inc(e.tensor_mul(pb16b[bank][:], pb16b[bank][:],
                                                       b01[:, s_l[kt] * 512:(s_l[kt] + 1) * 512]), "pV")
                                k.wait(e, "v", "pT", tt_pv)
                                k.inc(e.reciprocal(rrow_b[:], pD[0:1, 0:512]), "pV")
                                k.wait(e, "v", "pT", tt_bc)
                                k.inc(e.tensor_copy(rb_b[:], pE[:]), "pV")
                                k.wait(e, "v", "pV", k.st["pV"].n)
                                k.inc(e.tensor_mul(yF[:, h * NTOK + ci * 512:h * NTOK + ci * 512 + 512],
                                                   pC[:], rb_b[:]), "pV")
                            assert k.st["pV"].n == tv_y, (k.st["pV"].n, tv_y)

                            for kt in range(max(0, KT - NB4), KT):
                                last_expb[kt % NB4] = t_exp_l[kt]
                            for kt in range(KT):
                                if masked_l[kt]:
                                    last_maskb[kt % NB4] = tv_mask[kt]

                    ytok = b3ES.enter_context(nc.sbuf_tensor("ytok", [128, D], BF16))
                    ystage = b3ES.enter_context(nc.sbuf_tensor("ystage", [128, D], BF16))
                    for t in range(NTT):
                        @blk.vector
                        def _(e):
                            k.wait(e, "v", "pV", k.st["pV"].n)
                        transposeN(yF, [h * NTOK + t * 128 for h in range(16)], ytok,
                                   [h * 128 for h in range(16)], BF16, 16)
                        @blk.vector
                        def _(e, t=t):
                            k.wait(e, "v", "pV", k.st["pV"].n)
                            k.inc(e.reduce_max(am_b[:], ytok[:], axis=AX.X, apply_absolute_value=True), "pV")
                            k.wait(e, "v", "pV", k.st["pV"].n)
                            k.inc(e.tensor_scalar_max(am_b[:], am_b[:], EPS_Q), "pV")
                            k.wait(e, "v", "pV", k.st["pV"].n)
                            k.inc(e.reciprocal(s127b[:], am_b[:]), "pV")
                            k.wait(e, "v", "pV", k.st["pV"].n)
                            k.inc(e.tensor_scalar_mul(s127b[:], s127b[:], 127.0), "pV")
                            k.inc(e.tensor_scalar_mul(rsy_t[:, t:t + 1], am_b[:], 1.0 / 127.0), "pV")
                            k.wait(e, "v", "pV", k.st["pV"].n)
                            k.inc(e.tensor_scalar_mul(yqb[:], ytok[:], s127b[:]), "pV")
                            k.wait(e, "v", "pV", k.st["pV"].n)
                            k.inc(e.tensor_scalar_add(yqb[:], yqb[:], RC), "pV")
                            k.wait(e, "v", "pV", k.st["pV"].n)
                            k.inc(e.tensor_copy(yqib[:], yqb[:]), "pV")
                            k.wait(e, "v", "pV", k.st["pV"].n)
                            k.wait(e, "v", "Lyst", k.st["Lyst"].n if "Lyst" in k.st else 0)
                            k.inc(e.tensor_scalar_add(xq16b[:], yqib[:], -RC), "pV")
                        @blk.vector
                        def _(e):
                            k.wait(e, "v", "Lyst", k.st["Lyst"].n if "Lyst" in k.st else 0)
                        transposeN(xq16b, [c * 128 for c in range(16)], ystage,
                                   [c * 128 for c in range(16)], BF16, 16)
                        tv_y = k.st["pV"].n
                        @blk.scalar
                        def _(e, t=t, tv_y=tv_y):
                            k.wait(e, "s", "pV", tv_y)
                            k.dma(e, "s", "Lyst",
                                  out=yqT_d.ap().rearrange("(c p) tok -> p c tok", p=128)[:, :, t * 128:(t + 1) * 128],
                                  in_=ystage[:].rearrange("p (c j) -> p c j", j=128))

                b4ES = contextlib.ExitStack()
                with b4ES:
                    wpb = [b4ES.enter_context(nc.sbuf_tensor(f"wpb{i}", [128, 16 * 512], BF16)) for i in range(2)]
                    yql = [b4ES.enter_context(nc.sbuf_tensor(f"yql{i}", [128, 16 * 128], BF16)) for i in range(2)]
                    pslab = b4ES.enter_context(nc.sbuf_tensor("pslab", [128, 512], F32))
                    t_wp_used = [0, 0]
                    t_yql_used = [0, 0]
                    for dm in range(4):
                        @blk.sync
                        def _(e, dm=dm):
                            k.join(e, "sync") if dm < 2 else k.wait(e, "sync", "pT", t_wp_used[dm % 2])
                            k.wait(e, "sync", "cc", 6)
                            k.dma(e, "sync", f"Lwp{dm % 2}",
                                  out=wpb[dm % 2][:].rearrange("p (c n) -> p c n", n=512),
                                  in_=cc_wp_out.ap().rearrange("(c p) o -> p c o", p=128)[:, :, dm * 512:(dm + 1) * 512])
                        t_wp = k.st[f"Lwp{dm % 2}"].n

                        for t in range(NTT):
                            @blk.scalar
                            def _(e, t=t):
                                k.join(e, "s") if t < 2 else k.wait(e, "s", "pT", t_yql_used[t % 2])
                                k.wait(e, "s", "Lyst", k.st["Lyst"].n)
                                k.dma(e, "s", f"Lyql{t % 2}",
                                      out=yql[t % 2][:].rearrange("p (c j) -> p c j", j=128),
                                      in_=yqT_d.ap().rearrange("(c p) tok -> p c tok", p=128)[:, :, t * 128:(t + 1) * 128])
                            t_yq = k.st[f"Lyql{t % 2}"].n

                            @blk.tensor
                            def _(e, dm=dm, t=t, t_wp=t_wp, t_yq=t_yq):
                                k.wait(e, "t", f"Lwp{dm % 2}", t_wp)
                                k.wait(e, "t", f"Lyql{t % 2}", t_yq)
                                k.wait(e, "t", "pS", k.st["pS"].n)
                                ps = [pA, pB][t % 2]
                                for cd in range(16):
                                    k.inc(e.matmul(ps[:], yql[t % 2][:, cd * 128:(cd + 1) * 128],
                                                   wpb[dm % 2][:, cd * 512:(cd + 1) * 512],
                                                   start=(cd == 0), stop=(cd == 15)), "pT")
                            tt_ = k.st["pT"].n
                            t_wp_used[dm % 2] = tt_
                            t_yql_used[t % 2] = tt_

                            @blk.scalar
                            def _(e, dm=dm, t=t, tt_=tt_):
                                k.wait(e, "s", "pS", k.st["pS"].n)
                                k.wait(e, "s", "Lyp", k.st["Lyp"].n)
                                k.wait(e, "s", "pT", tt_)
                                ps = [pA, pB][t % 2]
                                k.inc(e.activation(pslab[:], ps[:], AF.Identity,
                                                   scale=rsy_t[:, t:t + 1]), "pS")
                            tS_y = k.st["pS"].n

                            @blk.sync
                            def _(e, dm=dm, t=t, tS_y=tS_y):
                                k.wait(e, "sync", "pS", tS_y)
                                k.dma(e, "sync", "Lyp", out=yproj.ap()[t * 128:(t + 1) * 128, dm * 512:(dm + 1) * 512],
                                      in_=pslab[:])

            residual_pass(src_d, yproj, dst_d)

        if upto >= 3:
            attnb_phase(xres[1], xres[0])
        if upto >= 4:
            ffn_phase(xres[0], xres[1], "f2")

        cur = {0: x_own, 1: xres[0], 2: xres[1], 3: xres[0]}.get(min(upto, 4), xres[1])
        @blk.sync
        def _(e):
            k.wait(e, "sync", "pV", k.st["pV"].n)
            k.wait(e, "sync", "pS", k.st["pS"].n)
            k.wait(e, "sync", "Lresw", k.st["Lresw"].n if "Lresw" in k.st else 0)
            k.dma(e, "sync", "Lout", out=out_t.ap(), in_=cur.ap())
            k.wait(e, "sync", "Lout", k.st["Lout"].n)

    return nc


_PROG_CACHE = {}


def _get_program(upto=99):
    if upto not in _PROG_CACHE:
        _PROG_CACHE[upto] = build_program(upto)
    return _PROG_CACHE[upto]


def _in_map_for_core(core, inp, shared, upto=99):
    x = inp["L0_attn_a_x"]
    qs = _pos_quarters(core)
    b = core // 2
    xsh = np.concatenate([x[b, q * 512:(q + 1) * 512, :] for q in qs], axis=0)
    t = _host_tables(core)
    wdp = shared["w_down_padded"]
    m = {
        "x_own": np.ascontiguousarray(xsh, np.float32),
        "w_q": inp["L0_attn_a_w_q"],
        "b_q": inp["L0_attn_a_b_q"].reshape(1, -1),
        "w_down_a": inp["L0_attn_a_w_down"],
        "b_down_r": np.ascontiguousarray(inp["L0_attn_a_b_down"].reshape(4, 128).T),
        "w_up_a": inp["L0_attn_a_w_up"],
        "b_up_r": np.ascontiguousarray(inp["L0_attn_a_b_up"].reshape(8, 128).T),
        "w_o": inp["L0_attn_a_w_o"],
        "b_o": inp["L0_attn_a_b_o"].reshape(1, -1),
        "qn_w": inp["L0_attn_a_qn_w"].reshape(1, -1),
        "kn_w": inp["L0_attn_a_kn_w"].reshape(1, -1),
        "gate_w": inp["L0_attn_a_gate_w"],
        "gate_b_r": np.ascontiguousarray(inp["L0_attn_a_gate_b"].T),
        "w_qkv": inp["L0_attn_b_w_qkv"],
        "w_proj": inp["L0_attn_b_w_proj"],
        "w_gate_f": inp["L0_ffn_w_gate"],
        "w_up_f": inp["L0_ffn_w_up"],
        "w_down_f": wdp,
        "rope_c": t["rope_c"], "rope_s": t["rope_s"],
        "rope_own_c": t["rope_own_c"], "rope_own_s": t["rope_own_s"],
        "offtab_sel": t["offtab_sel"], "offtab_b": t["offtab_b"],
        "aa_bias": t["aa_bias"],
        "ident32": shared["ident32"], "ident16": shared["ident16"],
        "ones_col16": shared["ones_col16"], "ones_row": shared["ones_row"],
        "half_row": shared["half_row"],
        "mdiv": np.array([[1.0 / (6144 * 2048), 1.0 / (2048 * 2048)]], np.float32),
        "w_gate_sl": np.ascontiguousarray(inp["L0_ffn_w_gate"][:, core * 256:(core + 1) * 256]),
        "w_up_sl": np.ascontiguousarray(inp["L0_ffn_w_up"][:, core * 256:(core + 1) * 256]),
        "w_down_sl": np.ascontiguousarray(wdp[:, core * 768:(core + 1) * 768]),
        "w_qkv_sl": np.ascontiguousarray(inp["L0_attn_b_w_qkv"][:, core * 256:(core + 1) * 256]),
        "w_proj_sl": np.ascontiguousarray(inp["L0_attn_b_w_proj"][:, core * 256:(core + 1) * 256]),
    }
    for kk in list(m):
        a = np.asarray(m[kk])
        if a.dtype == np.float64:
            m[kk] = a.astype(np.float32)
    return m


def kernel(**inputs):
    upto = int(inputs.pop("_upto", 99))
    shared = _shared_consts()
    shared["w_down_padded"] = np.ascontiguousarray(
        np.pad(np.asarray(inputs["L0_ffn_w_down"], np.float32), ((0, 0), (0, DFFP - DFF))))
    nc = _get_program(upto)
    in_maps = [_in_map_for_core(c, inputs, shared, upto) for c in range(8)]
    res = run_bass_kernel_spmd(nc, in_maps, core_ids=list(range(8)))
    out = np.zeros((B, T, D), np.float32)
    for c in range(8):
        o = res.results[c]["out"]
        b = c // 2
        for qi, q in enumerate(_pos_quarters(c)):
            out[b, q * 512:(q + 1) * 512, :] = o[qi * 512:(qi + 1) * 512, :]
    return out


# version 09
# speedup vs baseline: 1.0377x; 1.0193x over previous
import numpy as np
import ml_dtypes
import concourse.bass as bass
import concourse.mybir as mybir
from concourse.bass_utils import run_bass_kernel_spmd

F32 = mybir.dt.float32
BF16 = mybir.dt.bfloat16
I32 = mybir.dt.int32
AF = mybir.ActivationFunctionType
ALU = mybir.AluOpType
AX = mybir.AxisListType
BF16NP = ml_dtypes.bfloat16

B, T, D = 4, 2048, 2048
HEADS, HD = 16, 128
NKV = 4
LAT = 512
DFF = 5504
DFFP = 6144
NTOK = 1024
NTT = 8
RC = 12582912.0
NEGBIG = -3.0e38
NEGM = -3.3e28
EPS_LN = 1e-5
EPS_RMS = 1e-6
EPS_Q = 1e-5
SOFT = 30.0
ISQD = 1.0 / np.sqrt(128.0)

SLOPES = None
def _alibi_slopes(n):
    import math
    def pot(p):
        start = 2.0 ** (-(2.0 ** (-(math.log2(p) - 3))))
        return [start * start ** i for i in range(p)]
    if math.log2(n).is_integer():
        return pot(n)
    cp = 2 ** int(math.floor(math.log2(n)))
    s = pot(cp)
    s += _alibi_slopes(2 * cp)[0::2][: n - cp]
    return s
SLOPES = _alibi_slopes(HEADS)

QCH_KT = [8, 16]
SEL_NCH = [2, 2, 2, 2, 4, 4, 4, 4]


class Stream:
    def __init__(self, nc, name, step=1):
        self._cm = nc.semaphore(name)
        self.sem = self._cm.__enter__()
        self.n = 0
        self.step = step
        self.name = name

    def inc(self, inst, k=None):
        k = self.step if k is None else k
        inst.then_inc(self.sem, k)
        self.n += k
        return self.n


class Ctx:

    def __init__(self, nc, blk):
        self.nc = nc
        self.blk = blk
        self.st = {}
        self.wm = {}

    def stream(self, name, step=1):
        s = Stream(self.nc, name, step)
        self.st[name] = s
        return s

    def wait(self, eng, engname, stname, tok):
        if tok <= 0:
            return
        wm = self.wm.setdefault(engname, {})
        if wm.get(stname, 0) >= tok:
            return
        eng.wait_ge(self.st[stname].sem, tok)
        wm[stname] = tok

    def inc(self, inst, stname, k=None):
        return self.st[stname].inc(inst, k)

    def join(self, e, engname):
        for stn in ("pT", "pV", "pS", "pG"):
            if stn in self.st:
                self.wait(e, engname, stn, self.st[stn].n)

    def dma(self, e, engname, lane, first=True, **kw):
        if lane not in self.st:
            self.st[lane] = Stream(self.nc, lane, 16)
        if first:
            self.wait(e, engname, lane, self.st[lane].n)
        return self.st[lane].inc(e.dma_start(**kw))


def _pos_quarters(core):
    return [0, 3] if core % 2 == 0 else [1, 2]


def _own_positions(core):
    qs = _pos_quarters(core)
    return np.concatenate([np.arange(q * 512, (q + 1) * 512) for q in qs])


def _rope_tables():
    inv = (1.0 / (10000.0 ** (np.arange(0, HD, 2, dtype=np.float32) / HD))).astype(np.float32)
    f = (np.arange(T, dtype=np.float32)[:, None] * inv[None, :]).astype(np.float32)
    idx = (2 * np.arange(64)) % 64
    emb2 = f[:, idx]
    return np.cos(emb2).astype(np.float32), np.sin(emb2).astype(np.float32)


def _host_tables(core):
    qs = _pos_quarters(core)
    ctab, stab = _rope_tables()
    own = _own_positions(core)
    t = {}
    t["rope_c"] = ctab
    t["rope_s"] = stab
    t["rope_own_c"] = ctab[own]
    t["rope_own_s"] = stab[own]
    offsel = np.full((24,), 0.0, np.float32)
    k = 0
    offsel_list = []
    for tt in range(NTT):
        q0 = qs[tt // 4] * 512 + (tt % 4) * 128
        for ch in range(SEL_NCH[tt]):
            offsel_list.append(float(q0 - ch * 512))
    offsel = np.array(offsel_list, np.float32)
    t["offtab_sel"] = np.tile(offsel[None, :], (128, 1)).astype(np.float32)
    offb = []
    for ci in range(2):
        q0 = qs[ci] * 512
        for kt in range(QCH_KT[ci]):
            k0 = kt * 128
            if k0 + 127 < q0:
                offb.append(-1e9)
            elif k0 > q0 + 511:
                offb.append(1e9)
            else:
                offb.append(float(k0 - q0))
    t["offtab_b"] = np.tile(np.array(offb, np.float32)[None, :], (128, 1))
    ab = np.zeros((HEADS, 24), np.float32)
    col = 0
    cols = []
    for ci in range(2):
        q0 = qs[ci] * 512
        for kt in range(QCH_KT[ci]):
            k0 = kt * 128
            cols.append((ci, kt, k0 - q0))
    for h in range(HEADS):
        for j, (ci, kt, dk) in enumerate(cols):
            ab[h, j] = SLOPES[h] * dk
    t["aa_bias"] = np.tile(ab.reshape(1, -1), (128, 1)).astype(np.float32)
    return t


def _shared_consts():
    c = {}
    c["ident32"] = np.eye(128, dtype=np.float32)
    c["ident16"] = np.eye(128, dtype=BF16NP)
    c["ones_col16"] = np.ones((128, 1), BF16NP)
    c["ones_row"] = np.ones((1, 128), np.float32)
    c["half_row"] = np.full((1, 128), 0.5, np.float32)
    return c


PREP_SKIP = set()

def build_program(upto=99):
    import contextlib
    nc = bass.Bass("TRN2", target_bir_lowering=False, num_devices=8)

    dt_in = {}
    def din(name, shape, dtype=F32):
        dt_in[name] = nc.dram_tensor(name, list(shape), dtype, kind="ExternalInput")
        return dt_in[name]

    x_own = din("x_own", [NTOK, D])
    w_q = din("w_q", [D, D])
    b_q = din("b_q", [1, D])
    w_down_a = din("w_down_a", [D, LAT])
    b_down_r = din("b_down_r", [128, 4])
    w_up_a = din("w_up_a", [LAT, 2 * NKV * HD])
    b_up_r = din("b_up_r", [128, 8])
    w_o = din("w_o", [D, D])
    b_o = din("b_o", [1, D])
    qn_w = din("qn_w", [1, HD])
    kn_w = din("kn_w", [1, HD])
    gate_w = din("gate_w", [HEADS, HD, HD])
    gate_b_r = din("gate_b_r", [128, HEADS])
    w_qkv = din("w_qkv", [3 * D, D])
    w_proj = din("w_proj", [D, D])
    w_gate_f = din("w_gate_f", [DFF, D])
    w_up_f = din("w_up_f", [DFF, D])
    w_down_f = din("w_down_f", [D, DFFP])
    rope_c = din("rope_c", [T, 64])
    rope_s = din("rope_s", [T, 64])
    rope_own_c = din("rope_own_c", [NTOK, 64])
    rope_own_s = din("rope_own_s", [NTOK, 64])
    offtab_sel = din("offtab_sel", [128, 24])
    offtab_b = din("offtab_b", [128, 24])
    aa_bias = din("aa_bias", [128, 384])
    ident32_d = din("ident32", [128, 128])
    ident16_d = din("ident16", [128, 128], BF16)
    ones_col16_d = din("ones_col16", [128, 1], BF16)
    ones_row_d = din("ones_row", [1, 128])
    half_row_d = din("half_row", [1, 128])
    mdiv_d = din("mdiv", [1, 2])

    out_t = nc.dram_tensor("out", [NTOK, D], F32, kind="ExternalOutput")

    def dint(name, shape, dtype=F32):
        return nc.dram_tensor(name, list(shape), dtype, kind="Internal")

    xres = [dint("xres0", [NTOK, D]), dint("xres1", [NTOK, D])]
    cc_in_x = dint("cc_in_x", [D, NTOK])
    cc_out_x = dint("cc_out_x", [2 * D, NTOK])
    cc_in_x2 = dint("cc_in_x2", [D, NTOK], BF16)
    cc_out_x2 = dint("cc_out_x2", [2 * D, NTOK], BF16)
    cc_ps_in = dint("cc_ps_in", [1, 512])
    cc_ps_out = dint("cc_ps_out", [1, 512])
    cc_wg_in = dint("cc_wg_in", [256, DFF], BF16)
    cc_wg_out = dint("cc_wg_out", [D, DFF], BF16)
    cc_wu_in = dint("cc_wu_in", [256, DFF], BF16)
    cc_wu_out = dint("cc_wu_out", [D, DFF], BF16)
    cc_wd_in = dint("cc_wd_in", [768, D], BF16)
    cc_wd_out = dint("cc_wd_out", [DFFP, D], BF16)
    cc_wqkv_in = dint("cc_wqkv_in", [256, 3 * D], BF16)
    cc_wqkv_out = dint("cc_wqkv_out", [D, 3 * D], BF16)
    cc_wp_in = dint("cc_wp_in", [256, D], BF16)
    cc_wp_out = dint("cc_wp_out", [D, D], BF16)

    ES = contextlib.ExitStack()
    with ES:
        blk = ES.enter_context(nc.Block())
        k = Ctx(nc, blk)
        for snm in ("dS", "dSc", "dG"):
            s = Stream(nc, snm, 16)
            ES.enter_context(contextlib.nullcontext())
            k.st[snm] = s
        for snm in ("cc",):
            k.st[snm] = Stream(nc, snm, 1)
        for snm in ("pV", "pS", "pT", "pG"):
            k.st[snm] = Stream(nc, snm, 1)

        def sb(name, shape, dtype=F32):
            return ES.enter_context(nc.sbuf_tensor(name, list(shape), dtype))

        ident32 = sb("ident32_s", [128, 128])
        ident16 = sb("ident16_s", [128, 128], BF16)
        ones_col16 = sb("ones_col16_s", [128, 1], BF16)
        ones_col32 = sb("ones_col32_s", [128, 1])
        ones_row = sb("ones_row_s", [1, 128])
        half_row = sb("half_row_s", [1, 128])
        iotaJI = sb("iotaJI", [128, 512], I32)
        bq_s = sb("bq_s", [1, D])
        bo_s = sb("bo_s", [1, D])
        bdr_s = sb("bdr_s", [128, 4])
        bur_s = sb("bur_s", [128, 8])
        gbh_s = sb("gbh_s", [128, HEADS])
        offsel_s = sb("offsel_s", [128, 24])
        offb_s = sb("offb_s", [128, 24])
        aab_s = sb("aab_s", [128, 384])
        mdiv_s = sb("mdiv_s", [1, 2])
        qnw_t = sb("qnw_t", [128, 128])
        knw_t = sb("knw_t", [128, 128])

        pA = ES.enter_context(nc.psum_tensor("pA", [128, 512], F32))
        pB = ES.enter_context(nc.psum_tensor("pB", [128, 512], F32))
        pC = ES.enter_context(nc.psum_tensor("pC", [128, 512], F32))
        pD = ES.enter_context(nc.psum_tensor("pD", [128, 512], F32))
        pE = ES.enter_context(nc.psum_tensor("pE", [128, 512], F32))
        pF = ES.enter_context(nc.psum_tensor("pF", [128, 512], F32))
        pT16 = ES.enter_context(nc.psum_tensor("pT16", [128, 1024], BF16))
        pT32 = ES.enter_context(nc.psum_tensor("pT32", [128, 512], F32))

        @blk.sync
        def _(e):
            for i_, (src, dst) in enumerate((
                (ident32_d, ident32), (ident16_d, ident16),
                (ones_col16_d, ones_col16), (ones_row_d, ones_row),
                (half_row_d, half_row),
                (b_q, bq_s), (b_o, bo_s), (b_down_r, bdr_s), (b_up_r, bur_s),
                (gate_b_r, gbh_s), (offtab_sel, offsel_s), (offtab_b, offb_s),
                (aa_bias, aab_s), (mdiv_d, mdiv_s),
            )):
                k.dma(e, "sync", "Lconst", first=(i_ == 0), out=dst[:], in_=src.ap())
        t_constload = k.st["Lconst"].n

        @blk.gpsimd
        def _(e):
            k.inc(e.iota(iotaJI[:], pattern=[[1, 512]], base=0, channel_multiplier=-1), "pG")
            k.inc(e.memset(ones_col32[:], 1.0), "pG")
        t_iota = k.st["pG"].n

        @blk.vector
        def _(e):
            k.wait(e, "v", "Lconst", t_constload)
            k.inc(e.tensor_scalar_mul(gbh_s[:], gbh_s[:], 0.5), "pV")
        t_gbh = k.st["pV"].n

        @blk.sync
        def _(e):
            k.dma(e, "sync", "Lqnw", first=True, out=qnw_t[0:1, :], in_=qn_w.ap())
            k.dma(e, "sync", "Lqnw", first=False, out=knw_t[0:1, :], in_=kn_w.ap())
        t_qnwload = k.st["Lqnw"].n

        @blk.tensor
        def _(e):
            k.wait(e, "t", "Lqnw", t_qnwload)
            k.wait(e, "t", "Lconst", t_constload)
            k.inc(e.matmul(pT32[:, 0:128], ones_row[:], qnw_t[0:1, :], start=True, stop=True), "pT")
            k.inc(e.matmul(pT32[:, 128:256], ones_row[:], knw_t[0:1, :], start=True, stop=True), "pT")
        t_qnwmm = k.st["pT"].n

        @blk.vector
        def _(e):
            k.wait(e, "v", "pT", t_qnwmm)
            k.inc(e.tensor_copy(qnw_t[:], pT32[:, 0:128]), "pV")
            k.inc(e.tensor_copy(knw_t[:], pT32[:, 128:256]), "pV")
        t_qnwt = k.st["pV"].n

        w_gate_sl = din("w_gate_sl", [DFF, 256])
        w_up_sl = din("w_up_sl", [DFF, 256])
        w_down_sl = din("w_down_sl", [D, 768])
        w_qkv_sl = din("w_qkv_sl", [3 * D, 256])
        w_proj_sl = din("w_proj_sl", [D, 256])

        prepES = contextlib.ExitStack()
        with prepES:
            wslab = prepES.enter_context(nc.sbuf_tensor("wslab", [128, 768], F32))
            wslabB = prepES.enter_context(nc.sbuf_tensor("wslabB", [128, 768], F32))
            wslab2 = [wslab, wslabB]
            wabs = prepES.enter_context(nc.sbuf_tensor("wabs", [128, 768], F32))
            sclb = prepES.enter_context(nc.sbuf_tensor("sclb", [128, 6], F32))
            rsclb = prepES.enter_context(nc.sbuf_tensor("rsclb", [128, 6], F32))
            tq1 = prepES.enter_context(nc.sbuf_tensor("tq1", [128, 768], F32))
            tqi = prepES.enter_context(nc.sbuf_tensor("tqi", [128, 768], I32))
            wqv = prepES.enter_context(nc.sbuf_tensor("wqv", [128, 768], F32))
            stage = prepES.enter_context(nc.sbuf_tensor("stage", [128, 768], BF16))
            acc_ps = prepES.enter_context(nc.sbuf_tensor("acc_ps", [128, 2], F32))
            red1 = prepES.enter_context(nc.sbuf_tensor("red1", [128, 1], F32))
            pstot = prepES.enter_context(nc.sbuf_tensor("pstot", [1, 2], F32))
            cmb = prepES.enter_context(nc.sbuf_tensor("cmb", [128, 2], F32))
            sbq = prepES.enter_context(nc.sbuf_tensor("sbq", [128, 2], F32))
            zrow = prepES.enter_context(nc.sbuf_tensor("zrow", [1, 512], F32))

            @blk.vector
            def _(e):
                k.inc(e.memset(acc_ps[:], 0.0), "pV")
                k.inc(e.memset(zrow[:], 0.0), "pV")
            tv0 = k.st["pV"].n

            for mi, (mat, ntiles) in enumerate(((w_qkv_sl, 48), (w_proj_sl, 16))):
                ot = 0
                while ot < ntiles:
                    nb = min(3, ntiles - ot)
                    w = nb * 256
                    @blk.scalar
                    def _(e, mat=mat, ot=ot, nb=nb, w=w):
                        k.wait(e, "s", "pV", k.st["pV"].n)
                        td = k.dma(e, "s", "Lslab",
                                   out=wslab[:, 0:w].rearrange("p (a c) -> p a c", c=256),
                                   in_=mat.ap()[ot * 128:(ot + nb) * 128, :].rearrange("(a p) c -> p a c", p=128))
                        k.wait(e, "s", "Lslab", td)
                        k.inc(e.activation(wabs[:, 0:w], wslab[:, 0:w], AF.Abs), "pS")
                    ts_ = k.st["pS"].n

                    @blk.vector
                    def _(e, mi=mi, ts_=ts_, w=w):
                        k.wait(e, "v", "pS", ts_)
                        k.wait(e, "v", "pV", k.st["pV"].n)
                        k.inc(e.reduce_sum(red1[:], wabs[:, 0:w], axis=AX.X), "pV")
                        k.wait(e, "v", "pV", k.st["pV"].n)
                        k.inc(e.tensor_add(acc_ps[:, mi:mi + 1], acc_ps[:, mi:mi + 1], red1[:]), "pV")
                    ot += nb
            tv_acc = k.st["pV"].n

            @blk.tensor
            def _(e):
                k.wait(e, "t", "pV", tv_acc)
                k.wait(e, "t", "pV", t_qnwt)
                k.wait(e, "t", "pG", t_iota)
                k.inc(e.matmul(pT32[0:1, 0:1], acc_ps[:, 0:1], ones_col32[:], start=True, stop=True), "pT")
                k.inc(e.matmul(pT32[0:1, 1:2], acc_ps[:, 1:2], ones_col32[:], start=True, stop=True), "pT")
            tps = k.st["pT"].n

            @blk.vector
            def _(e):
                k.wait(e, "v", "pT", tps)
                k.inc(e.tensor_copy(pstot[0:1, 0:2], pT32[0:1, 0:2]), "pV")
            tv_pstot = k.st["pV"].n

            @blk.scalar
            def _(e):
                k.wait(e, "s", "pV", tv_pstot)
                k.dma(e, "s", "Lccps", out=cc_ps_in.ap(), in_=zrow[:])
                k.dma(e, "s", "Lccps", out=cc_ps_in.ap()[0:1, 0:2], in_=pstot[:])
            t_psin = k.st["Lccps"].n

            @blk.gpsimd
            def _(e):
                k.wait(e, "g", "Lccps", t_psin)
                if "cr" in PREP_SKIP:
                    k.inc(e.engine_nop(), "cc")
                else:
                    k.inc(e.collective_compute(
                        "AllReduce", ALU.add,
                        replica_groups=[[0, 1, 2, 3, 4, 5, 6, 7]],
                        ins=[cc_ps_in.ap().opt()], outs=[cc_ps_out.ap().opt()]), "cc")
            t_ccps = k.st["cc"].n

            def quant_group(mat_sl, ntiles, icols, ngrp, cc_dst, tscale_col=None):
                nbmax = max(1, 768 // icols)
                last_mul = [0, 0]
                ts_entry = k.st["pS"].n
                pv_entry = k.st["pV"].n
                it = 0
                ot = 0
                while ot < ntiles:
                    nb = min(nbmax, ntiles - ot)
                    W = nb * icols
                    G = nb * ngrp
                    buf = it % 2
                    wsl = wslab2[buf]
                    @blk.scalar
                    def _(e, ot=ot, nb=nb, W=W, buf=buf, wsl=wsl, it=it):
                        if it < 2:
                            k.wait(e, "s", "pS", ts_entry)
                            k.wait(e, "s", "pV", pv_entry)
                        else:
                            k.wait(e, "s", "pV", last_mul[buf])
                        td = k.dma(e, "s", f"Lslab{buf}",
                                   out=wsl[:, 0:W].rearrange("p (a c) -> p a c", c=icols),
                                   in_=mat_sl.ap()[ot * 128:(ot + nb) * 128, :].rearrange("(a p) c -> p a c", p=128))
                        k.wait(e, "s", f"Lslab{buf}", td)
                        if tscale_col is None:
                            k.wait(e, "s", "pV", k.st["pV"].n)
                            k.inc(e.activation(wabs[:, 0:W], wsl[:, 0:W], AF.Abs), "pS")
                    ts_ = k.st["pS"].n
                    td_slab = k.st[f"Lslab{buf}"].n

                    @blk.vector
                    def _(e, ts_=ts_, td_slab=td_slab, W=W, G=G, buf=buf, wsl=wsl):
                        k.wait(e, "v", "pS", ts_)
                        k.wait(e, "v", f"Lslab{buf}", td_slab)
                        k.wait(e, "v", "pV", k.st["pV"].n)
                        if tscale_col is None:
                            k.inc(e.reduce_sum(
                                sclb[:, 0:G],
                                wabs[:, 0:W].rearrange("p (g c) -> p g c", c=128),
                                axis=AX.X), "pV")
                            k.wait(e, "v", "pV", k.st["pV"].n)
                            k.inc(e.tensor_scalar(sclb[:, 0:G], sclb[:, 0:G],
                                                  1.0 / 128.0, EPS_LN, ALU.mult, ALU.add), "pV")
                            k.wait(e, "v", "pV", k.st["pV"].n)
                            k.inc(e.reciprocal(rsclb[:, 0:G], sclb[:, 0:G]), "pV")
                            k.wait(e, "v", "pV", k.st["pV"].n)
                            rscl_b = rsclb[:, 0:G].unsqueeze(2).to_broadcast([128, G, 128])
                            k.inc(e.tensor_mul(tq1[:, 0:W].rearrange("p (g c) -> p g c", c=128),
                                               wsl[:, 0:W].rearrange("p (g c) -> p g c", c=128),
                                               rscl_b), "pV")
                        else:
                            k.inc(e.tensor_scalar_mul(tq1[:, 0:W], wsl[:, 0:W],
                                                      sbq[:, tscale_col:tscale_col + 1]), "pV")
                        last_mul[buf] = k.st["pV"].n
                        k.wait(e, "v", "pV", k.st["pV"].n)
                        k.inc(e.tensor_scalar(tq1[:, 0:W], tq1[:, 0:W],
                                              1.0, -1.0, ALU.min, ALU.max), "pV")
                        k.wait(e, "v", "pV", k.st["pV"].n)
                        k.inc(e.tensor_scalar_add(tq1[:, 0:W], tq1[:, 0:W], RC), "pV")
                        k.wait(e, "v", "pV", k.st["pV"].n)
                        k.inc(e.tensor_copy(tqi[:, 0:W], tq1[:, 0:W]), "pV")
                        k.wait(e, "v", "pV", k.st["pV"].n)
                        if tscale_col is None:
                            scl_b = sclb[:, 0:G].unsqueeze(2).to_broadcast([128, G, 128])
                            k.inc(e.scalar_tensor_tensor(
                                wqv[:, 0:W].rearrange("p (g c) -> p g c", c=128),
                                tqi[:, 0:W].rearrange("p (g c) -> p g c", c=128),
                                -RC, scl_b, ALU.add, ALU.mult), "pV")
                        else:
                            cm_b = cmb[:, tscale_col:tscale_col + 1].to_broadcast([128, W])
                            k.inc(e.scalar_tensor_tensor(
                                wqv[:, 0:W], tqi[:, 0:W],
                                -RC, cm_b, ALU.add, ALU.mult), "pV")
                    tv_ = k.st["pV"].n

                    glist = [] if "nt" in PREP_SKIP else [
                        (j, i) for j in range(ngrp) for i in range(nb)]
                    for gidx, (j, i) in enumerate(glist):
                        @blk.tensor
                        def _(e, gidx=gidx, j=j, i=i, tv_=tv_):
                            k.wait(e, "t", "pV", tv_)
                            k.wait(e, "t", "pV", k.st["pV"].n)
                            k.inc(e.transpose(pT32[:, (gidx % 4) * 128:(gidx % 4) * 128 + 128],
                                              wqv[:, (i * icols + j * 128):(i * icols + j * 128) + 128],
                                              ident32[:]), "pT")
                        tt_ = k.st["pT"].n

                        @blk.vector
                        def _(e, gidx=gidx, j=j, i=i, tt_=tt_, nb=nb):
                            k.wait(e, "v", "pT", tt_)
                            k.inc(e.tensor_copy(stage[:, (j * nb + i) * 128:(j * nb + i) * 128 + 128],
                                                pT32[:, (gidx % 4) * 128:(gidx % 4) * 128 + 128]), "pV")
                    tv2_ = k.st["pV"].n

                    @blk.sync
                    def _(e, ot=ot, nb=nb, W=W, tv2_=tv2_):
                        if "ns" in PREP_SKIP or "nt" in PREP_SKIP:
                            return
                        k.wait(e, "sync", "pV", tv2_)
                        k.dma(e, "sync", "Lstage",
                              out=cc_dst.ap().rearrange("(a p) o -> p a o", p=128)[:, :, ot * 128:(ot + nb) * 128],
                              in_=stage[:, 0:W].rearrange("p (a o) -> p a o", o=nb * 128))
                    @blk.vector
                    def _(e):
                        k.wait(e, "v", "Lstage", k.st["Lstage"].n if "Lstage" in k.st else 0)
                    ot += nb
                    it += 1

            if "g" not in PREP_SKIP:
                quant_group(w_gate_sl, 43, 256, 2, cc_wg_in)
            @blk.gpsimd
            def _(e):
                k.wait(e, "g", "Lstage", k.st["Lstage"].n if "Lstage" in k.st else 0)
                if "cg" in PREP_SKIP:
                    k.inc(e.engine_nop(), "cc")
                else:
                    k.inc(e.collective_compute(
                        "AllGather", ALU.bypass,
                        replica_groups=[[0, 1, 2, 3, 4, 5, 6, 7]],
                        ins=[cc_wg_in.ap().opt()], outs=[cc_wg_out.ap().opt()]), "cc")

            if "u" not in PREP_SKIP:
                quant_group(w_up_sl, 43, 256, 2, cc_wu_in)
            @blk.gpsimd
            def _(e):
                k.wait(e, "g", "Lstage", k.st["Lstage"].n if "Lstage" in k.st else 0)
                if "cu" in PREP_SKIP:
                    k.inc(e.engine_nop(), "cc")
                else:
                    k.inc(e.collective_compute(
                        "AllGather", ALU.bypass,
                        replica_groups=[[0, 1, 2, 3, 4, 5, 6, 7]],
                        ins=[cc_wu_in.ap().opt()], outs=[cc_wu_out.ap().opt()]), "cc")

            if "d" not in PREP_SKIP:
                quant_group(w_down_sl, 16, 768, 6, cc_wd_in)
            @blk.gpsimd
            def _(e):
                k.wait(e, "g", "Lstage", k.st["Lstage"].n if "Lstage" in k.st else 0)
                if "cd" in PREP_SKIP:
                    k.inc(e.engine_nop(), "cc")
                else:
                    k.inc(e.collective_compute(
                        "AllGather", ALU.bypass,
                        replica_groups=[[0, 1, 2, 3, 4, 5, 6, 7]],
                        ins=[cc_wd_in.ap().opt()], outs=[cc_wd_out.ap().opt()]), "cc")

            @blk.scalar
            def _(e):
                k.wait(e, "s", "cc", t_ccps)
                td = k.dma(e, "s", "Lccps", out=pstot[:], in_=cc_ps_out.ap()[0:1, 0:2])
                k.wait(e, "s", "Lccps", td)
            t_pstot2 = k.st["Lccps"].n

            @blk.vector
            def _(e):
                k.wait(e, "v", "Lccps", t_pstot2)
                k.wait(e, "v", "Lconst", t_constload)
                k.inc(e.tensor_mul(pstot[:], pstot[:], mdiv_s[:]), "pV")
                k.wait(e, "v", "pV", k.st["pV"].n)
                k.inc(e.tensor_scalar_max(pstot[:], pstot[:], EPS_Q), "pV")
            tv_cm = k.st["pV"].n

            @blk.tensor
            def _(e):
                k.wait(e, "t", "pV", tv_cm)
                k.wait(e, "t", "pV", k.st["pV"].n)
                k.inc(e.matmul(pT32[:, 0:2], ones_row[:], pstot[:], start=True, stop=True), "pT")
            t_cmmm = k.st["pT"].n

            @blk.vector
            def _(e):
                k.wait(e, "v", "pT", t_cmmm)
                k.inc(e.tensor_copy(cmb[:], pT32[:, 0:2]), "pV")
                k.wait(e, "v", "pV", k.st["pV"].n)
                k.inc(e.reciprocal(sbq[:], cmb[:]), "pV")
            tv_sbq = k.st["pV"].n

            if "q" not in PREP_SKIP:
                quant_group(w_qkv_sl, 48, 256, 2, cc_wqkv_in, tscale_col=0)
            @blk.gpsimd
            def _(e):
                k.wait(e, "g", "Lstage", k.st["Lstage"].n if "Lstage" in k.st else 0)
                if "cq" in PREP_SKIP:
                    k.inc(e.engine_nop(), "cc")
                else:
                    k.inc(e.collective_compute(
                        "AllGather", ALU.bypass,
                        replica_groups=[[0, 1, 2, 3, 4, 5, 6, 7]],
                        ins=[cc_wqkv_in.ap().opt()], outs=[cc_wqkv_out.ap().opt()]), "cc")

            if "p" not in PREP_SKIP:
                quant_group(w_proj_sl, 16, 256, 2, cc_wp_in, tscale_col=1)
            @blk.gpsimd
            def _(e):
                k.wait(e, "g", "Lstage", k.st["Lstage"].n if "Lstage" in k.st else 0)
                if "cp" in PREP_SKIP:
                    k.inc(e.engine_nop(), "cc")
                else:
                    k.inc(e.collective_compute(
                        "AllGather", ALU.bypass,
                        replica_groups=[[0, 1, 2, 3, 4, 5, 6, 7]],
                        ins=[cc_wp_in.ap().opt()], outs=[cc_wp_out.ap().opt()]), "cc")
            t_cc_weights = k.st["cc"].n

        def ln_stats(src_d, t, xtile, xsq, st1, st2, var_all, mean_all, col):
            @blk.scalar
            def _(e, t=t):
                k.join(e, "s")
                k.wait(e, "s", "Lresw", k.st["Lresw"].n if "Lresw" in k.st else 0)
                td = k.dma(e, "s", "Lx", out=xtile[:], in_=src_d.ap()[t * 128:(t + 1) * 128, :])
                k.wait(e, "s", "Lx", td)
                k.inc(e.activation(xsq[:], xtile[:], AF.Square, accum_out=st2[:]), "pS")
            tS = k.st["pS"].n
            td_x = k.st["Lx"].n

            @blk.vector
            def _(e, tS=tS, td_x=td_x, col=col):
                k.wait(e, "v", "Lx", td_x)
                k.wait(e, "v", "pS", tS)
                k.wait(e, "v", "pV", k.st["pV"].n)
                k.inc(e.reduce_sum(st1[:], xtile[:], axis=AX.X), "pV")
                k.wait(e, "v", "pV", k.st["pV"].n)
                k.inc(e.tensor_scalar_mul(mean_all[:, col:col + 1], st1[:], 1.0 / D), "pV")
                k.wait(e, "v", "pV", k.st["pV"].n)
                k.inc(e.tensor_mul(st1[:], mean_all[:, col:col + 1], mean_all[:, col:col + 1]), "pV")
                k.wait(e, "v", "pV", k.st["pV"].n)
                k.inc(e.scalar_tensor_tensor(var_all[:, col:col + 1], st2[:], 1.0 / D, st1[:],
                                             ALU.mult, ALU.subtract), "pV")

        def batch_rsqrt(var_all, rstd_all, n, eps):
            @blk.vector
            def _(e):
                k.wait(e, "v", "pV", k.st["pV"].n)
                k.inc(e.tensor_scalar_add(var_all[:, 0:n], var_all[:, 0:n], eps), "pV")
                k.wait(e, "v", "pV", k.st["pV"].n)
                k.inc(e.reciprocal(rstd_all[:, 0:n], var_all[:, 0:n]), "pV")
            tv = k.st["pV"].n
            @blk.scalar
            def _(e, tv=tv):
                k.wait(e, "s", "pV", tv)
                k.wait(e, "s", "pS", k.st["pS"].n)
                k.inc(e.activation(rstd_all[:, 0:n], rstd_all[:, 0:n], AF.Sqrt), "pS")
            return k.st["pS"].n

        def ln_apply(src_d, t, xtile, rstd_all, mean_all, nmr, col, xn_out):
            @blk.scalar
            def _(e, t=t):
                k.join(e, "s")
                td = k.dma(e, "s", "Lx", out=xtile[:], in_=src_d.ap()[t * 128:(t + 1) * 128, :])
                k.wait(e, "s", "Lx", td)
            td_x = k.st["Lx"].n
            @blk.vector
            def _(e, td_x=td_x, col=col):
                k.wait(e, "v", "pV", k.st["pV"].n)
                k.wait(e, "v", "pS", k.st["pS"].n)
                k.inc(e.scalar_tensor_tensor(nmr[:], mean_all[:, col:col + 1], -1.0,
                                             rstd_all[:, col:col + 1], ALU.mult, ALU.mult), "pV")
            tV2 = k.st["pV"].n
            @blk.scalar
            def _(e, tV2=tV2, td_x=td_x, col=col):
                k.wait(e, "s", "pV", tV2)
                k.wait(e, "s", "Lx", td_x)
                k.wait(e, "s", "pT", k.st["pT"].n)
                k.inc(e.activation(xn_out[:], xtile[:], AF.Identity,
                                   bias=nmr[:], scale=rstd_all[:, col:col + 1]), "pS")
            return k.st["pS"].n

        def ln_tile(src_d, t, xtile, xsq, st1, st2, st3, rstd, nmr, xn_out):
            @blk.scalar
            def _(e, t=t):
                k.join(e, "s")
                k.wait(e, "s", "Lresw", k.st["Lresw"].n if "Lresw" in k.st else 0)
                td = k.dma(e, "s", "Lx", out=xtile[:], in_=src_d.ap()[t * 128:(t + 1) * 128, :])
                k.wait(e, "s", "Lx", td)
                k.inc(e.activation(xsq[:], xtile[:], AF.Square, accum_out=st2[:]), "pS")
            tS = k.st["pS"].n
            td_x = k.st["Lx"].n

            @blk.vector
            def _(e, tS=tS, td_x=td_x):
                k.wait(e, "v", "Lx", td_x)
                k.wait(e, "v", "pS", tS)
                k.wait(e, "v", "pV", k.st["pV"].n)
                k.inc(e.reduce_sum(st1[:], xtile[:], axis=AX.X), "pV")
                k.wait(e, "v", "pV", k.st["pV"].n)
                k.inc(e.tensor_scalar_mul(st1[:], st1[:], 1.0 / D), "pV")
                k.wait(e, "v", "pV", k.st["pV"].n)
                k.inc(e.tensor_mul(st3[:], st1[:], st1[:]), "pV")
                k.wait(e, "v", "pV", k.st["pV"].n)
                k.inc(e.scalar_tensor_tensor(st3[:], st2[:], 1.0 / D, st3[:],
                                             ALU.mult, ALU.subtract), "pV")
                k.wait(e, "v", "pV", k.st["pV"].n)
                k.inc(e.tensor_scalar_add(st3[:], st3[:], EPS_LN), "pV")
                k.wait(e, "v", "pV", k.st["pV"].n)
                k.inc(e.reciprocal(st2[:], st3[:]), "pV")
            tV = k.st["pV"].n

            @blk.scalar
            def _(e, tV=tV):
                k.wait(e, "s", "pV", tV)
                k.inc(e.activation(rstd[:], st2[:], AF.Sqrt), "pS")
            tS2 = k.st["pS"].n

            @blk.vector
            def _(e, tS2=tS2):
                k.wait(e, "v", "pS", tS2)
                k.wait(e, "v", "pV", k.st["pV"].n)
                k.inc(e.scalar_tensor_tensor(nmr[:], st1[:], -1.0, rstd[:],
                                             ALU.mult, ALU.mult), "pV")
            tV2 = k.st["pV"].n

            @blk.scalar
            def _(e, tV2=tV2):
                k.wait(e, "s", "pV", tV2)
                k.wait(e, "s", "pT", k.st["pT"].n)
                k.inc(e.activation(xn_out[:], xtile[:], AF.Identity,
                                   bias=nmr[:], scale=rstd[:]), "pS")
            return k.st["pS"].n

        def transposeN(src, src_cols, dst, dst_cols, dtype, n, extra=None):
            ps = pT32 if dtype == F32 else pT16
            idn = ident32 if dtype == F32 else ident16
            for i in range(n):
                @blk.tensor
                def _(e, i=i):
                    k.wait(e, "t", "pV", k.st["pV"].n)
                    k.wait(e, "t", "pS", k.st["pS"].n)
                    c0 = src_cols[i]
                    k.inc(e.transpose(ps[:, (i % 4) * 128:(i % 4) * 128 + 128],
                                      src[:, c0:c0 + 128], idn[:]), "pT")
                tt = k.st["pT"].n

                @blk.vector
                def _(e, i=i, tt=tt):
                    k.wait(e, "v", "pT", tt)
                    k.wait(e, "v", "pS", k.st["pS"].n)
                    if dst is not None:
                        d0 = dst_cols[i]
                        k.inc(e.tensor_copy(dst[:, d0:d0 + 128],
                                            ps[:, (i % 4) * 128:(i % 4) * 128 + 128]), "pV")
                    if extra is not None:
                        extra(i, e)

        _rp_count = [0]
        def residual_pass(src_d, proj_d, dst_d):
            _rp_count[0] += 1
            rix = _rp_count[0]
            rES = contextlib.ExitStack()
            with rES:
                xo = rES.enter_context(nc.sbuf_tensor(f"xo_r{rix}", [128, D], F32))
                yo = rES.enter_context(nc.sbuf_tensor(f"yo_r{rix}", [128, D], F32))
                xnw = rES.enter_context(nc.sbuf_tensor(f"xnw_r{rix}", [128, D], F32))
                for t in range(NTT):
                    @blk.scalar
                    def _(e, t=t):
                        k.join(e, "s")
                        k.wait(e, "s", "Lyp", k.st["Lyp"].n if "Lyp" in k.st else 0)
                        k.wait(e, "s", "Lresw", k.st["Lresw"].n if "Lresw" in k.st else 0)
                        k.dma(e, "s", "Lres", first=True, out=xo[:], in_=src_d.ap()[t * 128:(t + 1) * 128, :])
                        k.dma(e, "s", "Lres", first=False, out=yo[:], in_=proj_d.ap()[t * 128:(t + 1) * 128, :])
                    td = k.st["Lres"].n

                    @blk.vector
                    def _(e, t=t, td=td):
                        k.wait(e, "v", "Lres", td)
                        k.wait(e, "v", "Lresw", k.st["Lresw"].n if "Lresw" in k.st else 0)
                        k.wait(e, "v", "pV", k.st["pV"].n)
                        k.inc(e.tensor_add(xnw[:], xo[:], yo[:]), "pV")
                    tv = k.st["pV"].n

                    @blk.sync
                    def _(e, t=t, tv=tv):
                        k.wait(e, "sync", "pV", tv)
                        k.dma(e, "sync", "Lresw", out=dst_d.ap()[t * 128:(t + 1) * 128, :], in_=xnw[:])
                @blk.vector
                def _(e):
                    k.wait(e, "v", "Lresw", k.st["Lresw"].n)

        yproj = dint("yproj", [NTOK, D])
        q_d = dint("q_d", [NTOK, D])
        ygall_d = dint("ygall_d", [D, NTOK], BF16)
        kaT_d = dint("kaT_d", [NKV * HD, T], BF16)
        kraw_d = dint("kraw_d", [T, NKV * HD])
        vaT_d = dint("vaT_d", [NKV * HD, T], BF16)

        if upto >= 1:
            aES = contextlib.ExitStack()
            with aES:
                qhT = aES.enter_context(nc.sbuf_tensor("qhT", [128, 16 * NTOK], BF16))
                qselT = aES.enter_context(nc.sbuf_tensor("qselT", [64, NTOK], F32))
                kselT = aES.enter_context(nc.sbuf_tensor("kselT", [64, T], F32))
                m01T_buf = aES.enter_context(nc.sbuf_tensor("m01T_buf", [128, 24 * 512], BF16))
                ropec_s = aES.enter_context(nc.sbuf_tensor("ropec_s", [128, 16 * 64], F32))
                ropes_s = aES.enter_context(nc.sbuf_tensor("ropes_s", [128, 16 * 64], F32))
                ropeoc_s = aES.enter_context(nc.sbuf_tensor("ropeoc_s", [128, 8 * 64], F32))
                ropeos_s = aES.enter_context(nc.sbuf_tensor("ropeos_s", [128, 8 * 64], F32))
                xtile = aES.enter_context(nc.sbuf_tensor("xtile", [128, D], F32))
                xsq = aES.enter_context(nc.sbuf_tensor("xsq", [128, D], F32))
                xn = aES.enter_context(nc.sbuf_tensor("xn", [128, D], F32))
                qr = aES.enter_context(nc.sbuf_tensor("qr", [128, D], F32))
                st1 = aES.enter_context(nc.sbuf_tensor("st1", [128, 1], F32))
                st2 = aES.enter_context(nc.sbuf_tensor("st2", [128, 1], F32))
                st3 = aES.enter_context(nc.sbuf_tensor("st3", [128, 1], F32))
                rstd = aES.enter_context(nc.sbuf_tensor("rstd", [128, 1], F32))
                nmr = aES.enter_context(nc.sbuf_tensor("nmr", [128, 1], F32))
                ssh = aES.enter_context(nc.sbuf_tensor("ssh", [128, 16], F32))
                qssh = aES.enter_context(nc.sbuf_tensor("qssh", [128, 128], F32))
                kssh = aES.enter_context(nc.sbuf_tensor("kssh", [128, 64], F32))
                ttmp = xtile[:, 0:1024]
                ttmp2 = xtile[:, 1024:2048]

                @blk.sync
                def _(e):
                    k.dma(e, "sync", "Lrope", first=True, out=ropec_s[:].rearrange("p (a j) -> p a j", j=64),
                          in_=rope_c.ap().rearrange("(a p) j -> p a j", p=128))
                    k.dma(e, "sync", "Lrope", first=False, out=ropes_s[:].rearrange("p (a j) -> p a j", j=64),
                          in_=rope_s.ap().rearrange("(a p) j -> p a j", p=128))
                    k.dma(e, "sync", "Lrope", first=False, out=ropeoc_s[:].rearrange("p (a j) -> p a j", j=64),
                          in_=rope_own_c.ap().rearrange("(a p) j -> p a j", p=128))
                    k.dma(e, "sync", "Lrope", first=False, out=ropeos_s[:].rearrange("p (a j) -> p a j", j=64),
                          in_=rope_own_s.ap().rearrange("(a p) j -> p a j", p=128))
                t_ropeload = k.st["Lrope"].n

                for t in range(NTT):
                    tS = ln_tile(x_own, t, xtile, xsq, st1, st2, st3, rstd, nmr, xn)
                    @blk.vector
                    def _(e, tS=tS):
                        k.wait(e, "v", "pS", tS)
                        k.wait(e, "v", "Lccx", k.st["Lccx"].n if "Lccx" in k.st else 0)
                    transposeN(xn, [c * 128 for c in range(16)], qr,
                               [c * 128 for c in range(16)], F32, 16)
                    tv_t = k.st["pV"].n
                    @blk.sync
                    def _(e, t=t, tv_t=tv_t):
                        k.wait(e, "sync", "pV", tv_t)
                        k.dma(e, "sync", "Lccx",
                              out=cc_in_x.ap().rearrange("(c p) tok -> p c tok", p=128)[:, :, t * 128:(t + 1) * 128],
                              in_=qr[:].rearrange("p (c j) -> p c j", j=128))
                t_ccin = k.st["Lccx"].n

                @blk.gpsimd
                def _(e):
                    k.wait(e, "g", "Lccx", t_ccin)
                    k.inc(e.collective_compute(
                        "AllGather", ALU.bypass,
                        replica_groups=[[0, 1], [2, 3], [4, 5], [6, 7]],
                        ins=[cc_in_x.ap().opt()], outs=[cc_out_x.ap().opt()]), "cc")
                t_cc_x = k.st["cc"].n

                a2ES = contextlib.ExitStack()
                with a2ES:
                    xnTb_own = a2ES.enter_context(nc.sbuf_tensor("xnTb_own", [128, 16 * NTOK], BF16))
                    wq16 = [a2ES.enter_context(nc.sbuf_tensor(f"wq16_{i}", [128, 16 * 512], BF16)) for i in range(2)]
                    qslab = a2ES.enter_context(nc.sbuf_tensor("qslab", [128, 512], F32))

                    @blk.gpsimd
                    def _(e):
                        k.join(e, "g")
                        k.wait(e, "g", "Lccx", t_ccin)
                        k.dma(e, "g", "Lxno",
                              out=xnTb_own[:].rearrange("p (c tok) -> p c tok", tok=NTOK),
                              in_=cc_in_x.ap().rearrange("(c p) tok -> p c tok", p=128))
                    t_xnTo = k.st["Lxno"].n

                    t_wq_ready = [0, 0]
                    t_wq_used = [0, 0]
                    for qd in range(4):
                        @blk.gpsimd
                        def _(e, qd=qd):
                            k.join(e, "g") if qd < 2 else k.wait(e, "g", "pT", t_wq_used[qd % 2])
                            k.dma(e, "g", f"Lwq{qd % 2}",
                                  out=wq16[qd % 2][:].rearrange("p (c n) -> p c n", n=512),
                                  in_=w_q.ap().rearrange("(c p) o -> p c o", p=128)[:, :, qd * 512:(qd + 1) * 512])
                        t_wq_ready[qd % 2] = k.st[f"Lwq{qd % 2}"].n

                        for t in range(NTT):
                            @blk.tensor
                            def _(e, qd=qd, t=t):
                                k.wait(e, "t", f"Lwq{qd % 2}", t_wq_ready[qd % 2])
                                k.wait(e, "t", "Lxno", t_xnTo)
                                k.wait(e, "t", "Lconst", t_constload)
                                ps = [pA, pB][t % 2]
                                k.wait(e, "t", "pS", k.st["pS"].n)
                                k.inc(e.matmul(ps[:], ones_row[:], bq_s[0:1, qd * 512:(qd + 1) * 512],
                                               start=True, stop=False), "pT")
                                for dd in range(16):
                                    k.inc(e.matmul(ps[:], xnTb_own[:, dd * NTOK + t * 128:dd * NTOK + t * 128 + 128],
                                                   wq16[qd % 2][:, dd * 512:(dd + 1) * 512],
                                                   start=False, stop=(dd == 15)), "pT")
                            tt_ = k.st["pT"].n
                            t_wq_used[qd % 2] = tt_

                            @blk.scalar
                            def _(e, qd=qd, t=t, tt_=tt_):
                                k.wait(e, "s", "pS", k.st["pS"].n)
                                k.wait(e, "s", "Lqs", k.st["Lqs"].n if "Lqs" in k.st else 0)
                                k.wait(e, "s", "pT", tt_)
                                ps = [pA, pB][t % 2]
                                k.inc(e.activation(qslab[:], ps[:], AF.Identity), "pS")
                                k.wait(e, "s", "pS", k.st["pS"].n)
                                k.dma(e, "s", "Lqs", out=q_d.ap()[t * 128:(t + 1) * 128, qd * 512:(qd + 1) * 512],
                                      in_=qslab[:])

                    def rms_stats(src, W, nh, ssh_all, col0):
                        @blk.scalar
                        def _(e):
                            k.wait(e, "s", "pV", k.st["pV"].n)
                            k.wait(e, "s", "pS", k.st["pS"].n)
                            k.inc(e.activation(xsq[:, 0:W], src, AF.Square), "pS")
                        tS = k.st["pS"].n
                        @blk.vector
                        def _(e, tS=tS, nh=nh, col0=col0):
                            k.wait(e, "v", "pS", tS)
                            k.wait(e, "v", "pV", k.st["pV"].n)
                            k.inc(e.reduce_sum(ssh[:, 0:nh],
                                               xsq[:, 0:W].rearrange("p (h d) -> p h d", d=128),
                                               axis=AX.X), "pV")
                            k.wait(e, "v", "pV", k.st["pV"].n)
                            k.inc(e.tensor_scalar_mul(ssh_all[:, col0:col0 + nh], ssh[:, 0:nh],
                                                      1.0 / 128.0), "pV")

                    def rope_apply(src, W, nh, ctab, stab, gain_tile, rstd_all, col0,
                                   dst16, dst_cols, sel_dst=None, sel_col=None):
                        @blk.vector
                        def _(e, nh=nh, W=W, col0=col0):
                            k.wait(e, "v", "pV", k.st["pV"].n)
                            k.wait(e, "v", "pS", k.st["pS"].n)
                            sv = src.rearrange("p (h d) -> p h d", d=128)
                            k.inc(e.tensor_mul(sv, sv,
                                               rstd_all[:, col0:col0 + nh].unsqueeze(2).to_broadcast([128, nh, 128])), "pV")
                            k.wait(e, "v", "pV", k.st["pV"].n)
                            k.inc(e.tensor_mul(sv, sv,
                                               gain_tile[:].unsqueeze(1).to_broadcast([128, nh, 128])), "pV")
                            k.wait(e, "v", "pV", k.st["pV"].n)
                            s_e = src.rearrange("p (h j t) -> p h j t", j=64, t=2)[:, :, :, 0]
                            s_o = src.rearrange("p (h j t) -> p h j t", j=64, t=2)[:, :, :, 1]
                            q_r = qr[:, 0:W].rearrange("p (h j t) -> p h j t", j=64, t=2)
                            c_b = ctab.unsqueeze(1).to_broadcast([128, nh, 64])
                            sb_ = stab.unsqueeze(1).to_broadcast([128, nh, 64])
                            tt1 = ttmp[:, 0:nh * 64].rearrange("p (h j) -> p h j", j=64)
                            tt2 = ttmp2[:, 0:nh * 64].rearrange("p (h j) -> p h j", j=64)
                            k.inc(e.tensor_mul(tt1, s_e, c_b), "pV")
                            k.inc(e.tensor_mul(tt2, s_o, sb_), "pV")
                            k.wait(e, "v", "pV", k.st["pV"].n)
                            k.inc(e.tensor_sub(q_r[:, :, :, 0], tt1, tt2), "pV")
                            k.wait(e, "v", "pV", k.st["pV"].n)
                            k.inc(e.tensor_mul(tt1, s_e, sb_), "pV")
                            k.inc(e.tensor_mul(tt2, s_o, c_b), "pV")
                            k.wait(e, "v", "pV", k.st["pV"].n)
                            k.inc(e.tensor_add(q_r[:, :, :, 1], tt1, tt2), "pV")

                        def sel_extra(i, e):
                            if sel_dst is not None and i == 0:
                                k.inc(e.tensor_copy(sel_dst[0:64, sel_col:sel_col + 128],
                                                    pT32[0:64, 0:128]), "pV")
                        transposeN(qr, [c * 128 for c in range(nh)], dst16, dst_cols,
                                   F32, nh, extra=sel_extra)

                    ctv = ropeoc_s[:].rearrange("p (a j) -> p a j", j=64)
                    stv = ropeos_s[:].rearrange("p (a j) -> p a j", j=64)
                    def q_reload(t):
                        @blk.scalar
                        def _(e, t=t):
                            k.wait(e, "s", "pV", k.st["pV"].n)
                            k.wait(e, "s", "pS", k.st["pS"].n)
                            k.wait(e, "s", "Lqs", k.st["Lqs"].n)
                            td = k.dma(e, "s", "Lx", out=xn[:], in_=q_d.ap()[t * 128:(t + 1) * 128, :])
                            k.wait(e, "s", "Lx", td)
                        td_q = k.st["Lx"].n
                        @blk.vector
                        def _(e, td_q=td_q):
                            k.wait(e, "v", "Lx", td_q)
                            k.wait(e, "v", "Lrope", t_ropeload)
                    for t in range(NTT):
                        q_reload(t)
                        rms_stats(xn[:], D, 16, qssh, t * 16)
                    batch_rsqrt(qssh, qssh, 128, EPS_RMS)
                    for t in range(NTT):
                        q_reload(t)
                        rope_apply(xn[:], D, 16, ctv[:, t, :], stv[:, t, :], qnw_t,
                                   qssh, t * 16,
                                   qhT, [c * NTOK + t * 128 for c in range(16)],
                                   sel_dst=qselT, sel_col=t * 128)

                a3ES = contextlib.ExitStack()
                with a3ES:
                    xnb_tc = [a3ES.enter_context(nc.sbuf_tensor(f"xnb_tc{i}", [128, 16 * 512], BF16)) for i in range(2)]
                    kvd_tc = a3ES.enter_context(nc.sbuf_tensor("kvd_tc", [128, 4 * 512], BF16))
                    wkv16 = a3ES.enter_context(nc.sbuf_tensor("wkv16", [128, 16 * 128], BF16))
                    ktok_tc = a3ES.enter_context(nc.sbuf_tensor("ktok_tc", [128, 4 * 512], F32))
                    kvtmp = a3ES.enter_context(nc.sbuf_tensor("kvtmp", [128, 512], F32))
                    kvtmp16 = a3ES.enter_context(nc.sbuf_tensor("kvtmp16", [128, 512], BF16))
                    kstage = a3ES.enter_context(nc.sbuf_tensor("kstage", [128, 512], BF16))
                    POS_BLOCKS = ((0, 0), (1, 0), (1, 512), (0, 512))
                    ctk = ropec_s[:].rearrange("p (a j) -> p a j", j=64)
                    stk = ropes_s[:].rearrange("p (a j) -> p a j", j=64)
                    t_xnb = [0, 0]
                    for tc in range(4):
                        r_, co_ = POS_BLOCKS[tc]
                        @blk.gpsimd
                        def _(e, tc=tc, r_=r_, co_=co_):
                            k.wait(e, "g", "cc", t_cc_x)
                            k.join(e, "g")
                            ccv = cc_out_x.ap().rearrange("(r c p) tok -> r p c tok", p=128, c=16)
                            k.dma(e, "g", f"Lxnb{tc % 2}",
                                  out=xnb_tc[tc % 2][:].rearrange("p (c j) -> p c j", j=512),
                                  in_=ccv[r_, :, :, co_:co_ + 512])
                        t_xnb[tc % 2] = k.st[f"Lxnb{tc % 2}"].n

                        for lc in range(4):
                            @blk.gpsimd
                            def _(e, lc=lc):
                                k.join(e, "g")
                                k.dma(e, "g", "Lwkv",
                                      out=wkv16[:].rearrange("p (c n) -> p c n", n=128),
                                      in_=w_down_a.ap().rearrange("(c p) o -> p c o", p=128)[:, :, lc * 128:(lc + 1) * 128])
                            t_wkv = k.st["Lwkv"].n

                            @blk.tensor
                            def _(e, tc=tc, lc=lc, t_wkv=t_wkv):
                                k.wait(e, "t", "Lwkv", t_wkv)
                                k.wait(e, "t", f"Lxnb{tc % 2}", t_xnb[tc % 2])
                                ps = [pA, pB][lc % 2]
                                k.wait(e, "t", "pS", k.st["pS"].n)
                                for dd in range(16):
                                    k.inc(e.matmul(ps[:], wkv16[:, dd * 128:(dd + 1) * 128],
                                                   xnb_tc[tc % 2][:, dd * 512:(dd + 1) * 512],
                                                   start=(dd == 0), stop=(dd == 15)), "pT")
                            tt_ = k.st["pT"].n

                            @blk.scalar
                            def _(e, lc=lc, tt_=tt_):
                                k.wait(e, "s", "pT", tt_)
                                k.wait(e, "s", "pS", k.st["pS"].n)
                                ps = [pA, pB][lc % 2]
                                k.inc(e.activation(kvd_tc[:, lc * 512:(lc + 1) * 512],
                                                   ps[:], AF.Identity, bias=bdr_s[:, lc:lc + 1]), "pS")
                        tS_kvd = k.st["pS"].n

                        for kc in range(8):
                            @blk.gpsimd
                            def _(e, kc=kc):
                                k.join(e, "g")
                                k.dma(e, "g", "Lwkv",
                                      out=wkv16[:, 0:4 * 128].rearrange("p (c n) -> p c n", n=128),
                                      in_=w_up_a.ap().rearrange("(c p) o -> p c o", p=128)[:, :, kc * 128:(kc + 1) * 128])
                            t_wkv = k.st["Lwkv"].n

                            @blk.tensor
                            def _(e, kc=kc, t_wkv=t_wkv, tS_kvd=tS_kvd):
                                k.wait(e, "t", "Lwkv", t_wkv)
                                k.wait(e, "t", "pS", tS_kvd)
                                ps = [pA, pB][kc % 2]
                                k.wait(e, "t", "pS", k.st["pS"].n)
                                k.wait(e, "t", "pV", k.st["pV"].n)
                                for lc in range(4):
                                    k.inc(e.matmul(ps[:], wkv16[:, lc * 128:(lc + 1) * 128],
                                                   kvd_tc[:, lc * 512:(lc + 1) * 512],
                                                   start=(lc == 0), stop=(lc == 3)), "pT")
                            tt_ = k.st["pT"].n

                            @blk.scalar
                            def _(e, kc=kc, tt_=tt_):
                                k.wait(e, "s", "pT", tt_)
                                k.wait(e, "s", "pS", k.st["pS"].n)
                                ps = [pA, pB][kc % 2]
                                dst = kvtmp[:] if kc < 4 else kvtmp16[:]
                                k.inc(e.activation(dst, ps[:], AF.Identity,
                                                   bias=bur_s[:, kc:kc + 1]), "pS")
                            tS_ = k.st["pS"].n

                            if kc < 4:
                                for j in range(4):
                                    @blk.tensor
                                    def _(e, j=j, tS_=tS_):
                                        k.wait(e, "t", "pS", tS_)
                                        k.wait(e, "t", "pV", k.st["pV"].n)
                                        k.inc(e.transpose(pT32[:, (j % 4) * 128:(j % 4) * 128 + 128],
                                                          kvtmp[:, j * 128:(j + 1) * 128], ident32[:]), "pT")
                                    tt2 = k.st["pT"].n
                                    @blk.vector
                                    def _(e, kc=kc, j=j, tt2=tt2):
                                        k.wait(e, "v", "pT", tt2)
                                        k.wait(e, "v", "Lkraw", k.st["Lkraw"].n if "Lkraw" in k.st else 0)
                                        k.inc(e.tensor_copy(
                                            ktok_tc[:, j * 512 + kc * 128:j * 512 + kc * 128 + 128],
                                            pT32[:, (j % 4) * 128:(j % 4) * 128 + 128]), "pV")
                            else:
                                for j in range(4):
                                    @blk.tensor
                                    def _(e, j=j, tS_=tS_):
                                        k.wait(e, "t", "pS", tS_)
                                        k.wait(e, "t", "pV", k.st["pV"].n)
                                        k.inc(e.transpose(pT16[:, (j % 4) * 128:(j % 4) * 128 + 128],
                                                          kvtmp16[:, j * 128:(j + 1) * 128], ident16[:]), "pT")
                                    tt2 = k.st["pT"].n
                                    @blk.vector
                                    def _(e, j=j, tt2=tt2):
                                        k.wait(e, "v", "pT", tt2)
                                        k.wait(e, "v", "Lkst", k.st["Lkst"].n if "Lkst" in k.st else 0)
                                        k.wait(e, "v", "Lvst", k.st["Lvst"].n if "Lvst" in k.st else 0)
                                        k.inc(e.tensor_copy(kstage[:, j * 128:(j + 1) * 128],
                                                            pT16[:, (j % 4) * 128:(j % 4) * 128 + 128]), "pV")
                                tv_v = k.st["pV"].n
                                @blk.scalar
                                def _(e, kc=kc, tc=tc, tv_v=tv_v):
                                    k.wait(e, "s", "pV", tv_v)
                                    k.dma(e, "s", "Lvst",
                                          out=vaT_d.ap()[(kc - 4) * 128:(kc - 3) * 128, tc * 512:(tc + 1) * 512],
                                          in_=kstage[:])

                        for j in range(4):
                            tt_idx = tc * 4 + j
                            rms_stats(ktok_tc[:, j * 512:(j + 1) * 512], 512, 4,
                                      kssh, tt_idx * 4)
                            tv_k = k.st["pV"].n
                            @blk.scalar
                            def _(e, j=j, tt_idx=tt_idx, tv_k=tv_k):
                                k.wait(e, "s", "pV", tv_k)
                                k.wait(e, "s", "pS", k.st["pS"].n)
                                k.dma(e, "s", "Lkraw",
                                      out=kraw_d.ap()[tt_idx * 128:(tt_idx + 1) * 128, :],
                                      in_=ktok_tc[:, j * 512:(j + 1) * 512])

                batch_rsqrt(kssh, kssh, 64, EPS_RMS)
                for tt_idx in range(16):
                    @blk.scalar
                    def _(e, tt_idx=tt_idx):
                        k.join(e, "s")
                        k.wait(e, "s", "Lkraw", k.st["Lkraw"].n)
                        td = k.dma(e, "s", "Lx", out=ktok_tc[:, 0:512],
                                   in_=kraw_d.ap()[tt_idx * 128:(tt_idx + 1) * 128, :])
                        k.wait(e, "s", "Lx", td)
                    td_k = k.st["Lx"].n
                    @blk.vector
                    def _(e, td_k=td_k):
                        k.wait(e, "v", "Lx", td_k)
                        k.wait(e, "v", "Lkst", k.st["Lkst"].n if "Lkst" in k.st else 0)
                        k.wait(e, "v", "Lvst", k.st["Lvst"].n if "Lvst" in k.st else 0)
                        k.wait(e, "v", "Lrope", t_ropeload)
                    rope_apply(ktok_tc[:, 0:512], 512, 4,
                               ctk[:, tt_idx, :], stk[:, tt_idx, :], knw_t,
                               kssh, tt_idx * 4,
                               kstage, [h * 128 for h in range(4)],
                               sel_dst=kselT, sel_col=tt_idx * 128)
                    tv_k2 = k.st["pV"].n
                    @blk.scalar
                    def _(e, tt_idx=tt_idx, tv_k2=tv_k2):
                        k.wait(e, "s", "pV", tv_k2)
                        k.dma(e, "s", "Lkst",
                              out=kaT_d.ap().rearrange("(a p) c -> p a c", p=128)[:, :, tt_idx * 128:(tt_idx + 1) * 128],
                              in_=kstage[:].rearrange("p (a c) -> p a c", c=128))

                a4ES = contextlib.ExitStack()
                with a4ES:
                    selbuf = a4ES.enter_context(nc.sbuf_tensor("selbuf", [128, 2048], F32))
                    mrb = [a4ES.enter_context(nc.sbuf_tensor(f"mrb{i}", [128, 2048], F32)) for i in range(2)]
                    m8 = a4ES.enter_context(nc.sbuf_tensor("m8", [128, 8], F32))
                    thr = a4ES.enter_context(nc.sbuf_tensor("thr", [128, 1], F32))
                    g01 = a4ES.enter_context(nc.sbuf_tensor("g01", [128, 512], F32))
                    m01 = a4ES.enter_context(nc.sbuf_tensor("m01", [128, 512], F32))
                    m01a = a4ES.enter_context(nc.sbuf_tensor("m01a", [128, 512], BF16))

                    colidx = 0
                    for t in range(NTT):
                        nch = SEL_NCH[t]
                        ci = t // 4
                        cols_t = list(range(colidx, colidx + nch))
                        colidx += nch
                        for ch in range(nch):
                            @blk.tensor
                            def _(e, t=t, ch=ch):
                                k.wait(e, "t", "pV", k.st["pV"].n)
                                k.inc(e.matmul(pE[:], qselT[:, t * 128:(t + 1) * 128],
                                               kselT[:, ch * 512:(ch + 1) * 512],
                                               start=True, stop=True), "pT")
                            tt_ = k.st["pT"].n

                            @blk.vector
                            def _(e, t=t, ch=ch, tt_=tt_, col=cols_t[ch]):
                                k.wait(e, "v", "pT", tt_)
                                k.wait(e, "v", "pV", k.st["pV"].n)
                                k.wait(e, "v", "pG", t_iota)
                                k.inc(e.tensor_scalar(g01[:], iotaJI[:], offsel_s[:, col:col + 1],
                                                      None, ALU.is_gt), "pV")
                                k.wait(e, "v", "pV", k.st["pV"].n)
                                k.inc(e.scalar_tensor_tensor(selbuf[:, ch * 512:(ch + 1) * 512],
                                                             g01[:], NEGBIG, pE[:],
                                                             ALU.mult, ALU.add), "pV")

                        W = nch * 512
                        for r in range(8):
                            @blk.vector
                            def _(e, r=r, W=W):
                                k.wait(e, "v", "pV", k.st["pV"].n)
                                src = selbuf if r == 0 else mrb[(r - 1) % 2]
                                k.inc(e.max(out=m8[:], in_=src[:, 0:W]), "pV")
                                if r < 7:
                                    k.wait(e, "v", "pV", k.st["pV"].n)
                                    k.inc(e.match_replace(out=mrb[r % 2][:, 0:W], in_to_replace=m8[:],
                                                          in_values=src[:, 0:W], imm_value=NEGBIG), "pV")
                        @blk.vector
                        def _(e):
                            k.wait(e, "v", "pV", k.st["pV"].n)
                            k.inc(e.tensor_scalar_max(thr[:], m8[:, 7:8], -1e29), "pV")

                        for ch in range(nch):
                            @blk.vector
                            def _(e, t=t, ch=ch, col=cols_t[ch]):
                                k.wait(e, "v", "pV", k.st["pV"].n)
                                k.inc(e.tensor_scalar(m01[:], selbuf[:, ch * 512:(ch + 1) * 512],
                                                      thr[:], None, ALU.is_ge), "pV")
                                k.inc(e.tensor_scalar(g01[:], iotaJI[:], offsel_s[:, col:col + 1],
                                                      None, ALU.is_equal), "pV")
                                k.wait(e, "v", "pV", k.st["pV"].n)
                                k.inc(e.tensor_max(m01[:], m01[:], g01[:]), "pV")
                                k.wait(e, "v", "pV", k.st["pV"].n)
                                k.inc(e.tensor_scalar(m01a[:], m01[:], -1.0, 3.3e28,
                                                      ALU.add, ALU.mult), "pV")
                            slot0 = (0 if ci == 0 else 8)
                            for j in range(4):
                                kt = 4 * ch + j
                                s_ = slot0 + kt
                                @blk.tensor
                                def _(e, j=j):
                                    k.wait(e, "t", "pV", k.st["pV"].n)
                                    k.inc(e.transpose(pT16[:, (j % 4) * 128:(j % 4) * 128 + 128],
                                                      m01a[:, j * 128:(j + 1) * 128], ident16[:]), "pT")
                                tt2 = k.st["pT"].n
                                @blk.vector
                                def _(e, j=j, s_=s_, t=t, tt2=tt2):
                                    k.wait(e, "v", "pT", tt2)
                                    k.inc(e.tensor_copy(
                                        m01T_buf[:, s_ * 512 + (t % 4) * 128:s_ * 512 + (t % 4) * 128 + 128],
                                        pT16[:, (j % 4) * 128:(j % 4) * 128 + 128]), "pV")

                a5ES = contextlib.ExitStack()
                with a5ES:
                    gw16 = a5ES.enter_context(nc.sbuf_tensor("gw16", [128, 16 * 128], BF16))
                    khb = [a5ES.enter_context(nc.sbuf_tensor(f"khb{i}", [128, T], BF16)) for i in range(2)]
                    vhb = [a5ES.enter_context(nc.sbuf_tensor(f"vhb{i}", [128, T], BF16)) for i in range(2)]
                    d0w_c = [a5ES.enter_context(nc.sbuf_tensor(f"d0wc{i}", [128, 512], F32)) for i in range(2)]
                    mc = [a5ES.enter_context(nc.sbuf_tensor(f"mc{i}", [128, 512], F32)) for i in range(4)]
                    zb = [a5ES.enter_context(nc.sbuf_tensor(f"zb{i}", [128, 512], F32)) for i in range(3)]
                    z2b = [a5ES.enter_context(nc.sbuf_tensor(f"z2b{i}", [128, 512], F32)) for i in range(3)]
                    pb16 = [a5ES.enter_context(nc.sbuf_tensor(f"pb16_{i}", [128, 512], BF16)) for i in range(3)]
                    rrow = a5ES.enter_context(nc.sbuf_tensor("rrow", [1, 512], F32))
                    rb = a5ES.enter_context(nc.sbuf_tensor("rb", [128, 512], F32))
                    ybuf = a5ES.enter_context(nc.sbuf_tensor("ybuf", [128, 512], F32))
                    yb16 = a5ES.enter_context(nc.sbuf_tensor("yb16", [128, 512], BF16))
                    tg = a5ES.enter_context(nc.sbuf_tensor("tg", [128, 512], F32))
                    ygslab = a5ES.enter_context(nc.sbuf_tensor("ygslab", [128, 512], BF16))

                    @blk.vector
                    def _(e):
                        k.join(e, "v")
                    tv_d0 = k.st["pV"].n

                    @blk.gpsimd
                    def _(e):
                        k.join(e, "g")
                        k.dma(e, "g", "Lgw", out=gw16[:].rearrange("p (h e) -> p h e", e=128),
                              in_=gate_w.ap().rearrange("h d e -> d h e"))
                    t_gw = k.st["Lgw"].n

                    NBK = 3
                    banksS = [pA, pB, pT32]
                    last_tanh = {0: 0, 1: 0, 2: 0}
                    last_mcread = {i: k.st["pV"].n for i in range(4)}
                    t_kvload = 0
                    for ci in range(2):
                        KT = QCH_KT[ci]
                        for h in range(HEADS):
                            hkv = h // 4
                            pT0 = k.st["pT"].n
                            pV0 = k.st["pV"].n
                            pS0 = k.st["pS"].n
                            pG0 = k.st["pG"].n
                            tt_mm = [pT0 + 3 * kt + 1 for kt in range(KT)]
                            tt_pv = pT0 + 3 * KT
                            tt_bc = tt_pv + 1
                            tt_g = tt_bc + 1
                            tv_z2 = [pV0 + kt + 1 for kt in range(KT)]
                            tv_r = pV0 + KT + 1
                            tv_y = pV0 + KT + 4
                            tv_yg = pV0 + KT + 5
                            t_mc = [pG0 + kt + 2 for kt in range(KT)]
                            t_tanh_l = [pS0 + 2 * kt + 1 for kt in range(KT)]
                            t_exp_l = [pS0 + 2 * kt + 2 for kt in range(KT)]
                            t_tg = pS0 + 2 * KT + 1
                            lyg_prev = k.st["Lyg"].n if "Lyg" in k.st else 0

                            @blk.gpsimd
                            def _(e, ci=ci, h=h, KT=KT, tv_z2=tv_z2,
                                  lmr=dict(last_mcread)):
                                k.wait(e, "g", "pG", k.st["pG"].n)
                                k.inc(e.tensor_scalar_mul(d0w_c[h % 2][:], iotaJI[:],
                                                          -SLOPES[h] / SOFT), "pG")
                                for kt in range(KT):
                                    s_ = (0 if ci == 0 else 8) + kt
                                    k.wait(e, "g", "pV",
                                           tv_z2[kt - 4] if kt >= 4 else lmr[kt % 4])
                                    if kt == 0:
                                        k.wait(e, "g", "pG", k.st["pG"].n)
                                    k.inc(e.tensor_add(mc[kt % 4][:], d0w_c[h % 2][:],
                                                       m01T_buf[:, s_ * 512:(s_ + 1) * 512]), "pG")
                            assert k.st["pG"].n == t_mc[KT - 1], (k.st["pG"].n,)
                            for kt in range(KT):
                                last_mcread[kt % 4] = tv_z2[kt]

                            @blk.scalar
                            def _(e, ci=ci, h=h, KT=KT, hkv=hkv, tt_mm=tt_mm,
                                  tv_z2=tv_z2, tt_g=tt_g, tv_yg=tv_yg,
                                  pT0=pT0, pV0=pV0, pS0=pS0, pG0=pG0):
                                if h % 4 == 0:
                                    k.wait(e, "s", "pT", pT0)
                                    k.wait(e, "s", "pV", pV0)
                                    k.wait(e, "s", "pS", pS0)
                                    k.wait(e, "s", "pG", pG0)
                                    k.wait(e, "s", "Lkst", k.st["Lkst"].n)
                                    k.wait(e, "s", "Lvst", k.st["Lvst"].n)
                                    k.dma(e, "s", f"Lkv{hkv % 2}", first=True, out=khb[hkv % 2][:],
                                          in_=kaT_d.ap()[hkv * 128:(hkv + 1) * 128, :])
                                    k.dma(e, "s", f"Lkv{hkv % 2}", first=False, out=vhb[hkv % 2][:],
                                          in_=vaT_d.ap()[hkv * 128:(hkv + 1) * 128, :])
                                for kt in range(KT):
                                    bank = kt % NBK
                                    psS = banksS[bank]
                                    jcol = kt if ci == 0 else 8 + kt
                                    k.wait(e, "s", "pS", k.st["pS"].n)
                                    k.wait(e, "s", "pT", tt_mm[kt])
                                    k.inc(e.activation(zb[bank][:], psS[:], AF.Tanh,
                                                       scale=float(ISQD / SOFT)), "pS")
                                    k.wait(e, "s", "pV", tv_z2[kt])
                                    k.inc(e.activation(pb16[bank][:], z2b[bank][:], AF.Exp,
                                                       scale=SOFT, bias=aab_s[:, h * 24 + jcol:h * 24 + jcol + 1]), "pS")
                                k.wait(e, "s", "pT", tt_g)
                                k.inc(e.activation(tg[:], pF[:], AF.Tanh,
                                                   scale=1.0, bias=gbh_s[:, h:h + 1]), "pS")
                                k.wait(e, "s", "pV", tv_yg)
                                k.dma(e, "s", "Lyg",
                                      out=ygall_d.ap()[h * 128:(h + 1) * 128, ci * 512:(ci + 1) * 512],
                                      in_=ygslab[:])
                            assert k.st["pS"].n == t_tg, (k.st["pS"].n, t_tg)
                            if h % 4 == 0:
                                t_kvload = k.st[f"Lkv{hkv % 2}"].n

                            @blk.tensor
                            def _(e, ci=ci, h=h, KT=KT, hkv=hkv, pV0=pV0,
                                  t_kvload=t_kvload, t_tanh_l=t_tanh_l, t_exp_l=t_exp_l,
                                  tv_r=tv_r, tv_y=tv_y,
                                  lts=dict(last_tanh)):
                                k.wait(e, "t", f"Lkv{hkv % 2}", t_kvload)
                                for kt in range(KT):
                                    bank = kt % NBK
                                    psS = banksS[bank]
                                    k.wait(e, "t", "pS",
                                           t_tanh_l[kt - NBK] if kt >= NBK else lts[bank])
                                    if kt == 1:
                                        k.wait(e, "t", "pV", pV0)
                                    k.inc(e.matmul(psS[:], khb[hkv % 2][:, kt * 128:(kt + 1) * 128],
                                                   qhT[:, h * NTOK + ci * 512:h * NTOK + ci * 512 + 512],
                                                   start=True, stop=True), "pT")
                                    if kt >= 1:
                                        k.wait(e, "t", "pS", t_exp_l[kt - 1])
                                        pbprev = pb16[(kt - 1) % NBK]
                                        k.inc(e.matmul(pD[0:1, 0:512], ones_col16[:], pbprev[:],
                                                       start=(kt - 1 == 0), stop=False), "pT")
                                        k.inc(e.matmul(pC[:], vhb[hkv % 2][:, (kt - 1) * 128:kt * 128],
                                                       pbprev[:], start=(kt - 1 == 0), stop=False), "pT")
                                k.wait(e, "t", "pS", t_exp_l[KT - 1])
                                pbprev = pb16[(KT - 1) % NBK]
                                k.inc(e.matmul(pD[0:1, 0:512], ones_col16[:], pbprev[:],
                                               start=(KT - 1 == 0), stop=True), "pT")
                                k.inc(e.matmul(pC[:], vhb[hkv % 2][:, (KT - 1) * 128:KT * 128],
                                               pbprev[:], start=(KT - 1 == 0), stop=True), "pT")
                                k.wait(e, "t", "pV", tv_r)
                                k.inc(e.matmul(pE[:], half_row[:], rrow[:], start=True, stop=True), "pT")
                                k.wait(e, "t", "pV", tv_y)
                                k.wait(e, "t", "Lgw", t_gw)
                                k.inc(e.matmul(pF[:], gw16[:, h * 128:(h + 1) * 128], yb16[:],
                                               start=True, stop=True), "pT")
                            assert k.st["pT"].n == tt_g, (k.st["pT"].n, tt_g)

                            @blk.vector
                            def _(e, ci=ci, h=h, KT=KT, tt_pv=tt_pv, tt_bc=tt_bc,
                                  t_tanh_l=t_tanh_l, t_tg=t_tg, lyg_prev=lyg_prev,
                                  t_mc=t_mc):
                                for kt in range(KT):
                                    bank = kt % NBK
                                    k.wait(e, "v", "pS", t_tanh_l[kt])
                                    k.wait(e, "v", "pG", t_mc[kt])
                                    k.inc(e.tensor_add(z2b[bank][:], zb[bank][:],
                                                       mc[kt % 4][:]), "pV")
                                k.wait(e, "v", "pT", tt_pv)
                                k.inc(e.reciprocal(rrow[:], pD[0:1, 0:512]), "pV")
                                k.wait(e, "v", "pT", tt_bc)
                                k.inc(e.tensor_copy(rb[:], pE[:]), "pV")
                                k.wait(e, "v", "pV", k.st["pV"].n)
                                k.inc(e.tensor_mul(ybuf[:], pC[:], rb[:]), "pV")
                                k.wait(e, "v", "pV", k.st["pV"].n)
                                k.inc(e.tensor_copy(yb16[:], ybuf[:]), "pV")
                                k.wait(e, "v", "pS", t_tg)
                                k.wait(e, "v", "Lyg", lyg_prev)
                                k.inc(e.scalar_tensor_tensor(ygslab[:], tg[:], 1.0, ybuf[:],
                                                             ALU.add, ALU.mult), "pV")
                            assert k.st["pV"].n == tv_yg, (k.st["pV"].n, tv_yg)

                            for kt in range(max(0, KT - NBK), KT):
                                last_tanh[kt % NBK] = t_tanh_l[kt]

                a6ES = contextlib.ExitStack()
                with a6ES:
                    wo16 = [a6ES.enter_context(nc.sbuf_tensor(f"wo16_{i}", [128, 16 * 512], BF16)) for i in range(2)]
                    ygl = a6ES.enter_context(nc.sbuf_tensor("ygl", [128, 16 * 128], BF16))
                    yslab = a6ES.enter_context(nc.sbuf_tensor("yslab", [128, 512], F32))
                    t_wo_used = [0, 0]
                    for dm in range(4):
                        @blk.gpsimd
                        def _(e, dm=dm):
                            k.join(e, "g") if dm < 2 else k.wait(e, "g", "pT", t_wo_used[dm % 2])
                            k.dma(e, "g", f"Lwo{dm % 2}",
                                  out=wo16[dm % 2][:].rearrange("p (c n) -> p c n", n=512),
                                  in_=w_o.ap().rearrange("(c p) o -> p c o", p=128)[:, :, dm * 512:(dm + 1) * 512])
                        t_wo = k.st[f"Lwo{dm % 2}"].n

                        for t in range(NTT):
                            @blk.scalar
                            def _(e, t=t):
                                k.join(e, "s")
                                k.wait(e, "s", "Lyg", k.st["Lyg"].n)
                                k.dma(e, "s", "Lygl",
                                      out=ygl[:].rearrange("p (h j) -> p h j", j=128),
                                      in_=ygall_d.ap().rearrange("(h p) tok -> p h tok", p=128)[:, :, t * 128:(t + 1) * 128])
                            t_ygl = k.st["Lygl"].n

                            @blk.tensor
                            def _(e, dm=dm, t=t, t_wo=t_wo, t_ygl=t_ygl):
                                k.wait(e, "t", f"Lwo{dm % 2}", t_wo)
                                k.wait(e, "t", "Lygl", t_ygl)
                                k.wait(e, "t", "pS", k.st["pS"].n)
                                ps = [pA, pB][t % 2]
                                k.inc(e.matmul(ps[:], ones_row[:], bo_s[0:1, dm * 512:(dm + 1) * 512],
                                               start=True, stop=False), "pT")
                                for hh in range(16):
                                    k.inc(e.matmul(ps[:], ygl[:, hh * 128:(hh + 1) * 128],
                                                   wo16[dm % 2][:, hh * 512:(hh + 1) * 512],
                                                   start=False, stop=(hh == 15)), "pT")
                            tt_ = k.st["pT"].n
                            t_wo_used[dm % 2] = tt_

                            @blk.scalar
                            def _(e, dm=dm, t=t, tt_=tt_):
                                k.wait(e, "s", "pS", k.st["pS"].n)
                                k.wait(e, "s", "Lyp", k.st["Lyp"].n if "Lyp" in k.st else 0)
                                k.wait(e, "s", "pT", tt_)
                                ps = [pA, pB][t % 2]
                                k.inc(e.activation(yslab[:], ps[:], AF.Identity), "pS")
                            tS_y = k.st["pS"].n

                            @blk.sync
                            def _(e, dm=dm, t=t, tS_y=tS_y):
                                k.wait(e, "sync", "pS", tS_y)
                                k.dma(e, "sync", "Lyp", out=yproj.ap()[t * 128:(t + 1) * 128, dm * 512:(dm + 1) * 512],
                                      in_=yslab[:])

            @blk.scalar
            def _(e):
                k.wait(e, "s", "Lyp", k.st["Lyp"].n)
            residual_pass(x_own, yproj, xres[0])

        hq_d = dint("hq_d", [DFF, NTOK], BF16)

        def ffn_phase(src_d, dst_d, tag):
            fES = contextlib.ExitStack()
            with fES:
                xqT = fES.enter_context(nc.sbuf_tensor(f"xqT_{tag}", [128, 16 * NTOK], BF16))
                am = fES.enter_context(nc.sbuf_tensor(f"am_{tag}", [128, 1], F32))
                s127 = fES.enter_context(nc.sbuf_tensor(f"s127_{tag}", [128, 1], F32))
                rs_t = fES.enter_context(nc.sbuf_tensor(f"rs_{tag}", [128, 8], F32))
                rs2_t = fES.enter_context(nc.sbuf_tensor(f"rs2_{tag}", [128, 8], F32))
                rs3_t = fES.enter_context(nc.sbuf_tensor(f"rs3_{tag}", [128, 8], F32))
                rsh_t = fES.enter_context(nc.sbuf_tensor(f"rsh_{tag}", [128, 8], F32))
                hq16 = fES.enter_context(nc.sbuf_tensor(f"hq16_{tag}", [128, 512], BF16))

                fqES = contextlib.ExitStack()
                with fqES:
                    xtile_f = fqES.enter_context(nc.sbuf_tensor(f"xtf_{tag}", [128, D], F32))
                    xsq_f = fqES.enter_context(nc.sbuf_tensor(f"xsqf_{tag}", [128, D], F32))
                    xn_f = fqES.enter_context(nc.sbuf_tensor(f"xnf_{tag}", [128, D], F32))
                    yq = fqES.enter_context(nc.sbuf_tensor(f"yq_{tag}", [128, D], F32))
                    yqi = fqES.enter_context(nc.sbuf_tensor(f"yqi_{tag}", [128, D], I32))
                    xq16 = fqES.enter_context(nc.sbuf_tensor(f"xq16_{tag}", [128, D], BF16))
                    s1 = fqES.enter_context(nc.sbuf_tensor(f"s1_{tag}", [128, 1], F32))
                    s2 = fqES.enter_context(nc.sbuf_tensor(f"s2_{tag}", [128, 1], F32))
                    nmr_f = fqES.enter_context(nc.sbuf_tensor(f"nmrf_{tag}", [128, 1], F32))
                    var_f = fqES.enter_context(nc.sbuf_tensor(f"varf_{tag}", [128, 8], F32))
                    mean_f = fqES.enter_context(nc.sbuf_tensor(f"meanf_{tag}", [128, 8], F32))
                    rstd_fa = fqES.enter_context(nc.sbuf_tensor(f"rstdfa_{tag}", [128, 8], F32))

                    for t in range(NTT):
                        ln_stats(src_d, t, xtile_f, xsq_f, s1, s2, var_f, mean_f, t)
                    batch_rsqrt(var_f, rstd_fa, NTT, EPS_LN)
                    for t in range(NTT):
                        tS = ln_apply(src_d, t, xtile_f, rstd_fa, mean_f, nmr_f, t, xn_f)
                        @blk.vector
                        def _(e, t=t, tS=tS):
                            k.wait(e, "v", "pS", tS)
                            k.wait(e, "v", "pV", k.st["pV"].n)
                            k.inc(e.reduce_max(am[:], xn_f[:], axis=AX.X, apply_absolute_value=True), "pV")
                            k.wait(e, "v", "pV", k.st["pV"].n)
                            k.inc(e.tensor_scalar_max(am[:], am[:], EPS_Q), "pV")
                            k.wait(e, "v", "pV", k.st["pV"].n)
                            k.inc(e.reciprocal(s127[:], am[:]), "pV")
                            k.wait(e, "v", "pV", k.st["pV"].n)
                            k.inc(e.tensor_scalar_mul(s127[:], s127[:], 127.0), "pV")
                            k.inc(e.tensor_scalar_mul(rs_t[:, t:t + 1], am[:], 1.0 / 127.0), "pV")
                            k.wait(e, "v", "pV", k.st["pV"].n)
                            k.inc(e.tensor_scalar_mul(rs2_t[:, t:t + 1], rs_t[:, t:t + 1], 0.5), "pV")
                            k.inc(e.tensor_mul(rs3_t[:, t:t + 1], rs_t[:, t:t + 1], rs_t[:, t:t + 1]), "pV")
                            k.wait(e, "v", "pV", k.st["pV"].n)
                            k.inc(e.tensor_scalar_mul(rs3_t[:, t:t + 1], rs3_t[:, t:t + 1], 0.5), "pV")
                            k.inc(e.tensor_scalar_mul(yq[:], xn_f[:], s127[:]), "pV")
                            k.wait(e, "v", "pV", k.st["pV"].n)
                            k.inc(e.tensor_scalar_add(yq[:], yq[:], RC), "pV")
                            k.wait(e, "v", "pV", k.st["pV"].n)
                            k.inc(e.tensor_copy(yqi[:], yq[:]), "pV")
                            k.wait(e, "v", "pV", k.st["pV"].n)
                            k.inc(e.tensor_scalar_add(xq16[:], yqi[:], -RC), "pV")
                        transposeN(xq16, [c * 128 for c in range(16)], xqT,
                                   [c * NTOK + t * 128 for c in range(16)], BF16, 16)

                f2ES = contextlib.ExitStack()
                with f2ES:
                    h16 = f2ES.enter_context(nc.sbuf_tensor(f"h16_{tag}", [128, 8 * DFF], BF16))
                    f3ES = contextlib.ExitStack()
                    with f3ES:
                        FW = 448
                        NFC = 13
                        wgb = [f3ES.enter_context(nc.sbuf_tensor(f"wgb{i}_{tag}", [128, 16 * FW], BF16)) for i in range(2)]
                        wub = [f3ES.enter_context(nc.sbuf_tensor(f"wub{i}_{tag}", [128, 16 * FW], BF16)) for i in range(2)]
                        tgb = f3ES.enter_context(nc.sbuf_tensor(f"tgb_{tag}", [128, FW], F32))
                        t_w_used = [0, 0]
                        prev_tanh = 0
                        prev_silu = 0
                        for fc in range(NFC):
                            fw = FW if fc < NFC - 1 else DFF - (NFC - 1) * FW
                            @blk.sync
                            def _(e, fc=fc, fw=fw):
                                k.join(e, "sync") if fc < 2 else k.wait(e, "sync", "pT", t_w_used[fc % 2])
                                k.wait(e, "sync", "cc", 3)
                                k.dma(e, "sync", f"Lwg{fc % 2}", first=True,
                                      out=wgb[fc % 2][:, 0:16 * fw].rearrange("p (c n) -> p c n", n=fw),
                                      in_=cc_wg_out.ap().rearrange("(c p) o -> p c o", p=128)[:, :, fc * FW:fc * FW + fw])
                                k.dma(e, "sync", f"Lwg{fc % 2}", first=False,
                                      out=wub[fc % 2][:, 0:16 * fw].rearrange("p (c n) -> p c n", n=fw),
                                      in_=cc_wu_out.ap().rearrange("(c p) o -> p c o", p=128)[:, :, fc * FW:fc * FW + fw])
                            t_w = k.st[f"Lwg{fc % 2}"].n

                            pT0 = k.st["pT"].n
                            pV0 = k.st["pV"].n
                            pS0 = k.st["pS"].n
                            tt_l = [pT0 + 32 * (t + 1) for t in range(NTT)]
                            ts_l = [pS0 + t + 1 for t in range(NTT)]
                            sv1_l = [pV0 + 2 * t + 1 for t in range(NTT)]
                            sv2_l = [pV0 + 2 * t + 2 for t in range(NTT)]

                            @blk.scalar
                            def _(e, fw=fw, tt_l=tt_l, sv2_l=sv2_l, prev_silu=prev_silu):
                                for t in range(NTT):
                                    k.wait(e, "s", "pS", k.st["pS"].n)
                                    k.wait(e, "s", "pT", tt_l[t])
                                    k.wait(e, "s", "pV", sv2_l[t - 1] if t >= 1 else prev_silu)
                                    k.inc(e.activation(tgb[:, 0:fw], pA[:, 0:fw], AF.Tanh,
                                                       scale=rs2_t[:, t:t + 1]), "pS")
                            assert k.st["pS"].n == ts_l[-1]

                            for t in range(NTT):
                                @blk.tensor
                                def _(e, fc=fc, fw=fw, t=t, t_w=t_w, ts_l=ts_l, sv2_l=sv2_l,
                                      prev_tanh=prev_tanh, prev_silu=prev_silu):
                                    k.wait(e, "t", f"Lwg{fc % 2}", t_w)
                                    k.wait(e, "t", "pV", sv2_l[t - 1] if t >= 1 else prev_silu)
                                    k.wait(e, "t", "pS", ts_l[t - 1] if t >= 1 else prev_tanh)
                                    for dd in range(16):
                                        k.inc(e.matmul(pA[:, 0:fw], xqT[:, dd * NTOK + t * 128:dd * NTOK + t * 128 + 128],
                                                       wgb[fc % 2][:, dd * fw:(dd + 1) * fw],
                                                       start=(dd == 0), stop=(dd == 15)), "pT")
                                    for dd in range(16):
                                        k.inc(e.matmul(pB[:, 0:fw], xqT[:, dd * NTOK + t * 128:dd * NTOK + t * 128 + 128],
                                                       wub[fc % 2][:, dd * fw:(dd + 1) * fw],
                                                       start=(dd == 0), stop=(dd == 15)), "pT")
                            assert k.st["pT"].n == tt_l[-1]
                            t_w_used[fc % 2] = tt_l[-1]

                            @blk.vector
                            def _(e, fc=fc, fw=fw, ts_l=ts_l):
                                for t in range(NTT):
                                    k.wait(e, "v", "pS", ts_l[t])
                                    k.wait(e, "v", "pV", k.st["pV"].n)
                                    k.inc(e.scalar_tensor_tensor(tgb[:, 0:fw], tgb[:, 0:fw], 1.0,
                                                                 pA[:, 0:fw], ALU.add, ALU.mult), "pV")
                                    k.wait(e, "v", "pV", k.st["pV"].n)
                                    k.inc(e.scalar_tensor_tensor(
                                        h16[:, t * DFF + fc * FW:t * DFF + fc * FW + fw],
                                        tgb[:, 0:fw], rs3_t[:, t:t + 1], pB[:, 0:fw],
                                        ALU.mult, ALU.mult), "pV")
                            assert k.st["pV"].n == sv2_l[-1]
                            prev_tanh = ts_l[-1]
                            prev_silu = sv2_l[-1]

                    f4ES = contextlib.ExitStack()
                    with f4ES:
                        yq2 = f4ES.enter_context(nc.sbuf_tensor(f"yq2_{tag}", [128, 512], F32))
                        yqi2 = f4ES.enter_context(nc.sbuf_tensor(f"yqi2_{tag}", [128, 512], I32))
                        hstage = f4ES.enter_context(nc.sbuf_tensor(f"hstage_{tag}", [128, 512], BF16))
                        for t in range(NTT):
                            @blk.vector
                            def _(e, t=t):
                                k.wait(e, "v", "pV", k.st["pV"].n)
                                k.inc(e.reduce_max(am[:], h16[:, t * DFF:(t + 1) * DFF],
                                                   axis=AX.X, apply_absolute_value=True), "pV")
                                k.wait(e, "v", "pV", k.st["pV"].n)
                                k.inc(e.tensor_scalar_max(am[:], am[:], EPS_Q), "pV")
                                k.wait(e, "v", "pV", k.st["pV"].n)
                                k.inc(e.reciprocal(s127[:], am[:]), "pV")
                                k.wait(e, "v", "pV", k.st["pV"].n)
                                k.inc(e.tensor_scalar_mul(s127[:], s127[:], 127.0), "pV")
                                k.inc(e.tensor_scalar_mul(rsh_t[:, t:t + 1], am[:], 1.0 / 127.0), "pV")
                            for fc in range(11):
                                fw = 512 if fc < 10 else DFF - 10 * 512
                                nblk = fw // 128
                                @blk.vector
                                def _(e, t=t, fc=fc, fw=fw):
                                    k.wait(e, "v", "pV", k.st["pV"].n)
                                    k.inc(e.tensor_scalar_mul(yq2[:, 0:fw],
                                                              h16[:, t * DFF + fc * 512:t * DFF + fc * 512 + fw],
                                                              s127[:]), "pV")
                                    k.wait(e, "v", "pV", k.st["pV"].n)
                                    k.inc(e.tensor_scalar_add(yq2[:, 0:fw], yq2[:, 0:fw], RC), "pV")
                                    k.wait(e, "v", "pV", k.st["pV"].n)
                                    k.inc(e.tensor_copy(yqi2[:, 0:fw], yq2[:, 0:fw]), "pV")
                                    k.wait(e, "v", "pV", k.st["pV"].n)
                                    k.inc(e.tensor_scalar_add(hq16[:, 0:fw], yqi2[:, 0:fw], -RC), "pV")
                                @blk.vector
                                def _(e):
                                    k.wait(e, "v", "Lhst", k.st["Lhst"].n if "Lhst" in k.st else 0)
                                transposeN(hq16, [j * 128 for j in range(nblk)], hstage,
                                           [j * 128 for j in range(nblk)], BF16, nblk)
                                tv_h = k.st["pV"].n
                                @blk.scalar
                                def _(e, t=t, fc=fc, nblk=nblk, tv_h=tv_h):
                                    k.wait(e, "s", "pV", tv_h)
                                    k.dma(e, "s", "Lhst",
                                          out=hq_d.ap().rearrange("(a p) tok -> p a tok", p=128)[:, fc * 4:fc * 4 + nblk, t * 128:(t + 1) * 128],
                                          in_=hstage[:, 0:nblk * 128].rearrange("p (a j) -> p a j", j=128))

                f5ES = contextlib.ExitStack()
                with f5ES:
                    wdb = f5ES.enter_context(nc.sbuf_tensor(f"wdb_{tag}", [128, 43 * 512], BF16))
                    hql = [f5ES.enter_context(nc.sbuf_tensor(f"hql{i}_{tag}", [128, 43 * 128], BF16)) for i in range(2)]
                    dslab = f5ES.enter_context(nc.sbuf_tensor(f"dslab_{tag}", [128, 512], F32))
                    t_hql_used = [0, 0]
                    prev_evac = [0, 0]
                    for dm in range(4):
                        @blk.sync
                        def _(e, dm=dm):
                            k.join(e, "sync")
                            k.wait(e, "sync", "cc", 4)
                            for ft in range(43):
                                r, sl = ft // 6, ft % 6
                                row = r * 768 + sl * 128
                                k.dma(e, "sync", "Lwd", first=(ft == 0),
                                      out=wdb[:, ft * 512:(ft + 1) * 512],
                                      in_=cc_wd_out.ap()[row:row + 128, dm * 512:(dm + 1) * 512])
                        t_wd = k.st["Lwd"].n

                        pT0 = k.st["pT"].n
                        pS0 = k.st["pS"].n
                        lyp0 = k.st["Lyp"].n if "Lyp" in k.st else 0
                        tt_l = [pT0 + 43 * (t + 1) for t in range(NTT)]
                        ts_l = [pS0 + t + 1 for t in range(NTT)]
                        lhq_tok = []

                        @blk.scalar
                        def _(e, dm=dm, tt_l=tt_l, lyp0=lyp0):
                            for t in range(NTT):
                                if dm == 0 and t < 2:
                                    k.join(e, "s")
                                else:
                                    k.wait(e, "s", "pT", t_hql_used[t % 2])
                                k.wait(e, "s", "Lhst", k.st["Lhst"].n)
                                lhq_tok.append(k.dma(e, "s", f"Lhq{t % 2}",
                                      out=hql[t % 2][:].rearrange("p (a j) -> p a j", j=128),
                                      in_=hq_d.ap().rearrange("(a p) tok -> p a tok", p=128)[:, :, t * 128:(t + 1) * 128]))
                                k.wait(e, "s", "pS", k.st["pS"].n)
                                k.wait(e, "s", "Lyp", lyp0 + 16 * t)
                                k.wait(e, "s", "pT", tt_l[t])
                                ps = [pA, pB][t % 2]
                                k.inc(e.activation(dslab[:], ps[:], AF.Identity,
                                                   scale=rsh_t[:, t:t + 1]), "pS")
                                t_hql_used[t % 2] = tt_l[t]
                        assert k.st["pS"].n == ts_l[-1]

                        for t in range(NTT):
                            @blk.tensor
                            def _(e, dm=dm, t=t, t_wd=t_wd, t_hq=lhq_tok[t], ts_l=ts_l,
                                  pe0=prev_evac[0], pe1=prev_evac[1]):
                                k.wait(e, "t", "Lwd", t_wd)
                                k.wait(e, "t", f"Lhq{t % 2}", t_hq)
                                k.wait(e, "t", "pS",
                                       ts_l[t - 2] if t >= 2 else (pe0 if t % 2 == 0 else pe1))
                                ps = [pA, pB][t % 2]
                                for ft in range(43):
                                    k.inc(e.matmul(ps[:], hql[t % 2][:, ft * 128:(ft + 1) * 128],
                                                   wdb[:, ft * 512:(ft + 1) * 512],
                                                   start=(ft == 0), stop=(ft == 42)), "pT")

                            @blk.sync
                            def _(e, dm=dm, t=t, ts_l=ts_l):
                                k.wait(e, "sync", "pS", ts_l[t])
                                k.dma(e, "sync", "Lyp", out=yproj.ap()[t * 128:(t + 1) * 128, dm * 512:(dm + 1) * 512],
                                      in_=dslab[:])
                        assert k.st["pT"].n == tt_l[-1]
                        prev_evac[0] = ts_l[NTT - 2]
                        prev_evac[1] = ts_l[NTT - 1]

            residual_pass(src_d, yproj, dst_d)

        if upto >= 2:
            ffn_phase(xres[0], xres[1], "f1")

        def attnb_phase(src_d, dst_d):
            kT_d = dint("kT_d", [D, T], BF16)
            vT_d = dint("vT_d", [D, T], BF16)
            yqT_d = dint("yqT_d", [D, NTOK], BF16)
            bES = contextlib.ExitStack()
            with bES:
                qbT = bES.enter_context(nc.sbuf_tensor("qbT", [128, 16 * NTOK], BF16))
                yF = bES.enter_context(nc.sbuf_tensor("yF", [128, 16 * NTOK], BF16))
                am_b = bES.enter_context(nc.sbuf_tensor("amb", [128, 1], F32))
                s127b = bES.enter_context(nc.sbuf_tensor("s127b", [128, 1], F32))
                rsx_t = bES.enter_context(nc.sbuf_tensor("rsx_t", [128, 8], F32))
                rsy_t = bES.enter_context(nc.sbuf_tensor("rsy_t", [128, 8], F32))
                yqb = bES.enter_context(nc.sbuf_tensor("yqb", [128, D], F32))
                yqib = bES.enter_context(nc.sbuf_tensor("yqib", [128, D], I32))
                xq16b = bES.enter_context(nc.sbuf_tensor("xq16b", [128, D], BF16))
                wkvb = [bES.enter_context(nc.sbuf_tensor(f"wkvb{i}", [128, 16 * 128], BF16)) for i in range(2)]

                b1ES = contextlib.ExitStack()
                with b1ES:
                    xtile_b = b1ES.enter_context(nc.sbuf_tensor("xtb", [128, D], F32))
                    xsq_b = b1ES.enter_context(nc.sbuf_tensor("xsqb", [128, D], F32))
                    xn_b = b1ES.enter_context(nc.sbuf_tensor("xnb", [128, D], F32))
                    s1b = b1ES.enter_context(nc.sbuf_tensor("s1b", [128, 1], F32))
                    s2b = b1ES.enter_context(nc.sbuf_tensor("s2b", [128, 1], F32))
                    nmr_b = b1ES.enter_context(nc.sbuf_tensor("nmrb", [128, 1], F32))
                    var_b = b1ES.enter_context(nc.sbuf_tensor("varb", [128, 8], F32))
                    mean_b = b1ES.enter_context(nc.sbuf_tensor("meanb", [128, 8], F32))
                    rstd_ba = b1ES.enter_context(nc.sbuf_tensor("rstdba", [128, 8], F32))
                    xqT2 = b1ES.enter_context(nc.sbuf_tensor("xqT2", [128, 16 * NTOK], BF16))

                    for t in range(NTT):
                        ln_stats(src_d, t, xtile_b, xsq_b, s1b, s2b, var_b, mean_b, t)
                    batch_rsqrt(var_b, rstd_ba, NTT, EPS_LN)
                    for t in range(NTT):
                        tS = ln_apply(src_d, t, xtile_b, rstd_ba, mean_b, nmr_b, t, xn_b)
                        @blk.vector
                        def _(e, t=t, tS=tS):
                            k.wait(e, "v", "pS", tS)
                            k.wait(e, "v", "pV", k.st["pV"].n)
                            k.inc(e.reduce_max(am_b[:], xn_b[:], axis=AX.X, apply_absolute_value=True), "pV")
                            k.wait(e, "v", "pV", k.st["pV"].n)
                            k.inc(e.tensor_scalar_max(am_b[:], am_b[:], EPS_Q), "pV")
                            k.wait(e, "v", "pV", k.st["pV"].n)
                            k.inc(e.reciprocal(s127b[:], am_b[:]), "pV")
                            k.wait(e, "v", "pV", k.st["pV"].n)
                            k.inc(e.tensor_scalar_mul(s127b[:], s127b[:], 127.0), "pV")
                            k.inc(e.tensor_scalar_mul(rsx_t[:, t:t + 1], am_b[:], 1.0 / 127.0), "pV")
                            k.wait(e, "v", "pV", k.st["pV"].n)
                            k.inc(e.tensor_scalar_mul(yqb[:], xn_b[:], s127b[:]), "pV")
                            k.wait(e, "v", "pV", k.st["pV"].n)
                            k.inc(e.tensor_scalar_add(yqb[:], yqb[:], RC), "pV")
                            k.wait(e, "v", "pV", k.st["pV"].n)
                            k.inc(e.tensor_copy(yqib[:], yqb[:]), "pV")
                            k.wait(e, "v", "pV", k.st["pV"].n)
                            k.inc(e.tensor_scalar(xq16b[:], yqib[:], -RC, rsx_t[:, t:t + 1],
                                                  ALU.add, ALU.mult), "pV")
                        transposeN(xq16b, [c * 128 for c in range(16)], xqT2,
                                   [c * NTOK + t * 128 for c in range(16)], BF16, 16)
                    tv_xq = k.st["pV"].n

                    @blk.sync
                    def _(e):
                        k.wait(e, "sync", "pV", tv_xq)
                        k.dma(e, "sync", "Lccx2",
                              out=cc_in_x2.ap().rearrange("(c p) tok -> p c tok", p=128),
                              in_=xqT2[:].rearrange("p (c tok) -> p c tok", tok=NTOK))
                    t_ccin2 = k.st["Lccx2"].n

                    @blk.gpsimd
                    def _(e):
                        k.wait(e, "g", "Lccx2", t_ccin2)
                        k.inc(e.collective_compute(
                            "AllGather", ALU.bypass,
                            replica_groups=[[0, 1], [2, 3], [4, 5], [6, 7]],
                            ins=[cc_in_x2.ap().opt()], outs=[cc_out_x2.ap().opt()]), "cc")
                    t_cc_x2 = k.st["cc"].n

                    t_wb_used = [0, 0]
                    for oc in range(16):
                        @blk.sync
                        def _(e, oc=oc):
                            k.join(e, "sync") if oc < 2 else k.wait(e, "sync", "pT", t_wb_used[oc % 2])
                            k.wait(e, "sync", "cc", 5)
                            k.dma(e, "sync", f"Lwb{oc % 2}",
                                  out=wkvb[oc % 2][:].rearrange("p (c n) -> p c n", n=128),
                                  in_=cc_wqkv_out.ap().rearrange("(c p) o -> p c o", p=128)[:, :, oc * 128:(oc + 1) * 128])
                        t_wb = k.st[f"Lwb{oc % 2}"].n

                        for tc in range(2):
                            @blk.tensor
                            def _(e, oc=oc, tc=tc, t_wb=t_wb):
                                k.wait(e, "t", f"Lwb{oc % 2}", t_wb)
                                k.wait(e, "t", "pV", k.st["pV"].n)
                                ps = [pA, pB][tc % 2]
                                for cd in range(16):
                                    k.inc(e.matmul(ps[:], wkvb[oc % 2][:, cd * 128:(cd + 1) * 128],
                                                   xqT2[:, cd * NTOK + tc * 512:cd * NTOK + tc * 512 + 512],
                                                   start=(cd == 0), stop=(cd == 15)), "pT")
                            tt_ = k.st["pT"].n
                            t_wb_used[oc % 2] = tt_

                            @blk.vector
                            def _(e, oc=oc, tc=tc, tt_=tt_):
                                k.wait(e, "v", "pT", tt_)
                                ps = [pA, pB][tc % 2]
                                k.inc(e.tensor_copy(qbT[:, oc * NTOK + tc * 512:oc * NTOK + tc * 512 + 512], ps[:]), "pV")

                b2ES = contextlib.ExitStack()
                with b2ES:
                    xqTall = b2ES.enter_context(nc.sbuf_tensor("xqTall", [128, 16 * T], BF16))
                    ktmp16 = b2ES.enter_context(nc.sbuf_tensor("ktmp16", [128, 512], BF16))
                    vstage = b2ES.enter_context(nc.sbuf_tensor("vstage", [128, 512], BF16))

                    @blk.gpsimd
                    def _(e):
                        k.wait(e, "g", "cc", t_cc_x2)
                        k.join(e, "g")
                        ccv = cc_out_x2.ap().rearrange("(r c p) tok -> r p c tok", p=128, c=16)
                        dst = xqTall[:].rearrange("p (c tok) -> p c tok", tok=T)
                        for bi, (r, co) in enumerate(((0, 0), (1, 0), (1, 512), (0, 512))):
                            k.dma(e, "g", "Lxqa", first=(bi == 0), out=dst[:, :, bi * 512:(bi + 1) * 512],
                                  in_=ccv[r, :, :, co:co + 512])
                    t_xqall = k.st["Lxqa"].n

                    t_wb_used = [0, 0]
                    for oc in range(16, 48):
                        @blk.sync
                        def _(e, oc=oc):
                            k.join(e, "sync") if oc < 18 else k.wait(e, "sync", "pT", t_wb_used[oc % 2])
                            k.dma(e, "sync", f"Lwb{oc % 2}",
                                  out=wkvb[oc % 2][:].rearrange("p (c n) -> p c n", n=128),
                                  in_=cc_wqkv_out.ap().rearrange("(c p) o -> p c o", p=128)[:, :, oc * 128:(oc + 1) * 128])
                        t_wb = k.st[f"Lwb{oc % 2}"].n

                        for tc in range(4):
                            @blk.tensor
                            def _(e, oc=oc, tc=tc, t_wb=t_wb):
                                k.wait(e, "t", f"Lwb{oc % 2}", t_wb)
                                k.wait(e, "t", "Lxqa", t_xqall)
                                k.wait(e, "t", "pV", k.st["pV"].n)
                                k.wait(e, "t", "pS", k.st["pS"].n)
                                ps = [pA, pB][tc % 2]
                                for cd in range(16):
                                    k.inc(e.matmul(ps[:], wkvb[oc % 2][:, cd * 128:(cd + 1) * 128],
                                                   xqTall[:, cd * T + tc * 512:cd * T + tc * 512 + 512],
                                                   start=(cd == 0), stop=(cd == 15)), "pT")
                            tt_ = k.st["pT"].n
                            t_wb_used[oc % 2] = tt_

                            if oc < 32:
                                @blk.vector
                                def _(e, tc=tc, tt_=tt_):
                                    k.wait(e, "v", "pT", tt_)
                                    k.wait(e, "v", "Lkst2", k.st["Lkst2"].n if "Lkst2" in k.st else 0)
                                    k.wait(e, "v", "Lvst2", k.st["Lvst2"].n if "Lvst2" in k.st else 0)
                                    ps = [pA, pB][tc % 2]
                                    k.inc(e.tensor_copy(ktmp16[:], ps[:]), "pV")
                                tv_ = k.st["pV"].n
                                @blk.scalar
                                def _(e, oc=oc, tc=tc, tv_=tv_):
                                    k.wait(e, "s", "pV", tv_)
                                    k.dma(e, "s", "Lkst2",
                                          out=kT_d.ap()[(oc - 16) * 128:(oc - 15) * 128, tc * 512:(tc + 1) * 512],
                                          in_=ktmp16[:])
                            else:
                                @blk.vector
                                def _(e, tc=tc, tt_=tt_):
                                    k.wait(e, "v", "pT", tt_)
                                    k.wait(e, "v", "pV", k.st["pV"].n)
                                    k.wait(e, "v", "Lkst2", k.st["Lkst2"].n if "Lkst2" in k.st else 0)
                                    ps = [pA, pB][tc % 2]
                                    k.inc(e.tensor_copy(ktmp16[:], ps[:]), "pV")
                                tv_ = k.st["pV"].n
                                for j in range(4):
                                    @blk.tensor
                                    def _(e, j=j, tv_=tv_):
                                        k.wait(e, "t", "pV", k.st["pV"].n)
                                        k.inc(e.transpose(pT16[:, (j % 4) * 128:(j % 4) * 128 + 128],
                                                          ktmp16[:, j * 128:(j + 1) * 128], ident16[:]), "pT")
                                    tt2 = k.st["pT"].n
                                    @blk.vector
                                    def _(e, j=j, tt2=tt2):
                                        k.wait(e, "v", "pT", tt2)
                                        k.wait(e, "v", "Lvst2", k.st["Lvst2"].n if "Lvst2" in k.st else 0)
                                        k.inc(e.tensor_copy(vstage[:, j * 128:(j + 1) * 128],
                                                            pT16[:, (j % 4) * 128:(j % 4) * 128 + 128]), "pV")
                                tv2_ = k.st["pV"].n
                                @blk.scalar
                                def _(e, oc=oc, tc=tc, tv2_=tv2_):
                                    k.wait(e, "s", "pV", tv2_)
                                    k.dma(e, "s", "Lvst2",
                                          out=vT_d.ap()[(oc - 32) * 128:(oc - 31) * 128, tc * 512:(tc + 1) * 512],
                                          in_=vstage[:])

                b3ES = contextlib.ExitStack()
                with b3ES:
                    nrmsc = b3ES.enter_context(nc.sbuf_tensor("nrmsc", [128, T], F32))
                    nqk = b3ES.enter_context(nc.sbuf_tensor("nqk", [128, 256], F32))
                    nred = b3ES.enter_context(nc.sbuf_tensor("nred", [128, 16], F32))
                    ntr = b3ES.enter_context(nc.sbuf_tensor("ntr", [16, 128], F32))
                    nqm = b3ES.enter_context(nc.sbuf_tensor("nqm", [16, 1], F32))
                    nkm = b3ES.enter_context(nc.sbuf_tensor("nkm", [16, 1], F32))
                    brow = b3ES.enter_context(nc.sbuf_tensor("brow", [1, 16], F32))
                    Bneg = b3ES.enter_context(nc.sbuf_tensor("Bneg", [128, 16], F32))
                    khb2 = [b3ES.enter_context(nc.sbuf_tensor(f"khb2_{i}", [128, T], BF16)) for i in range(2)]
                    vhb2 = [b3ES.enter_context(nc.sbuf_tensor(f"vhb2_{i}", [128, T], BF16)) for i in range(2)]
                    b01 = b3ES.enter_context(nc.sbuf_tensor("b01", [128, 16 * 512], BF16))
                    pb16b = [b3ES.enter_context(nc.sbuf_tensor(f"pb16b{i}", [128, 512], BF16)) for i in range(4)]
                    rrow_b = b3ES.enter_context(nc.sbuf_tensor("rrow_b", [1, 512], F32))
                    rb_b = b3ES.enter_context(nc.sbuf_tensor("rb_b", [128, 512], F32))

                    def norm_pass(is_k, out_nm):
                        nt = 16 if is_k else 8
                        for h in range(HEADS):
                            if is_k:
                                @blk.scalar
                                def _(e, h=h):
                                    k.join(e, "s")
                                    k.wait(e, "s", "Lkst2", k.st["Lkst2"].n)
                                    k.dma(e, "s", f"Lkv2{h % 2}", out=khb2[h % 2][:],
                                          in_=kT_d.ap()[h * 128:(h + 1) * 128, :])
                                tS0 = k.st[f"Lkv2{h % 2}"].n
                            else:
                                tS0 = 0
                            @blk.scalar
                            def _(e, h=h, tS0=tS0, is_k=is_k):
                                k.wait(e, "s", "pS", k.st["pS"].n)
                                k.wait(e, "s", "pT", k.st["pT"].n)
                                if is_k:
                                    k.wait(e, "s", f"Lkv2{h % 2}", tS0)
                                    k.inc(e.activation(nrmsc[:, 0:T], khb2[h % 2][:], AF.Square), "pS")
                                else:
                                    k.inc(e.activation(nrmsc[:, 0:NTOK],
                                                       qbT[:, h * NTOK:(h + 1) * NTOK], AF.Square), "pS")
                            tS2 = k.st["pS"].n
                            @blk.tensor
                            def _(e, h=h, nt=nt, tS2=tS2):
                                k.wait(e, "t", "pS", tS2)
                                k.wait(e, "t", "pG", t_iota)
                                if h == 0:
                                    k.wait(e, "t", "pV", k.st["pV"].n)
                                for tt in range(nt):
                                    k.inc(e.matmul(pD[0:128, h * nt + tt:h * nt + tt + 1],
                                                   nrmsc[:, tt * 128:(tt + 1) * 128], ones_col32[:],
                                                   start=True, stop=True), "pT")
                        tt_ = k.st["pT"].n
                        @blk.vector
                        def _(e, nt=nt, tt_=tt_):
                            k.wait(e, "v", "pT", tt_)
                            k.wait(e, "v", "pV", k.st["pV"].n)
                            k.inc(e.tensor_copy(nqk[:, 0:16 * nt], pD[:, 0:16 * nt]), "pV")
                            k.wait(e, "v", "pV", k.st["pV"].n)
                            k.inc(e.reduce_max(nred[:], nqk[:, 0:16 * nt].rearrange(
                                "p (h t) -> p h t", t=nt), axis=AX.X), "pV")
                        tv_ = k.st["pV"].n
                        @blk.tensor
                        def _(e, tv_=tv_):
                            k.wait(e, "t", "pV", tv_)
                            k.inc(e.matmul(pT32[0:16, 0:128], nred[:], ident32[:], start=True, stop=True), "pT")
                        tt2 = k.st["pT"].n
                        @blk.vector
                        def _(e, out_nm=out_nm, tt2=tt2):
                            k.wait(e, "v", "pT", tt2)
                            k.wait(e, "v", "pV", k.st["pV"].n)
                            k.inc(e.tensor_copy(ntr[:], pT32[0:16, 0:128]), "pV")
                            k.wait(e, "v", "pV", k.st["pV"].n)
                            k.inc(e.reduce_max(out_nm[:], ntr[:], axis=AX.X), "pV")

                    norm_pass(False, nqm)
                    norm_pass(True, nkm)

                    @blk.vector
                    def _(e):
                        k.wait(e, "v", "pV", k.st["pV"].n)
                        k.inc(e.tensor_mul(nqm[:], nqm[:], nkm[:]), "pV")
                    tv_p = k.st["pV"].n

                    @blk.scalar
                    def _(e, tv_p=tv_p):
                        k.wait(e, "s", "pV", tv_p)
                        k.wait(e, "s", "pS", k.st["pS"].n)
                        k.inc(e.activation(nkm[:], nqm[:], AF.Sqrt, scale=1.0 / 128.0), "pS")
                    tS_B = k.st["pS"].n

                    @blk.tensor
                    def _(e, tS_B=tS_B):
                        k.wait(e, "t", "pS", tS_B)
                        k.inc(e.matmul(pT32[0:1, 0:16], nkm[:], ident32[0:16, 0:16], start=True, stop=True), "pT")
                    tt_B = k.st["pT"].n

                    @blk.vector
                    def _(e, tt_B=tt_B):
                        k.wait(e, "v", "pT", tt_B)
                        k.wait(e, "v", "pV", k.st["pV"].n)
                        k.inc(e.tensor_scalar_mul(brow[:], pT32[0:1, 0:16], -1.0), "pV")
                    tv_br = k.st["pV"].n

                    @blk.tensor
                    def _(e, tv_br=tv_br):
                        k.wait(e, "t", "pV", tv_br)
                        k.inc(e.matmul(pE[:, 0:16], ones_row[:], brow[:], start=True, stop=True), "pT")
                    tt_Bn = k.st["pT"].n

                    @blk.vector
                    def _(e, tt_Bn=tt_Bn):
                        k.wait(e, "v", "pT", tt_Bn)
                        k.wait(e, "v", "pV", k.st["pV"].n)
                        k.inc(e.tensor_copy(Bneg[:], pE[:, 0:16]), "pV")
                        k.wait(e, "v", "pG", t_iota)
                        k.wait(e, "v", "pV", k.st["pV"].n)
                        for ci in range(2):
                            for mi in range(8):
                                col = (0 + mi) if ci == 0 else (16 + mi)
                                s_ = ci * 8 + mi
                                k.inc(e.tensor_scalar(b01[:, s_ * 512:(s_ + 1) * 512], iotaJI[:],
                                                      offb_s[:, col:col + 1], None, ALU.is_ge), "pV")
                    tv_b01 = k.st["pV"].n

                    NB4 = 4
                    banks4 = [pA, pB, pF, pT32]
                    last_expb = {0: 0, 1: 0, 2: 0, 3: 0}
                    last_maskb = {0: 0, 1: 0, 2: 0, 3: 0}
                    for ci in range(2):
                        KT = QCH_KT[ci]
                        for h in range(HEADS):
                            masked_l = [(ci == 0) or (kt >= 8) for kt in range(KT)]
                            s_l = [(kt if ci == 0 else 8 + (kt - 8)) if masked_l[kt] else 0
                                   for kt in range(KT)]
                            pT0 = k.st["pT"].n
                            pV0 = k.st["pV"].n
                            pS0 = k.st["pS"].n
                            tt_mm = [pT0 + 3 * kt + 1 for kt in range(KT)]
                            tt_pv = pT0 + 3 * KT
                            tt_bc = tt_pv + 1
                            t_exp_l = [pS0 + kt + 1 for kt in range(KT)]
                            tv_mask = []
                            mv = 0
                            for kt in range(KT):
                                if masked_l[kt]:
                                    mv += 1
                                tv_mask.append(pV0 + mv)
                            tv_r = pV0 + mv + 1
                            tv_y = pV0 + mv + 3

                            @blk.scalar
                            def _(e, ci=ci, h=h, KT=KT, tt_mm=tt_mm, tv_mask=tv_mask, pV0=pV0):
                                k.join(e, "s")
                                k.wait(e, "s", "Lvst2", k.st["Lvst2"].n)
                                k.dma(e, "s", f"Lkv2{h % 2}", first=True, out=khb2[h % 2][:],
                                      in_=kT_d.ap()[h * 128:(h + 1) * 128, :])
                                k.dma(e, "s", f"Lkv2{h % 2}", first=False, out=vhb2[h % 2][:],
                                      in_=vT_d.ap()[h * 128:(h + 1) * 128, :])
                                for kt in range(KT):
                                    bank = kt % NB4
                                    psS = banks4[bank]
                                    k.wait(e, "s", "pS", k.st["pS"].n)
                                    k.wait(e, "s", "pT", tt_mm[kt])
                                    k.wait(e, "s", "pV", tv_mask[kt - NB4] if kt >= NB4 else pV0)
                                    k.inc(e.activation(pb16b[bank][:], psS[:], AF.Exp,
                                                       scale=float(ISQD), bias=Bneg[:, h:h + 1]), "pS")
                            assert k.st["pS"].n == t_exp_l[KT - 1], (k.st["pS"].n,)
                            t_kv = k.st[f"Lkv2{h % 2}"].n

                            @blk.tensor
                            def _(e, ci=ci, h=h, KT=KT, t_kv=t_kv, pV0=pV0,
                                  t_exp_l=t_exp_l, tv_mask=tv_mask, tv_r=tv_r,
                                  les=dict(last_expb), lms=dict(last_maskb)):
                                k.wait(e, "t", f"Lkv2{h % 2}", t_kv)
                                for kt in range(KT):
                                    bank = kt % NB4
                                    psS = banks4[bank]
                                    k.wait(e, "t", "pS",
                                           t_exp_l[kt - NB4] if kt >= NB4 else les[bank])
                                    if kt < NB4:
                                        k.wait(e, "t", "pV", lms[bank])
                                    if kt == 1:
                                        k.wait(e, "t", "pV", pV0)
                                    k.inc(e.matmul(psS[:], khb2[h % 2][:, kt * 128:(kt + 1) * 128],
                                                   qbT[:, h * NTOK + ci * 512:h * NTOK + ci * 512 + 512],
                                                   start=True, stop=True), "pT")
                                    if kt >= 1:
                                        k.wait(e, "t", "pS", t_exp_l[kt - 1])
                                        k.wait(e, "t", "pV", tv_mask[kt - 1])
                                        pbprev = pb16b[(kt - 1) % NB4]
                                        k.inc(e.matmul(pD[0:1, 0:512], ones_col16[:], pbprev[:],
                                                       start=(kt - 1 == 0), stop=False), "pT")
                                        k.inc(e.matmul(pC[:], vhb2[h % 2][:, (kt - 1) * 128:kt * 128],
                                                       pbprev[:], start=(kt - 1 == 0), stop=False), "pT")
                                k.wait(e, "t", "pS", t_exp_l[KT - 1])
                                k.wait(e, "t", "pV", tv_mask[KT - 1])
                                pbprev = pb16b[(KT - 1) % NB4]
                                k.inc(e.matmul(pD[0:1, 0:512], ones_col16[:], pbprev[:],
                                               start=(KT - 1 == 0), stop=True), "pT")
                                k.inc(e.matmul(pC[:], vhb2[h % 2][:, (KT - 1) * 128:KT * 128],
                                               pbprev[:], start=(KT - 1 == 0), stop=True), "pT")
                                k.wait(e, "t", "pV", tv_r)
                                k.inc(e.matmul(pE[:], ones_row[:], rrow_b[:], start=True, stop=True), "pT")
                            assert k.st["pT"].n == tt_bc, (k.st["pT"].n, tt_bc)

                            @blk.vector
                            def _(e, ci=ci, h=h, KT=KT, masked_l=masked_l, s_l=s_l,
                                  t_exp_l=t_exp_l, tt_pv=tt_pv, tt_bc=tt_bc):
                                for kt in range(KT):
                                    if not masked_l[kt]:
                                        continue
                                    bank = kt % NB4
                                    k.wait(e, "v", "pS", t_exp_l[kt])
                                    k.wait(e, "v", "pV", tv_b01)
                                    k.inc(e.tensor_mul(pb16b[bank][:], pb16b[bank][:],
                                                       b01[:, s_l[kt] * 512:(s_l[kt] + 1) * 512]), "pV")
                                k.wait(e, "v", "pT", tt_pv)
                                k.inc(e.reciprocal(rrow_b[:], pD[0:1, 0:512]), "pV")
                                k.wait(e, "v", "pT", tt_bc)
                                k.inc(e.tensor_copy(rb_b[:], pE[:]), "pV")
                                k.wait(e, "v", "pV", k.st["pV"].n)
                                k.inc(e.tensor_mul(yF[:, h * NTOK + ci * 512:h * NTOK + ci * 512 + 512],
                                                   pC[:], rb_b[:]), "pV")
                            assert k.st["pV"].n == tv_y, (k.st["pV"].n, tv_y)

                            for kt in range(max(0, KT - NB4), KT):
                                last_expb[kt % NB4] = t_exp_l[kt]
                            for kt in range(KT):
                                if masked_l[kt]:
                                    last_maskb[kt % NB4] = tv_mask[kt]

                    ytok = b3ES.enter_context(nc.sbuf_tensor("ytok", [128, D], BF16))
                    ystage = b3ES.enter_context(nc.sbuf_tensor("ystage", [128, D], BF16))
                    for t in range(NTT):
                        @blk.vector
                        def _(e):
                            k.wait(e, "v", "pV", k.st["pV"].n)
                        transposeN(yF, [h * NTOK + t * 128 for h in range(16)], ytok,
                                   [h * 128 for h in range(16)], BF16, 16)
                        @blk.vector
                        def _(e, t=t):
                            k.wait(e, "v", "pV", k.st["pV"].n)
                            k.inc(e.reduce_max(am_b[:], ytok[:], axis=AX.X, apply_absolute_value=True), "pV")
                            k.wait(e, "v", "pV", k.st["pV"].n)
                            k.inc(e.tensor_scalar_max(am_b[:], am_b[:], EPS_Q), "pV")
                            k.wait(e, "v", "pV", k.st["pV"].n)
                            k.inc(e.reciprocal(s127b[:], am_b[:]), "pV")
                            k.wait(e, "v", "pV", k.st["pV"].n)
                            k.inc(e.tensor_scalar_mul(s127b[:], s127b[:], 127.0), "pV")
                            k.inc(e.tensor_scalar_mul(rsy_t[:, t:t + 1], am_b[:], 1.0 / 127.0), "pV")
                            k.wait(e, "v", "pV", k.st["pV"].n)
                            k.inc(e.tensor_scalar_mul(yqb[:], ytok[:], s127b[:]), "pV")
                            k.wait(e, "v", "pV", k.st["pV"].n)
                            k.inc(e.tensor_scalar_add(yqb[:], yqb[:], RC), "pV")
                            k.wait(e, "v", "pV", k.st["pV"].n)
                            k.inc(e.tensor_copy(yqib[:], yqb[:]), "pV")
                            k.wait(e, "v", "pV", k.st["pV"].n)
                            k.wait(e, "v", "Lyst", k.st["Lyst"].n if "Lyst" in k.st else 0)
                            k.inc(e.tensor_scalar_add(xq16b[:], yqib[:], -RC), "pV")
                        @blk.vector
                        def _(e):
                            k.wait(e, "v", "Lyst", k.st["Lyst"].n if "Lyst" in k.st else 0)
                        transposeN(xq16b, [c * 128 for c in range(16)], ystage,
                                   [c * 128 for c in range(16)], BF16, 16)
                        tv_y = k.st["pV"].n
                        @blk.scalar
                        def _(e, t=t, tv_y=tv_y):
                            k.wait(e, "s", "pV", tv_y)
                            k.dma(e, "s", "Lyst",
                                  out=yqT_d.ap().rearrange("(c p) tok -> p c tok", p=128)[:, :, t * 128:(t + 1) * 128],
                                  in_=ystage[:].rearrange("p (c j) -> p c j", j=128))

                b4ES = contextlib.ExitStack()
                with b4ES:
                    wpb = [b4ES.enter_context(nc.sbuf_tensor(f"wpb{i}", [128, 16 * 512], BF16)) for i in range(2)]
                    yql = [b4ES.enter_context(nc.sbuf_tensor(f"yql{i}", [128, 16 * 128], BF16)) for i in range(2)]
                    pslab = b4ES.enter_context(nc.sbuf_tensor("pslab", [128, 512], F32))
                    t_wp_used = [0, 0]
                    t_yql_used = [0, 0]
                    for dm in range(4):
                        @blk.sync
                        def _(e, dm=dm):
                            k.join(e, "sync") if dm < 2 else k.wait(e, "sync", "pT", t_wp_used[dm % 2])
                            k.wait(e, "sync", "cc", 6)
                            k.dma(e, "sync", f"Lwp{dm % 2}",
                                  out=wpb[dm % 2][:].rearrange("p (c n) -> p c n", n=512),
                                  in_=cc_wp_out.ap().rearrange("(c p) o -> p c o", p=128)[:, :, dm * 512:(dm + 1) * 512])
                        t_wp = k.st[f"Lwp{dm % 2}"].n

                        for t in range(NTT):
                            @blk.scalar
                            def _(e, t=t):
                                k.join(e, "s") if t < 2 else k.wait(e, "s", "pT", t_yql_used[t % 2])
                                k.wait(e, "s", "Lyst", k.st["Lyst"].n)
                                k.dma(e, "s", f"Lyql{t % 2}",
                                      out=yql[t % 2][:].rearrange("p (c j) -> p c j", j=128),
                                      in_=yqT_d.ap().rearrange("(c p) tok -> p c tok", p=128)[:, :, t * 128:(t + 1) * 128])
                            t_yq = k.st[f"Lyql{t % 2}"].n

                            @blk.tensor
                            def _(e, dm=dm, t=t, t_wp=t_wp, t_yq=t_yq):
                                k.wait(e, "t", f"Lwp{dm % 2}", t_wp)
                                k.wait(e, "t", f"Lyql{t % 2}", t_yq)
                                k.wait(e, "t", "pS", k.st["pS"].n)
                                ps = [pA, pB][t % 2]
                                for cd in range(16):
                                    k.inc(e.matmul(ps[:], yql[t % 2][:, cd * 128:(cd + 1) * 128],
                                                   wpb[dm % 2][:, cd * 512:(cd + 1) * 512],
                                                   start=(cd == 0), stop=(cd == 15)), "pT")
                            tt_ = k.st["pT"].n
                            t_wp_used[dm % 2] = tt_
                            t_yql_used[t % 2] = tt_

                            @blk.scalar
                            def _(e, dm=dm, t=t, tt_=tt_):
                                k.wait(e, "s", "pS", k.st["pS"].n)
                                k.wait(e, "s", "Lyp", k.st["Lyp"].n)
                                k.wait(e, "s", "pT", tt_)
                                ps = [pA, pB][t % 2]
                                k.inc(e.activation(pslab[:], ps[:], AF.Identity,
                                                   scale=rsy_t[:, t:t + 1]), "pS")
                            tS_y = k.st["pS"].n

                            @blk.sync
                            def _(e, dm=dm, t=t, tS_y=tS_y):
                                k.wait(e, "sync", "pS", tS_y)
                                k.dma(e, "sync", "Lyp", out=yproj.ap()[t * 128:(t + 1) * 128, dm * 512:(dm + 1) * 512],
                                      in_=pslab[:])

            residual_pass(src_d, yproj, dst_d)

        if upto >= 3:
            attnb_phase(xres[1], xres[0])
        if upto >= 4:
            ffn_phase(xres[0], xres[1], "f2")

        cur = {0: x_own, 1: xres[0], 2: xres[1], 3: xres[0]}.get(min(upto, 4), xres[1])
        @blk.sync
        def _(e):
            k.wait(e, "sync", "pV", k.st["pV"].n)
            k.wait(e, "sync", "pS", k.st["pS"].n)
            k.wait(e, "sync", "Lresw", k.st["Lresw"].n if "Lresw" in k.st else 0)
            k.dma(e, "sync", "Lout", out=out_t.ap(), in_=cur.ap())
            k.wait(e, "sync", "Lout", k.st["Lout"].n)

    return nc


_PROG_CACHE = {}


def _get_program(upto=99):
    if upto not in _PROG_CACHE:
        _PROG_CACHE[upto] = build_program(upto)
    return _PROG_CACHE[upto]


def _in_map_for_core(core, inp, shared, upto=99):
    x = inp["L0_attn_a_x"]
    qs = _pos_quarters(core)
    b = core // 2
    xsh = np.concatenate([x[b, q * 512:(q + 1) * 512, :] for q in qs], axis=0)
    t = _host_tables(core)
    wdp = shared["w_down_padded"]
    m = {
        "x_own": np.ascontiguousarray(xsh, np.float32),
        "w_q": inp["L0_attn_a_w_q"],
        "b_q": inp["L0_attn_a_b_q"].reshape(1, -1),
        "w_down_a": inp["L0_attn_a_w_down"],
        "b_down_r": np.ascontiguousarray(inp["L0_attn_a_b_down"].reshape(4, 128).T),
        "w_up_a": inp["L0_attn_a_w_up"],
        "b_up_r": np.ascontiguousarray(inp["L0_attn_a_b_up"].reshape(8, 128).T),
        "w_o": inp["L0_attn_a_w_o"],
        "b_o": inp["L0_attn_a_b_o"].reshape(1, -1),
        "qn_w": inp["L0_attn_a_qn_w"].reshape(1, -1),
        "kn_w": inp["L0_attn_a_kn_w"].reshape(1, -1),
        "gate_w": inp["L0_attn_a_gate_w"],
        "gate_b_r": np.ascontiguousarray(inp["L0_attn_a_gate_b"].T),
        "w_qkv": inp["L0_attn_b_w_qkv"],
        "w_proj": inp["L0_attn_b_w_proj"],
        "w_gate_f": inp["L0_ffn_w_gate"],
        "w_up_f": inp["L0_ffn_w_up"],
        "w_down_f": wdp,
        "rope_c": t["rope_c"], "rope_s": t["rope_s"],
        "rope_own_c": t["rope_own_c"], "rope_own_s": t["rope_own_s"],
        "offtab_sel": t["offtab_sel"], "offtab_b": t["offtab_b"],
        "aa_bias": t["aa_bias"],
        "ident32": shared["ident32"], "ident16": shared["ident16"],
        "ones_col16": shared["ones_col16"], "ones_row": shared["ones_row"],
        "half_row": shared["half_row"],
        "mdiv": np.array([[1.0 / (6144 * 2048), 1.0 / (2048 * 2048)]], np.float32),
        "w_gate_sl": np.ascontiguousarray(inp["L0_ffn_w_gate"][:, core * 256:(core + 1) * 256]),
        "w_up_sl": np.ascontiguousarray(inp["L0_ffn_w_up"][:, core * 256:(core + 1) * 256]),
        "w_down_sl": np.ascontiguousarray(wdp[:, core * 768:(core + 1) * 768]),
        "w_qkv_sl": np.ascontiguousarray(inp["L0_attn_b_w_qkv"][:, core * 256:(core + 1) * 256]),
        "w_proj_sl": np.ascontiguousarray(inp["L0_attn_b_w_proj"][:, core * 256:(core + 1) * 256]),
    }
    for kk in list(m):
        a = np.asarray(m[kk])
        if a.dtype == np.float64:
            m[kk] = a.astype(np.float32)
    return m


def kernel(**inputs):
    upto = int(inputs.pop("_upto", 99))
    shared = _shared_consts()
    shared["w_down_padded"] = np.ascontiguousarray(
        np.pad(np.asarray(inputs["L0_ffn_w_down"], np.float32), ((0, 0), (0, DFFP - DFF))))
    nc = _get_program(upto)
    in_maps = [_in_map_for_core(c, inputs, shared, upto) for c in range(8)]
    res = run_bass_kernel_spmd(nc, in_maps, core_ids=list(range(8)))
    out = np.zeros((B, T, D), np.float32)
    for c in range(8):
        o = res.results[c]["out"]
        b = c // 2
        for qi, q in enumerate(_pos_quarters(c)):
            out[b, q * 512:(q + 1) * 512, :] = o[qi * 512:(qi + 1) * 512, :]
    return out
